# version 01
# baseline (speedup 1.0000x reference)
import contextlib
from collections import defaultdict

import numpy as np
import concourse.bass as bass
import concourse.mybir as mybir
from concourse.bass_utils import run_bass_kernel_spmd

F32 = mybir.dt.float32
BF16 = mybir.dt.bfloat16
I16 = mybir.dt.int16
AF = mybir.ActivationFunctionType
ALU = mybir.AluOpType
AX = mybir.AxisListType

NCORES = 8
BPC = 2
T = 2048
NT = BPC * T
D = 1024
DC = D // 128
TJ = NT // 512

COMPUTE_ENGINES = ("tensor", "vector", "scalar", "gpsimd", "sync")


class Kfw:
    def __init__(self, nc):
        self.nc = nc
        self.es = contextlib.ExitStack()
        self.sems = {}
        for e in COMPUTE_ENGINES:
            self.sems[e] = self.es.enter_context(nc.semaphore(f"S_{e}"))
        self.dma_queues = {"qs": "sync", "qa": "scalar", "qg": "gpsimd"}
        self.cnt = defaultdict(int)
        self.waited = defaultdict(dict)
        self.regions = {}
        self.cmds = defaultdict(list)
        self.stream_consumers = defaultdict(list)

    def sb(self, name, shape, dtype):
        return self.es.enter_context(self.nc.sbuf_tensor(name, list(shape), dtype))

    def psum(self, name, shape, dtype=F32):
        return self.es.enter_context(self.nc.psum_tensor(name, list(shape), dtype))

    def dram(self, name, shape, dtype, kind="Internal"):
        return self.nc.dram_tensor(name, list(shape), dtype, kind=kind).ap()

    def _reg(self, key):
        r = self.regions.get(key)
        if r is None:
            r = {"w": {}, "r": {}}
            self.regions[key] = r
        return r

    def _dma_sem(self, queue, stream):
        name = f"{queue}:{stream}"
        if name not in self.sems:
            self.sems[name] = self.es.enter_context(
                self.nc.semaphore(f"S_{queue}_{stream}"))
        return name

    def _plan_waits(self, reads, writes):
        need = {}
        for key in reads:
            for s, v in self._reg(key)["w"].items():
                need[s] = max(need.get(s, 0), v)
        for key in writes:
            r = self._reg(key)
            for s, v in r["w"].items():
                need[s] = max(need.get(s, 0), v)
            for s, v in r["r"].items():
                need[s] = max(need.get(s, 0), v)
        for s in list(need):
            if ":" in s:
                need[s] = self.cnt[s]
        return sorted(need.items())

    def _mark(self, sem_name, val, reads, writes):
        for key in reads:
            r = self._reg(key)
            r["r"][sem_name] = max(r["r"].get(sem_name, 0), val)
        for key in writes:
            r = self._reg(key)
            r["w"] = {sem_name: val}
            r["r"] = {}

    def _note_stream_waits(self, need, my_sem, my_val):
        for s, _v in need:
            if ":" in s:
                self.stream_consumers[s].append((my_sem, my_val))

    def _forced_boundary(self, sname, need):
        pend = self.stream_consumers.pop(sname, None)
        if not pend:
            return []
        out = {}
        for cs, cv in pend:
            if any(s == cs and v >= cv for s, v in need):
                continue
            out[cs] = max(out.get(cs, 0), cv)
        return sorted(out.items())

    def _emit_wait(self, eng, engine_obj, sem_name, val):
        if val <= 0:
            return
        if self.waited[eng].get(sem_name, 0) >= val:
            return
        engine_obj.wait_ge(self.sems[sem_name], val)
        self.waited[eng][sem_name] = val

    def op(self, eng, fn, reads=(), writes=()):
        need = self._plan_waits(reads, writes)
        self.cnt[eng] += 1
        val = self.cnt[eng]
        self._note_stream_waits(need, eng, val)
        self._mark(eng, val, reads, writes)

        def closure(engine_obj, need=need, fn=fn, sem=self.sems[eng], eng=eng):
            for s_name, v in need:
                self._emit_wait(eng, engine_obj, s_name, v)
            last = fn(engine_obj)
            assert last is not None, f"op on {eng} returned None"
            last.then_inc(sem, 1)

        self.cmds[eng].append(closure)
        return val

    def dma(self, queue, out, in_, reads=(), writes=(), stream="d", **kw):
        eng = self.dma_queues[queue]
        sname = self._dma_sem(queue, stream)
        need = self._plan_waits(reads, writes)
        need = self._forced_boundary(sname, need) + need
        self.cnt[sname] += 16
        val = self.cnt[sname]
        self._note_stream_waits(need, sname, val)
        self._mark(sname, val, reads, writes)

        def closure(engine_obj, need=need, out=out, in_=in_, sem=self.sems[sname],
                    kw=kw, eng=eng):
            for s_name, v in need:
                self._emit_wait(eng, engine_obj, s_name, v)
            engine_obj.dma_start(out=out, in_=in_, **kw).then_inc(sem, 16)

        self.cmds[eng].append(closure)
        return val

    def qop(self, eng, fn, reads=(), writes=(), queue="qcc", stream="cc"):
        sname = self._dma_sem(queue, stream)
        need = self._plan_waits(reads, writes)
        need = self._forced_boundary(sname, need) + need
        self.cnt[sname] += 16
        val = self.cnt[sname]
        self._note_stream_waits(need, sname, val)
        self._mark(sname, val, reads, writes)

        def closure(engine_obj, need=need, fn=fn, sem=self.sems[sname], eng=eng):
            for s_name, v in need:
                self._emit_wait(eng, engine_obj, s_name, v)
            fn(engine_obj).then_inc(sem, 16)

        self.cmds[eng].append(closure)
        return val

    def fence(self, eng, reads=(), writes=()):
        return self.op(eng, lambda e: e.nop(), reads=reads, writes=writes)

    def wait_all(self, eng):
        snap = [(s, self.cnt[s]) for s in list(self.cnt) if self.cnt[s] > 0]

        def closure(engine_obj, snap=snap, eng=eng):
            for s_name, v in snap:
                self._emit_wait(eng, engine_obj, s_name, v)

        self.cmds[eng].append(closure)

    def build(self):
        self.wait_all("sync")
        with self.nc.Block() as blk:
            for eng in COMPUTE_ENGINES:
                cmds = self.cmds[eng]
                if not cmds:
                    continue

                def runner(engine_obj, cmds=cmds):
                    for f in cmds:
                        f(engine_obj)

                getattr(blk, eng)(runner)
        self.es.close()


class Bufs:
    def __init__(self, k: Kfw):
        self.big = [k.sb(f"BIG{i}", [128, 32768], BF16) for i in (1, 2, 3)]
        self.aux = k.sb("AUX", [128, 4096], BF16)
        self.misc = self.aux[:, 1024:4096].bitcast(F32)
        self.scr = k.sb("SCR", [128, 2048], BF16)
        self.pp = [k.psum(f"PP{i}", [128, 512], F32) for i in range(8)]

    def bigv(self, i, c):
        return self.big[i][:, c * 4096:(c + 1) * 4096]

    def bigf32(self, i, c):
        return self.big[i][:, :].bitcast(F32)[:, c * 2048:(c + 1) * 2048]


def bkey(i, c):
    return f"B{i}:{c}"


def emit_tri_le(k, C, dst_ap, wkeys):
    k.op("vector", lambda v: v.tensor_scalar(dst_ap, C.iota16[:, 0:128],
                                             C.chan[:], None, ALU.is_ge),
         reads=["c_iota", "c_chan"], writes=list(wkeys))


BKALL = lambda i: [bkey(i, c) for c in range(8)]


class Consts:
    def __init__(self, k: Kfw):
        self.ones_col = k.sb("c_ones_col", [128, 1], F32)
        self.ones_row = k.sb("c_ones_row", [1, 128], F32)
        self.iota16 = k.sb("c_iota16", [128, 512], I16)
        self.chan = k.sb("c_chan", [128, 1], F32)
        self.ident = k.sb("c_ident", [128, 128], BF16)
        self.sel8 = k.sb("c_sel8", [40, 8 * 128], BF16)
        self.ones_bf = k.sb("c_ones_bf", [128, 1], BF16)
        k.op("gpsimd", lambda g: g.iota(self.iota16[:], [[1, 512]], channel_multiplier=0,
                                        allow_small_or_imprecise_dtypes=True),
             writes=["c_iota"])
        k.op("gpsimd", lambda g: g.iota(self.chan[:], [[0, 1]], channel_multiplier=1,
                                        allow_small_or_imprecise_dtypes=True),
             writes=["c_chan"])
        k.op("vector", lambda v: v.memset(self.ones_col[:], 1.0), writes=["c_onesc"])
        k.op("vector", lambda v: v.memset(self.ones_bf[:], 1.0), writes=["c_onesb"])
        k.op("vector", lambda v: v.memset(self.ones_row[:], 1.0), writes=["c_onesr"])
        k.op("vector", lambda v: v.tensor_scalar(self.ident[:], self.iota16[:, 0:128],
                                                 self.chan[:], None, ALU.is_equal),
             reads=["c_iota", "c_chan"], writes=["c_ident"])

        def mk_sel8(v):
            for e in range(8):
                v.tensor_scalar(self.sel8[0:8, e * 128:(e + 1) * 128],
                                self.chan[0:8, 0:1].broadcast_to((8, 128)),
                                float(e), None, ALU.is_equal)
                last = v.tensor_scalar(self.sel8[32:40, e * 128:(e + 1) * 128],
                                       self.chan[32:40, 0:1].broadcast_to((8, 128)),
                                       float(32 + e), None, ALU.is_equal)
            return last

        k.op("vector", mk_sel8, reads=["c_chan"], writes=["c_sel8"])


def emit_ln(k: Kfw, C: Consts, bf: Bufs, src_dram, src_keys, dst_i, tag,
            hold_i=1, dram_out=None):
    misc = bf.misc
    m_row = misc[0:1, 0:512]
    v_row = misc[0:1, 512:1024]
    b_row = misc[0:1, 1024:1536]
    PSTAT, PSTATQ, PB0, PB1 = bf.pp[5], bf.pp[4], bf.pp[6], bf.pp[7]
    holdf32 = bf.big[hold_i][:, :].bitcast(F32)
    scrf = bf.scr[:, :].bitcast(F32)

    hk = [f"lnh_{c}" for c in range(DC)]
    k.fence("sync", reads=[], writes=[bkey(hold_i, cb) for cb in range(8)] + hk +
            ["pp4", "pp5", "pp6", "pp7", "pp5s", "pp4q", "lnsq", "lnsq2",
             "ln_m", "ln_v", "ln_b"])
    for j in range(TJ):
        for c in range(DC):
            tile = holdf32[:, c * 512:(c + 1) * 512]
            k.dma("qs", tile, src_dram[c * 128:(c + 1) * 128, j * 512:(j + 1) * 512],
                  reads=[src_keys(c, j)], writes=[hk[c]], stream=f"ln{c}")
            k.op("scalar", lambda s, tile=tile: s.activation(scrf[:, 0:512], tile, AF.Square),
                 reads=[hk[c]], writes=["lnsq", "SCR:0", "SCR:1"])
            k.op("tensor", lambda t, tile=tile, c=c: t.matmul(
                PSTAT[0:1, :], C.ones_col[:], tile, start=(c == 0), stop=(c == DC - 1)),
                 reads=[hk[c], "c_onesc"], writes=["pp5s"])
            k.op("tensor", lambda t, c=c: t.matmul(
                PSTATQ[0:1, :], C.ones_col[:], scrf[:, 0:512], start=(c == 0),
                stop=(c == DC - 1)),
                 reads=["lnsq", "SCR:0", "SCR:1", "c_onesc"], writes=["pp4q"])
        k.op("vector", lambda v: v.tensor_scalar(m_row, PSTAT[0:1, :], 1.0 / D, None,
                                                 ALU.mult),
             reads=["pp5s"], writes=["ln_m", "AUX:1"])
        k.op("vector", lambda v: v.tensor_scalar(v_row, PSTATQ[0:1, :], 1.0 / D, None,
                                                 ALU.mult),
             reads=["pp4q"], writes=["ln_v", "AUX:2"])
        k.op("vector", lambda v: v.scalar_tensor_tensor(b_row, m_row, -1.0, m_row,
                                                        ALU.mult, ALU.mult),
             reads=["ln_m", "AUX:1"], writes=["ln_b", "AUX:3"])
        k.op("vector", lambda v: v.tensor_add(v_row, v_row, b_row),
             reads=["ln_v", "ln_b", "AUX:3"], writes=["ln_v", "AUX:2"])
        k.op("vector", lambda v: v.tensor_scalar(v_row, v_row, 0.0, 1e-5,
                                                 ALU.max, ALU.add),
             reads=["ln_v"], writes=["ln_v", "AUX:2"])
        k.op("scalar", lambda s: s.activation(v_row, v_row, AF.Ln),
             reads=["ln_v"], writes=["ln_v", "AUX:2"])
        k.op("scalar", lambda s: s.activation(v_row, v_row, AF.Exp, scale=-0.5),
             reads=["ln_v"], writes=["ln_v", "AUX:2"])
        k.op("vector", lambda v: v.scalar_tensor_tensor(b_row, m_row, -1.0, v_row,
                                                        ALU.mult, ALU.mult),
             reads=["ln_m", "ln_v", "AUX:1", "AUX:2"], writes=["ln_b", "AUX:3"])
        k.op("tensor", lambda t: t.matmul(PB0[:], C.ones_row[:], v_row,
                                          start=True, stop=True),
             reads=["ln_v", "AUX:2", "c_onesr"], writes=["pp6"])
        k.op("tensor", lambda t: t.matmul(PB1[:], C.ones_row[:], b_row,
                                          start=True, stop=True),
             reads=["ln_b", "AUX:3", "c_onesr"], writes=["pp7"])
        for c in range(DC):
            tile = holdf32[:, c * 512:(c + 1) * 512]
            k.op("vector", lambda v, tile=tile: v.tensor_tensor(
                scrf[:, 512:1024], tile, PB0[:], ALU.mult),
                 reads=[hk[c], "pp6"], writes=["lnsq2", "SCR:2", "SCR:3"])
            k.op("vector", lambda v, c=c, j=j: v.tensor_tensor(
                bf.bigv(dst_i, c)[:, j * 512:(j + 1) * 512], scrf[:, 512:1024],
                PB1[:], ALU.add),
                 reads=["lnsq2", "SCR:2", "SCR:3", "pp7"], writes=[bkey(dst_i, c)])
            if dram_out is not None:
                k.dma("qs", dram_out[c * 128:(c + 1) * 128, j * 512:(j + 1) * 512],
                      bf.bigv(dst_i, c)[:, j * 512:(j + 1) * 512],
                      reads=[bkey(dst_i, c)], writes=["xnd"], stream=f"xno{c % 2}")
    k.fence("sync", reads=hk, writes=[bkey(hold_i, cb) for cb in range(8)])


def _t(a):
    return np.ascontiguousarray(np.asarray(a, dtype=np.float32).T)


def _c(a):
    return np.ascontiguousarray(np.asarray(a, dtype=np.float32))


def _col128(v):
    v = np.asarray(v, dtype=np.float32)
    return np.ascontiguousarray(v.reshape(-1, 128).T)


A0, A1, A2, A3 = "AUX:0", "AUX:1", "AUX:2", "AUX:3"
S0, S1, S2, S3 = "SCR:0", "SCR:1", "SCR:2", "SCR:3"


def emit_l0_attn(k: Kfw, C: Consts, bf: Bufs, ins, dst_dram, dst_key, tag="a0"):
    aux = bf.aux
    scr = bf.scr
    scrf = scr[:, :].bitcast(F32)
    vtd = k.dram(f"{tag}_vtd", [NT, D], BF16)
    ktd = k.dram(f"{tag}_ktd", [NT, D], BF16)
    PP = bf.pp
    H, DH, NCH = 16, 64, T // 128

    for which, wd, outd, do_phi in (("v", ins["l0a_wvT"], vtd, False),
                                    ("kt", ins["l0a_wkT"], ktd, True)):
        for nb in range(2):
            k.dma("qg", aux[:, :].rearrange("p (kc m) -> p kc m", kc=8),
                  wd[:, nb * 512:(nb + 1) * 512].rearrange("(kc p) m -> p kc m", p=128),
                  writes=[A0, A1, A2, A3], stream="wtok")
            for tc in range(NT // 128):
                ppi = tc % 2
                def mm(t, tc=tc, ppi=ppi):
                    for kc in range(DC):
                        last = t.matmul(PP[ppi][:], bf.bigv(0, kc)[:, tc * 128:(tc + 1) * 128],
                                        aux[:, kc * 512:(kc + 1) * 512],
                                        start=(kc == 0), stop=(kc == DC - 1))
                    return last
                k.op("tensor", mm, reads=[A0, A1, A2, A3] + BKALL(0), writes=[f"pp{ppi}"])
                st_bf = scr[:, ppi * 512:ppi * 512 + 512]
                stk = f"SCR:{ppi}"
                if do_phi:
                    est = scr[:, 1024 + ppi * 512:1024 + (ppi + 1) * 512]
                    estk = f"SCR:{2 + ppi}"
                    k.op("vector", lambda v, ppi=ppi: v.tensor_scalar_min(
                        PP[4][:], PP[ppi][:], 0.0), reads=[f"pp{ppi}"], writes=["pp4"])
                    k.op("scalar", lambda s, est=est: s.activation(est, PP[4][:], AF.Exp),
                         reads=["pp4"], writes=[estk])
                    k.op("vector", lambda v, st_bf=st_bf, ppi=ppi, est=est:
                         v.scalar_tensor_tensor(
                         st_bf, PP[ppi][:], 0.0, est, ALU.max, ALU.add),
                         reads=[f"pp{ppi}", estk], writes=[stk])
                else:
                    k.op("vector", lambda v, st_bf=st_bf, ppi=ppi: v.tensor_copy(
                        st_bf, PP[ppi][:]), reads=[f"pp{ppi}"], writes=[stk])
                k.dma("qs", outd[tc * 128:(tc + 1) * 128, nb * 512:(nb + 1) * 512],
                      st_bf, reads=[stk], writes=[f"{tag}{which}d"],
                      stream=f"st{ppi}")

    for dst_i, wd in ((1, ins["l0a_wqT"]), (2, ins["l0a_wkT"])):
        for oc in range(DC):
            sl = oc % 2
            slab = scr[:, sl * 1024:(sl + 1) * 1024]
            slks = [f"SCR:{2 * sl}", f"SCR:{2 * sl + 1}"]
            k.dma("qg", slab.rearrange("p (kc m) -> p kc m", kc=8),
                  wd[:, oc * 128:(oc + 1) * 128].rearrange("(kc p) m -> p kc m", p=128),
                  writes=slks, stream=f"w{sl}")
            for j in range(TJ):
                ppi = j % 2
                def mm(t, slab=slab, j=j, ppi=ppi):
                    for kc in range(DC):
                        last = t.matmul(PP[ppi][:], slab[:, kc * 128:(kc + 1) * 128],
                                        bf.bigv(0, kc)[:, j * 512:(j + 1) * 512],
                                        start=(kc == 0), stop=(kc == DC - 1))
                    return last
                k.op("tensor", mm, reads=slks + BKALL(0), writes=[f"pp{ppi}"])
                est = aux[:, ppi * 1024:ppi * 1024 + 512]
                estk = ["AUX:0", "AUX:1"][ppi]
                k.op("vector", lambda v, ppi=ppi: v.tensor_scalar_min(
                    PP[4][:], PP[ppi][:], 0.0), reads=[f"pp{ppi}"], writes=["pp4"])
                k.op("scalar", lambda s, est=est: s.activation(est, PP[4][:], AF.Exp),
                     reads=["pp4"], writes=[estk])
                wkeys = [f"pq{dst_i}/{oc}/{hf}/{j}" for hf in range(2)]
                k.op("vector", lambda v, dst_i=dst_i, oc=oc, j=j, ppi=ppi, est=est:
                     v.scalar_tensor_tensor(
                         bf.bigv(dst_i, oc)[:, j * 512:(j + 1) * 512],
                         PP[ppi][:], 0.0, est, ALU.max, ALU.add),
                     reads=[f"pp{ppi}", estk, bkey(dst_i, oc)], writes=wkeys)

    emit_tri_le(k, C, scr[:, 1280:1408], [S2])
    misc2 = aux[:, 2048:4096].bitcast(F32)
    den_row = misc2[0:1, 320:448]
    recip_row = misc2[0:1, 192:320]
    bo_sb = misc2[:, 448:456]
    at_sb = scr[:, 512:640]

    for b in range(BPC):
        for h in range(H):
            hp, half = h // 2, h % 2
            rb = half * 64
            pqv = bf.bigv(1, hp)
            pkv = bf.bigv(2, hp)
            S_f = misc2[rb:rb + 64, 0:65]
            S_bf = aux[rb:rb + 64, 2304:2369]
            rb_sb = scr[rb:rb + 64, 768:896]
            k.dma("qs", aux[:, 0:1024].rearrange("p (i f) -> p i f", i=16),
                  vtd[b * T:(b + 1) * T, h * DH:(h + 1) * DH]
                  .rearrange("(i p) f -> p i f", p=128),
                  reads=[f"{tag}vd"], writes=[A0], stream="vt")
            k.dma("qs", aux[:, 1024:2048].rearrange("p (i f) -> p i f", i=16),
                  ktd[b * T:(b + 1) * T, h * DH:(h + 1) * DH]
                  .rearrange("(i p) f -> p i f", p=128),
                  reads=[f"{tag}ktd"], writes=[A1], stream="kt")
            k.op("vector", lambda v, S_f=S_f: v.memset(S_f, 0.0), writes=[A2])
            k.op("vector", lambda v, S_bf=S_bf: v.memset(S_bf, 0.0), writes=[A2])
            for i in range(NCH):
                s0 = b * T + i * 128
                pa, pak = PP[i % 2], f"pp{i % 2}"
                pn, pnk = PP[2 + i % 2], f"pp{2 + i % 2}"
                qk = f"pq1/{hp}/{half}/{b * 4 + i // 4}"
                kk = f"pq2/{hp}/{half}/{b * 4 + i // 4}"
                k.op("tensor", lambda t, pa=pa, s0=s0, pqv=pqv, pkv=pkv, rb=rb:
                     t.matmul(pa[:, 0:128], pkv[rb:rb + 64, s0:s0 + 128],
                              pqv[rb:rb + 64, s0:s0 + 128], start=True, stop=True),
                     reads=[qk, kk], writes=[pak])
                k.op("vector", lambda v, pa=pa: v.tensor_tensor(
                    at_sb, pa[:, 0:128], scr[:, 1280:1408], ALU.mult),
                     reads=[pak, S2], writes=[S1])
                def num_mm(t, pn=pn, i=i, s0=s0, pqv=pqv, rb=rb, S_bf=S_bf):
                    t.matmul(pn[rb:rb + 64, 0:128], aux[:, i * 64:(i + 1) * 64], at_sb,
                             start=True, stop=False, tile_position=(0, rb))
                    t.matmul(pn[rb:rb + 64, 0:128], S_bf[:, 0:64],
                             pqv[rb:rb + 64, s0:s0 + 128], start=False, stop=True,
                             tile_position=(rb, rb))
                    t.matmul(PP[6][0:1, 0:128], C.ones_bf[:, 0:1], at_sb,
                             start=True, stop=False)
                    return t.matmul(PP[6][0:1, 0:128], S_bf[:, 64:65],
                                    pqv[rb:rb + 64, s0:s0 + 128], start=False, stop=True)
                k.op("tensor", num_mm, reads=[S1, A0, A2, qk, "c_onesb"],
                     writes=[pnk, "pp6"])
                def s_mm(t, i=i, rb=rb):
                    t.matmul(PP[5][rb:rb + 64, 0:64],
                             aux[:, 1024 + i * 64:1024 + (i + 1) * 64],
                             aux[:, i * 64:(i + 1) * 64], start=True, stop=True,
                             tile_position=(0, rb))
                    return t.matmul(PP[5][rb:rb + 64, 64:65],
                                    aux[:, 1024 + i * 64:1024 + (i + 1) * 64],
                                    C.ones_bf[:, 0:1], start=True, stop=True,
                                    tile_position=(0, rb))
                k.op("tensor", s_mm, reads=[A0, A1, "c_onesb"], writes=["pp5"])
                k.op("vector", lambda v: v.tensor_scalar_add(
                    den_row, PP[6][0:1, 0:128], 1e-5), reads=["pp6"], writes=[A2])
                k.op("vector", lambda v: v.reciprocal(recip_row, den_row),
                     reads=[A2], writes=[A2])
                k.op("tensor", lambda t, rb=rb: t.matmul(
                    PP[4][rb:rb + 64, 0:128], C.ones_row[:, 0:64], recip_row,
                    start=True, stop=True, tile_position=(0, rb)),
                     reads=[A2, "c_onesr"], writes=["pp4"])
                k.op("vector", lambda v, rb_sb=rb_sb, rb=rb: v.tensor_copy(
                    rb_sb, PP[4][rb:rb + 64, 0:128]), reads=["pp4"], writes=[S1])
                k.op("vector", lambda v, pn=pn, s0=s0, rb=rb, pqv=pqv, rb_sb=rb_sb:
                     v.tensor_tensor(pqv[rb:rb + 64, s0:s0 + 128],
                                     pn[rb:rb + 64, 0:128], rb_sb, ALU.mult),
                     reads=[pnk, S1, qk], writes=[qk])
                k.op("vector", lambda v, S_f=S_f, rb=rb: v.tensor_tensor(
                    S_f, S_f, PP[5][rb:rb + 64, 0:65], ALU.add),
                     reads=["pp5", A2], writes=[A2])
                k.op("vector", lambda v, S_f=S_f, S_bf=S_bf: v.tensor_copy(S_bf, S_f),
                     reads=[A2], writes=[A2])
    k.fence("sync", reads=[f"pq1/{hp}/{hf}/{x}" for hp in range(8) for hf in range(2)
                           for x in range(8)],
            writes=[bkey(1, c) for c in range(8)])

    k.dma("qs", bo_sb, ins["l0a_bo"], writes=["bias0", A2], stream="bias")
    for oc in range(DC):
        sl = oc % 2
        slab = aux[:, sl * 1024:(sl + 1) * 1024]
        slk = [A0, A1][sl]
        k.dma("qg", slab.rearrange("p (kc m) -> p kc m", kc=8),
              ins["l0a_woT"][:, oc * 128:(oc + 1) * 128]
              .rearrange("(kc p) m -> p kc m", p=128),
              writes=[slk], stream=f"w{sl}")
        for j in range(TJ):
            ppi = j % 2
            def mm(t, slab=slab, j=j, ppi=ppi):
                for kc in range(DC):
                    last = t.matmul(PP[ppi][:], slab[:, kc * 128:(kc + 1) * 128],
                                    bf.bigv(1, kc)[:, j * 512:(j + 1) * 512],
                                    start=(kc == 0), stop=(kc == DC - 1))
                return last
            k.op("tensor", mm, reads=[slk] + [bkey(1, c) for c in range(8)],
                 writes=[f"pp{ppi}"])
            stage = scrf[:, ppi * 512:(ppi + 1) * 512]
            stks = [f"SCR:{2 * ppi}", f"SCR:{2 * ppi + 1}"]
            k.op("scalar", lambda s, stage=stage, oc=oc, ppi=ppi: s.activation(
                stage, PP[ppi][:], AF.Identity, bias=bo_sb[:, oc:oc + 1]),
                 reads=[f"pp{ppi}", "bias0"], writes=stks)
            k.dma("qg", dst_dram[oc * 128:(oc + 1) * 128, j * 512:(j + 1) * 512],
                  stage, reads=stks + [dst_key(oc, j)], writes=[dst_key(oc, j)],
                  accum_op=ALU.add, stream=f"acc{ppi}")


def emit_l0_ffn(k: Kfw, C: Consts, bf: Bufs, ins, xnd, dst_dram, dst_key, tag="f0"):
    b1_ = bf.big[0]
    b1f = b1_[:, :].bitcast(F32)
    aux, scr = bf.aux, bf.scr
    PP = bf.pp
    STG = [b1f[:, 8192 + s * 512:8192 + (s + 1) * 512] for s in range(2)]
    WM = b1_[0:8, 18432:22528]
    WB = [b1_[:, 22528 + s * 512:22528 + (s + 1) * 512] for s in range(2)]
    WGU = b1_[0:8, 23552:23568]
    WGUT = b1_[0:16, 23568:23576]
    WRT = b1_[:, 23680:23808]
    WMT = b1_[:, 23808:23816]
    NM = b1f[0:128, 12160:12161]
    ET = b1f[:, 12164:12172]
    RS = b1f[:, 12172:12173]
    SC = b1f[:, 12173:12174]
    WG = b1f[0:8, 12176:12192]
    WGS = b1f[0:8, 12192:12208]
    SS = b1f[0:8, 12208:12209]
    B1ROW = b1_[0:1, 24448:28544]
    B2ROW = b1_[0:1, 28544:29568]
    ones512 = b1_[0:1, 29568:30080]
    MKEYS = (["xr", "wmrows", "wmt", "wgu", "wgut", "wrt", "wgf", "wgs", "ss",
              "nm", "et", "rs", "sc", "stg0", "stg1", "wb0", "wb1", "b1row",
              "b2row", "ones512"] + [f"h/{i}" for i in range(32)])
    k.fence("sync", writes=[bkey(0, c) for c in range(8)] + MKEYS)

    k.dma("qs", WG, ins["l0f_wg"], writes=["wgf"], stream="bias")
    k.dma("qg", WRT.rearrange("p (dc f) -> p dc f", dc=8),
          ins["l0f_wgredT"].rearrange("(dc p) f -> p dc f", p=128),
          writes=["wrt"], stream="bias2")
    k.op("scalar", lambda s: s.activation(WGS, WG, AF.Square),
         reads=["wgf"], writes=["wgs"])
    k.op("vector", lambda v: v.tensor_reduce(SS, WGS, AX.X, ALU.add),
         reads=["wgs"], writes=["ss"])
    k.op("vector", lambda v: v.tensor_scalar_max(SS, SS, 1e-8),
         reads=["ss"], writes=["ss"])
    k.op("scalar", lambda s: s.activation(SS, SS, AF.Ln), reads=["ss"], writes=["ss"])
    k.op("scalar", lambda s: s.activation(SS, SS, AF.Exp, scale=-0.5),
         reads=["ss"], writes=["ss"])
    k.op("vector", lambda v: v.tensor_scalar(WGU, WG, SS, None, ALU.mult),
         reads=["wgf", "ss"], writes=["wgu"])
    ppbf = PP[4][:, :].bitcast(BF16)
    k.op("tensor", lambda t: t.transpose(ppbf[0:16, 0:8], WGU, C.ident[0:8, 0:8]),
         reads=["wgu", "c_ident"], writes=["pp4"])
    k.op("vector", lambda v: v.tensor_copy(WGUT, ppbf[0:16, 0:8]),
         reads=["pp4"], writes=["wgut"])
    for tb in range(TJ):
        k.dma("qg", aux[:, :].rearrange("p (dc f) -> p dc f", dc=8),
              xnd[:, tb * 512:(tb + 1) * 512].rearrange("(dc p) f -> p dc f", p=128),
              reads=["xnd"], writes=[A0, A1, A2, A3], stream="xnl")
        def mmxr(t, tb=tb):
            for dc in range(DC):
                last = t.matmul(PP[6][0:16, :], WRT[:, dc * 16:(dc + 1) * 16],
                                aux[:, dc * 512:(dc + 1) * 512],
                                start=(dc == 0), stop=(dc == DC - 1))
            return last
        k.op("tensor", mmxr, reads=["wrt", A0, A1, A2, A3], writes=["pp6"])
        k.op("vector", lambda v, tb=tb: v.tensor_copy(
            b1_[0:16, tb * 512:(tb + 1) * 512], PP[6][0:16, :]),
             reads=["pp6"], writes=["xr"])
    for tc in range(NT // 128):
        k.op("tensor", lambda t, tc=tc: t.matmul(
            PP[7][:, 0:8], b1_[0:16, tc * 128:(tc + 1) * 128], WGUT,
            start=True, stop=True), reads=["xr", "wgut"], writes=["pp7"])
        k.op("vector", lambda v: v.tensor_reduce(NM, PP[7][:, 0:8], AX.X, ALU.max,
                                                 negate=True),
             reads=["pp7"], writes=["nm"])
        k.op("scalar", lambda s: s.activation(ET, PP[7][:, 0:8], AF.Exp, bias=NM,
                                              accum_out=RS),
             reads=["pp7", "nm"], writes=["et", "rs"])
        k.op("vector", lambda v: v.reciprocal(SC, RS), reads=["rs"], writes=["sc"])
        k.op("vector", lambda v: v.tensor_scalar(ET, PP[7][:, 0:8], NM, 0.0,
                                                 ALU.add, ALU.is_equal),
             reads=["pp7", "nm"], writes=["et"])
        k.op("vector", lambda v: v.tensor_scalar(WMT, ET, SC, None, ALU.mult),
             reads=["et", "sc"], writes=["wmt"])
        k.op("tensor", lambda t: t.transpose(ppbf[0:8, 0:128], WMT,
                                             C.ident[:, :]),
             reads=["wmt", "c_ident"], writes=["pp4"])
        k.op("vector", lambda v, tc=tc: v.tensor_copy(
            WM[:, tc * 128:(tc + 1) * 128], ppbf[0:8, 0:128]),
             reads=["pp4"], writes=["wmrows"])

    k.op("vector", lambda v: v.memset(ones512, 1.0), writes=["ones512"])

    for e in range(8):
        k.dma("qg", bf.big[1][:, :].rearrange("p (dc i) -> p dc i", dc=8),
              ins["l0f_w1"][e].rearrange("(dc p) i -> p dc i", p=128),
              writes=BKALL(1), stream="w1")
        k.dma("qg", bf.big[2][:, :].rearrange("p (ic d) -> p ic d", ic=32),
              ins["l0f_w2"][e].rearrange("(ic p) d -> p ic d", p=128),
              writes=BKALL(2), stream="w2")
        k.dma("qg", B1ROW, ins["l0f_b1"][e:e + 1, :],
              writes=["b1row"], stream="b1a")
        k.dma("qg", B2ROW, ins["l0f_b2"][e:e + 1, :],
              writes=["b2row"], stream="b2")
        for tb in range(TJ):
            k.dma("qg", aux[:, :].rearrange("p (dc f) -> p dc f", dc=8),
                  xnd[:, tb * 512:(tb + 1) * 512].rearrange("(dc p) f -> p dc f", p=128),
                  reads=["xnd"], writes=[A0, A1, A2, A3], stream="xnl")
            k.op("tensor", lambda t, e=e, tb=tb: t.matmul(
                PP[5][:], C.sel8[0:8, e * 128:(e + 1) * 128],
                WM[:, tb * 512:(tb + 1) * 512], start=True, stop=True),
                 reads=["wmrows", "c_sel8"], writes=["pp5"])
            wbs = WB[tb % 2]
            k.op("vector", lambda v, wbs=wbs: v.tensor_copy(wbs, PP[5][:]),
                 reads=["pp5"], writes=[f"wb{tb % 2}"])
            for ic in range(32):
                ph, phk = PP[ic % 2], f"pp{ic % 2}"
                def mmh(t, ic=ic, ph=ph):
                    for dc in range(DC):
                        t.matmul(ph[:], bf.big[1][:, dc * 4096 + ic * 128:
                                                  dc * 4096 + (ic + 1) * 128],
                                 aux[:, dc * 512:(dc + 1) * 512],
                                 start=(dc == 0), stop=False)
                    return t.matmul(ph[:], B1ROW[:, ic * 128:(ic + 1) * 128],
                                    ones512, start=False, stop=True)
                k.op("tensor", mmh,
                     reads=BKALL(1) + [A0, A1, A2, A3, "b1row", "ones512"],
                     writes=[phk])
                k.op("scalar", lambda s, ic=ic, ph=ph: s.activation(
                    b1_[:, ic * 512:(ic + 1) * 512], ph[:], AF.Gelu),
                     reads=[phk], writes=[f"h/{ic}"])
            for dc in range(DC):
                py, pyk = PP[2 + dc % 2], f"pp{2 + dc % 2}"
                def mmy(t, dc=dc, py=py):
                    for ic in range(32):
                        t.matmul(py[:], bf.big[2][:, ic * 1024 + dc * 128:
                                                  ic * 1024 + (dc + 1) * 128],
                                 b1_[:, ic * 512:(ic + 1) * 512],
                                 start=(ic == 0), stop=False)
                    return t.matmul(py[:], B2ROW[:, dc * 128:(dc + 1) * 128],
                                    ones512, start=False, stop=True)
                k.op("tensor", mmy,
                     reads=BKALL(2) + [f"h/{ic}" for ic in range(32)] +
                           ["b2row", "ones512"],
                     writes=[pyk])
                stg = STG[dc % 2]
                k.op("vector", lambda v, stg=stg, py=py, tb=tb: v.tensor_tensor(
                    stg, py[:], WB[tb % 2], ALU.mult),
                     reads=[pyk, f"wb{tb % 2}"], writes=[f"stg{dc % 2}"])
                k.dma("qg", dst_dram[dc * 128:(dc + 1) * 128,
                                     tb * 512:(tb + 1) * 512],
                      stg, reads=[f"stg{dc % 2}", dst_key(dc, tb)],
                      writes=[dst_key(dc, tb)], accum_op=ALU.add,
                      stream=f"acc{dc % 2}")


def emit_l1_attn(k: Kfw, C: Consts, bf: Bufs, ins, xnd_unused, dst_dram, dst_key,
                 tag="a1"):
    aux, scr = bf.aux, bf.scr
    auxf = aux[:, :].bitcast(F32)
    scrf = scr[:, :].bitcast(F32)
    PP = bf.pp
    vtd = k.dram(f"{tag}_vtd", [NT, D], BF16)
    b1_ = bf.big[0]

    acc = [auxf[:, s * 512:(s + 1) * 512] for s in range(2)]
    otok = aux[:, 2048:3072]
    nmc = auxf[:, 1536:1537]
    nm2c = auxf[:, 1537:1538]
    corrc = auxf[:, 1538:1539]
    lc = auxf[:, 1539:1540]
    rsc = auxf[:, 1540:1541]
    mxc = auxf[:, 1541:1542]
    rlc = auxf[:, 1542:1543]
    bqs = auxf[:, 1544:1552]
    bks = auxf[:, 1552:1560]
    bvrow = scr[0:1, 1024:2048]
    onesr = aux[0:1, 3120:3248]

    L1KEYS = ["l1stats", "l1bias", "l1ones", "l1acc0", "l1acc1", "l1ot",
              "l1l", "l1rs"]
    k.fence("sync", writes=[A0, A1, A2, A3] + L1KEYS)
    k.dma("qs", bqs, ins["l1a_bq"], writes=["l1bias"], stream="bias")
    k.dma("qs", bks, ins["l1a_bk"], writes=["l1bias"], stream="bias")
    k.dma("qg", bvrow, ins["l1a_bv"], writes=["l1bias", "SCR:2", "SCR:3"],
          stream="bias2")
    k.op("vector", lambda v: v.tensor_scalar_mul(bqs, bqs, 1.0 / 32.0),
         reads=["l1bias"], writes=["l1bias"])
    k.op("vector", lambda v: v.memset(onesr, 1.0), writes=["l1ones"])

    for nb in range(2):
        k.dma("qg", bf.big[1][:, 0:4096].rearrange("p (kc m) -> p kc m", kc=8),
              ins["l1a_wv"][:, nb * 512:(nb + 1) * 512]
              .rearrange("(kc p) m -> p kc m", p=128),
              writes=[bkey(1, 0)], stream="wtok")
        for tc in range(NT // 128):
            ppi = tc % 2
            def mm(t, tc=tc, ppi=ppi, nb=nb):
                for kc in range(DC):
                    t.matmul(PP[ppi][:], bf.bigv(0, kc)[:, tc * 128:(tc + 1) * 128],
                             bf.big[1][:, kc * 512:(kc + 1) * 512],
                             start=(kc == 0), stop=False)
                return t.matmul(PP[ppi][:], onesr,
                                bvrow[:, nb * 512:(nb + 1) * 512],
                                start=False, stop=True)
            k.op("tensor", mm, reads=[bkey(1, 0), "l1ones", "l1bias", "SCR:2",
                                      "SCR:3"] + BKALL(0),
                 writes=[f"pp{ppi}"])
            st_bf = scr[:, ppi * 512:ppi * 512 + 512]
            stk = f"SCR:{ppi}"
            k.op("vector", lambda v, st_bf=st_bf, ppi=ppi: v.tensor_copy(
                st_bf, PP[ppi][:]), reads=[f"pp{ppi}"], writes=[stk])
            k.dma("qs", vtd[tc * 128:(tc + 1) * 128, nb * 512:(nb + 1) * 512],
                  st_bf, reads=[stk], writes=[f"{tag}vd"], stream=f"st{ppi}")

    for dst_i, wd, bias_ap, bkf, scl in ((1, ins["l1a_wq"], bqs, "l1bias", 1 / 32.0),
                                         (2, ins["l1a_wk"], bks, "l1bias", 1.0)):
        for oc in range(DC):
            sl = oc % 2
            slab = scr[:, sl * 1024:(sl + 1) * 1024]
            slks = [f"SCR:{2 * sl}", f"SCR:{2 * sl + 1}"]
            k.dma("qg", slab.rearrange("p (kc m) -> p kc m", kc=8),
                  wd[:, oc * 128:(oc + 1) * 128].rearrange("(kc p) m -> p kc m", p=128),
                  writes=slks, stream=f"w{sl}")
            for j in range(TJ):
                ppi = j % 2
                def mm(t, slab=slab, j=j, ppi=ppi):
                    for kc in range(DC):
                        last = t.matmul(PP[ppi][:], slab[:, kc * 128:(kc + 1) * 128],
                                        bf.bigv(0, kc)[:, j * 512:(j + 1) * 512],
                                        start=(kc == 0), stop=(kc == DC - 1))
                    return last
                k.op("tensor", mm, reads=slks + BKALL(0), writes=[f"pp{ppi}"])
                k.op("scalar", lambda s, dst_i=dst_i, oc=oc, j=j, ppi=ppi,
                     bias_ap=bias_ap, scl=scl: s.activation(
                         bf.bigv(dst_i, oc)[:, j * 512:(j + 1) * 512], PP[ppi][:],
                         AF.Identity, bias=bias_ap[:, oc:oc + 1], scale=scl),
                     reads=[f"pp{ppi}", bkf, bkey(dst_i, oc)], writes=[bkey(dst_i, oc)])

    k.fence("sync", writes=BKALL(0))
    k.dma("qs", b1_[:, :].rearrange("p (ktc dv) -> p ktc dv", ktc=32),
          vtd[:, :].rearrange("(ktc p) dv -> p ktc dv", p=128),
          reads=[f"{tag}vd"], writes=BKALL(0), stream="vrl")

    for b in range(BPC):
        for qt in range(T // 128):
            q0 = b * T + qt * 128
            k.op("vector", lambda v: v.memset(nmc, 3e38), writes=["l1stats"])
            k.op("vector", lambda v: v.memset(lc, 0.0), writes=["l1stats"])
            k.op("vector", lambda v: v.memset(acc[0], 0.0), writes=["l1acc0"])
            k.op("vector", lambda v: v.memset(acc[1], 0.0), writes=["l1acc1"])
            for kt in range(4):
                k0 = b * T + kt * 512
                ps, psk = PP[kt % 2], f"pp{kt % 2}"
                def mms(t, q0=q0, k0=k0, ps=ps):
                    for dc in range(DC):
                        last = t.matmul(ps[:], bf.bigv(1, dc)[:, q0:q0 + 128],
                                        bf.bigv(2, dc)[:, k0:k0 + 512],
                                        start=(dc == 0), stop=(dc == DC - 1))
                    return last
                k.op("tensor", mms, reads=[bkey(1, c) for c in range(8)] +
                     [bkey(2, c) for c in range(8)], writes=[psk])
                k.op("vector", lambda v, ps=ps: v.tensor_reduce(
                    mxc, ps[:], AX.X, ALU.max, negate=True),
                     reads=[psk], writes=["l1stats"])
                k.op("vector", lambda v: v.tensor_tensor(nm2c, nmc, mxc, ALU.min),
                     reads=["l1stats"], writes=["l1stats"])
                k.op("vector", lambda v: v.tensor_sub(corrc, nm2c, nmc),
                     reads=["l1stats"], writes=["l1stats"])
                k.op("scalar", lambda s: s.activation(corrc, corrc, AF.Exp),
                     reads=["l1stats"], writes=["l1stats"])
                k.op("vector", lambda v: v.tensor_copy(nmc, nm2c),
                     reads=["l1stats"], writes=["l1stats"])
                pt_sb = scr[:, 0:512]
                k.op("scalar", lambda s, ps=ps: s.activation(
                    pt_sb, ps[:], AF.Exp, bias=nm2c, accum_out=rsc),
                     reads=[psk, "l1stats"], writes=["SCR:0", "l1rs"])
                k.op("vector", lambda v: v.scalar_tensor_tensor(
                    lc, lc, corrc, rsc, ALU.mult, ALU.add),
                     reads=["l1stats", "l1rs", "l1l"], writes=["l1l"])
                pp6b = PP[6][:, :].bitcast(BF16)
                def trs(t):
                    for ks in range(4):
                        last = t.transpose(pp6b[:, ks * 128:(ks + 1) * 128],
                                           pt_sb[:, ks * 128:(ks + 1) * 128],
                                           C.ident[:, :])
                    return last
                k.op("tensor", trs, reads=["SCR:0", "c_ident"], writes=["pp6"])
                k.op("vector", lambda v: v.tensor_copy(scr[:, 1024:1536],
                                                       pp6b[:, 0:512]),
                     reads=["pp6"], writes=["SCR:2"])
                pva, pvb = PP[2 + 2 * (kt % 2)], PP[3 + 2 * (kt % 2)]
                pvak, pvbk = f"pp{2 + 2 * (kt % 2)}", f"pp{3 + 2 * (kt % 2)}"
                def mmpv(t, kt=kt, b=b, pva=pva, pvb=pvb):
                    for ks in range(4):
                        ktc = b * 16 + kt * 4 + ks
                        t.matmul(pva[:], scr[:, 1024 + ks * 128:1024 + (ks + 1) * 128],
                                 b1_[:, ktc * 1024:ktc * 1024 + 512],
                                 start=(ks == 0), stop=(ks == 3))
                    for ks in range(4):
                        ktc = b * 16 + kt * 4 + ks
                        last = t.matmul(pvb[:],
                                        scr[:, 1024 + ks * 128:1024 + (ks + 1) * 128],
                                        b1_[:, ktc * 1024 + 512:(ktc + 1) * 1024],
                                        start=(ks == 0), stop=(ks == 3))
                    return last
                k.op("tensor", mmpv, reads=["SCR:2"] + BKALL(0),
                     writes=[pvak, pvbk])
                k.op("vector", lambda v, pva=pva: v.scalar_tensor_tensor(
                    acc[0], acc[0], corrc, pva[:], ALU.mult, ALU.add),
                     reads=[pvak, "l1stats", "l1acc0"], writes=["l1acc0"])
                k.op("vector", lambda v, pvb=pvb: v.scalar_tensor_tensor(
                    acc[1], acc[1], corrc, pvb[:], ALU.mult, ALU.add),
                     reads=[pvbk, "l1stats", "l1acc1"], writes=["l1acc1"])
            k.op("vector", lambda v: v.reciprocal(rlc, lc),
                 reads=["l1l"], writes=["l1stats"])
            k.op("vector", lambda v: v.tensor_scalar(otok[:, 0:512], acc[0], rlc,
                                                     None, ALU.mult),
                 reads=["l1acc0", "l1stats"], writes=["l1ot"])
            k.op("vector", lambda v: v.tensor_scalar(otok[:, 512:1024], acc[1], rlc,
                                                     None, ALU.mult),
                 reads=["l1acc1", "l1stats"], writes=["l1ot"])
            pp7b = PP[7][:, :].bitcast(BF16)
            def tro(t):
                for dc in range(DC):
                    last = t.transpose(pp7b[:, dc * 128:(dc + 1) * 128],
                                       otok[:, dc * 128:(dc + 1) * 128],
                                       C.ident[:, :])
                return last
            k.op("tensor", tro, reads=["l1ot", "c_ident"], writes=["pp7"])
            k.op("vector", lambda v: v.tensor_copy(scrf[:, 0:1024], pp7b[:, 0:1024]),
                 reads=["pp7"], writes=["SCR:0", "SCR:1", "SCR:2", "SCR:3"])
            jj = q0 // 512
            k.dma("qg", dst_dram[:, q0:q0 + 128].rearrange("(dc p) t -> p dc t", p=128),
                  scrf[:, 0:1024].rearrange("p (dc t) -> p dc t", dc=8),
                  reads=["SCR:0", "SCR:1", "SCR:2", "SCR:3"] +
                        [dst_key(c, jj) for c in range(8)],
                  writes=[dst_key(c, jj) for c in range(8)],
                  accum_op=ALU.add, stream=f"acc{qt % 2}")
    k.fence("sync", reads=L1KEYS, writes=[A0, A1, A2, A3])


def emit_l1_ffn(k: Kfw, C: Consts, bf: Bufs, ins, dst_dram, dst_key, tag="f1"):
    aux, scr = bf.aux, bf.scr
    auxf = aux[:, :].bitcast(F32)
    b1_, b2_, b3_ = bf.big[0], bf.big[1], bf.big[2]
    b1f = b1_[:, :].bitcast(F32)
    b3f = b3_[:, :].bitcast(F32)
    PP = bf.pp
    KJ = 512

    AFFR = auxf[:, 0:2048]
    AFFT = b3_[:, 18432:18688]
    GWT = b3_[:, 18688:18752]
    SILU = b3_[:, 18752:19776]
    NM3 = b3f[:, 9888:9889]
    RS3 = b3f[:, 9889:9890]
    SC3 = b3f[:, 9890:9891]
    AFF32 = b3f[:, 9892:9900]
    MASKA = b1_[0:64, 16384:18432]
    ZER = b1_[0:64, 18432:20480]
    POSA = b1f[0:64, 10240:12288]
    POST = b1f[:, 12288:13312]
    PHI_R = b1_[0:64, 26624:28672]
    PLO_R = b1_[0:64, 28672:30720]
    BIS = b1f[:, 15360:15392]
    LO, HI, MID = BIS[:, 0:1], BIS[:, 1:2], BIS[:, 2:3]
    CNT, SEL, NSEL, TT1 = BIS[:, 3:4], BIS[:, 4:5], BIS[:, 5:6], BIS[:, 6:7]
    JCC = b1f[:, 15392:15396]
    PTA = 0
    XSA = 8192
    YSA = 12288
    SLA = 16384
    GSEL = b3f[:, 9900:9904]

    B1K = ["maska", "zer", "posa", "post", "bis", "jcc", "phir", "plor", "h3"]
    B3K = ["pta", "xsa", "ysa", "sla0", "sla1", "afft", "gwt", "silu", "st3"]

    k.fence("sync", writes=[A0, A1, A2, A3, "affr"])
    k.fence("sync", writes=BKALL(2) + B3K + ["nm3", "rs3", "sc3", "aff32", "gsel", "rden"])
    k.op("vector", lambda v: v.memset(AFFR[0:64, :], 0.0), writes=["affr"])
    k.dma("qg", GWT.rearrange("p (dc f) -> p dc f", dc=8),
          ins["l1f_gwT"].rearrange("(dc p) f -> p dc f", p=128),
          writes=["gwt"], stream="bias")
    for tc in range(NT // 128):
        def mml(t, tc=tc):
            for dc in range(DC):
                last = t.matmul(PP[0][:, 0:8], bf.bigv(0, dc)[:, tc * 128:(tc + 1) * 128],
                                GWT[:, dc * 8:(dc + 1) * 8],
                                start=(dc == 0), stop=(dc == DC - 1))
            return last
        k.op("tensor", mml, reads=["gwt"] + BKALL(0), writes=["pp0"])
        k.op("vector", lambda v: v.tensor_reduce(NM3, PP[0][:, 0:8], AX.X, ALU.max,
                                                 negate=True),
             reads=["pp0"], writes=["nm3"])
        k.op("scalar", lambda s: s.activation(AFF32, PP[0][:, 0:8], AF.Exp, bias=NM3,
                                              accum_out=RS3),
             reads=["pp0", "nm3"], writes=["aff32", "rs3"])
        k.op("vector", lambda v: v.reciprocal(SC3, RS3), reads=["rs3"], writes=["sc3"])
        k.op("vector", lambda v, tc=tc: v.tensor_scalar(AFFT[:, tc * 8:(tc + 1) * 8],
                                                        AFF32, SC3, None, ALU.mult),
             reads=["aff32", "sc3"], writes=["afft"])
    for b in range(BPC):
        rb = 32 * b
        tp_m = (0, rb)
        for sl in range(4):
            s0 = b * T + sl * 512
            c0 = b * 32 * 64 if False else (sl * 512)
            def mmr(t, s0=s0, rb=rb):
                for dc in range(DC):
                    last = t.matmul(PP[0][rb:rb + 8, 0:512],
                                    GWT[:, dc * 8:(dc + 1) * 8],
                                    bf.bigv(0, dc)[:, s0:s0 + 512],
                                    start=(dc == 0), stop=(dc == DC - 1),
                                    tile_position=(0, rb))
                return last
            k.op("tensor", mmr, reads=["gwt"] + BKALL(0), writes=["pp0"])
            dst = AFFR[rb:rb + 8, sl * 512:(sl + 1) * 512]
            k.op("scalar", lambda s, dst=dst, rb=rb: s.activation(
                dst, PP[0][rb:rb + 8, 0:512], AF.Exp),
                 reads=["pp0"], writes=["affr"])
            k.op("tensor", lambda t, dst=dst, rb=rb: t.matmul(
                PP[1][0:1, 0:512], C.ones_col[rb:rb + 8, :], dst,
                start=True, stop=True, tile_position=(rb, 0)),
                 reads=["affr", "c_onesc"], writes=["pp1"])
            k.op("vector", lambda v: v.reciprocal(b3f[0:1, 9904:10416],
                                                  PP[1][0:1, 0:512]),
                 reads=["pp1"], writes=["rden"])
            k.op("tensor", lambda t, rb=rb: t.matmul(
                PP[2][rb:rb + 8, 0:512], C.ones_row[:, 0:8],
                b3f[0:1, 9904:10416], start=True, stop=True,
                tile_position=(0, rb)),
                 reads=["rden", "c_onesr"], writes=["pp2"])
            k.op("vector", lambda v, dst=dst, rb=rb: v.tensor_tensor(
                dst, dst, PP[2][rb:rb + 8, 0:512], ALU.mult),
                 reads=["affr", "pp2"], writes=["affr"])

    k.fence("sync", writes=[bkey(1, c) for c in range(8)])
    pp2b = PP[2][:, :].bitcast(BF16)
    for tc in range(NT // 128):
        def trx(t, tc=tc):
            for dc in range(DC):
                last = t.transpose(pp2b[:, dc * 128:(dc + 1) * 128],
                                   bf.bigv(0, dc)[:, tc * 128:(tc + 1) * 128],
                                   C.ident[:, :])
            return last
        k.op("tensor", trx, reads=BKALL(0) + ["c_ident"], writes=["pp2"])
        k.op("vector", lambda v, tc=tc: v.tensor_copy(
            b2_[:, tc * 1024:(tc + 1) * 1024], pp2b[:, 0:1024]),
             reads=["pp2"], writes=[bkey(1, tc // 4)])

    k.fence("sync", writes=BKALL(0) + B1K)
    k.op("vector", lambda v: v.memset(LO, 0.0), writes=["bis"])
    k.op("vector", lambda v: v.memset(HI, 1.0), writes=["bis"])
    k.op("vector", lambda v: v.memset(ZER[:, :], 0.0), writes=["zer"])
    for it in range(40):
        k.op("vector", lambda v: v.tensor_add(MID, LO, HI), reads=["bis"], writes=["bis"])
        k.op("vector", lambda v: v.tensor_scalar_mul(MID, MID, 0.5),
             reads=["bis"], writes=["bis"])
        k.op("vector", lambda v: v.tensor_scalar(MASKA[:, :], AFFR[0:64, :], MID[0:64],
                                                 None, ALU.is_gt,
                                                 op1=ALU.add, accum_out=CNT[0:64]),
             reads=["affr", "bis"], writes=["maska", "bis"])
        k.op("vector", lambda v: v.tensor_scalar(SEL, CNT, float(KJ), None, ALU.is_ge),
             reads=["bis"], writes=["bis"])
        k.op("vector", lambda v: v.tensor_sub(TT1, MID, LO), reads=["bis"], writes=["bis"])
        k.op("vector", lambda v: v.tensor_mul(TT1, TT1, SEL), reads=["bis"], writes=["bis"])
        k.op("vector", lambda v: v.tensor_add(LO, LO, TT1), reads=["bis"], writes=["bis"])
        k.op("vector", lambda v: v.tensor_scalar(NSEL, SEL, -1.0, 1.0, ALU.mult,
                                                 ALU.add), reads=["bis"], writes=["bis"])
        k.op("vector", lambda v: v.tensor_sub(TT1, MID, HI), reads=["bis"], writes=["bis"])
        k.op("vector", lambda v: v.tensor_mul(TT1, TT1, NSEL), reads=["bis"], writes=["bis"])
        k.op("vector", lambda v: v.tensor_add(HI, HI, TT1), reads=["bis"], writes=["bis"])
    k.op("vector", lambda v: v.tensor_scalar(MASKA[:, :], AFFR[0:64, :], LO[0:64],
                                             None, ALU.is_gt),
         reads=["affr", "bis"], writes=["maska"])
    k.op("vector", lambda v: v.tensor_tensor_scan(POSA, MASKA[:, :], ZER[:, :], 0.0,
                                                  ALU.add, ALU.add),
         reads=["maska", "zer"], writes=["posa"])
    k.op("vector", lambda v: v.tensor_tensor(POSA, POSA, MASKA[:, :], ALU.mult),
         reads=["posa", "maska"], writes=["posa"])
    k.op("vector", lambda v: v.tensor_scalar_add(POSA, POSA, -1.0),
         reads=["posa"], writes=["posa"])
    k.op("vector", lambda v: v.tensor_scalar(PHI_R[:, :], POSA, 256.0, 256.0,
                                             ALU.is_ge, ALU.mult),
         reads=["posa"], writes=["phir"])
    k.op("vector", lambda v: v.tensor_tensor(PLO_R[:, :], POSA, PHI_R[:, :],
                                             ALU.subtract),
         reads=["posa", "phir"], writes=["plor"])
    pp3b = PP[3][:, :].bitcast(BF16)
    for sc in range(16):
        def trp(t, sc=sc):
            t.transpose(pp3b[:, 0:64], PHI_R[:, sc * 128:(sc + 1) * 128],
                        C.ident[0:64, 0:64])
            return t.transpose(pp3b[:, 64:128], PLO_R[:, sc * 128:(sc + 1) * 128],
                               C.ident[0:64, 0:64])
        k.op("tensor", trp, reads=["phir", "plor", "c_ident"], writes=["pp3"])
        k.op("vector", lambda v: v.tensor_copy(SILU[:, 0:64], pp3b[:, 0:64]),
             reads=["pp3"], writes=["silu"])
        k.op("vector", lambda v, sc=sc: v.tensor_tensor(
            POST[:, sc * 64:(sc + 1) * 64], SILU[:, 0:64], pp3b[:, 64:128], ALU.add),
             reads=["silu", "pp3"], writes=["post"])
    for jc in range(4):
        k.op("vector", lambda v, jc=jc: v.tensor_scalar_add(JCC[:, jc:jc + 1],
                                                            C.chan[:], float(jc * 128)),
             reads=["c_chan"], writes=["jcc"])

    for e in range(8):
        for b in range(BPC):
            lane = 32 * b + e
            for sc in range(16):
                k.op("vector", lambda v, sc=sc, lane=lane: v.tensor_scalar(
                    b3_[:, PTA + sc * 512:PTA + (sc + 1) * 512], C.iota16[:],
                    POST[:, sc * 64 + lane:sc * 64 + lane + 1], None, ALU.is_equal),
                     reads=["post", "c_iota"], writes=["pta"])
            for dc in range(DC):
                ppi = dc % 2
                def mmx(t, dc=dc, b=b, ppi=ppi):
                    for sc in range(16):
                        tc = b * 16 + sc
                        last = t.matmul(PP[ppi][:],
                                        b2_[:, tc * 1024 + dc * 128:tc * 1024 + (dc + 1) * 128],
                                        b3_[:, PTA + sc * 512:PTA + (sc + 1) * 512],
                                        start=(sc == 0), stop=(sc == 15))
                    return last
                k.op("tensor", mmx, reads=["pta"] + [bkey(1, c) for c in range(8)],
                     writes=[f"pp{ppi}"])
                k.op("vector", lambda v, dc=dc, ppi=ppi: v.tensor_copy(
                    b3_[:, XSA + dc * 512:XSA + (dc + 1) * 512], PP[ppi][:]),
                     reads=[f"pp{ppi}"], writes=["xsa"])
            def mmg(t, b=b, e=e):
                for jc in range(4):
                    for sc in range(16):
                        tc = b * 16 + sc
                        last = t.matmul(PP[2][:, jc:jc + 1],
                                        b3_[:, PTA + sc * 512 + jc * 128:
                                            PTA + sc * 512 + (jc + 1) * 128],
                                        AFFT[:, tc * 8 + e:tc * 8 + e + 1],
                                        start=(sc == 0), stop=(sc == 15))
                return last
            k.op("tensor", mmg, reads=["pta", "afft"], writes=["pp2"])
            k.op("vector", lambda v: v.tensor_copy(GSEL, PP[2][:, 0:4]),
                 reads=["pp2"], writes=["gsel"])
            for ic in range(32):
                sl = ic % 2
                slab = b3_[:, SLA + sl * 1024:SLA + (sl + 1) * 1024]
                slk = f"sla{sl}"
                k.dma("qg", slab.rearrange("p (dc m) -> p dc m", dc=8),
                      ins["l1f_wgT"][e][:, ic * 128:(ic + 1) * 128]
                      .rearrange("(dc p) m -> p dc m", p=128),
                      writes=[slk], stream=f"w{sl}")
                def mmhg(t, slab=slab):
                    for dc in range(DC):
                        last = t.matmul(PP[4][:], slab[:, dc * 128:(dc + 1) * 128],
                                        b3_[:, XSA + dc * 512:XSA + (dc + 1) * 512],
                                        start=(dc == 0), stop=(dc == DC - 1))
                    return last
                k.op("tensor", mmhg, reads=[slk, "xsa"], writes=["pp4"])
                k.dma("qg", slab.rearrange("p (dc m) -> p dc m", dc=8),
                      ins["l1f_wuT"][e][:, ic * 128:(ic + 1) * 128]
                      .rearrange("(dc p) m -> p dc m", p=128),
                      writes=[slk], stream=f"w{sl}")
                def mmhu(t, slab=slab):
                    for dc in range(DC):
                        last = t.matmul(PP[5][:], slab[:, dc * 128:(dc + 1) * 128],
                                        b3_[:, XSA + dc * 512:XSA + (dc + 1) * 512],
                                        start=(dc == 0), stop=(dc == DC - 1))
                    return last
                k.op("tensor", mmhu, reads=[slk, "xsa"], writes=["pp5"])
                k.op("scalar", lambda s: s.activation(SILU[:, 0:512], PP[4][:], AF.Silu),
                     reads=["pp4"], writes=["silu"])
                k.op("vector", lambda v, ic=ic: v.tensor_tensor(
                    b1_[:, ic * 512:(ic + 1) * 512], SILU[:, 0:512], PP[5][:], ALU.mult),
                     reads=["silu", "pp5"], writes=["h3"])
            for ic in range(32):
                sl = ic % 2
                slab = b3_[:, SLA + sl * 1024:SLA + (sl + 1) * 1024]
                slk = f"sla{sl}"
                k.dma("qg", slab.rearrange("p (h m) -> p h m", h=2),
                      ins["l1f_wdT"][e][ic * 128:(ic + 1) * 128, :]
                      .rearrange("p (h m) -> p h m", h=2),
                      writes=[slk], stream=f"w{sl}")
                def mmy(t, ic=ic, slab=slab):
                    for jc in range(4):
                        for hh in range(2):
                            last = t.matmul(PP[jc * 2 + hh][:],
                                            b1_[:, ic * 512 + jc * 128:
                                                ic * 512 + (jc + 1) * 128],
                                            slab[:, hh * 512:(hh + 1) * 512],
                                            start=(ic == 0), stop=(ic == 31))
                    return last
                k.op("tensor", mmy, reads=[slk, "h3"],
                     writes=[f"pp{x}" for x in range(8)])
            for jc in range(4):
                for hh in range(2):
                    k.op("vector", lambda v, jc=jc, hh=hh: v.tensor_scalar(
                        b3_[:, YSA + jc * 1024 + hh * 512:YSA + jc * 1024 + (hh + 1) * 512],
                        PP[jc * 2 + hh][:], GSEL[:, jc:jc + 1], None, ALU.mult),
                         reads=[f"pp{jc * 2 + hh}", "gsel"], writes=["ysa"])
            for st in range(4):
                def mmb(t, b=b, e=e, st=st):
                    t.matmul(PP[0][:, 0:512],
                             C.sel8[32 * b:32 * b + 8, e * 128:(e + 1) * 128],
                             PHI_R[32 * b:32 * b + 8, st * 512:(st + 1) * 512],
                             start=True, stop=False)
                    return t.matmul(PP[0][:, 0:512],
                                    C.sel8[32 * b:32 * b + 8, e * 128:(e + 1) * 128],
                                    PLO_R[32 * b:32 * b + 8, st * 512:(st + 1) * 512],
                                    start=False, stop=True)
                k.op("tensor", mmb, reads=["phir", "plor", "c_sel8"], writes=["pp0"])
                for jc in range(4):
                    k.op("vector", lambda v, jc=jc, st=st: v.tensor_scalar(
                        b3_[:, PTA + jc * 2048 + st * 512:PTA + jc * 2048 + (st + 1) * 512],
                        PP[0][:, 0:512], JCC[:, jc:jc + 1], None, ALU.is_equal),
                         reads=["pp0", "jcc"], writes=["pta"])
            for dc in range(DC):
                for st in range(4):
                    ppi = 1 + (dc * 4 + st) % 2
                    def mmsc(t, dc=dc, st=st, ppi=ppi):
                        for jc in range(4):
                            last = t.matmul(PP[ppi][:],
                                            b3_[:, YSA + jc * 1024 + dc * 128:
                                                YSA + jc * 1024 + (dc + 1) * 128],
                                            b3_[:, PTA + jc * 2048 + st * 512:
                                                PTA + jc * 2048 + (st + 1) * 512],
                                            start=(jc == 0), stop=(jc == 3))
                        return last
                    k.op("tensor", mmsc, reads=["ysa", "pta"], writes=[f"pp{ppi}"])
                    stg = b3f[:, 10240 + (st % 2) * 512:10240 + (st % 2 + 1) * 512]
                    k.op("vector", lambda v, stg=stg, ppi=ppi: v.tensor_copy(
                        stg, PP[ppi][:]), reads=[f"pp{ppi}"], writes=[f"st3{st % 2}"])
                    jj = b * 4 + st
                    k.dma("qg", dst_dram[dc * 128:(dc + 1) * 128,
                                         jj * 512:(jj + 1) * 512],
                          stg, reads=[f"st3{st % 2}", dst_key(dc, jj)],
                          writes=[dst_key(dc, jj)], accum_op=ALU.add,
                          stream=f"acc{st % 2}")


def emit_l2_attn(k: Kfw, C: Consts, bf: Bufs, ins, dst_dram, dst_key, tag="a2"):
    aux, scr = bf.aux, bf.scr
    scrf = scr[:, :].bitcast(F32)
    b1_, b2_, b3_ = bf.big[0], bf.big[1], bf.big[2]
    b1f = b1_[:, :].bitcast(F32)
    PP = bf.pp
    DI = 2048
    NCC = 16
    zd = k.dram(f"{tag}_zd", [DI, NT], BF16)

    def xc_chunk(cc):
        return (bf.bigv(1, cc) if cc < 8 else bf.bigv(2, cc - 8))

    def xck(cc):
        return bkey(1, cc) if cc < 8 else bkey(2, cc - 8)

    DLT = b1f[0:1, 0:4096]
    BR = b1f[32:48, 0:4096]
    CR = b1f[64:80, 0:4096]
    BBF = b1_[32:48, 8192:12288]
    CBF = b1_[64:80, 8192:12288]
    DA_T = b1_[:, 12288:14336]
    DBU = b1_[:, 14336:16384]
    HT = b1_[:, 16384:18432]
    HC = b1_[:, 18432:20480]
    DBT = b1_[:, 20480:22528]
    CLT = b1_[:, 22528:24576]
    DBR = b1_[:, 24576:26624]
    ZS = b1_[:, 26624:27648]
    ZEROS = b1_[:, 27648:29696]
    REPC = b1_[:, 29696:31744]
    REPB = b1_[0:80, 32448:32576]
    DCOL = b1f[:, 15872:15888]
    CW = b1f[:, 15888:15952]
    CB = b1f[:, 15952:15968]
    ACOL = b1f[:, 15968 + 0:15968 + 256]
    B1K2 = ["dlt", "brow", "crow", "bbf", "cbf", "dat", "dbu", "ht", "hc",
            "dbt", "clt", "dbr", "zs", "zeros", "repc", "dcol", "cw", "cb",
            "acol", "summ", "repb", "tmpr", "xslab", "cstrip"]

    k.fence("sync", writes=[bkey(1, c) for c in range(8)] +
            [bkey(2, c) for c in range(8)] + [A0, A1, A2, A3])
    for oc in range(32):
        sl = oc % 2
        slab = scr[:, sl * 1024:(sl + 1) * 1024]
        slks = [f"SCR:{2 * sl}", f"SCR:{2 * sl + 1}"]
        k.dma("qg", slab.rearrange("p (kc m) -> p kc m", kc=8),
              ins["l2a_inprojT"][:, oc * 128:(oc + 1) * 128]
              .rearrange("(kc p) m -> p kc m", p=128),
              writes=slks, stream=f"w{sl}")
        for j in range(TJ):
            ppi = j % 2
            def mm(t, slab=slab, j=j, ppi=ppi):
                for kc in range(DC):
                    last = t.matmul(PP[ppi][:], slab[:, kc * 128:(kc + 1) * 128],
                                    bf.bigv(0, kc)[:, j * 512:(j + 1) * 512],
                                    start=(kc == 0), stop=(kc == DC - 1))
                return last
            k.op("tensor", mm, reads=slks + BKALL(0), writes=[f"pp{ppi}"])
            if oc < 16:
                k.op("vector", lambda v, oc=oc, j=j, ppi=ppi: v.tensor_copy(
                    xc_chunk(oc)[:, j * 512:(j + 1) * 512], PP[ppi][:]),
                     reads=[f"pp{ppi}"], writes=[xck(oc)])
            else:
                st_bf = aux[:, (j % 2) * 512:(j % 2) * 512 + 512]
                stk = ["AUX:0"] if j % 2 == 0 else ["AUX:0b"]
                k.op("vector", lambda v, st_bf=st_bf, ppi=ppi: v.tensor_copy(
                    st_bf, PP[ppi][:]), reads=[f"pp{ppi}"], writes=stk)
                k.dma("qs", zd[(oc - 16) * 128:(oc - 15) * 128,
                               j * 512:(j + 1) * 512],
                      st_bf, reads=stk, writes=["zd"], stream=f"st{j % 2}")

    k.fence("sync", writes=BKALL(0) + B1K2 + ["SCR:0", "SCR:1", "SCR:2", "SCR:3",
                                              "AUX:0", "AUX:0b", "AUX:1", "AUX:2",
                                              "AUX:3"])
    k.dma("qg", REPC.rearrange("p (m f) -> p m f", m=16),
          ins["cst_repc"].rearrange("(m p) f -> p m f", p=128),
          writes=["repc"], stream="cst")
    k.dma("qg", REPB[0:16, :], ins["cst_repb"], writes=["repb"], stream="cst")
    k.dma("qg", REPB[32:48, :], ins["cst_repb"], writes=["repb"], stream="cst")
    k.dma("qg", REPB[64:80, :], ins["cst_repb"], writes=["repb"], stream="cst")
    k.dma("qg", scr[:, :].rearrange("p (m f) -> p m f", m=16),
          ins["cst_summ"].rearrange("(m p) f -> p m f", p=128),
          writes=["summ", "SCR:0", "SCR:1", "SCR:2", "SCR:3"], stream="cst")
    k.dma("qs", CW, ins["l2a_convw"], writes=["cw"], stream="cst2")
    k.dma("qs", CB, ins["l2a_convb"], writes=["cb"], stream="cst2")
    k.dma("qs", DCOL, ins["l2a_dcol"], writes=["dcol"], stream="cst2")
    k.dma("qs", ACOL, ins["l2a_alog"], writes=["acol"], stream="cst2")
    k.op("scalar", lambda s: s.activation(ACOL, ACOL, AF.Exp),
         reads=["acol"], writes=["acol"])
    k.op("vector", lambda v: v.tensor_scalar_mul(ACOL, ACOL, -1.0),
         reads=["acol"], writes=["acol"])
    k.op("vector", lambda v: v.memset(ZEROS[:, :], 0.0), writes=["zeros"])
    for cc in range(NCC):
        for b in range(BPC):
            strip = xc_chunk(cc)[:, b * T:(b + 1) * T]
            astrip = aux[:, 0:T]
            k.op("vector", lambda v, strip=strip, astrip=astrip, cc=cc: v.tensor_scalar(
                astrip, strip, CW[:, cc * 4 + 3:cc * 4 + 4], None, ALU.mult),
                 reads=[xck(cc), "cw"], writes=["cstrip"])
            for kk in range(3):
                sh = 3 - kk
                k.op("vector", lambda v, strip=strip, astrip=astrip, cc=cc, kk=kk,
                     sh=sh: v.scalar_tensor_tensor(
                         astrip[:, sh:T], strip[:, 0:T - sh],
                         CW[:, cc * 4 + kk:cc * 4 + kk + 1],
                         astrip[:, sh:T], ALU.mult, ALU.add),
                     reads=[xck(cc), "cw", "cstrip"], writes=["cstrip"])
            k.op("scalar", lambda s, strip=strip, astrip=astrip, cc=cc: s.activation(
                strip, astrip, AF.Silu, bias=CB[:, cc:cc + 1]),
                 reads=["cstrip", "cb"], writes=[xck(cc)])

    for j in range(TJ):
        sl = j % 2
        slab = scr[:, :]
        xslab = aux[:, 0:2048]
        if j == 0:
            k.dma("qg", aux[:, 0:528].rearrange("p (kc m) -> p kc m", kc=16),
                  ins["l2a_xprojT"].rearrange("(kc p) m -> p kc m", p=128),
                  writes=["xslab", "cstrip"], stream="w0")
        def mmx(t, j=j):
            for kc in range(NCC):
                t.matmul(PP[4][0:1, :], xslab[:, kc * 33 + 0:kc * 33 + 1],
                         xc_chunk(kc)[:, j * 512:(j + 1) * 512],
                         start=(kc == 0), stop=(kc == NCC - 1))
            for kc in range(NCC):
                t.matmul(PP[4][32:48, :], xslab[:, kc * 33 + 1:kc * 33 + 17],
                         xc_chunk(kc)[:, j * 512:(j + 1) * 512],
                         start=(kc == 0), stop=(kc == NCC - 1),
                         tile_position=(0, 32))
            for kc in range(NCC):
                last = t.matmul(PP[4][64:80, :], xslab[:, kc * 33 + 17:kc * 33 + 33],
                                xc_chunk(kc)[:, j * 512:(j + 1) * 512],
                                start=(kc == 0), stop=(kc == NCC - 1),
                                tile_position=(0, 64))
            return last
        k.op("tensor", mmx, reads=["xslab"] + [xck(c) for c in range(16)],
             writes=["pp4"])
        k.op("vector", lambda v, j=j: v.tensor_copy(DLT[:, j * 512:(j + 1) * 512],
                                                    PP[4][0:1, :]),
             reads=["pp4"], writes=["dlt"])
        k.op("vector", lambda v, j=j: v.tensor_copy(BR[:, j * 512:(j + 1) * 512],
                                                    PP[4][32:48, :]),
             reads=["pp4"], writes=["brow"])
        k.op("vector", lambda v, j=j: v.tensor_copy(CR[:, j * 512:(j + 1) * 512],
                                                    PP[4][64:80, :]),
             reads=["pp4"], writes=["crow"])
    tmpr = b1f[0:1, 4096:8192]
    k.op("scalar", lambda s: s.activation(tmpr, DLT, AF.Abs),
         reads=["dlt"], writes=["tmpr"])
    k.op("scalar", lambda s: s.activation(tmpr, tmpr, AF.Exp, scale=-1.0),
         reads=["tmpr"], writes=["tmpr"])
    k.op("vector", lambda v: v.tensor_scalar_add(tmpr, tmpr, 1.0),
         reads=["tmpr"], writes=["tmpr"])
    k.op("scalar", lambda s: s.activation(tmpr, tmpr, AF.Ln),
         reads=["tmpr"], writes=["tmpr"])
    k.op("vector", lambda v: v.tensor_scalar_max(DLT, DLT, 0.0),
         reads=["dlt"], writes=["dlt"])
    k.op("vector", lambda v: v.tensor_add(DLT, DLT, tmpr),
         reads=["dlt", "tmpr"], writes=["dlt"])
    k.op("vector", lambda v: v.tensor_copy(BBF[:, :], BR),
         reads=["brow"], writes=["bbf"])
    k.op("vector", lambda v: v.tensor_copy(CBF[:, :], CR),
         reads=["crow"], writes=["cbf"])

    for b in range(BPC):
        bs = b * T
        for s4 in range(4):
            ppi = 4 + s4 % 2
            k.op("tensor", lambda t, bs=bs, s4=s4, ppi=ppi: t.matmul(
                PP[ppi][:], C.ones_row[:, :],
                DLT[:, bs + s4 * 512:bs + (s4 + 1) * 512], start=True, stop=True),
                 reads=["dlt", "c_onesr"], writes=[f"pp{ppi}"])
            k.op("vector", lambda v, s4=s4, ppi=ppi: v.tensor_copy(
                DBR[:, s4 * 512:(s4 + 1) * 512], PP[ppi][:]),
                 reads=[f"pp{ppi}"], writes=["dbr"])
        for s4 in range(4):
            k.op("tensor", lambda t, bs=bs, s4=s4: t.matmul(
                PP[4][:], REPB[32:48, :],
                BBF[:, bs + s4 * 512:bs + (s4 + 1) * 512], start=True, stop=True,
                tile_position=(32, 0)),
                 reads=["bbf", "repb"], writes=["pp4"])
            k.op("vector", lambda v, s4=s4: v.tensor_tensor(
                DBT[:, s4 * 512:(s4 + 1) * 512], DBR[:, s4 * 512:(s4 + 1) * 512],
                PP[4][:], ALU.mult),
                 reads=["pp4", "dbr"], writes=["dbt"])
            k.op("tensor", lambda t, bs=bs, s4=s4: t.matmul(
                PP[5][:], REPB[64:80, :],
                CBF[:, bs + s4 * 512:bs + (s4 + 1) * 512], start=True, stop=True,
                tile_position=(64, 0)),
                 reads=["cbf", "repb"], writes=["pp5"])
            k.op("vector", lambda v, s4=s4: v.tensor_copy(
                CLT[:, s4 * 512:(s4 + 1) * 512], PP[5][:]),
                 reads=["pp5"], writes=["clt"])
        for cc in range(NCC):
            for jj in range(16):
                lt = cc * 16 + jj
                acol = ACOL[:, lt:lt + 1]
                for s4 in range(4):
                    ppi = 4 + s4 % 2
                    k.op("tensor", lambda t, cc=cc, jj=jj, bs=bs, s4=s4, ppi=ppi:
                         t.matmul(PP[ppi][:], REPC[:, jj * 128:(jj + 1) * 128],
                                  xc_chunk(cc)[:, bs + s4 * 512:bs + (s4 + 1) * 512],
                                  start=True, stop=True),
                         reads=[xck(cc), "repc"], writes=[f"pp{ppi}"])
                    k.op("vector", lambda v, s4=s4, ppi=ppi: v.tensor_tensor(
                        DBU[:, s4 * 512:(s4 + 1) * 512],
                        DBT[:, s4 * 512:(s4 + 1) * 512], PP[ppi][:], ALU.mult),
                         reads=[f"pp{ppi}", "dbt"], writes=["dbu"])
                k.op("gpsimd", lambda g, acol=acol: g.tensor_scalar(
                    DA_T[:, :], DBR[:, :], acol, None, ALU.mult),
                     reads=["dbr", "acol"], writes=["dat"])
                k.op("scalar", lambda s: s.activation(DA_T[:, :], DA_T[:, :], AF.Exp),
                     reads=["dat"], writes=["dat"])
                k.op("vector", lambda v: v.tensor_tensor_scan(
                    HT[:, :], DA_T[:, :], DBU[:, :], 0.0, ALU.mult, ALU.add),
                     reads=["dat", "dbu"], writes=["ht"])
                k.op("gpsimd", lambda g: g.tensor_tensor(HC[:, :], HT[:, :],
                                                         CLT[:, :], ALU.mult),
                     reads=["ht", "clt"], writes=["hc"])
                def mmy(t, jj=jj):
                    for s4 in range(4):
                        last = t.matmul(PP[s4][:], scr[:, jj * 128:(jj + 1) * 128],
                                        HC[:, s4 * 512:(s4 + 1) * 512],
                                        start=(jj == 0), stop=(jj == 15))
                    return last
                k.op("tensor", mmy, reads=["hc", "summ"],
                     writes=["pp0", "pp1", "pp2", "pp3"])
            for s4 in range(4):
                zs = ZS[:, (s4 % 2) * 512:(s4 % 2) * 512 + 512]
                zsk = f"zs{s4 % 2}"
                k.dma("qg", zs, zd[cc * 128:(cc + 1) * 128,
                                   bs + s4 * 512:bs + (s4 + 1) * 512],
                      reads=["zd"], writes=[zsk], stream=f"zl{s4 % 2}")
                k.op("scalar", lambda s, zs=zs: s.activation(zs, zs, AF.Silu),
                     reads=[zsk], writes=[zsk])
                k.op("vector", lambda v, cc=cc, bs=bs, s4=s4: v.scalar_tensor_tensor(
                    HC[:, s4 * 512:(s4 + 1) * 512],
                    xc_chunk(cc)[:, bs + s4 * 512:bs + (s4 + 1) * 512],
                    DCOL[:, cc:cc + 1], PP[s4][:], ALU.mult, ALU.add),
                     reads=[xck(cc), "dcol", f"pp{s4}"], writes=["hc"])
                k.op("vector", lambda v, cc=cc, bs=bs, s4=s4, zs=zs: v.tensor_tensor(
                    xc_chunk(cc)[:, bs + s4 * 512:bs + (s4 + 1) * 512],
                    HC[:, s4 * 512:(s4 + 1) * 512], zs, ALU.mult),
                     reads=["hc", zsk], writes=[xck(cc)])

    for oc in range(DC):
        sl = oc % 2
        slab = aux[:, sl * 2048:(sl + 1) * 2048]
        slk = ["AUX:0", "AUX:2"][sl]
        k.dma("qg", slab.rearrange("p (kc m) -> p kc m", kc=16),
              ins["l2a_outprojT"][:, oc * 128:(oc + 1) * 128]
              .rearrange("(kc p) m -> p kc m", p=128),
              writes=[slk, {0: "AUX:1", 1: "AUX:3"}[sl], "cstrip", "xslab"],
              stream=f"w{sl}")
        for j in range(TJ):
            ppi = 6 + j % 2
            def mm(t, slab=slab, j=j, ppi=ppi):
                for kc in range(NCC):
                    last = t.matmul(PP[ppi][:], slab[:, kc * 128:(kc + 1) * 128],
                                    xc_chunk(kc)[:, j * 512:(j + 1) * 512],
                                    start=(kc == 0), stop=(kc == NCC - 1))
                return last
            k.op("tensor", mm, reads=[slk] + [xck(c) for c in range(16)],
                 writes=[f"pp{ppi}"])
            stage = scrf[:, (j % 2) * 512:(j % 2) * 512 + 512]
            stks = [f"SCR:{2 * (j % 2)}", f"SCR:{2 * (j % 2) + 1}"]
            k.op("vector", lambda v, stage=stage, ppi=ppi: v.tensor_copy(
                stage, PP[ppi][:]), reads=[f"pp{ppi}"], writes=stks)
            k.dma("qg", dst_dram[oc * 128:(oc + 1) * 128, j * 512:(j + 1) * 512],
                  stage, reads=stks + [dst_key(oc, j)], writes=[dst_key(oc, j)],
                  accum_op=ALU.add, stream=f"acc{j % 2}")


def mk_scan_consts():
    p = np.arange(128)
    repc = np.zeros((16, 128, 128), np.float32)
    summ = np.zeros((16, 128, 128), np.float32)
    for m in range(16):
        for pp_ in range(128):
            r = m * 8 + pp_ // 16
            repc[m, r, pp_] = 1.0
            summ[m, pp_, r] = 1.0
    repb = (p[None, :] % 16 == np.arange(16)[:, None]).astype(np.float32)
    return {"cst_repc": repc.reshape(2048, 128), "cst_summ": summ.reshape(2048, 128),
            "cst_repb": np.ascontiguousarray(repb)}


def emit_l2_ffn(k: Kfw, C: Consts, bf: Bufs, ins, dst_dram, dst_key, n_cores,
                tag="f2", sim_only=False, mm_out=None, mm_in=None):
    aux, scr = bf.aux, bf.scr
    auxf = aux[:, :].bitcast(F32)
    scrf = scr[:, :].bitcast(F32)
    b1_, b2_, b3_ = bf.big[0], bf.big[1], bf.big[2]
    PP = bf.pp
    H, DH = 16, 64
    vtd = k.dram(f"{tag}_vtd", [NT, D], BF16)
    ccin = k.dram(f"{tag}_ccin", [1, 2], F32)
    ccout = k.dram(f"{tag}_ccout", [1, 2], F32)

    SQT = aux[:, 0:1024]
    SIMR = auxf[:, 512:1536]
    POSR = aux[0:1, 3072:7168 - 3072 + 3072] if False else aux[0:1, 1024 + 2048:1024 + 2048 + 0]
    POSR = aux[0:1, 0:4096]
    B1K5 = ["vp5", "vn5", "mpc", "mnc", "posr", "mpr", "mnr", "simr", "sq5",
            "simacc", "n2r", "qgn", "ho", "thr", "pt5a", "pt5b", "rbp", "rbn",
            "xpn", "xnn", "vnh15", "vtile"]

    def vp_ap(h, kt):
        return b1_[:, h * 1040 + kt * 65:h * 1040 + (kt + 1) * 65]

    def vn_ap(h, kt):
        if h < 15:
            return b1_[:, 16640 + h * 1040 + kt * 65:16640 + h * 1040 + (kt + 1) * 65]
        return aux[:, kt * 65:(kt + 1) * 65]

    k.fence("sync", writes=[bkey(1, c) for c in range(8)] +
            [bkey(2, c) for c in range(8)] + ["SCR:0", "SCR:1", "SCR:2", "SCR:3"])
    for dst_i, base, scl in (((1, 0, 0.125),) if sim_only else
                             ((1, 0, 0.125), (2, 1024, 1.0))):
        for oc in range(DC):
            sl = oc % 2
            slab = scr[:, sl * 1024:(sl + 1) * 1024]
            slks = [f"SCR:{2 * sl}", f"SCR:{2 * sl + 1}"]
            k.dma("qg", slab.rearrange("p (kc m) -> p kc m", kc=8),
                  ins["l2f_wqkv"][:, base + oc * 128:base + (oc + 1) * 128]
                  .rearrange("(kc p) m -> p kc m", p=128),
                  writes=slks, stream=f"w{sl}")
            for j in range(TJ):
                ppi = j % 2
                def mm(t, slab=slab, j=j, ppi=ppi):
                    for kc in range(DC):
                        last = t.matmul(PP[ppi][:], slab[:, kc * 128:(kc + 1) * 128],
                                        bf.bigv(0, kc)[:, j * 512:(j + 1) * 512],
                                        start=(kc == 0), stop=(kc == DC - 1))
                    return last
                k.op("tensor", mm, reads=slks + BKALL(0), writes=[f"pp{ppi}"])
                k.op("scalar", lambda s, dst_i=dst_i, oc=oc, j=j, ppi=ppi, scl=scl:
                     s.activation(bf.bigv(dst_i, oc)[:, j * 512:(j + 1) * 512],
                                  PP[ppi][:], AF.Identity, scale=scl),
                     reads=[f"pp{ppi}", bkey(dst_i, oc)], writes=[bkey(dst_i, oc)])
    for nb in ([] if sim_only else range(4)):
        slab = scr[:, 0:2048]
        slks = ["SCR:0", "SCR:1", "SCR:2", "SCR:3"]
        k.dma("qg", slab.rearrange("p (kc m) -> p kc m", kc=8),
              ins["l2f_wqkv"][:, 2048 + nb * 256:2048 + (nb + 1) * 256]
              .rearrange("(kc p) m -> p kc m", p=128),
              writes=slks, stream="w0")
        for tc in range(NT // 128):
            ppi = 4 + tc % 2
            def mm(t, tc=tc, slab=slab, ppi=ppi):
                for kc in range(DC):
                    last = t.matmul(PP[ppi][:, 0:256],
                                    bf.bigv(0, kc)[:, tc * 128:(tc + 1) * 128],
                                    slab[:, kc * 256:(kc + 1) * 256],
                                    start=(kc == 0), stop=(kc == DC - 1))
                return last
            k.op("tensor", mm, reads=slks + BKALL(0), writes=[f"pp{ppi}"])
            stb = aux[:, 1024 + (tc % 2) * 256:1024 + (tc % 2) * 256 + 256]
            stk = f"vst{tc % 2}"
            k.op("vector", lambda v, stb=stb, ppi=ppi: v.tensor_copy(
                stb, PP[ppi][:, 0:256]), reads=[f"pp{ppi}"], writes=[stk])
            k.dma("qs", vtd[tc * 128:(tc + 1) * 128, nb * 256:(nb + 1) * 256],
                  stb, reads=[stk], writes=[f"{tag}vd"], stream=f"st{tc % 2}")

    k.fence("sync", writes=BKALL(0) + B1K5 + [A0, A1, A2, A3])
    b1f = b1_[:, :].bitcast(F32)
    HO = b1_[:, 0:2]
    QGC = b1f[:, 4:12]
    N2E = b1f[0:2, 16:24]
    QGN = b1_[:, 64:80]
    HI2 = b1_[0:2, 96:224]
    SACC = b1f[0:2, 128:640]
    RSQ = b1f[0:2, 640:1152]
    MMX = b1f[0:1, 1152:1160]
    SIMR = b1f[0:1, 2048:6144]
    SEXC = b1f[0:1, 6144:10240]

    PROWS = b1_[0:1, 20480:24576]
    MPRR = b1_[0:1, 24576:28672]
    MNRR = b1_[0:1, 28672:32768]
    SQB = aux[:, 0:512]
    posd = k.dram(f"{tag}_posd", [1, NT], BF16)

    k.op("vector", lambda v: v.tensor_scalar(HO[:, 0:1], C.chan[:], 64.0, None,
                                             ALU.is_lt), reads=["c_chan"], writes=["ho"])
    k.op("vector", lambda v: v.tensor_scalar(HO[:, 1:2], C.chan[:], 64.0, None,
                                             ALU.is_ge), reads=["c_chan"], writes=["ho"])
    pp6b = PP[6][:, :].bitcast(BF16)
    k.op("tensor", lambda t: t.transpose(pp6b[0:2, 0:128], HO[:, :], C.ident[:, :]),
         reads=["ho", "c_ident"], writes=["pp6"])
    k.op("vector", lambda v: v.tensor_copy(HI2[:, :], pp6b[0:2, 0:128]),
         reads=["pp6"], writes=["ho"])
    N2B = b1_[0:2, 2336:2344]
    k.dma("qs", QGC, ins["l2f_qg"], writes=["qgn"], stream="bias")
    k.op("scalar", lambda s: s.activation(SQB[:, 0:8], QGC, AF.Square),
         reads=["qgn"], writes=["sq5"])
    k.op("tensor", lambda t: t.matmul(PP[6][0:2, 0:8], HO[:, :], SQB[:, 0:8],
                                      start=True, stop=True),
         reads=["ho", "sq5"], writes=["pp6"])
    k.op("vector", lambda v: v.tensor_copy(N2E, PP[6][0:2, 0:8]),
         reads=["pp6"], writes=["n2r"])
    k.op("scalar", lambda s: s.activation(N2E, N2E, AF.Ln), reads=["n2r"], writes=["n2r"])
    k.op("scalar", lambda s: s.activation(N2B, N2E, AF.Exp, scale=-0.5),
         reads=["n2r"], writes=["n2r"])
    k.op("tensor", lambda t: t.matmul(PP[6][:, 0:8], HI2[:, :], N2B,
                                      start=True, stop=True),
         reads=["ho", "n2r"], writes=["pp6"])
    k.op("vector", lambda v: v.tensor_tensor(QGN[:, 0:8], QGC, PP[6][:, 0:8],
                                             ALU.mult),
         reads=["qgn", "pp6"], writes=["qgn"])
    for jt in range(TJ):
        k.op("vector", lambda v: v.memset(SACC, 0.0), writes=["simacc"])
        for hc in range(DC):
            k.op("scalar", lambda s, hc=hc, jt=jt: s.activation(
                SQB, bf.bigv(1, hc)[:, jt * 512:(jt + 1) * 512], AF.Square),
                 reads=[bkey(1, hc)], writes=["sq5"])
            k.op("tensor", lambda t: t.matmul(PP[6][0:2, :], HO[:, :], SQB,
                                              start=True, stop=True),
                 reads=["sq5", "ho"], writes=["pp6"])
            k.op("tensor", lambda t, hc=hc, jt=jt: t.matmul(
                PP[7][0:2, :], QGN[:, 2 * hc:2 * hc + 2],
                bf.bigv(1, hc)[:, jt * 512:(jt + 1) * 512], start=True, stop=True),
                 reads=["qgn", bkey(1, hc)], writes=["pp7"])
            k.op("vector", lambda v: v.tensor_copy(RSQ, PP[6][0:2, :]),
                 reads=["pp6"], writes=["n2r"])
            k.op("scalar", lambda s: s.activation(RSQ, RSQ, AF.Ln),
                 reads=["n2r"], writes=["n2r"])
            k.op("scalar", lambda s: s.activation(RSQ, RSQ, AF.Exp, scale=-0.5),
                 reads=["n2r"], writes=["n2r"])
            k.op("vector", lambda v: v.tensor_tensor(RSQ, RSQ, PP[7][0:2, :],
                                                     ALU.mult),
                 reads=["n2r", "pp7"], writes=["n2r"])
            k.op("vector", lambda v: v.tensor_add(SACC, SACC, RSQ),
                 reads=["n2r", "simacc"], writes=["simacc"])
        k.op("tensor", lambda t: t.matmul(PP[6][0:1, :], C.ones_col[0:2, :], SACC,
                                          start=True, stop=True),
             reads=["simacc", "c_onesc"], writes=["pp6"])
        k.op("vector", lambda v, jt=jt: v.tensor_scalar(
            SIMR[:, jt * 512:(jt + 1) * 512], PP[6][0:1, :], 1.0 / 16.0, None,
            ALU.mult),
             reads=["pp6"], writes=["simr"])
    for val, col in ((-3.0e38, 0), (3.0e38, 1)):
        k.op("vector", lambda v: v.tensor_copy(SEXC, SIMR),
             reads=["simr"], writes=["sexc"])
        for b in range(BPC):
            k.op("vector", lambda v, b=b, val=val: v.memset(
                SEXC[:, b * T:b * T + 5], val), reads=["sexc"], writes=["sexc"])
        if col == 0:
            k.op("vector", lambda v: v.tensor_reduce(MMX[:, 0:1], SEXC, AX.X,
                                                     ALU.max),
                 reads=["sexc"], writes=["thr"])
        else:
            k.op("vector", lambda v: v.tensor_reduce(MMX[:, 1:2], SEXC, AX.X,
                                                     ALU.max, negate=True),
                 reads=["sexc"], writes=["thr"])
    k.op("vector", lambda v: v.tensor_scalar_mul(SEXC, SEXC, -1.0),
         reads=["sexc"], writes=["sexc"])
    k.op("vector", lambda v: v.tensor_reduce(MMX[:, 1:2], SEXC, AX.X, ALU.max),
         reads=["sexc"], writes=["thr"])
    if sim_only:
        k.dma("qs", mm_out, MMX[:, 0:2], reads=["thr"], writes=["mmout"],
              stream="cc")
        return
    if mm_in is not None:
        MMI = b1f[0:1, 1176:1192]
        k.dma("qs", MMI, mm_in, writes=["ccst", "thr"], stream="cc")
        k.op("vector", lambda v: v.tensor_reduce(MMX[:, 0:1], MMI[:, 0:8],
                                                 AX.X, ALU.max),
             reads=["ccst"], writes=["thr"])
        k.op("vector", lambda v: v.tensor_reduce(MMX[:, 1:2], MMI[:, 8:16],
                                                 AX.X, ALU.max),
             reads=["ccst"], writes=["thr"])
    elif n_cores > 1:
        stb = aux[:, 512:516].bitcast(F32)[0:1, 0:2] if False else None
        k.op("vector", lambda v: v.tensor_copy(auxf[0:1, 256:258], MMX[:, 0:2]),
             reads=["thr"], writes=["ccst"])
        k.dma("qs", ccin, auxf[0:1, 256:258], reads=["ccst"], writes=["ccin"],
              stream="cc")
        def ccop(g):
            return g.collective_compute(
                "AllReduce", ALU.max,
                [[cid for cid in range(n_cores)]],
                [ccin.opt()], [ccout.opt()])
        k.qop("gpsimd", ccop, reads=["ccin"], writes=["ccout"])
        k.dma("qs", auxf[0:1, 256:258], ccout, reads=["ccout"], writes=["ccst"],
              stream="cc")
        k.op("vector", lambda v: v.tensor_copy(MMX[:, 0:2], auxf[0:1, 256:258]),
             reads=["ccst"], writes=["thr"])
    k.op("vector", lambda v: v.tensor_scalar_mul(MMX[:, 2:3], MMX[:, 0:1], 0.9),
         reads=["thr"], writes=["thr"])
    k.op("vector", lambda v: v.scalar_tensor_tensor(MMX[:, 2:3], MMX[:, 1:2],
                                                    -0.1, MMX[:, 2:3],
                                                    ALU.mult, ALU.add),
         reads=["thr"], writes=["thr"])
    k.op("vector", lambda v: v.tensor_scalar(PROWS, SIMR, MMX[0:1, 2:3], None,
                                             ALU.is_gt),
         reads=["simr", "thr"], writes=["posr"])
    k.op("vector", lambda v: v.tensor_scalar(MNRR, PROWS, -1.0, 1.0, ALU.mult,
                                             ALU.add),
         reads=["posr"], writes=["mnr"])
    k.op("vector", lambda v: v.tensor_copy(MPRR, PROWS), reads=["posr"],
         writes=["mpr"])
    for b in range(BPC):
        k.op("vector", lambda v, b=b: v.memset(MPRR[:, b * T:b * T + 5], 1.0),
             reads=["mpr"], writes=["mpr"])
        k.op("vector", lambda v, b=b: v.memset(MNRR[:, b * T:b * T + 5], 1.0),
             reads=["mnr"], writes=["mnr"])
        k.op("vector", lambda v, b=b: v.memset(PROWS[:, b * T:b * T + 5], 0.5),
             reads=["posr"], writes=["posr"])
    k.dma("qs", posd, PROWS, reads=["posr"], writes=["posd"], stream="cc")
    MPC = auxf[:, 520:552]
    MNC = auxf[:, 552:584]
    pp7bT = PP[7]
    pp7b = PP[7][:, :].bitcast(BF16)
    for g in range(8):
        def trm(t, g=g):
            for q in range(4):
                tc = g * 4 + q
                t.transpose(pp7b[:, 2 * q:2 * q + 1],
                            MPRR[:, tc * 128:(tc + 1) * 128], C.ident[0:1, 0:1])
                last = t.transpose(pp7b[:, 8 + 2 * q:9 + 2 * q],
                                   MNRR[:, tc * 128:(tc + 1) * 128],
                                   C.ident[0:1, 0:1])
            return last
        k.op("tensor", trm, reads=["mpr", "mnr", "c_ident"], writes=["pp7"])
        k.op("vector", lambda v, g=g: v.tensor_copy(
            MPC[:, g * 4:(g + 1) * 4],
            pp7b[:, 0:8].rearrange("p (a b) -> p a b", b=2)[:, :, 0:1]),
             reads=["pp7"], writes=["mpc"])
        k.op("vector", lambda v, g=g: v.tensor_copy(
            MNC[:, g * 4:(g + 1) * 4],
            pp7b[:, 8:16].rearrange("p (a b) -> p a b", b=2)[:, :, 0:1]),
             reads=["pp7"], writes=["mnc"])

    k.fence("sync", writes=["vp5", "vn5", "vnh15", "vtile", "prow", "ones64",
                            "neg10", "xpn", "xnn", "rbp", "rbn", "simr", "sexc",
                            "posr", "mpr", "mnr", "thr", "simacc", "n2r", "qgn",
                            "ho", "sq5", "ccst", "SCR:0", "SCR:1", "SCR:2",
                            "SCR:3", "vst0", "vst1"])
    ONES64 = aux[:, 3744:3808]
    NEG10 = auxf[:, 584:585]
    XPN = aux[:, 1184:1696]
    XNN = aux[:, 1696:2208]
    k.op("vector", lambda v: v.memset(ONES64[:, :], 1.0), writes=["ones64"])
    k.op("vector", lambda v: v.memset(NEG10, -10.0), writes=["neg10"])
    prow = scr[0:1, 1024:1536]
    for b in range(BPC):
        for kt in range(16):
            vt = aux[:, 2208:3232]
            k.dma("qg", vt, vtd[(b * 16 + kt) * 128:(b * 16 + kt + 1) * 128, :],
                  reads=[f"{tag}vd"], writes=["vtile"], stream="vt")
            for which, msk, mkey in (("p", MPC, "mpc"), ("n", MNC, "mnc")):
                nh = 16 if which == "p" else 15
                boff = (0 if which == "p" else 16640) + kt * 65
                def build(v, kt=kt, msk=msk, which=which, nh=nh, boff=boff, b=b):
                    col = msk[:, b * 16 + kt:b * 16 + kt + 1]
                    oap = bass.AP(b1_[:, 0:1].tensor, boff,
                                  [[32768, 128], [1040, nh], [1, 64]])
                    iap = bass.AP(vt.tensor, 2208, [[4096, 128], [64, nh], [1, 64]])
                    v.tensor_scalar(oap, iap, col, None, ALU.mult)
                    dap = bass.AP(b1_[:, 0:1].tensor, boff + 64,
                                  [[32768, 128], [1040, nh], [1, 1]])
                    return v.tensor_copy(dap, col.broadcast_to((128, nh, 1)))
                k.op("vector", build, reads=["vtile", mkey],
                     writes=["vp5" if which == "p" else "vn5"])
            def buildn15(v, b=b, kt=kt):
                col = MNC[:, b * 16 + kt:b * 16 + kt + 1]
                v.tensor_scalar(aux[:, kt * 65:kt * 65 + 64], vt[:, 15 * 64:16 * 64],
                                col, None, ALU.mult)
                return v.tensor_copy(aux[:, kt * 65 + 64:kt * 65 + 65], col)
            k.op("vector", buildn15, reads=["vtile", "mnc"], writes=["vnh15"])
        for qg in range(4):
            q0 = b * T + qg * 512
            k.dma("qg", prow, posd[0:1, q0:q0 + 512], reads=["posd"],
                  writes=["prow"], stream="pr")
            k.op("tensor", lambda t: t.matmul(PP[6][0:64, :], ONES64[0:1, :],
                                              prow, start=True, stop=True),
                 reads=["prow", "ones64"], writes=["pp6"])
            for h in range(16):
                hc, rb = h // 2, (h % 2) * 64
                for kt in range(16):
                    k0 = b * T + kt * 128
                    ps, psk = PP[kt % 2], f"pp{kt % 2}"
                    k.op("tensor", lambda t, ps=ps, hc=hc, rb=rb, k0=k0, q0=q0:
                         t.matmul(ps[:], bf.bigv(2, hc)[rb:rb + 64, k0:k0 + 128],
                                  bf.bigv(1, hc)[rb:rb + 64, q0:q0 + 512],
                                  start=True, stop=True),
                         reads=[bkey(2, hc), bkey(1, hc)], writes=[psk])
                    pt = scr[:, (kt % 2) * 512:(kt % 2) * 512 + 512]
                    ptk = f"SCR:{kt % 2}"
                    k.op("scalar", lambda s, pt=pt, ps=ps: s.activation(
                        pt, ps[:], AF.Exp, bias=NEG10),
                         reads=[psk, "neg10"], writes=[ptk])
                    k.op("tensor", lambda t, pt=pt, h=h, kt=kt: t.matmul(
                        PP[2][0:65, :], vp_ap(h, kt), pt,
                        start=(kt == 0), stop=(kt == 15)),
                         reads=[ptk, "vp5"], writes=["pp2"])
                    k.op("tensor", lambda t, pt=pt, h=h, kt=kt: t.matmul(
                        PP[3][0:65, :], vn_ap(h, kt), pt,
                        start=(kt == 0), stop=(kt == 15)),
                         reads=[ptk, "vn5", "vnh15"], writes=["pp3"])
                rbp = scr[64:65, 1536:2048]
                rbn = aux[64:65, 3232:3744]
                def recp(v, rbp=rbp):
                    with k.nc.allow_low_precision(
                            reason="bf16 recip of softmax den; P already bf16"):
                        return v.reciprocal(rbp, PP[2][64:65, :])
                def recn(v, rbn=rbn):
                    with k.nc.allow_low_precision(
                            reason="bf16 recip of softmax den; P already bf16"):
                        return v.reciprocal(rbn, PP[3][64:65, :])
                k.op("vector", recp, reads=["pp2"], writes=["rbp", "SCR:3"])
                k.op("vector", recn, reads=["pp3"], writes=["rbn"])
                k.op("tensor", lambda t, rbp=rbp: t.matmul(
                    PP[4][0:64, :], ONES64[64:65, :], rbp, start=True, stop=True,
                    tile_position=(64, 0)),
                     reads=["rbp", "SCR:3", "ones64"], writes=["pp4"])
                k.op("tensor", lambda t, rbn=rbn: t.matmul(
                    PP[5][0:64, :], ONES64[64:65, :], rbn, start=True, stop=True,
                    tile_position=(64, 0)),
                     reads=["rbn", "ones64"], writes=["pp5"])
                k.op("vector", lambda v: v.tensor_copy(scr[0:64, 0:512],
                                                       PP[4][0:64, :]),
                     reads=["pp4"], writes=["SCR:0"])
                k.op("vector", lambda v: v.tensor_copy(scr[0:64, 512:1024],
                                                       PP[5][0:64, :]),
                     reads=["pp5"], writes=["SCR:1"])
                k.op("vector", lambda v: v.tensor_tensor(
                    XPN[0:64, :], PP[2][0:64, :], scr[0:64, 0:512], ALU.mult),
                     reads=["pp2", "SCR:0"], writes=["xpn"])
                k.op("vector", lambda v: v.tensor_tensor(
                    XNN[0:64, :], PP[3][0:64, :], scr[0:64, 512:1024], ALU.mult),
                     reads=["pp3", "SCR:1"], writes=["xnn"])
                k.op("vector", lambda v: v.tensor_sub(XPN[0:64, :], XPN[0:64, :],
                                                      XNN[0:64, :]),
                     reads=["xpn", "xnn"], writes=["xpn"])
                k.op("vector", lambda v: v.tensor_tensor(
                    XPN[0:64, :], XPN[0:64, :], PP[6][0:64, :], ALU.mult),
                     reads=["xpn", "pp6"], writes=["xpn"])
                k.op("vector", lambda v: v.tensor_add(XPN[0:64, :], XPN[0:64, :],
                                                      XNN[0:64, :]),
                     reads=["xpn", "xnn"], writes=["xpn"])
                okey = f"ao5/{hc}/{h % 2}/{b}/{qg}"
                if rb == 0:
                    k.op("vector", lambda v, hc=hc, q0=q0: v.tensor_copy(
                        bf.bigv(1, hc)[0:64, q0:q0 + 512], XPN[0:64, :]),
                         reads=["xpn", bkey(1, hc)], writes=[okey])
                else:
                    k.op("tensor", lambda t: t.matmul(
                        PP[7][64:128, :], C.ident[0:64, 0:64], XPN[0:64, :],
                        start=True, stop=True, tile_position=(0, 64)),
                         reads=["xpn", "c_ident"], writes=["pp7"])
                    k.op("vector", lambda v, hc=hc, q0=q0: v.tensor_copy(
                        bf.bigv(1, hc)[64:128, q0:q0 + 512], PP[7][64:128, :]),
                         reads=["pp7", bkey(1, hc)], writes=[okey])
    k.fence("sync", reads=[f"ao5/{hc}/{hf}/{b}/{qg}" for hc in range(8)
                           for hf in range(2) for b in range(2) for qg in range(4)],
            writes=[bkey(1, c) for c in range(8)])

    BPJ = auxf[:, 326:334]
    k.dma("qs", BPJ, ins["l2f_bproj"], writes=["bpj", "vnh15"], stream="bias")
    for oc in range(DC):
        sl = oc % 2
        slab = scr[:, sl * 1024:(sl + 1) * 1024]
        slks = [f"SCR:{2 * sl}", f"SCR:{2 * sl + 1}"]
        k.dma("qg", slab.rearrange("p (kc m) -> p kc m", kc=8),
              ins["l2f_wproj"][:, oc * 128:(oc + 1) * 128]
              .rearrange("(kc p) m -> p kc m", p=128),
              writes=slks, stream=f"w{sl}")
        for j in range(TJ):
            ppi = j % 2
            def mm(t, slab=slab, j=j, ppi=ppi):
                for kc in range(DC):
                    last = t.matmul(PP[ppi][:], slab[:, kc * 128:(kc + 1) * 128],
                                    bf.bigv(1, kc)[:, j * 512:(j + 1) * 512],
                                    start=(kc == 0), stop=(kc == DC - 1))
                return last
            k.op("tensor", mm, reads=slks + [bkey(1, c) for c in range(8)],
                 writes=[f"pp{ppi}"])
            stage = aux[:, 1184 + (j % 2) * 1024:1184 + (j % 2) * 1024 + 1024].bitcast(F32)
            stk = f"stg5{j % 2}"
            k.op("scalar", lambda s, stage=stage, oc=oc, ppi=ppi: s.activation(
                stage, PP[ppi][:], AF.Identity, bias=BPJ[:, oc:oc + 1]),
                 reads=[f"pp{ppi}", "bpj"], writes=[stk, "xpn", "xnn", "vtile"])
            k.dma("qg", dst_dram[oc * 128:(oc + 1) * 128, j * 512:(j + 1) * 512],
                  stage, reads=[stk, dst_key(oc, j)], writes=[dst_key(oc, j)],
                  accum_op=ALU.add, stream=f"acc{j % 2}")


def emit_l3_attn(k: Kfw, C: Consts, bf: Bufs, ins, dst_dram, dst_key, tag="a3"):
    aux, scr = bf.aux, bf.scr
    scrf = scr[:, :].bitcast(F32)
    b1_, b2_, b3_ = bf.big[0], bf.big[1], bf.big[2]
    b1f = b1_[:, :].bitcast(F32)
    PP = bf.pp
    DI, NCC = 2048, 16
    zd = k.dram(f"{tag}_zd", [DI, NT], BF16)
    deltad = k.dram(f"{tag}_dd", [DI, NT], BF16)

    def xc_chunk(cc):
        return (bf.bigv(1, cc) if cc < 8 else bf.bigv(2, cc - 8))

    def xck(cc):
        return bkey(1, cc) if cc < 8 else bkey(2, cc - 8)

    DTR = b1_[0:64, 0:4096]
    DTW = b1_[0:64, 4096:6144]
    BBF = b1_[64:80, 0:4096]
    CBF = b1_[64:80, 4096:8192]
    DA_T = b1_[:, 8192:10240]
    DBU = b1_[:, 10240:12288]
    HT = b1_[:, 12288:14336]
    HC3 = b1_[:, 14336:16384]
    DLS = b1_[:, 16384:18432]
    CLT = b1_[:, 18432:20480]
    BLT = b1_[:, 20480:22528]
    REPC3 = b1_[:, 22528:24576]
    UD = b1_[:, 24576:26624]
    DCS = b1_[:, 26624:28672]
    ZS3 = b1_[:, 28672:29696]
    REPB3 = b1_[0:80, 29696:29824]
    CW3 = b1f[:, 14912:14976]
    CB3 = b1f[:, 14976:14992]
    DTB = b1f[:, 14992:15008]
    DCOL3 = b1f[:, 15008:15024]
    ACOL3 = b1f[:, 15024:15280]
    B1K3 = ["dtr", "bbf3", "cbf3", "dat3", "dbu3", "ht3",
            "hc3", "dls", "clt3", "blt3", "repc3", "dtw", "ud", "dcs", "zs3",
            "repb3", "cw3", "cb3", "dtb", "dcol3", "acol3",
            "xslab3", "dstg0", "dstg1", "ab0", "ab1"]

    k.fence("sync", writes=[bkey(1, c) for c in range(8)] +
            [bkey(2, c) for c in range(8)] + [A0, A1, A2, A3])
    for oc in range(32):
        sl = oc % 2
        slab = scr[:, sl * 1024:(sl + 1) * 1024]
        slks = [f"SCR:{2 * sl}", f"SCR:{2 * sl + 1}"]
        k.dma("qg", slab.rearrange("p (kc m) -> p kc m", kc=8),
              ins["l3a_inprojT"][:, oc * 128:(oc + 1) * 128]
              .rearrange("(kc p) m -> p kc m", p=128),
              writes=slks, stream=f"w{sl}")
        for j in range(TJ):
            ppi = j % 2
            def mm(t, slab=slab, j=j, ppi=ppi):
                for kc in range(DC):
                    last = t.matmul(PP[ppi][:], slab[:, kc * 128:(kc + 1) * 128],
                                    bf.bigv(0, kc)[:, j * 512:(j + 1) * 512],
                                    start=(kc == 0), stop=(kc == DC - 1))
                return last
            k.op("tensor", mm, reads=slks + BKALL(0), writes=[f"pp{ppi}"])
            if oc < 16:
                k.op("vector", lambda v, oc=oc, j=j, ppi=ppi: v.tensor_copy(
                    xc_chunk(oc)[:, j * 512:(j + 1) * 512], PP[ppi][:]),
                     reads=[f"pp{ppi}"], writes=[xck(oc)])
            else:
                st_bf = aux[:, (j % 2) * 512:(j % 2) * 512 + 512]
                stk = ["zst0", "zst1"][j % 2]
                k.op("vector", lambda v, st_bf=st_bf, ppi=ppi: v.tensor_copy(
                    st_bf, PP[ppi][:]), reads=[f"pp{ppi}"], writes=[stk, "AUX:0"])
                k.dma("qs", zd[(oc - 16) * 128:(oc - 15) * 128,
                               j * 512:(j + 1) * 512],
                      st_bf, reads=[stk], writes=["zd3"], stream=f"st{j % 2}")

    k.fence("sync", writes=BKALL(0) + B1K3 + ["zst0", "zst1", "SCR:0", "SCR:1",
                                              "SCR:2", "SCR:3"])
    k.dma("qg", REPC3.rearrange("p (m f) -> p m f", m=16),
          ins["cst_repc"].rearrange("(m p) f -> p m f", p=128),
          writes=["repc3"], stream="cst")
    k.dma("qg", REPB3[64:80, :], ins["cst_repb"], writes=["repb3"], stream="cst")
    k.dma("qs", CW3, ins["l3a_convw"], writes=["cw3"], stream="cst2")
    k.dma("qs", CB3, ins["l3a_convb"], writes=["cb3"], stream="cst2")
    k.dma("qs", DTB, ins["l3a_dtb"], writes=["dtb"], stream="cst2")
    k.dma("qs", DCOL3, ins["l3a_dcol"], writes=["dcol3"], stream="cst2")
    k.dma("qs", ACOL3, ins["l3a_alog"], writes=["acol3"], stream="cst2")
    k.dma("qg", DTW, ins["l3a_dtprojT"], writes=["dtw"], stream="cst")
    k.dma("qg", scr[:, :].rearrange("p (m f) -> p m f", m=16),
          ins["cst_summ"].rearrange("(m p) f -> p m f", p=128),
          writes=["summ3", "SCR:0", "SCR:1", "SCR:2", "SCR:3"], stream="cst")
    k.op("scalar", lambda s: s.activation(ACOL3, ACOL3, AF.Exp),
         reads=["acol3"], writes=["acol3"])
    k.op("vector", lambda v: v.tensor_scalar_mul(ACOL3, ACOL3, -1.0),
         reads=["acol3"], writes=["acol3"])
    for cc in range(NCC):
        for b in range(BPC):
            strip = xc_chunk(cc)[:, b * T:(b + 1) * T]
            astrip = aux[:, 0:T]
            k.op("vector", lambda v, strip=strip, astrip=astrip, cc=cc: v.tensor_scalar(
                astrip, strip, CW3[:, cc * 4 + 3:cc * 4 + 4], None, ALU.mult),
                 reads=[xck(cc), "cw3"], writes=["cstrip3", "AUX:0", "AUX:1"])
            for kk in range(3):
                sh = 3 - kk
                k.op("vector", lambda v, strip=strip, astrip=astrip, cc=cc, kk=kk,
                     sh=sh: v.scalar_tensor_tensor(
                         astrip[:, sh:T], strip[:, 0:T - sh],
                         CW3[:, cc * 4 + kk:cc * 4 + kk + 1],
                         astrip[:, sh:T], ALU.mult, ALU.add),
                     reads=[xck(cc), "cw3", "cstrip3"], writes=["cstrip3"])
            k.op("scalar", lambda s, strip=strip, astrip=astrip, cc=cc: s.activation(
                strip, astrip, AF.Silu, bias=CB3[:, cc:cc + 1]),
                 reads=["cstrip3", "cb3"], writes=[xck(cc)])

    for j in range(TJ):
        xslab = aux[:, 0:1536]
        if j == 0:
            k.dma("qg", xslab.rearrange("p (kc m) -> p kc m", kc=16),
                  ins["l3a_xprojT"].rearrange("(kc p) m -> p kc m", p=128),
                  writes=["xslab3", "cstrip3", "AUX:0", "AUX:1"], stream="w0")
        def mmx(t, j=j, xslab=xslab):
            for kc in range(NCC):
                t.matmul(PP[4][0:64, :], xslab[:, kc * 96 + 0:kc * 96 + 64],
                         xc_chunk(kc)[:, j * 512:(j + 1) * 512],
                         start=(kc == 0), stop=(kc == NCC - 1))
            for kc in range(NCC):
                t.matmul(PP[4][64:80, :], xslab[:, kc * 96 + 64:kc * 96 + 80],
                         xc_chunk(kc)[:, j * 512:(j + 1) * 512],
                         start=(kc == 0), stop=(kc == NCC - 1),
                         tile_position=(0, 64))
            for kc in range(NCC):
                last = t.matmul(PP[5][64:80, :], xslab[:, kc * 96 + 80:kc * 96 + 96],
                                xc_chunk(kc)[:, j * 512:(j + 1) * 512],
                                start=(kc == 0), stop=(kc == NCC - 1),
                                tile_position=(0, 64))
            return last
        k.op("tensor", mmx, reads=["xslab3"] + [xck(c) for c in range(16)],
             writes=["pp4", "pp5"])
        k.op("vector", lambda v, j=j: v.tensor_copy(DTR[:, j * 512:(j + 1) * 512],
                                                    PP[4][0:64, :]),
             reads=["pp4"], writes=["dtr"])
        k.op("vector", lambda v, j=j: v.tensor_copy(BBF[:, j * 512:(j + 1) * 512],
                                                    PP[4][64:80, :]),
             reads=["pp4"], writes=["bbf3"])
        k.op("vector", lambda v, j=j: v.tensor_copy(CBF[:, j * 512:(j + 1) * 512],
                                                    PP[5][64:80, :]),
             reads=["pp5"], writes=["cbf3"])

    for cc in range(NCC):
        for j in range(TJ):
            ppi = j % 2
            k.op("tensor", lambda t, cc=cc, j=j, ppi=ppi: t.matmul(
                PP[ppi][:], DTW[:, cc * 128:(cc + 1) * 128],
                DTR[:, j * 512:(j + 1) * 512], start=True, stop=True),
                 reads=["dtw", "dtr"], writes=[f"pp{ppi}"])
            t_re = scrf[:, (j % 2) * 256:(j % 2) * 256 + 256] if False else None
            t_ab = aux[:, 1536 + (j % 2) * 512:1536 + (j % 2) * 512 + 512]
            stg = aux[:, 2560 + (j % 2) * 512:2560 + (j % 2) * 512 + 512]
            abk = ["ab0", "ab1"][j % 2]
            stk = ["dstg0", "dstg1"][j % 2]
            k.op("scalar", lambda s, t_ab=t_ab, cc=cc, ppi=ppi: s.activation(
                t_ab, PP[ppi][:], AF.Abs, bias=DTB[:, cc:cc + 1]),
                 reads=[f"pp{ppi}", "dtb"], writes=[abk, "AUX:1", "AUX:2"])
            k.op("scalar", lambda s, t_ab=t_ab: s.activation(t_ab, t_ab, AF.Exp,
                                                             scale=-1.0),
                 reads=[abk], writes=[abk])
            k.op("vector", lambda v, t_ab=t_ab: v.tensor_scalar_add(t_ab, t_ab, 1.0),
                 reads=[abk], writes=[abk])
            k.op("scalar", lambda s, t_ab=t_ab: s.activation(t_ab, t_ab, AF.Ln),
                 reads=[abk], writes=[abk])
            k.op("vector", lambda v, stg=stg, cc=cc, ppi=ppi: v.tensor_scalar(
                stg, PP[ppi][:], DTB[:, cc:cc + 1], 0.0, ALU.add, ALU.max),
                 reads=[f"pp{ppi}", "dtb"], writes=[stk, "AUX:2", "AUX:3"])
            k.op("vector", lambda v, stg=stg, t_ab=t_ab: v.tensor_add(stg, stg, t_ab),
                 reads=[stk, abk], writes=[stk])
            k.dma("qs", deltad[cc * 128:(cc + 1) * 128, j * 512:(j + 1) * 512],
                  stg, reads=[stk], writes=["dd3"], stream=f"st{j % 2}")

    for b in range(BPC):
        bs = b * T
        for s4 in range(4):
            k.op("tensor", lambda t, bs=bs, s4=s4: t.matmul(
                PP[4][:], REPB3[64:80, :],
                BBF[:, bs + s4 * 512:bs + (s4 + 1) * 512], start=True, stop=True,
                tile_position=(64, 0)),
                 reads=["bbf3", "repb3"], writes=["pp4"])
            k.op("vector", lambda v, s4=s4: v.tensor_copy(
                BLT[:, s4 * 512:(s4 + 1) * 512], PP[4][:]),
                 reads=["pp4"], writes=["blt3"])
            k.op("tensor", lambda t, bs=bs, s4=s4: t.matmul(
                PP[5][:], REPB3[64:80, :],
                CBF[:, bs + s4 * 512:bs + (s4 + 1) * 512], start=True, stop=True,
                tile_position=(64, 0)),
                 reads=["cbf3", "repb3"], writes=["pp5"])
            k.op("vector", lambda v, s4=s4: v.tensor_copy(
                CLT[:, s4 * 512:(s4 + 1) * 512], PP[5][:]),
                 reads=["pp5"], writes=["clt3"])
        for cc in range(NCC):
            k.dma("qg", DCS, deltad[cc * 128:(cc + 1) * 128, bs:bs + T],
                  reads=["dd3"], writes=["dcs"], stream="dl")
            k.op("vector", lambda v, cc=cc, bs=bs: v.tensor_tensor(
                UD[:, :], DCS[:, :], xc_chunk(cc)[:, bs:bs + T], ALU.mult),
                 reads=["dcs", xck(cc)], writes=["ud"])
            for jj in range(16):
                lt = cc * 16 + jj
                acol = ACOL3[:, lt:lt + 1]
                for s4 in range(4):
                    pu, puk = PP[4 + s4 % 2], f"pp{4 + s4 % 2}"
                    pd, pdk = PP[6 + s4 % 2], f"pp{6 + s4 % 2}"
                    k.op("tensor", lambda t, jj=jj, s4=s4, pu=pu: t.matmul(
                        pu[:], REPC3[:, jj * 128:(jj + 1) * 128],
                        UD[:, s4 * 512:(s4 + 1) * 512], start=True, stop=True),
                         reads=["ud", "repc3"], writes=[puk])
                    k.op("tensor", lambda t, jj=jj, s4=s4, pd=pd: t.matmul(
                        pd[:], REPC3[:, jj * 128:(jj + 1) * 128],
                        DCS[:, s4 * 512:(s4 + 1) * 512], start=True, stop=True),
                         reads=["dcs", "repc3"], writes=[pdk])
                    k.op("vector", lambda v, s4=s4, pu=pu: v.tensor_tensor(
                        DBU[:, s4 * 512:(s4 + 1) * 512],
                        BLT[:, s4 * 512:(s4 + 1) * 512], pu[:], ALU.mult),
                         reads=[puk, "blt3"], writes=["dbu3"])
                    k.op("vector", lambda v, s4=s4, pd=pd: v.tensor_copy(
                        DLS[:, s4 * 512:(s4 + 1) * 512], pd[:]),
                         reads=[pdk], writes=["dls"])
                k.op("gpsimd", lambda g, acol=acol: g.tensor_scalar(
                    DA_T[:, :], DLS[:, :], acol, None, ALU.mult),
                     reads=["dls", "acol3"], writes=["dat3"])
                k.op("scalar", lambda s: s.activation(DA_T[:, :], DA_T[:, :], AF.Exp),
                     reads=["dat3"], writes=["dat3"])
                k.op("vector", lambda v: v.tensor_tensor_scan(
                    HT[:, :], DA_T[:, :], DBU[:, :], 0.0, ALU.mult, ALU.add),
                     reads=["dat3", "dbu3"], writes=["ht3"])
                k.op("gpsimd", lambda g: g.tensor_tensor(HC3[:, :], HT[:, :],
                                                         CLT[:, :], ALU.mult),
                     reads=["ht3", "clt3"], writes=["hc3"])
                def mmy(t, jj=jj):
                    for s4 in range(4):
                        last = t.matmul(PP[s4][:], scr[:, jj * 128:(jj + 1) * 128],
                                        HC3[:, s4 * 512:(s4 + 1) * 512],
                                        start=(jj == 0), stop=(jj == 15))
                    return last
                k.op("tensor", mmy, reads=["hc3", "summ3"],
                     writes=["pp0", "pp1", "pp2", "pp3"])
            for s4 in range(4):
                zs = ZS3[:, (s4 % 2) * 512:(s4 % 2) * 512 + 512]
                zsk = f"zs3{s4 % 2}"
                k.dma("qg", zs, zd[cc * 128:(cc + 1) * 128,
                                   bs + s4 * 512:bs + (s4 + 1) * 512],
                      reads=["zd3"], writes=[zsk, "zs3"], stream=f"zl{s4 % 2}")
                k.op("scalar", lambda s, zs=zs: s.activation(zs, zs, AF.Silu),
                     reads=[zsk], writes=[zsk])
                k.op("vector", lambda v, cc=cc, bs=bs, s4=s4: v.scalar_tensor_tensor(
                    HC3[:, s4 * 512:(s4 + 1) * 512],
                    xc_chunk(cc)[:, bs + s4 * 512:bs + (s4 + 1) * 512],
                    DCOL3[:, cc:cc + 1], PP[s4][:], ALU.mult, ALU.add),
                     reads=[xck(cc), "dcol3", f"pp{s4}"], writes=["hc3"])
                k.op("vector", lambda v, cc=cc, bs=bs, s4=s4, zs=zs: v.tensor_tensor(
                    xc_chunk(cc)[:, bs + s4 * 512:bs + (s4 + 1) * 512],
                    HC3[:, s4 * 512:(s4 + 1) * 512], zs, ALU.mult),
                     reads=["hc3", zsk], writes=[xck(cc)])

    for oc in range(DC):
        sl = oc % 2
        slab = aux[:, sl * 2048:(sl + 1) * 2048]
        k.dma("qg", slab.rearrange("p (kc m) -> p kc m", kc=16),
              ins["l3a_outprojT"][:, oc * 128:(oc + 1) * 128]
              .rearrange("(kc p) m -> p kc m", p=128),
              writes=[["AUX:0", "AUX:1"], ["AUX:2", "AUX:3"]][sl] +
                     ["cstrip3", "xslab3", "ab0", "ab1", "dstg0", "dstg1"],
              stream=f"w{sl}")
        for j in range(TJ):
            ppi = 6 + j % 2
            def mm(t, slab=slab, j=j, ppi=ppi):
                for kc in range(NCC):
                    last = t.matmul(PP[ppi][:], slab[:, kc * 128:(kc + 1) * 128],
                                    xc_chunk(kc)[:, j * 512:(j + 1) * 512],
                                    start=(kc == 0), stop=(kc == NCC - 1))
                return last
            k.op("tensor", mm, reads=[["AUX:0", "AUX:1"], ["AUX:2", "AUX:3"]][sl] +
                 [xck(c) for c in range(16)], writes=[f"pp{ppi}"])
            stage = scrf[:, (j % 2) * 512:(j % 2) * 512 + 512]
            stks = [f"SCR:{2 * (j % 2)}", f"SCR:{2 * (j % 2) + 1}", "summ3"]
            k.op("vector", lambda v, stage=stage, ppi=ppi: v.tensor_copy(
                stage, PP[ppi][:]), reads=[f"pp{ppi}"], writes=stks)
            k.dma("qg", dst_dram[oc * 128:(oc + 1) * 128, j * 512:(j + 1) * 512],
                  stage, reads=stks + [dst_key(oc, j)], writes=[dst_key(oc, j)],
                  accum_op=ALU.add, stream=f"acc{j % 2}")


def emit_l3_ffn(k: Kfw, C: Consts, bf: Bufs, ins, dst_dram, dst_key, tag="f3"):
    aux, scr = bf.aux, bf.scr
    b1_, b2_, b3_ = bf.big[0], bf.big[1], bf.big[2]
    b1f = b1_[:, :].bitcast(F32)
    PP = bf.pp
    B1K7 = ["sr", "wsr", "bis3", "wrc", "b1c7", "b2c7", "h7",
            "u7", "s17", "stg7", "xstg0", "xstg1"]
    SR = b3_[:, :].bitcast(F32)[0:1, 8192:12288]
    WSR = b3_[:, :].bitcast(F32)[0:1, 12288:16384]
    BIS3 = aux[:, 3168:3200].bitcast(F32)[0:1, :]
    LO3, HI3, MID3 = BIS3[:, 0:1], BIS3[:, 1:2], BIS3[:, 2:3]
    CNT3, SEL3, NS3, T13 = (BIS3[:, 3:4], BIS3[:, 4:5], BIS3[:, 5:6],
                            BIS3[:, 6:7])
    WRC = aux[:, 3200:3216].bitcast(F32)
    B1C = aux[:, 3072:3136].bitcast(F32)
    B2C = aux[:, 3136:3152].bitcast(F32)
    WRCB = aux[:, 3152:3160]
    MSCR = aux[:, 3232:4096].bitcast(F32)[0:1, 0:432]

    k.fence("sync", writes=[bkey(1, c) for c in range(8)] +
            [bkey(2, c) for c in range(8)] + [A0, A1, A2, A3] + B1K7 +
            ["SCR:0", "SCR:1", "SCR:2", "SCR:3"])
    k.dma("qs", WRC, ins["l3f_wrc"], writes=["wrc"], stream="bias")
    k.dma("qs", B1C, ins["l3f_b1c"], writes=["b1c7"], stream="bias")
    k.dma("qs", B2C, ins["l3f_b2c"], writes=["b2c7"], stream="bias")
    k.op("vector", lambda v: v.tensor_copy(WRCB, WRC), reads=["wrc", "b1c7",
                                                              "b2c7"],
         writes=["wrc"])
    for j in range(TJ):
        def mms(t, j=j):
            for dc in range(DC):
                last = t.matmul(PP[6][0:1, :], WRCB[:, dc:dc + 1],
                                bf.bigv(0, dc)[:, j * 512:(j + 1) * 512],
                                start=(dc == 0), stop=(dc == DC - 1))
            return last
        k.op("tensor", mms, reads=["wrc"] + BKALL(0), writes=["pp6"])
        k.op("vector", lambda v, j=j: v.tensor_copy(SR[:, j * 512:(j + 1) * 512],
                                                    PP[6][0:1, :]),
             reads=["pp6"], writes=["sr"])
    for b in range(BPC):
        srb = SR[:, b * T:(b + 1) * T]
        k.op("vector", lambda v, srb=srb: v.tensor_reduce(LO3, srb, AX.X, ALU.min),
             reads=["sr"], writes=["bis3"])
        k.op("vector", lambda v, srb=srb: v.tensor_reduce(HI3, srb, AX.X, ALU.max),
             reads=["sr"], writes=["bis3"])
        for it in range(45):
            k.op("vector", lambda v: v.tensor_add(MID3, LO3, HI3),
                 reads=["bis3"], writes=["bis3"])
            k.op("vector", lambda v: v.tensor_scalar_mul(MID3, MID3, 0.5),
                 reads=["bis3"], writes=["bis3"])
            k.op("vector", lambda v, srb=srb: v.tensor_scalar(
                WSR[:, 0:2048], srb, MID3, None, ALU.is_gt, op1=ALU.add,
                accum_out=CNT3),
                 reads=["sr", "bis3"], writes=["wsr", "bis3"])
            k.op("vector", lambda v: v.tensor_scalar(SEL3, CNT3, 1024.0, None,
                                                     ALU.is_ge),
                 reads=["bis3"], writes=["bis3"])
            k.op("vector", lambda v: v.tensor_sub(T13, MID3, LO3),
                 reads=["bis3"], writes=["bis3"])
            k.op("vector", lambda v: v.tensor_mul(T13, T13, SEL3),
                 reads=["bis3"], writes=["bis3"])
            k.op("vector", lambda v: v.tensor_add(LO3, LO3, T13),
                 reads=["bis3"], writes=["bis3"])
            k.op("vector", lambda v: v.tensor_scalar(NS3, SEL3, -1.0, 1.0,
                                                     ALU.mult, ALU.add),
                 reads=["bis3"], writes=["bis3"])
            k.op("vector", lambda v: v.tensor_sub(T13, MID3, HI3),
                 reads=["bis3"], writes=["bis3"])
            k.op("vector", lambda v: v.tensor_mul(T13, T13, NS3),
                 reads=["bis3"], writes=["bis3"])
            k.op("vector", lambda v: v.tensor_add(HI3, HI3, T13),
                 reads=["bis3"], writes=["bis3"])
        k.op("vector", lambda v, b=b: v.tensor_copy(
            BIS3[:, 8 + b:9 + b], LO3), reads=["bis3"], writes=["bis3"])
    k.op("scalar", lambda s: s.activation(WSR, SR, AF.Sigmoid),
         reads=["sr"], writes=["wsr"])
    for b in range(BPC):
        k.op("vector", lambda v, b=b: v.tensor_scalar(
            SR[:, b * T:(b + 1) * T], SR[:, b * T:(b + 1) * T],
            BIS3[0:1, 8 + b:9 + b], None, ALU.is_gt),
             reads=["sr", "bis3"], writes=["sr"])
    k.op("vector", lambda v: v.tensor_tensor(WSR, WSR, SR, ALU.mult),
         reads=["wsr", "sr"], writes=["wsr"])
    for c in range(DC):
        for j in range(TJ):
            stg = aux[:, (j % 2) * 1024:(j % 2) * 1024 + 1024].bitcast(F32)
            stk = [f"xstg{j % 2}", f"AUX:{j % 2}"]
            k.op("vector", lambda v, c=c, j=j, stg=stg: v.tensor_copy(
                stg, bf.bigv(0, c)[:, j * 512:(j + 1) * 512]),
                 reads=[bkey(0, c)], writes=stk)
            k.dma("qg", dst_dram[c * 128:(c + 1) * 128, j * 512:(j + 1) * 512],
                  stg, reads=stk + [dst_key(c, j)], writes=[dst_key(c, j)],
                  accum_op=ALU.add, stream=f"acc{j % 2}")
    k.dma("qg", b2_[:, :].rearrange("p (dc i) -> p dc i", dc=8),
          ins["l3f_w1"].rearrange("(dc p) i -> p dc i", p=128),
          writes=[bkey(1, c) for c in range(8)], stream="w1")
    k.fence("sync", writes=["h7", "xstg0", "xstg1", "u7", "s17", "stg7",
                             "AUX:0", "AUX:1", "AUX:2", "AUX:3"])
    GC1, GC2 = 0.7978845608028654, 0.044715
    U7 = aux[:, 0:1024].bitcast(F32)
    S17 = aux[:, 1024:2048].bitcast(F32)
    for tb in range(TJ):
        for ic in range(32):
            ph, phk = PP[4 + ic % 2], f"pp{4 + ic % 2}"
            def mmh(t, ic=ic, tb=tb, ph=ph):
                for dc in range(DC):
                    last = t.matmul(ph[:], b2_[:, dc * 4096 + ic * 128:
                                               dc * 4096 + (ic + 1) * 128],
                                    bf.bigv(0, dc)[:, tb * 512:(tb + 1) * 512],
                                    start=(dc == 0), stop=(dc == DC - 1))
                return last
            k.op("tensor", mmh, reads=[bkey(1, c) for c in range(8)] + BKALL(0),
                 writes=[phk])
            k.op("scalar", lambda s, ic=ic, ph=ph: s.activation(
                U7, ph[:], AF.Identity, bias=B1C[:, ic:ic + 1]),
                 reads=[phk, "b1c7"], writes=["u7", "AUX:0", "AUX:1"])
            k.op("scalar", lambda s: s.activation(S17, U7, AF.Square),
                 reads=["u7"], writes=["s17", "AUX:2", "AUX:3"])
            k.op("vector", lambda v: v.tensor_mul(S17, S17, U7),
                 reads=["s17", "u7"], writes=["s17"])
            k.op("vector", lambda v: v.scalar_tensor_tensor(
                S17, S17, GC2, U7, ALU.mult, ALU.add),
                 reads=["s17", "u7"], writes=["s17"])
            k.op("scalar", lambda s: s.activation(S17, S17, AF.Tanh, scale=GC1),
                 reads=["s17"], writes=["s17"])
            k.op("vector", lambda v: v.tensor_scalar(S17, S17, 0.5, 0.5,
                                                     ALU.mult, ALU.add),
                 reads=["s17"], writes=["s17"])
            k.op("vector", lambda v, ic=ic: v.tensor_tensor(
                b3_[:, ic * 512:(ic + 1) * 512], S17, U7, ALU.mult),
                 reads=["s17", "u7"], writes=["h7"])
        k.op("tensor", lambda t, tb=tb: t.matmul(
            PP[6][:], C.ones_row[:, :],
            WSR[:, tb * 512:(tb + 1) * 512], start=True, stop=True),
             reads=["wsr", "c_onesr"], writes=["pp6"])
        for dc in range(DC):
            ppi = dc % 2
            for ih in range(2):
                slab = scr[:, :]
                slks = ["SCR:0", "SCR:1", "SCR:2", "SCR:3"]
                k.dma("qg", slab.rearrange("p (a m) -> p a m", a=16),
                      ins["l3f_w2"][ih * 2048:(ih + 1) * 2048,
                                    dc * 128:(dc + 1) * 128]
                      .rearrange("(a p) m -> p a m", p=128),
                      writes=slks, stream="w0")
                def mmy(t, dc=dc, ih=ih, ppi=ppi):
                    for a in range(16):
                        ic = ih * 16 + a
                        last = t.matmul(PP[ppi][:], scr[:, a * 128:(a + 1) * 128],
                                        b3_[:, ic * 512:(ic + 1) * 512],
                                        start=(ic == 0), stop=(ic == 31))
                    return last
                k.op("tensor", mmy, reads=slks + ["h7"], writes=[f"pp{ppi}"])
            stg = aux[:, 2048:3072].bitcast(F32)
            stk = "stg7"
            k.op("scalar", lambda s, stg=stg, dc=dc, ppi=ppi: s.activation(
                stg, PP[ppi][:], AF.Identity, bias=B2C[:, dc:dc + 1]),
                 reads=[f"pp{ppi}", "b2c7"], writes=[stk])
            k.op("vector", lambda v, stg=stg: v.tensor_tensor(stg, stg, PP[6][:],
                                                              ALU.mult),
                 reads=[stk, "pp6"], writes=[stk])
            k.dma("qg", dst_dram[dc * 128:(dc + 1) * 128, tb * 512:(tb + 1) * 512],
                  stg, reads=[stk, dst_key(dc, tb)], writes=[dst_key(dc, tb)],
                  accum_op=ALU.add, stream="acc0")


_WA = {
    "l0a_wqT": [1024, 1024], "l0a_wkT": [1024, 1024], "l0a_wvT": [1024, 1024],
    "l0a_woT": [1024, 1024], "l0a_bo": [128, 8],
    "l0f_wgredT": [1024, 16], "l0f_wg": [8, 16], "l0f_w1": [8, 1024, 4096],
    "l0f_b1": [8, 4096], "l0f_w2": [8, 4096, 1024], "l0f_b2": [8, 1024],
    "l1a_wq": [1024, 1024], "l1a_wk": [1024, 1024], "l1a_wv": [1024, 1024],
    "l1a_bq": [128, 8], "l1a_bk": [128, 8], "l1a_bv": [1, 1024],
    "l1f_gwT": [1024, 8], "l1f_wgT": [8, 1024, 4096],
    "l1f_wuT": [8, 1024, 4096], "l1f_wdT": [8, 4096, 1024],
    "l2a_inprojT": [1024, 4096], "l2a_convw": [128, 64], "l2a_convb": [128, 16],
    "l2a_xprojT": [2048, 33], "l2a_alog": [128, 256], "l2a_dcol": [128, 16],
    "l2a_outprojT": [2048, 1024],
    "l2f_wqkv": [1024, 3072], "l2f_qg": [128, 8],
    "cst_repc": [2048, 128], "cst_summ": [2048, 128], "cst_repb": [16, 128],
}
_WB = {
    "l2f_wqkv": [1024, 3072], "l2f_qg": [128, 8], "l2f_wproj": [1024, 1024],
    "l2f_bproj": [128, 8],
    "l3a_inprojT": [1024, 4096], "l3a_convw": [128, 64], "l3a_convb": [128, 16],
    "l3a_xprojT": [2048, 96], "l3a_dtprojT": [64, 2048], "l3a_dtb": [128, 16],
    "l3a_alog": [128, 256], "l3a_dcol": [128, 16], "l3a_outprojT": [2048, 1024],
    "l3f_wrc": [128, 8], "l3f_b1c": [128, 32], "l3f_b2c": [128, 8],
    "l3f_w1": [1024, 4096], "l3f_w2": [4096, 1024],
    "cst_repc": [2048, 128], "cst_summ": [2048, 128], "cst_repb": [16, 128],
}


def _init_res(k, out, xin, dst_key):
    for c in range(DC):
        for j in range(TJ):
            k.dma("qs", out[c * 128:(c + 1) * 128, j * 512:(j + 1) * 512],
                  xin[c * 128:(c + 1) * 128, j * 512:(j + 1) * 512],
                  writes=[dst_key(c, j)], stream="init")


def _build_a():
    nc = bass.Bass("TRN2", target_bir_lowering=False)
    xin = nc.dram_tensor("xin", [1024, 4096], F32, kind="ExternalInput").ap()
    ins = {nm: nc.dram_tensor(nm, shp, F32, kind="ExternalInput").ap()
           for nm, shp in _WA.items()}
    out = nc.dram_tensor("out", [1024, 4096], F32, kind="ExternalOutput").ap()
    mm = nc.dram_tensor("mm", [1, 2], F32, kind="ExternalOutput").ap()
    k = Kfw(nc)
    bf = Bufs(k)
    C = Consts(k)
    dst_key = lambda c, j: f"xres/{c}/{j}"
    _init_res(k, out, xin, dst_key)
    xnd = k.dram("xnd", [1024, 4096], BF16)
    emit_ln(k, C, bf, out, dst_key, dst_i=0, tag="n0", hold_i=1)
    emit_l0_attn(k, C, bf, ins, out, dst_key, tag="a0")
    emit_ln(k, C, bf, out, dst_key, dst_i=0, tag="n1", hold_i=1, dram_out=xnd)
    emit_l0_ffn(k, C, bf, ins, xnd, out, dst_key, tag="f0")
    emit_ln(k, C, bf, out, dst_key, dst_i=0, tag="n2", hold_i=1)
    emit_l1_attn(k, C, bf, ins, None, out, dst_key, tag="a1")
    emit_ln(k, C, bf, out, dst_key, dst_i=0, tag="n3", hold_i=1)
    emit_l1_ffn(k, C, bf, ins, out, dst_key, tag="f1")
    emit_ln(k, C, bf, out, dst_key, dst_i=0, tag="n4", hold_i=1)
    emit_l2_attn(k, C, bf, ins, out, dst_key, tag="a2")
    emit_ln(k, C, bf, out, dst_key, dst_i=0, tag="n5", hold_i=1)
    emit_l2_ffn(k, C, bf, ins, None, dst_key, 1, tag="f2", sim_only=True,
                mm_out=mm)
    k.build()
    return nc


def _build_b():
    nc = bass.Bass("TRN2", target_bir_lowering=False)
    xin = nc.dram_tensor("xin", [1024, 4096], F32, kind="ExternalInput").ap()
    mmall = nc.dram_tensor("mmall", [1, 16], F32, kind="ExternalInput").ap()
    ins = {nm: nc.dram_tensor(nm, shp, F32, kind="ExternalInput").ap()
           for nm, shp in _WB.items()}
    out = nc.dram_tensor("out", [1024, 4096], F32, kind="ExternalOutput").ap()
    k = Kfw(nc)
    bf = Bufs(k)
    C = Consts(k)
    dst_key = lambda c, j: f"xres/{c}/{j}"
    _init_res(k, out, xin, dst_key)
    emit_ln(k, C, bf, out, dst_key, dst_i=0, tag="n5", hold_i=1)
    emit_l2_ffn(k, C, bf, ins, out, dst_key, 1, tag="f2", mm_in=mmall)
    emit_ln(k, C, bf, out, dst_key, dst_i=0, tag="n6", hold_i=1)
    emit_l3_attn(k, C, bf, ins, out, dst_key, tag="a3")
    emit_ln(k, C, bf, out, dst_key, dst_i=0, tag="n7", hold_i=1)
    emit_l3_ffn(k, C, bf, ins, out, dst_key, tag="f3")
    k.build()
    return nc


def _cw_stage(conv_w):
    cw = np.asarray(conv_w, np.float32)[:, 0, :]
    return np.ascontiguousarray(
        cw.reshape(16, 128, 4).transpose(1, 0, 2).reshape(128, 64))


def _alog_stage(a_log):
    return np.ascontiguousarray(
        np.asarray(a_log, np.float32).reshape(-1).reshape(256, 128).T)


def _stage_weights(I):
    cst = mk_scan_consts()
    g_info = np.asarray(I["L2_ffn_g_info"], np.float32)
    wA = {
        "l0a_wqT": _t(I["L0_attn_Wq"]), "l0a_wkT": _t(I["L0_attn_Wk"]),
        "l0a_wvT": _t(I["L0_attn_Wv"]), "l0a_woT": _t(I["L0_attn_Wo"]),
        "l0a_bo": _col128(I["L0_attn_bo"]),
        "l0f_wgredT": _t(I["L0_ffn_wg_red"]), "l0f_wg": _c(I["L0_ffn_wg"]),
        "l0f_w1": _c(I["L0_ffn_w1"]), "l0f_b1": _c(I["L0_ffn_b1"]),
        "l0f_w2": _c(I["L0_ffn_w2"]), "l0f_b2": _c(I["L0_ffn_b2"]),
        "l1a_wq": _c(I["L1_attn_Wq"]), "l1a_wk": _c(I["L1_attn_Wk"]),
        "l1a_wv": _c(I["L1_attn_Wv"]), "l1a_bq": _col128(I["L1_attn_bq"]),
        "l1a_bk": _col128(I["L1_attn_bk"]),
        "l1a_bv": np.ascontiguousarray(
            np.asarray(I["L1_attn_bv"], np.float32)[None, :]),
        "l1f_gwT": _t(I["L1_ffn_gate_w"]),
        "l1f_wgT": np.ascontiguousarray(
            np.asarray(I["L1_ffn_w_gate"], np.float32).transpose(0, 2, 1)),
        "l1f_wuT": np.ascontiguousarray(
            np.asarray(I["L1_ffn_w_up"], np.float32).transpose(0, 2, 1)),
        "l1f_wdT": np.ascontiguousarray(
            np.asarray(I["L1_ffn_w_down"], np.float32).transpose(0, 2, 1)),
        "l2a_inprojT": _t(I["L2_attn_in_proj_w"]),
        "l2a_convw": _cw_stage(I["L2_attn_conv_w"]),
        "l2a_convb": _col128(I["L2_attn_conv_b"]),
        "l2a_xprojT": _t(I["L2_attn_x_proj_w"]),
        "l2a_alog": _alog_stage(I["L2_attn_A_log"]),
        "l2a_dcol": _col128(I["L2_attn_D_param"]),
        "l2a_outprojT": _t(I["L2_attn_out_proj_w"]),
        "l2f_wqkv": _c(I["L2_ffn_W_qkv"]), "l2f_qg": _col128(g_info[0, 0]),
    }
    wA.update(cst)
    wB = {
        "l2f_wqkv": wA["l2f_wqkv"], "l2f_qg": wA["l2f_qg"],
        "l2f_wproj": _c(I["L2_ffn_W_proj"]),
        "l2f_bproj": _col128(I["L2_ffn_b_proj"]),
        "l3a_inprojT": _t(I["L3_attn_in_proj_w"]),
        "l3a_convw": _cw_stage(I["L3_attn_conv_w"]),
        "l3a_convb": _col128(I["L3_attn_conv_b"]),
        "l3a_xprojT": _t(I["L3_attn_x_proj_w"]),
        "l3a_dtprojT": _t(I["L3_attn_dt_proj_w"]),
        "l3a_dtb": _col128(I["L3_attn_dt_proj_b"]),
        "l3a_alog": _alog_stage(I["L3_attn_A_log"]),
        "l3a_dcol": _col128(I["L3_attn_D"]),
        "l3a_outprojT": _t(I["L3_attn_out_proj_w"]),
        "l3f_wrc": _col128(I["L3_ffn_w_router"]),
        "l3f_b1c": _col128(I["L3_ffn_b1"]),
        "l3f_b2c": _col128(I["L3_ffn_b2"]),
        "l3f_w1": _c(I["L3_ffn_W1"]), "l3f_w2": _c(I["L3_ffn_W2"]),
    }
    wB.update(cst)
    return wA, wB


def kernel(**inputs):
    x = np.asarray(inputs["L0_attn_x"], np.float32)
    wA, wB = _stage_weights(inputs)
    xfm = [np.ascontiguousarray(
        x[2 * c:2 * c + 2].transpose(2, 0, 1).reshape(1024, 4096))
        for c in range(NCORES)]
    nc_a = _build_a()
    res_a = run_bass_kernel_spmd(
        nc_a, [dict(xin=xfm[c], **wA) for c in range(NCORES)],
        core_ids=list(range(NCORES))).results
    mmall = np.zeros((1, 16), np.float32)
    for c in range(NCORES):
        mm = np.asarray(res_a[c]["mm"], np.float32)
        mmall[0, c] = mm[0, 0]
        mmall[0, 8 + c] = mm[0, 1]
    nc_b = _build_b()
    res_b = run_bass_kernel_spmd(
        nc_b, [dict(xin=np.asarray(res_a[c]["out"], np.float32), mmall=mmall,
                    **wB) for c in range(NCORES)],
        core_ids=list(range(NCORES))).results
    out = np.empty((16, 2048, 1024), np.float32)
    for c in range(NCORES):
        out[2 * c:2 * c + 2] = np.asarray(res_b[c]["out"], np.float32) \
            .reshape(1024, 2, 2048).transpose(1, 2, 0)
    return out


# version 02
# speedup vs baseline: 1.4546x; 1.4546x over previous
import contextlib
from collections import defaultdict

import numpy as np
import concourse.bass as bass
import concourse.mybir as mybir
from concourse.bass_utils import run_bass_kernel_spmd

F32 = mybir.dt.float32
BF16 = mybir.dt.bfloat16
I16 = mybir.dt.int16
AF = mybir.ActivationFunctionType
ALU = mybir.AluOpType
AX = mybir.AxisListType

NCORES = 8
BPC = 2
T = 2048
NT = BPC * T
D = 1024
DC = D // 128
TJ = NT // 512

COMPUTE_ENGINES = ("tensor", "vector", "scalar", "gpsimd", "sync")


class Kfw:
    def __init__(self, nc):
        self.nc = nc
        self.es = contextlib.ExitStack()
        self.sems = {}
        for e in COMPUTE_ENGINES:
            self.sems[e] = self.es.enter_context(nc.semaphore(f"S_{e}"))
        self.dma_queues = {"qs": "sync", "qa": "scalar", "qg": "gpsimd"}
        self.cnt = defaultdict(int)
        self.waited = defaultdict(dict)
        self.regions = {}
        self.cmds = defaultdict(list)
        self.stream_consumers = defaultdict(list)

    def sb(self, name, shape, dtype):
        return self.es.enter_context(self.nc.sbuf_tensor(name, list(shape), dtype))

    def psum(self, name, shape, dtype=F32):
        return self.es.enter_context(self.nc.psum_tensor(name, list(shape), dtype))

    def dram(self, name, shape, dtype, kind="Internal"):
        return self.nc.dram_tensor(name, list(shape), dtype, kind=kind).ap()

    def _reg(self, key):
        r = self.regions.get(key)
        if r is None:
            r = {"w": {}, "r": {}}
            self.regions[key] = r
        return r

    def _dma_sem(self, queue, stream):
        name = f"{queue}:{stream}"
        if name not in self.sems:
            self.sems[name] = self.es.enter_context(
                self.nc.semaphore(f"S_{queue}_{stream}"))
        return name

    def _plan_waits(self, reads, writes):
        need = {}
        for key in reads:
            for s, v in self._reg(key)["w"].items():
                need[s] = max(need.get(s, 0), v)
        for key in writes:
            r = self._reg(key)
            for s, v in r["w"].items():
                need[s] = max(need.get(s, 0), v)
            for s, v in r["r"].items():
                need[s] = max(need.get(s, 0), v)
        for s in list(need):
            if ":" in s:
                need[s] = self.cnt[s]
        return sorted(need.items())

    def _mark(self, sem_name, val, reads, writes):
        for key in reads:
            r = self._reg(key)
            r["r"][sem_name] = max(r["r"].get(sem_name, 0), val)
        for key in writes:
            r = self._reg(key)
            r["w"] = {sem_name: val}
            r["r"] = {}

    def _note_stream_waits(self, need, my_sem, my_val):
        for s, _v in need:
            if ":" in s:
                self.stream_consumers[s].append((my_sem, my_val))

    def _forced_boundary(self, sname, need):
        pend = self.stream_consumers.pop(sname, None)
        if not pend:
            return []
        out = {}
        for cs, cv in pend:
            if any(s == cs and v >= cv for s, v in need):
                continue
            out[cs] = max(out.get(cs, 0), cv)
        return sorted(out.items())

    def _emit_wait(self, eng, engine_obj, sem_name, val):
        if val <= 0:
            return
        if self.waited[eng].get(sem_name, 0) >= val:
            return
        engine_obj.wait_ge(self.sems[sem_name], val)
        self.waited[eng][sem_name] = val

    def op(self, eng, fn, reads=(), writes=()):
        need = self._plan_waits(reads, writes)
        self.cnt[eng] += 1
        val = self.cnt[eng]
        self._note_stream_waits(need, eng, val)
        self._mark(eng, val, reads, writes)

        def closure(engine_obj, need=need, fn=fn, sem=self.sems[eng], eng=eng):
            for s_name, v in need:
                self._emit_wait(eng, engine_obj, s_name, v)
            last = fn(engine_obj)
            assert last is not None, f"op on {eng} returned None"
            last.then_inc(sem, 1)

        self.cmds[eng].append(closure)
        return val

    def dma(self, queue, out, in_, reads=(), writes=(), stream="d", **kw):
        eng = self.dma_queues[queue]
        sname = self._dma_sem(queue, stream)
        need = self._plan_waits(reads, writes)
        need = self._forced_boundary(sname, need) + need
        self.cnt[sname] += 16
        val = self.cnt[sname]
        self._note_stream_waits(need, sname, val)
        self._mark(sname, val, reads, writes)

        def closure(engine_obj, need=need, out=out, in_=in_, sem=self.sems[sname],
                    kw=kw, eng=eng):
            for s_name, v in need:
                self._emit_wait(eng, engine_obj, s_name, v)
            engine_obj.dma_start(out=out, in_=in_, **kw).then_inc(sem, 16)

        self.cmds[eng].append(closure)
        return val

    def qop(self, eng, fn, reads=(), writes=(), queue="qcc", stream="cc"):
        sname = self._dma_sem(queue, stream)
        need = self._plan_waits(reads, writes)
        need = self._forced_boundary(sname, need) + need
        self.cnt[sname] += 16
        val = self.cnt[sname]
        self._note_stream_waits(need, sname, val)
        self._mark(sname, val, reads, writes)

        def closure(engine_obj, need=need, fn=fn, sem=self.sems[sname], eng=eng):
            for s_name, v in need:
                self._emit_wait(eng, engine_obj, s_name, v)
            fn(engine_obj).then_inc(sem, 16)

        self.cmds[eng].append(closure)
        return val

    def fence(self, eng, reads=(), writes=()):
        return self.op(eng, lambda e: e.nop(), reads=reads, writes=writes)

    def wait_all(self, eng):
        snap = [(s, self.cnt[s]) for s in list(self.cnt) if self.cnt[s] > 0]

        def closure(engine_obj, snap=snap, eng=eng):
            for s_name, v in snap:
                self._emit_wait(eng, engine_obj, s_name, v)

        self.cmds[eng].append(closure)

    def build(self):
        self.wait_all("sync")
        with self.nc.Block() as blk:
            for eng in COMPUTE_ENGINES:
                cmds = self.cmds[eng]
                if not cmds:
                    continue

                def runner(engine_obj, cmds=cmds):
                    for f in cmds:
                        f(engine_obj)

                getattr(blk, eng)(runner)
        self.es.close()


class Bufs:
    def __init__(self, k: Kfw):
        self.big = [k.sb(f"BIG{i}", [128, 32768], BF16) for i in (1, 2, 3)]
        self.aux = k.sb("AUX", [128, 4096], BF16)
        self.misc = self.aux[:, 1024:4096].bitcast(F32)
        self.scr = k.sb("SCR", [128, 2048], BF16)
        self.pp = [k.psum(f"PP{i}", [128, 512], F32) for i in range(8)]

    def bigv(self, i, c):
        return self.big[i][:, c * 4096:(c + 1) * 4096]

    def bigf32(self, i, c):
        return self.big[i][:, :].bitcast(F32)[:, c * 2048:(c + 1) * 2048]


def bkey(i, c):
    return f"B{i}:{c}"


def emit_tri_le(k, C, dst_ap, wkeys):
    k.op("vector", lambda v: v.tensor_scalar(dst_ap, C.iota16[:, 0:128],
                                             C.chan[:], None, ALU.is_ge),
         reads=["c_iota", "c_chan"], writes=list(wkeys))


BKALL = lambda i: [bkey(i, c) for c in range(8)]


class Consts:
    def __init__(self, k: Kfw):
        self.ones_col = k.sb("c_ones_col", [128, 1], F32)
        self.ones_row = k.sb("c_ones_row", [1, 128], F32)
        self.iota16 = k.sb("c_iota16", [128, 512], I16)
        self.chan = k.sb("c_chan", [128, 1], F32)
        self.ident = k.sb("c_ident", [128, 128], BF16)
        self.sel8 = k.sb("c_sel8", [40, 8 * 128], BF16)
        self.ones_bf = k.sb("c_ones_bf", [128, 1], BF16)
        k.op("gpsimd", lambda g: g.iota(self.iota16[:], [[1, 512]], channel_multiplier=0,
                                        allow_small_or_imprecise_dtypes=True),
             writes=["c_iota"])
        k.op("gpsimd", lambda g: g.iota(self.chan[:], [[0, 1]], channel_multiplier=1,
                                        allow_small_or_imprecise_dtypes=True),
             writes=["c_chan"])
        k.op("vector", lambda v: v.memset(self.ones_col[:], 1.0), writes=["c_onesc"])
        k.op("vector", lambda v: v.memset(self.ones_bf[:], 1.0), writes=["c_onesb"])
        k.op("vector", lambda v: v.memset(self.ones_row[:], 1.0), writes=["c_onesr"])
        k.op("vector", lambda v: v.tensor_scalar(self.ident[:], self.iota16[:, 0:128],
                                                 self.chan[:], None, ALU.is_equal),
             reads=["c_iota", "c_chan"], writes=["c_ident"])

        def mk_sel8(v):
            for e in range(8):
                v.tensor_scalar(self.sel8[0:8, e * 128:(e + 1) * 128],
                                self.chan[0:8, 0:1].broadcast_to((8, 128)),
                                float(e), None, ALU.is_equal)
                last = v.tensor_scalar(self.sel8[32:40, e * 128:(e + 1) * 128],
                                       self.chan[32:40, 0:1].broadcast_to((8, 128)),
                                       float(32 + e), None, ALU.is_equal)
            return last

        k.op("vector", mk_sel8, reads=["c_chan"], writes=["c_sel8"])


def emit_ln(k: Kfw, C: Consts, bf: Bufs, src_dram, src_keys, dst_i, tag,
            hold_i=1, dram_out=None):
    misc = bf.misc
    m_row = misc[0:1, 0:512]
    v_row = misc[0:1, 512:1024]
    b_row = misc[0:1, 1024:1536]
    PSTAT, PSTATQ, PB0, PB1 = bf.pp[5], bf.pp[4], bf.pp[6], bf.pp[7]
    holdf32 = bf.big[hold_i][:, :].bitcast(F32)
    scrf = bf.scr[:, :].bitcast(F32)

    hk = [f"lnh_{c}" for c in range(DC)]
    k.fence("sync", reads=[], writes=[bkey(hold_i, cb) for cb in range(8)] + hk +
            ["pp4", "pp5", "pp6", "pp7", "pp5s", "pp4q", "lnsq", "lnsq2",
             "ln_m", "ln_v", "ln_b"])
    for j in range(TJ):
        for c in range(DC):
            tile = holdf32[:, c * 512:(c + 1) * 512]
            k.dma("qs", tile, src_dram[c * 128:(c + 1) * 128, j * 512:(j + 1) * 512],
                  reads=[src_keys(c, j)], writes=[hk[c]], stream=f"ln{c}")
            k.op("scalar", lambda s, tile=tile: s.activation(scrf[:, 0:512], tile, AF.Square),
                 reads=[hk[c]], writes=["lnsq", "SCR:0", "SCR:1"])
            k.op("tensor", lambda t, tile=tile, c=c: t.matmul(
                PSTAT[0:1, :], C.ones_col[:], tile, start=(c == 0), stop=(c == DC - 1)),
                 reads=[hk[c], "c_onesc"], writes=["pp5s"])
            k.op("tensor", lambda t, c=c: t.matmul(
                PSTATQ[0:1, :], C.ones_col[:], scrf[:, 0:512], start=(c == 0),
                stop=(c == DC - 1)),
                 reads=["lnsq", "SCR:0", "SCR:1", "c_onesc"], writes=["pp4q"])
        k.op("vector", lambda v: v.tensor_scalar(m_row, PSTAT[0:1, :], 1.0 / D, None,
                                                 ALU.mult),
             reads=["pp5s"], writes=["ln_m", "AUX:1"])
        k.op("vector", lambda v: v.tensor_scalar(v_row, PSTATQ[0:1, :], 1.0 / D, None,
                                                 ALU.mult),
             reads=["pp4q"], writes=["ln_v", "AUX:2"])
        k.op("vector", lambda v: v.scalar_tensor_tensor(b_row, m_row, -1.0, m_row,
                                                        ALU.mult, ALU.mult),
             reads=["ln_m", "AUX:1"], writes=["ln_b", "AUX:3"])
        k.op("vector", lambda v: v.tensor_add(v_row, v_row, b_row),
             reads=["ln_v", "ln_b", "AUX:3"], writes=["ln_v", "AUX:2"])
        k.op("vector", lambda v: v.tensor_scalar(v_row, v_row, 0.0, 1e-5,
                                                 ALU.max, ALU.add),
             reads=["ln_v"], writes=["ln_v", "AUX:2"])
        k.op("scalar", lambda s: s.activation(v_row, v_row, AF.Ln),
             reads=["ln_v"], writes=["ln_v", "AUX:2"])
        k.op("scalar", lambda s: s.activation(v_row, v_row, AF.Exp, scale=-0.5),
             reads=["ln_v"], writes=["ln_v", "AUX:2"])
        k.op("vector", lambda v: v.scalar_tensor_tensor(b_row, m_row, -1.0, v_row,
                                                        ALU.mult, ALU.mult),
             reads=["ln_m", "ln_v", "AUX:1", "AUX:2"], writes=["ln_b", "AUX:3"])
        k.op("tensor", lambda t: t.matmul(PB0[:], C.ones_row[:], v_row,
                                          start=True, stop=True),
             reads=["ln_v", "AUX:2", "c_onesr"], writes=["pp6"])
        k.op("tensor", lambda t: t.matmul(PB1[:], C.ones_row[:], b_row,
                                          start=True, stop=True),
             reads=["ln_b", "AUX:3", "c_onesr"], writes=["pp7"])
        for c in range(DC):
            tile = holdf32[:, c * 512:(c + 1) * 512]
            k.op("vector", lambda v, tile=tile: v.tensor_tensor(
                scrf[:, 512:1024], tile, PB0[:], ALU.mult),
                 reads=[hk[c], "pp6"], writes=["lnsq2", "SCR:2", "SCR:3"])
            k.op("vector", lambda v, c=c, j=j: v.tensor_tensor(
                bf.bigv(dst_i, c)[:, j * 512:(j + 1) * 512], scrf[:, 512:1024],
                PB1[:], ALU.add),
                 reads=["lnsq2", "SCR:2", "SCR:3", "pp7"], writes=[bkey(dst_i, c)])
            if dram_out is not None:
                k.dma("qs", dram_out[c * 128:(c + 1) * 128, j * 512:(j + 1) * 512],
                      bf.bigv(dst_i, c)[:, j * 512:(j + 1) * 512],
                      reads=[bkey(dst_i, c)], writes=["xnd"], stream=f"xno{c % 2}")
    k.fence("sync", reads=hk, writes=[bkey(hold_i, cb) for cb in range(8)])


def _t(a):
    return np.ascontiguousarray(np.asarray(a, dtype=np.float32).T)


def _c(a):
    return np.ascontiguousarray(np.asarray(a, dtype=np.float32))


def _col128(v):
    v = np.asarray(v, dtype=np.float32)
    return np.ascontiguousarray(v.reshape(-1, 128).T)


A0, A1, A2, A3 = "AUX:0", "AUX:1", "AUX:2", "AUX:3"
S0, S1, S2, S3 = "SCR:0", "SCR:1", "SCR:2", "SCR:3"


def emit_l0_attn(k: Kfw, C: Consts, bf: Bufs, ins, dst_dram, dst_key, tag="a0"):
    aux = bf.aux
    scr = bf.scr
    scrf = scr[:, :].bitcast(F32)
    vtd = k.dram(f"{tag}_vtd", [NT, D], BF16)
    ktd = k.dram(f"{tag}_ktd", [NT, D], BF16)
    PP = bf.pp
    H, DH, NCH = 16, 64, T // 128

    for which, wd, outd, do_phi in (("v", ins["l0a_wvT"], vtd, False),
                                    ("kt", ins["l0a_wkT"], ktd, True)):
        for nb in range(2):
            k.dma("qg", aux[:, :].rearrange("p (kc m) -> p kc m", kc=8),
                  wd[:, nb * 512:(nb + 1) * 512].rearrange("(kc p) m -> p kc m", p=128),
                  writes=[A0, A1, A2, A3], stream="wtok")
            for tc in range(NT // 128):
                ppi = tc % 2
                def mm(t, tc=tc, ppi=ppi):
                    for kc in range(DC):
                        last = t.matmul(PP[ppi][:], bf.bigv(0, kc)[:, tc * 128:(tc + 1) * 128],
                                        aux[:, kc * 512:(kc + 1) * 512],
                                        start=(kc == 0), stop=(kc == DC - 1))
                    return last
                k.op("tensor", mm, reads=[A0, A1, A2, A3] + BKALL(0), writes=[f"pp{ppi}"])
                st_bf = scr[:, ppi * 512:ppi * 512 + 512]
                stk = f"SCR:{ppi}"
                if do_phi:
                    est = scr[:, 1024 + ppi * 512:1024 + (ppi + 1) * 512]
                    estk = f"SCR:{2 + ppi}"
                    k.op("vector", lambda v, ppi=ppi: v.tensor_scalar_min(
                        PP[4][:], PP[ppi][:], 0.0), reads=[f"pp{ppi}"], writes=["pp4"])
                    k.op("scalar", lambda s, est=est: s.activation(est, PP[4][:], AF.Exp),
                         reads=["pp4"], writes=[estk])
                    k.op("vector", lambda v, st_bf=st_bf, ppi=ppi, est=est:
                         v.scalar_tensor_tensor(
                         st_bf, PP[ppi][:], 0.0, est, ALU.max, ALU.add),
                         reads=[f"pp{ppi}", estk], writes=[stk])
                else:
                    k.op("vector", lambda v, st_bf=st_bf, ppi=ppi: v.tensor_copy(
                        st_bf, PP[ppi][:]), reads=[f"pp{ppi}"], writes=[stk])
                k.dma("qs", outd[tc * 128:(tc + 1) * 128, nb * 512:(nb + 1) * 512],
                      st_bf, reads=[stk], writes=[f"{tag}{which}d"],
                      stream=f"st{ppi}")

    for dst_i, wd in ((1, ins["l0a_wqT"]), (2, ins["l0a_wkT"])):
        for oc in range(DC):
            sl = oc % 2
            slab = scr[:, sl * 1024:(sl + 1) * 1024]
            slks = [f"SCR:{2 * sl}", f"SCR:{2 * sl + 1}"]
            k.dma("qg", slab.rearrange("p (kc m) -> p kc m", kc=8),
                  wd[:, oc * 128:(oc + 1) * 128].rearrange("(kc p) m -> p kc m", p=128),
                  writes=slks, stream=f"w{sl}")
            for j in range(TJ):
                ppi = j % 2
                def mm(t, slab=slab, j=j, ppi=ppi):
                    for kc in range(DC):
                        last = t.matmul(PP[ppi][:], slab[:, kc * 128:(kc + 1) * 128],
                                        bf.bigv(0, kc)[:, j * 512:(j + 1) * 512],
                                        start=(kc == 0), stop=(kc == DC - 1))
                    return last
                k.op("tensor", mm, reads=slks + BKALL(0), writes=[f"pp{ppi}"])
                est = aux[:, ppi * 1024:ppi * 1024 + 512]
                estk = ["AUX:0", "AUX:1"][ppi]
                k.op("vector", lambda v, ppi=ppi: v.tensor_scalar_min(
                    PP[4][:], PP[ppi][:], 0.0), reads=[f"pp{ppi}"], writes=["pp4"])
                k.op("scalar", lambda s, est=est: s.activation(est, PP[4][:], AF.Exp),
                     reads=["pp4"], writes=[estk])
                wkeys = [f"pq{dst_i}/{oc}/{hf}/{j}" for hf in range(2)]
                k.op("vector", lambda v, dst_i=dst_i, oc=oc, j=j, ppi=ppi, est=est:
                     v.scalar_tensor_tensor(
                         bf.bigv(dst_i, oc)[:, j * 512:(j + 1) * 512],
                         PP[ppi][:], 0.0, est, ALU.max, ALU.add),
                     reads=[f"pp{ppi}", estk, bkey(dst_i, oc)], writes=wkeys)

    emit_tri_le(k, C, scr[:, 1280:1408], [S2])
    misc2 = aux[:, 2048:4096].bitcast(F32)
    den_row = misc2[0:1, 320:448]
    recip_row = misc2[0:1, 192:320]
    bo_sb = misc2[:, 448:456]
    at_sb = scr[:, 512:640]

    for b in range(BPC):
        for h in range(H):
            hp, half = h // 2, h % 2
            rb = half * 64
            pqv = bf.bigv(1, hp)
            pkv = bf.bigv(2, hp)
            S_f = misc2[rb:rb + 64, 0:65]
            S_bf = aux[rb:rb + 64, 2304:2369]
            rb_sb = scr[rb:rb + 64, 768:896]
            k.dma("qs", aux[:, 0:1024].rearrange("p (i f) -> p i f", i=16),
                  vtd[b * T:(b + 1) * T, h * DH:(h + 1) * DH]
                  .rearrange("(i p) f -> p i f", p=128),
                  reads=[f"{tag}vd"], writes=[A0], stream="vt")
            k.dma("qs", aux[:, 1024:2048].rearrange("p (i f) -> p i f", i=16),
                  ktd[b * T:(b + 1) * T, h * DH:(h + 1) * DH]
                  .rearrange("(i p) f -> p i f", p=128),
                  reads=[f"{tag}ktd"], writes=[A1], stream="kt")
            k.op("vector", lambda v, S_f=S_f: v.memset(S_f, 0.0), writes=[A2])
            k.op("vector", lambda v, S_bf=S_bf: v.memset(S_bf, 0.0), writes=[A2])
            for i in range(NCH):
                s0 = b * T + i * 128
                pa, pak = PP[i % 2], f"pp{i % 2}"
                pn, pnk = PP[2 + i % 2], f"pp{2 + i % 2}"
                qk = f"pq1/{hp}/{half}/{b * 4 + i // 4}"
                kk = f"pq2/{hp}/{half}/{b * 4 + i // 4}"
                k.op("tensor", lambda t, pa=pa, s0=s0, pqv=pqv, pkv=pkv, rb=rb:
                     t.matmul(pa[:, 0:128], pkv[rb:rb + 64, s0:s0 + 128],
                              pqv[rb:rb + 64, s0:s0 + 128], start=True, stop=True),
                     reads=[qk, kk], writes=[pak])
                k.op("vector", lambda v, pa=pa: v.tensor_tensor(
                    at_sb, pa[:, 0:128], scr[:, 1280:1408], ALU.mult),
                     reads=[pak, S2], writes=[S1])
                def num_mm(t, pn=pn, i=i, s0=s0, pqv=pqv, rb=rb, S_bf=S_bf):
                    t.matmul(pn[rb:rb + 64, 0:128], aux[:, i * 64:(i + 1) * 64], at_sb,
                             start=True, stop=False, tile_position=(0, rb))
                    t.matmul(pn[rb:rb + 64, 0:128], S_bf[:, 0:64],
                             pqv[rb:rb + 64, s0:s0 + 128], start=False, stop=True,
                             tile_position=(rb, rb))
                    t.matmul(PP[6][0:1, 0:128], C.ones_bf[:, 0:1], at_sb,
                             start=True, stop=False)
                    return t.matmul(PP[6][0:1, 0:128], S_bf[:, 64:65],
                                    pqv[rb:rb + 64, s0:s0 + 128], start=False, stop=True)
                k.op("tensor", num_mm, reads=[S1, A0, A2, qk, "c_onesb"],
                     writes=[pnk, "pp6"])
                def s_mm(t, i=i, rb=rb):
                    t.matmul(PP[5][rb:rb + 64, 0:64],
                             aux[:, 1024 + i * 64:1024 + (i + 1) * 64],
                             aux[:, i * 64:(i + 1) * 64], start=True, stop=True,
                             tile_position=(0, rb))
                    return t.matmul(PP[5][rb:rb + 64, 64:65],
                                    aux[:, 1024 + i * 64:1024 + (i + 1) * 64],
                                    C.ones_bf[:, 0:1], start=True, stop=True,
                                    tile_position=(0, rb))
                k.op("tensor", s_mm, reads=[A0, A1, "c_onesb"], writes=["pp5"])
                k.op("vector", lambda v: v.tensor_scalar_add(
                    den_row, PP[6][0:1, 0:128], 1e-5), reads=["pp6"], writes=[A2])
                k.op("vector", lambda v: v.reciprocal(recip_row, den_row),
                     reads=[A2], writes=[A2])
                k.op("tensor", lambda t, rb=rb: t.matmul(
                    PP[4][rb:rb + 64, 0:128], C.ones_row[:, 0:64], recip_row,
                    start=True, stop=True, tile_position=(0, rb)),
                     reads=[A2, "c_onesr"], writes=["pp4"])
                k.op("vector", lambda v, rb_sb=rb_sb, rb=rb: v.tensor_copy(
                    rb_sb, PP[4][rb:rb + 64, 0:128]), reads=["pp4"], writes=[S1])
                k.op("vector", lambda v, pn=pn, s0=s0, rb=rb, pqv=pqv, rb_sb=rb_sb:
                     v.tensor_tensor(pqv[rb:rb + 64, s0:s0 + 128],
                                     pn[rb:rb + 64, 0:128], rb_sb, ALU.mult),
                     reads=[pnk, S1, qk], writes=[qk])
                k.op("vector", lambda v, S_f=S_f, rb=rb: v.tensor_tensor(
                    S_f, S_f, PP[5][rb:rb + 64, 0:65], ALU.add),
                     reads=["pp5", A2], writes=[A2])
                k.op("vector", lambda v, S_f=S_f, S_bf=S_bf: v.tensor_copy(S_bf, S_f),
                     reads=[A2], writes=[A2])
    k.fence("sync", reads=[f"pq1/{hp}/{hf}/{x}" for hp in range(8) for hf in range(2)
                           for x in range(8)],
            writes=[bkey(1, c) for c in range(8)])

    k.dma("qs", bo_sb, ins["l0a_bo"], writes=["bias0", A2], stream="bias")
    for oc in range(DC):
        sl = oc % 2
        slab = aux[:, sl * 1024:(sl + 1) * 1024]
        slk = [A0, A1][sl]
        k.dma("qg", slab.rearrange("p (kc m) -> p kc m", kc=8),
              ins["l0a_woT"][:, oc * 128:(oc + 1) * 128]
              .rearrange("(kc p) m -> p kc m", p=128),
              writes=[slk], stream=f"w{sl}")
        for j in range(TJ):
            ppi = j % 2
            def mm(t, slab=slab, j=j, ppi=ppi):
                for kc in range(DC):
                    last = t.matmul(PP[ppi][:], slab[:, kc * 128:(kc + 1) * 128],
                                    bf.bigv(1, kc)[:, j * 512:(j + 1) * 512],
                                    start=(kc == 0), stop=(kc == DC - 1))
                return last
            k.op("tensor", mm, reads=[slk] + [bkey(1, c) for c in range(8)],
                 writes=[f"pp{ppi}"])
            stage = scrf[:, ppi * 512:(ppi + 1) * 512]
            stks = [f"SCR:{2 * ppi}", f"SCR:{2 * ppi + 1}"]
            k.op("scalar", lambda s, stage=stage, oc=oc, ppi=ppi: s.activation(
                stage, PP[ppi][:], AF.Identity, bias=bo_sb[:, oc:oc + 1]),
                 reads=[f"pp{ppi}", "bias0"], writes=stks)
            k.dma("qg", dst_dram[oc * 128:(oc + 1) * 128, j * 512:(j + 1) * 512],
                  stage, reads=stks + [dst_key(oc, j)], writes=[dst_key(oc, j)],
                  accum_op=ALU.add, stream=f"acc{ppi}")


def emit_l0_ffn(k: Kfw, C: Consts, bf: Bufs, ins, xnd, dst_dram, dst_key, tag="f0"):
    b1_ = bf.big[0]
    b1f = b1_[:, :].bitcast(F32)
    aux, scr = bf.aux, bf.scr
    PP = bf.pp
    STG = [b1f[:, 8192 + s * 512:8192 + (s + 1) * 512] for s in range(2)]
    WM = b1_[0:8, 18432:22528]
    WB = [b1_[:, 22528 + s * 512:22528 + (s + 1) * 512] for s in range(2)]
    WGU = b1_[0:8, 23552:23568]
    WGUT = b1_[0:16, 23568:23576]
    WRT = b1_[:, 23680:23808]
    WMT = b1_[:, 23808:23816]
    NM = b1f[0:128, 12160:12161]
    ET = b1f[:, 12164:12172]
    RS = b1f[:, 12172:12173]
    SC = b1f[:, 12173:12174]
    WG = b1f[0:8, 12176:12192]
    WGS = b1f[0:8, 12192:12208]
    SS = b1f[0:8, 12208:12209]
    B1ROW = b1_[0:1, 24448:28544]
    B2ROW = b1_[0:1, 28544:29568]
    ones512 = b1_[0:1, 29568:30080]
    MKEYS = (["xr", "wmrows", "wmt", "wgu", "wgut", "wrt", "wgf", "wgs", "ss",
              "nm", "et", "rs", "sc", "stg0", "stg1", "wb0", "wb1", "b1row",
              "b2row", "ones512"] + [f"h/{i}" for i in range(32)])
    k.fence("sync", writes=[bkey(0, c) for c in range(8)] + MKEYS)

    k.dma("qs", WG, ins["l0f_wg"], writes=["wgf"], stream="bias")
    k.dma("qg", WRT.rearrange("p (dc f) -> p dc f", dc=8),
          ins["l0f_wgredT"].rearrange("(dc p) f -> p dc f", p=128),
          writes=["wrt"], stream="bias2")
    k.op("scalar", lambda s: s.activation(WGS, WG, AF.Square),
         reads=["wgf"], writes=["wgs"])
    k.op("vector", lambda v: v.tensor_reduce(SS, WGS, AX.X, ALU.add),
         reads=["wgs"], writes=["ss"])
    k.op("vector", lambda v: v.tensor_scalar_max(SS, SS, 1e-8),
         reads=["ss"], writes=["ss"])
    k.op("scalar", lambda s: s.activation(SS, SS, AF.Ln), reads=["ss"], writes=["ss"])
    k.op("scalar", lambda s: s.activation(SS, SS, AF.Exp, scale=-0.5),
         reads=["ss"], writes=["ss"])
    k.op("vector", lambda v: v.tensor_scalar(WGU, WG, SS, None, ALU.mult),
         reads=["wgf", "ss"], writes=["wgu"])
    ppbf = PP[4][:, :].bitcast(BF16)
    k.op("tensor", lambda t: t.transpose(ppbf[0:16, 0:8], WGU, C.ident[0:8, 0:8]),
         reads=["wgu", "c_ident"], writes=["pp4"])
    k.op("vector", lambda v: v.tensor_copy(WGUT, ppbf[0:16, 0:8]),
         reads=["pp4"], writes=["wgut"])
    for tb in range(TJ):
        k.dma("qg", aux[:, :].rearrange("p (dc f) -> p dc f", dc=8),
              xnd[:, tb * 512:(tb + 1) * 512].rearrange("(dc p) f -> p dc f", p=128),
              reads=["xnd"], writes=[A0, A1, A2, A3], stream="xnl")
        def mmxr(t, tb=tb):
            for dc in range(DC):
                last = t.matmul(PP[6][0:16, :], WRT[:, dc * 16:(dc + 1) * 16],
                                aux[:, dc * 512:(dc + 1) * 512],
                                start=(dc == 0), stop=(dc == DC - 1))
            return last
        k.op("tensor", mmxr, reads=["wrt", A0, A1, A2, A3], writes=["pp6"])
        k.op("vector", lambda v, tb=tb: v.tensor_copy(
            b1_[0:16, tb * 512:(tb + 1) * 512], PP[6][0:16, :]),
             reads=["pp6"], writes=["xr"])
    for tc in range(NT // 128):
        k.op("tensor", lambda t, tc=tc: t.matmul(
            PP[7][:, 0:8], b1_[0:16, tc * 128:(tc + 1) * 128], WGUT,
            start=True, stop=True), reads=["xr", "wgut"], writes=["pp7"])
        k.op("vector", lambda v: v.tensor_reduce(NM, PP[7][:, 0:8], AX.X, ALU.max,
                                                 negate=True),
             reads=["pp7"], writes=["nm"])
        k.op("scalar", lambda s: s.activation(ET, PP[7][:, 0:8], AF.Exp, bias=NM,
                                              accum_out=RS),
             reads=["pp7", "nm"], writes=["et", "rs"])
        k.op("vector", lambda v: v.reciprocal(SC, RS), reads=["rs"], writes=["sc"])
        k.op("vector", lambda v: v.tensor_scalar(ET, PP[7][:, 0:8], NM, 0.0,
                                                 ALU.add, ALU.is_equal),
             reads=["pp7", "nm"], writes=["et"])
        k.op("vector", lambda v: v.tensor_scalar(WMT, ET, SC, None, ALU.mult),
             reads=["et", "sc"], writes=["wmt"])
        k.op("tensor", lambda t: t.transpose(ppbf[0:8, 0:128], WMT,
                                             C.ident[:, :]),
             reads=["wmt", "c_ident"], writes=["pp4"])
        k.op("vector", lambda v, tc=tc: v.tensor_copy(
            WM[:, tc * 128:(tc + 1) * 128], ppbf[0:8, 0:128]),
             reads=["pp4"], writes=["wmrows"])

    k.op("vector", lambda v: v.memset(ones512, 1.0), writes=["ones512"])

    for e in range(8):
        k.dma("qg", bf.big[1][:, :].rearrange("p (dc i) -> p dc i", dc=8),
              ins["l0f_w1"][e].rearrange("(dc p) i -> p dc i", p=128),
              writes=BKALL(1), stream="w1")
        k.dma("qg", bf.big[2][:, :].rearrange("p (ic d) -> p ic d", ic=32),
              ins["l0f_w2"][e].rearrange("(ic p) d -> p ic d", p=128),
              writes=BKALL(2), stream="w2")
        k.dma("qg", B1ROW, ins["l0f_b1"][e:e + 1, :],
              writes=["b1row"], stream="b1a")
        k.dma("qg", B2ROW, ins["l0f_b2"][e:e + 1, :],
              writes=["b2row"], stream="b2")
        for tb in range(TJ):
            k.dma("qg", aux[:, :].rearrange("p (dc f) -> p dc f", dc=8),
                  xnd[:, tb * 512:(tb + 1) * 512].rearrange("(dc p) f -> p dc f", p=128),
                  reads=["xnd"], writes=[A0, A1, A2, A3], stream="xnl")
            k.op("tensor", lambda t, e=e, tb=tb: t.matmul(
                PP[5][:], C.sel8[0:8, e * 128:(e + 1) * 128],
                WM[:, tb * 512:(tb + 1) * 512], start=True, stop=True),
                 reads=["wmrows", "c_sel8"], writes=["pp5"])
            wbs = WB[tb % 2]
            k.op("vector", lambda v, wbs=wbs: v.tensor_copy(wbs, PP[5][:]),
                 reads=["pp5"], writes=[f"wb{tb % 2}"])
            for ic in range(32):
                ph, phk = PP[ic % 2], f"pp{ic % 2}"
                def mmh(t, ic=ic, ph=ph):
                    for dc in range(DC):
                        t.matmul(ph[:], bf.big[1][:, dc * 4096 + ic * 128:
                                                  dc * 4096 + (ic + 1) * 128],
                                 aux[:, dc * 512:(dc + 1) * 512],
                                 start=(dc == 0), stop=False)
                    return t.matmul(ph[:], B1ROW[:, ic * 128:(ic + 1) * 128],
                                    ones512, start=False, stop=True)
                k.op("tensor", mmh,
                     reads=BKALL(1) + [A0, A1, A2, A3, "b1row", "ones512"],
                     writes=[phk])
                k.op("scalar", lambda s, ic=ic, ph=ph: s.activation(
                    b1_[:, ic * 512:(ic + 1) * 512], ph[:], AF.Gelu),
                     reads=[phk], writes=[f"h/{ic}"])
            for dc in range(DC):
                py, pyk = PP[2 + dc % 2], f"pp{2 + dc % 2}"
                def mmy(t, dc=dc, py=py):
                    for ic in range(32):
                        t.matmul(py[:], bf.big[2][:, ic * 1024 + dc * 128:
                                                  ic * 1024 + (dc + 1) * 128],
                                 b1_[:, ic * 512:(ic + 1) * 512],
                                 start=(ic == 0), stop=False)
                    return t.matmul(py[:], B2ROW[:, dc * 128:(dc + 1) * 128],
                                    ones512, start=False, stop=True)
                k.op("tensor", mmy,
                     reads=BKALL(2) + [f"h/{ic}" for ic in range(32)] +
                           ["b2row", "ones512"],
                     writes=[pyk])
                stg = STG[dc % 2]
                k.op("vector", lambda v, stg=stg, py=py, tb=tb: v.tensor_tensor(
                    stg, py[:], WB[tb % 2], ALU.mult),
                     reads=[pyk, f"wb{tb % 2}"], writes=[f"stg{dc % 2}"])
                k.dma("qg", dst_dram[dc * 128:(dc + 1) * 128,
                                     tb * 512:(tb + 1) * 512],
                      stg, reads=[f"stg{dc % 2}", dst_key(dc, tb)],
                      writes=[dst_key(dc, tb)], accum_op=ALU.add,
                      stream=f"acc{dc % 2}")


def emit_l1_attn(k: Kfw, C: Consts, bf: Bufs, ins, xnd_unused, dst_dram, dst_key,
                 tag="a1"):
    aux, scr = bf.aux, bf.scr
    auxf = aux[:, :].bitcast(F32)
    scrf = scr[:, :].bitcast(F32)
    PP = bf.pp
    vtd = k.dram(f"{tag}_vtd", [NT, D], BF16)
    b1_ = bf.big[0]

    acc = [auxf[:, s * 512:(s + 1) * 512] for s in range(2)]
    otok = aux[:, 2048:3072]
    nmc = auxf[:, 1536:1537]
    nm2c = auxf[:, 1537:1538]
    corrc = auxf[:, 1538:1539]
    lc = auxf[:, 1539:1540]
    rsc = auxf[:, 1540:1541]
    mxc = auxf[:, 1541:1542]
    rlc = auxf[:, 1542:1543]
    bqs = auxf[:, 1544:1552]
    bks = auxf[:, 1552:1560]
    bvrow = scr[0:1, 1024:2048]
    onesr = aux[0:1, 3120:3248]

    L1KEYS = ["l1stats", "l1bias", "l1ones", "l1acc0", "l1acc1", "l1ot",
              "l1l", "l1rs"]
    k.fence("sync", writes=[A0, A1, A2, A3] + L1KEYS)
    k.dma("qs", bqs, ins["l1a_bq"], writes=["l1bias"], stream="bias")
    k.dma("qs", bks, ins["l1a_bk"], writes=["l1bias"], stream="bias")
    k.dma("qg", bvrow, ins["l1a_bv"], writes=["l1bias", "SCR:2", "SCR:3"],
          stream="bias2")
    k.op("vector", lambda v: v.tensor_scalar_mul(bqs, bqs, 1.0 / 32.0),
         reads=["l1bias"], writes=["l1bias"])
    k.op("vector", lambda v: v.memset(onesr, 1.0), writes=["l1ones"])

    for nb in range(2):
        k.dma("qg", bf.big[1][:, 0:4096].rearrange("p (kc m) -> p kc m", kc=8),
              ins["l1a_wv"][:, nb * 512:(nb + 1) * 512]
              .rearrange("(kc p) m -> p kc m", p=128),
              writes=[bkey(1, 0)], stream="wtok")
        for tc in range(NT // 128):
            ppi = tc % 2
            def mm(t, tc=tc, ppi=ppi, nb=nb):
                for kc in range(DC):
                    t.matmul(PP[ppi][:], bf.bigv(0, kc)[:, tc * 128:(tc + 1) * 128],
                             bf.big[1][:, kc * 512:(kc + 1) * 512],
                             start=(kc == 0), stop=False)
                return t.matmul(PP[ppi][:], onesr,
                                bvrow[:, nb * 512:(nb + 1) * 512],
                                start=False, stop=True)
            k.op("tensor", mm, reads=[bkey(1, 0), "l1ones", "l1bias", "SCR:2",
                                      "SCR:3"] + BKALL(0),
                 writes=[f"pp{ppi}"])
            st_bf = scr[:, ppi * 512:ppi * 512 + 512]
            stk = f"SCR:{ppi}"
            k.op("vector", lambda v, st_bf=st_bf, ppi=ppi: v.tensor_copy(
                st_bf, PP[ppi][:]), reads=[f"pp{ppi}"], writes=[stk])
            k.dma("qs", vtd[tc * 128:(tc + 1) * 128, nb * 512:(nb + 1) * 512],
                  st_bf, reads=[stk], writes=[f"{tag}vd"], stream=f"st{ppi}")

    for dst_i, wd, bias_ap, bkf, scl in ((1, ins["l1a_wq"], bqs, "l1bias", 1 / 32.0),
                                         (2, ins["l1a_wk"], bks, "l1bias", 1.0)):
        for oc in range(DC):
            sl = oc % 2
            slab = scr[:, sl * 1024:(sl + 1) * 1024]
            slks = [f"SCR:{2 * sl}", f"SCR:{2 * sl + 1}"]
            k.dma("qg", slab.rearrange("p (kc m) -> p kc m", kc=8),
                  wd[:, oc * 128:(oc + 1) * 128].rearrange("(kc p) m -> p kc m", p=128),
                  writes=slks, stream=f"w{sl}")
            for j in range(TJ):
                ppi = j % 2
                def mm(t, slab=slab, j=j, ppi=ppi):
                    for kc in range(DC):
                        last = t.matmul(PP[ppi][:], slab[:, kc * 128:(kc + 1) * 128],
                                        bf.bigv(0, kc)[:, j * 512:(j + 1) * 512],
                                        start=(kc == 0), stop=(kc == DC - 1))
                    return last
                k.op("tensor", mm, reads=slks + BKALL(0), writes=[f"pp{ppi}"])
                k.op("scalar", lambda s, dst_i=dst_i, oc=oc, j=j, ppi=ppi,
                     bias_ap=bias_ap, scl=scl: s.activation(
                         bf.bigv(dst_i, oc)[:, j * 512:(j + 1) * 512], PP[ppi][:],
                         AF.Identity, bias=bias_ap[:, oc:oc + 1], scale=scl),
                     reads=[f"pp{ppi}", bkf, bkey(dst_i, oc)], writes=[bkey(dst_i, oc)])

    k.fence("sync", writes=BKALL(0))
    k.dma("qs", b1_[:, :].rearrange("p (ktc dv) -> p ktc dv", ktc=32),
          vtd[:, :].rearrange("(ktc p) dv -> p ktc dv", p=128),
          reads=[f"{tag}vd"], writes=BKALL(0), stream="vrl")

    for b in range(BPC):
        for qt in range(T // 128):
            q0 = b * T + qt * 128
            k.op("vector", lambda v: v.memset(nmc, 3e38), writes=["l1stats"])
            k.op("vector", lambda v: v.memset(lc, 0.0), writes=["l1stats"])
            k.op("vector", lambda v: v.memset(acc[0], 0.0), writes=["l1acc0"])
            k.op("vector", lambda v: v.memset(acc[1], 0.0), writes=["l1acc1"])
            for kt in range(4):
                k0 = b * T + kt * 512
                ps, psk = PP[kt % 2], f"pp{kt % 2}"
                def mms(t, q0=q0, k0=k0, ps=ps):
                    for dc in range(DC):
                        last = t.matmul(ps[:], bf.bigv(1, dc)[:, q0:q0 + 128],
                                        bf.bigv(2, dc)[:, k0:k0 + 512],
                                        start=(dc == 0), stop=(dc == DC - 1))
                    return last
                k.op("tensor", mms, reads=[bkey(1, c) for c in range(8)] +
                     [bkey(2, c) for c in range(8)], writes=[psk])
                k.op("vector", lambda v, ps=ps: v.tensor_reduce(
                    mxc, ps[:], AX.X, ALU.max, negate=True),
                     reads=[psk], writes=["l1stats"])
                k.op("vector", lambda v: v.tensor_tensor(nm2c, nmc, mxc, ALU.min),
                     reads=["l1stats"], writes=["l1stats"])
                k.op("vector", lambda v: v.tensor_sub(corrc, nm2c, nmc),
                     reads=["l1stats"], writes=["l1stats"])
                k.op("scalar", lambda s: s.activation(corrc, corrc, AF.Exp),
                     reads=["l1stats"], writes=["l1stats"])
                k.op("vector", lambda v: v.tensor_copy(nmc, nm2c),
                     reads=["l1stats"], writes=["l1stats"])
                pt_sb = scr[:, 0:512]
                k.op("scalar", lambda s, ps=ps: s.activation(
                    pt_sb, ps[:], AF.Exp, bias=nm2c, accum_out=rsc),
                     reads=[psk, "l1stats"], writes=["SCR:0", "l1rs"])
                k.op("vector", lambda v: v.scalar_tensor_tensor(
                    lc, lc, corrc, rsc, ALU.mult, ALU.add),
                     reads=["l1stats", "l1rs", "l1l"], writes=["l1l"])
                pp6b = PP[6][:, :].bitcast(BF16)
                def trs(t):
                    for ks in range(4):
                        last = t.transpose(pp6b[:, ks * 128:(ks + 1) * 128],
                                           pt_sb[:, ks * 128:(ks + 1) * 128],
                                           C.ident[:, :])
                    return last
                k.op("tensor", trs, reads=["SCR:0", "c_ident"], writes=["pp6"])
                k.op("vector", lambda v: v.tensor_copy(scr[:, 1024:1536],
                                                       pp6b[:, 0:512]),
                     reads=["pp6"], writes=["SCR:2"])
                pva, pvb = PP[2 + 2 * (kt % 2)], PP[3 + 2 * (kt % 2)]
                pvak, pvbk = f"pp{2 + 2 * (kt % 2)}", f"pp{3 + 2 * (kt % 2)}"
                def mmpv(t, kt=kt, b=b, pva=pva, pvb=pvb):
                    for ks in range(4):
                        ktc = b * 16 + kt * 4 + ks
                        t.matmul(pva[:], scr[:, 1024 + ks * 128:1024 + (ks + 1) * 128],
                                 b1_[:, ktc * 1024:ktc * 1024 + 512],
                                 start=(ks == 0), stop=(ks == 3))
                    for ks in range(4):
                        ktc = b * 16 + kt * 4 + ks
                        last = t.matmul(pvb[:],
                                        scr[:, 1024 + ks * 128:1024 + (ks + 1) * 128],
                                        b1_[:, ktc * 1024 + 512:(ktc + 1) * 1024],
                                        start=(ks == 0), stop=(ks == 3))
                    return last
                k.op("tensor", mmpv, reads=["SCR:2"] + BKALL(0),
                     writes=[pvak, pvbk])
                k.op("vector", lambda v, pva=pva: v.scalar_tensor_tensor(
                    acc[0], acc[0], corrc, pva[:], ALU.mult, ALU.add),
                     reads=[pvak, "l1stats", "l1acc0"], writes=["l1acc0"])
                k.op("vector", lambda v, pvb=pvb: v.scalar_tensor_tensor(
                    acc[1], acc[1], corrc, pvb[:], ALU.mult, ALU.add),
                     reads=[pvbk, "l1stats", "l1acc1"], writes=["l1acc1"])
            k.op("vector", lambda v: v.reciprocal(rlc, lc),
                 reads=["l1l"], writes=["l1stats"])
            k.op("vector", lambda v: v.tensor_scalar(otok[:, 0:512], acc[0], rlc,
                                                     None, ALU.mult),
                 reads=["l1acc0", "l1stats"], writes=["l1ot"])
            k.op("vector", lambda v: v.tensor_scalar(otok[:, 512:1024], acc[1], rlc,
                                                     None, ALU.mult),
                 reads=["l1acc1", "l1stats"], writes=["l1ot"])
            pp7b = PP[7][:, :].bitcast(BF16)
            def tro(t):
                for dc in range(DC):
                    last = t.transpose(pp7b[:, dc * 128:(dc + 1) * 128],
                                       otok[:, dc * 128:(dc + 1) * 128],
                                       C.ident[:, :])
                return last
            k.op("tensor", tro, reads=["l1ot", "c_ident"], writes=["pp7"])
            k.op("vector", lambda v: v.tensor_copy(scrf[:, 0:1024], pp7b[:, 0:1024]),
                 reads=["pp7"], writes=["SCR:0", "SCR:1", "SCR:2", "SCR:3"])
            jj = q0 // 512
            k.dma("qg", dst_dram[:, q0:q0 + 128].rearrange("(dc p) t -> p dc t", p=128),
                  scrf[:, 0:1024].rearrange("p (dc t) -> p dc t", dc=8),
                  reads=["SCR:0", "SCR:1", "SCR:2", "SCR:3"] +
                        [dst_key(c, jj) for c in range(8)],
                  writes=[dst_key(c, jj) for c in range(8)],
                  accum_op=ALU.add, stream=f"acc{qt % 2}")
    k.fence("sync", reads=L1KEYS, writes=[A0, A1, A2, A3])


def emit_l1_ffn(k: Kfw, C: Consts, bf: Bufs, ins, dst_dram, dst_key, tag="f1"):
    aux, scr = bf.aux, bf.scr
    auxf = aux[:, :].bitcast(F32)
    b1_, b2_, b3_ = bf.big[0], bf.big[1], bf.big[2]
    b1f = b1_[:, :].bitcast(F32)
    b3f = b3_[:, :].bitcast(F32)
    PP = bf.pp
    KJ = 512

    AFFR = auxf[:, 0:2048]
    AFFT = b3_[:, 18432:18688]
    GWT = b3_[:, 18688:18752]
    SILU = b3_[:, 18752:19776]
    NM3 = b3f[:, 9888:9889]
    RS3 = b3f[:, 9889:9890]
    SC3 = b3f[:, 9890:9891]
    AFF32 = b3f[:, 9892:9900]
    MASKA = b1_[0:64, 16384:18432]
    ZER = b1_[0:64, 18432:20480]
    POSA = b1f[0:64, 10240:12288]
    POST = b1f[:, 12288:13312]
    PHI_R = b1_[0:64, 26624:28672]
    PLO_R = b1_[0:64, 28672:30720]
    BIS = b1f[:, 15360:15392]
    LO, HI, MID = BIS[:, 0:1], BIS[:, 1:2], BIS[:, 2:3]
    CNT, SEL, NSEL, TT1 = BIS[:, 3:4], BIS[:, 4:5], BIS[:, 5:6], BIS[:, 6:7]
    JCC = b1f[:, 15392:15396]
    PTA = 0
    XSA = 8192
    YSA = 12288
    SLA = 16384
    GSEL = b3f[:, 9900:9904]

    B1K = ["maska", "zer", "posa", "post", "bis", "jcc", "phir", "plor", "h3"]
    B3K = ["pta", "xsa", "ysa", "sla0", "sla1", "afft", "gwt", "silu", "st3"]

    k.fence("sync", writes=[A0, A1, A2, A3, "affr"])
    k.fence("sync", writes=BKALL(2) + B3K + ["nm3", "rs3", "sc3", "aff32", "gsel", "rden"])
    k.op("vector", lambda v: v.memset(AFFR[0:64, :], 0.0), writes=["affr"])
    k.dma("qg", GWT.rearrange("p (dc f) -> p dc f", dc=8),
          ins["l1f_gwT"].rearrange("(dc p) f -> p dc f", p=128),
          writes=["gwt"], stream="bias")
    for tc in range(NT // 128):
        def mml(t, tc=tc):
            for dc in range(DC):
                last = t.matmul(PP[0][:, 0:8], bf.bigv(0, dc)[:, tc * 128:(tc + 1) * 128],
                                GWT[:, dc * 8:(dc + 1) * 8],
                                start=(dc == 0), stop=(dc == DC - 1))
            return last
        k.op("tensor", mml, reads=["gwt"] + BKALL(0), writes=["pp0"])
        k.op("vector", lambda v: v.tensor_reduce(NM3, PP[0][:, 0:8], AX.X, ALU.max,
                                                 negate=True),
             reads=["pp0"], writes=["nm3"])
        k.op("scalar", lambda s: s.activation(AFF32, PP[0][:, 0:8], AF.Exp, bias=NM3,
                                              accum_out=RS3),
             reads=["pp0", "nm3"], writes=["aff32", "rs3"])
        k.op("vector", lambda v: v.reciprocal(SC3, RS3), reads=["rs3"], writes=["sc3"])
        k.op("vector", lambda v, tc=tc: v.tensor_scalar(AFFT[:, tc * 8:(tc + 1) * 8],
                                                        AFF32, SC3, None, ALU.mult),
             reads=["aff32", "sc3"], writes=["afft"])
    for b in range(BPC):
        rb = 32 * b
        tp_m = (0, rb)
        for sl in range(4):
            s0 = b * T + sl * 512
            c0 = b * 32 * 64 if False else (sl * 512)
            def mmr(t, s0=s0, rb=rb):
                for dc in range(DC):
                    last = t.matmul(PP[0][rb:rb + 8, 0:512],
                                    GWT[:, dc * 8:(dc + 1) * 8],
                                    bf.bigv(0, dc)[:, s0:s0 + 512],
                                    start=(dc == 0), stop=(dc == DC - 1),
                                    tile_position=(0, rb))
                return last
            k.op("tensor", mmr, reads=["gwt"] + BKALL(0), writes=["pp0"])
            dst = AFFR[rb:rb + 8, sl * 512:(sl + 1) * 512]
            k.op("scalar", lambda s, dst=dst, rb=rb: s.activation(
                dst, PP[0][rb:rb + 8, 0:512], AF.Exp),
                 reads=["pp0"], writes=["affr"])
            k.op("tensor", lambda t, dst=dst, rb=rb: t.matmul(
                PP[1][0:1, 0:512], C.ones_col[rb:rb + 8, :], dst,
                start=True, stop=True, tile_position=(rb, 0)),
                 reads=["affr", "c_onesc"], writes=["pp1"])
            k.op("vector", lambda v: v.reciprocal(b3f[0:1, 9904:10416],
                                                  PP[1][0:1, 0:512]),
                 reads=["pp1"], writes=["rden"])
            k.op("tensor", lambda t, rb=rb: t.matmul(
                PP[2][rb:rb + 8, 0:512], C.ones_row[:, 0:8],
                b3f[0:1, 9904:10416], start=True, stop=True,
                tile_position=(0, rb)),
                 reads=["rden", "c_onesr"], writes=["pp2"])
            k.op("vector", lambda v, dst=dst, rb=rb: v.tensor_tensor(
                dst, dst, PP[2][rb:rb + 8, 0:512], ALU.mult),
                 reads=["affr", "pp2"], writes=["affr"])

    k.fence("sync", writes=[bkey(1, c) for c in range(8)])
    pp2b = PP[2][:, :].bitcast(BF16)
    for tc in range(NT // 128):
        def trx(t, tc=tc):
            for dc in range(DC):
                last = t.transpose(pp2b[:, dc * 128:(dc + 1) * 128],
                                   bf.bigv(0, dc)[:, tc * 128:(tc + 1) * 128],
                                   C.ident[:, :])
            return last
        k.op("tensor", trx, reads=BKALL(0) + ["c_ident"], writes=["pp2"])
        k.op("vector", lambda v, tc=tc: v.tensor_copy(
            b2_[:, tc * 1024:(tc + 1) * 1024], pp2b[:, 0:1024]),
             reads=["pp2"], writes=[bkey(1, tc // 4)])

    k.fence("sync", writes=BKALL(0) + B1K)
    k.op("vector", lambda v: v.memset(LO, 0.0), writes=["bis"])
    k.op("vector", lambda v: v.memset(HI, 1.0), writes=["bis"])
    k.op("vector", lambda v: v.memset(ZER[:, :], 0.0), writes=["zer"])
    for it in range(40):
        k.op("vector", lambda v: v.tensor_add(MID, LO, HI), reads=["bis"], writes=["bis"])
        k.op("vector", lambda v: v.tensor_scalar_mul(MID, MID, 0.5),
             reads=["bis"], writes=["bis"])
        k.op("vector", lambda v: v.tensor_scalar(MASKA[:, :], AFFR[0:64, :], MID[0:64],
                                                 None, ALU.is_gt,
                                                 op1=ALU.add, accum_out=CNT[0:64]),
             reads=["affr", "bis"], writes=["maska", "bis"])
        k.op("vector", lambda v: v.tensor_scalar(SEL, CNT, float(KJ), None, ALU.is_ge),
             reads=["bis"], writes=["bis"])
        k.op("vector", lambda v: v.tensor_sub(TT1, MID, LO), reads=["bis"], writes=["bis"])
        k.op("vector", lambda v: v.tensor_mul(TT1, TT1, SEL), reads=["bis"], writes=["bis"])
        k.op("vector", lambda v: v.tensor_add(LO, LO, TT1), reads=["bis"], writes=["bis"])
        k.op("vector", lambda v: v.tensor_scalar(NSEL, SEL, -1.0, 1.0, ALU.mult,
                                                 ALU.add), reads=["bis"], writes=["bis"])
        k.op("vector", lambda v: v.tensor_sub(TT1, MID, HI), reads=["bis"], writes=["bis"])
        k.op("vector", lambda v: v.tensor_mul(TT1, TT1, NSEL), reads=["bis"], writes=["bis"])
        k.op("vector", lambda v: v.tensor_add(HI, HI, TT1), reads=["bis"], writes=["bis"])
    k.op("vector", lambda v: v.tensor_scalar(MASKA[:, :], AFFR[0:64, :], LO[0:64],
                                             None, ALU.is_gt),
         reads=["affr", "bis"], writes=["maska"])
    k.op("vector", lambda v: v.tensor_tensor_scan(POSA, MASKA[:, :], ZER[:, :], 0.0,
                                                  ALU.add, ALU.add),
         reads=["maska", "zer"], writes=["posa"])
    k.op("vector", lambda v: v.tensor_tensor(POSA, POSA, MASKA[:, :], ALU.mult),
         reads=["posa", "maska"], writes=["posa"])
    k.op("vector", lambda v: v.tensor_scalar_add(POSA, POSA, -1.0),
         reads=["posa"], writes=["posa"])
    k.op("vector", lambda v: v.tensor_scalar(PHI_R[:, :], POSA, 256.0, 256.0,
                                             ALU.is_ge, ALU.mult),
         reads=["posa"], writes=["phir"])
    k.op("vector", lambda v: v.tensor_tensor(PLO_R[:, :], POSA, PHI_R[:, :],
                                             ALU.subtract),
         reads=["posa", "phir"], writes=["plor"])
    pp3b = PP[3][:, :].bitcast(BF16)
    for sc in range(16):
        def trp(t, sc=sc):
            t.transpose(pp3b[:, 0:64], PHI_R[:, sc * 128:(sc + 1) * 128],
                        C.ident[0:64, 0:64])
            return t.transpose(pp3b[:, 64:128], PLO_R[:, sc * 128:(sc + 1) * 128],
                               C.ident[0:64, 0:64])
        k.op("tensor", trp, reads=["phir", "plor", "c_ident"], writes=["pp3"])
        k.op("vector", lambda v: v.tensor_copy(SILU[:, 0:64], pp3b[:, 0:64]),
             reads=["pp3"], writes=["silu"])
        k.op("vector", lambda v, sc=sc: v.tensor_tensor(
            POST[:, sc * 64:(sc + 1) * 64], SILU[:, 0:64], pp3b[:, 64:128], ALU.add),
             reads=["silu", "pp3"], writes=["post"])
    for jc in range(4):
        k.op("vector", lambda v, jc=jc: v.tensor_scalar_add(JCC[:, jc:jc + 1],
                                                            C.chan[:], float(jc * 128)),
             reads=["c_chan"], writes=["jcc"])

    for e in range(8):
        for b in range(BPC):
            lane = 32 * b + e
            for sc in range(16):
                k.op("vector", lambda v, sc=sc, lane=lane: v.tensor_scalar(
                    b3_[:, PTA + sc * 512:PTA + (sc + 1) * 512], C.iota16[:],
                    POST[:, sc * 64 + lane:sc * 64 + lane + 1], None, ALU.is_equal),
                     reads=["post", "c_iota"], writes=["pta"])
            for dc in range(DC):
                ppi = dc % 2
                def mmx(t, dc=dc, b=b, ppi=ppi):
                    for sc in range(16):
                        tc = b * 16 + sc
                        last = t.matmul(PP[ppi][:],
                                        b2_[:, tc * 1024 + dc * 128:tc * 1024 + (dc + 1) * 128],
                                        b3_[:, PTA + sc * 512:PTA + (sc + 1) * 512],
                                        start=(sc == 0), stop=(sc == 15))
                    return last
                k.op("tensor", mmx, reads=["pta"] + [bkey(1, c) for c in range(8)],
                     writes=[f"pp{ppi}"])
                k.op("vector", lambda v, dc=dc, ppi=ppi: v.tensor_copy(
                    b3_[:, XSA + dc * 512:XSA + (dc + 1) * 512], PP[ppi][:]),
                     reads=[f"pp{ppi}"], writes=["xsa"])
            def mmg(t, b=b, e=e):
                for jc in range(4):
                    for sc in range(16):
                        tc = b * 16 + sc
                        last = t.matmul(PP[2][:, jc:jc + 1],
                                        b3_[:, PTA + sc * 512 + jc * 128:
                                            PTA + sc * 512 + (jc + 1) * 128],
                                        AFFT[:, tc * 8 + e:tc * 8 + e + 1],
                                        start=(sc == 0), stop=(sc == 15))
                return last
            k.op("tensor", mmg, reads=["pta", "afft"], writes=["pp2"])
            k.op("vector", lambda v: v.tensor_copy(GSEL, PP[2][:, 0:4]),
                 reads=["pp2"], writes=["gsel"])
            for ic in range(32):
                sl = ic % 2
                slab = b3_[:, SLA + sl * 1024:SLA + (sl + 1) * 1024]
                slk = f"sla{sl}"
                k.dma("qg", slab.rearrange("p (dc m) -> p dc m", dc=8),
                      ins["l1f_wgT"][e][:, ic * 128:(ic + 1) * 128]
                      .rearrange("(dc p) m -> p dc m", p=128),
                      writes=[slk], stream=f"w{sl}")
                def mmhg(t, slab=slab):
                    for dc in range(DC):
                        last = t.matmul(PP[4][:], slab[:, dc * 128:(dc + 1) * 128],
                                        b3_[:, XSA + dc * 512:XSA + (dc + 1) * 512],
                                        start=(dc == 0), stop=(dc == DC - 1))
                    return last
                k.op("tensor", mmhg, reads=[slk, "xsa"], writes=["pp4"])
                k.dma("qg", slab.rearrange("p (dc m) -> p dc m", dc=8),
                      ins["l1f_wuT"][e][:, ic * 128:(ic + 1) * 128]
                      .rearrange("(dc p) m -> p dc m", p=128),
                      writes=[slk], stream=f"w{sl}")
                def mmhu(t, slab=slab):
                    for dc in range(DC):
                        last = t.matmul(PP[5][:], slab[:, dc * 128:(dc + 1) * 128],
                                        b3_[:, XSA + dc * 512:XSA + (dc + 1) * 512],
                                        start=(dc == 0), stop=(dc == DC - 1))
                    return last
                k.op("tensor", mmhu, reads=[slk, "xsa"], writes=["pp5"])
                k.op("scalar", lambda s: s.activation(SILU[:, 0:512], PP[4][:], AF.Silu),
                     reads=["pp4"], writes=["silu"])
                k.op("vector", lambda v, ic=ic: v.tensor_tensor(
                    b1_[:, ic * 512:(ic + 1) * 512], SILU[:, 0:512], PP[5][:], ALU.mult),
                     reads=["silu", "pp5"], writes=["h3"])
            for ic in range(32):
                sl = ic % 2
                slab = b3_[:, SLA + sl * 1024:SLA + (sl + 1) * 1024]
                slk = f"sla{sl}"
                k.dma("qg", slab.rearrange("p (h m) -> p h m", h=2),
                      ins["l1f_wdT"][e][ic * 128:(ic + 1) * 128, :]
                      .rearrange("p (h m) -> p h m", h=2),
                      writes=[slk], stream=f"w{sl}")
                def mmy(t, ic=ic, slab=slab):
                    for jc in range(4):
                        for hh in range(2):
                            last = t.matmul(PP[jc * 2 + hh][:],
                                            b1_[:, ic * 512 + jc * 128:
                                                ic * 512 + (jc + 1) * 128],
                                            slab[:, hh * 512:(hh + 1) * 512],
                                            start=(ic == 0), stop=(ic == 31))
                    return last
                k.op("tensor", mmy, reads=[slk, "h3"],
                     writes=[f"pp{x}" for x in range(8)])
            for jc in range(4):
                for hh in range(2):
                    k.op("vector", lambda v, jc=jc, hh=hh: v.tensor_scalar(
                        b3_[:, YSA + jc * 1024 + hh * 512:YSA + jc * 1024 + (hh + 1) * 512],
                        PP[jc * 2 + hh][:], GSEL[:, jc:jc + 1], None, ALU.mult),
                         reads=[f"pp{jc * 2 + hh}", "gsel"], writes=["ysa"])
            for st in range(4):
                def mmb(t, b=b, e=e, st=st):
                    t.matmul(PP[0][:, 0:512],
                             C.sel8[32 * b:32 * b + 8, e * 128:(e + 1) * 128],
                             PHI_R[32 * b:32 * b + 8, st * 512:(st + 1) * 512],
                             start=True, stop=False)
                    return t.matmul(PP[0][:, 0:512],
                                    C.sel8[32 * b:32 * b + 8, e * 128:(e + 1) * 128],
                                    PLO_R[32 * b:32 * b + 8, st * 512:(st + 1) * 512],
                                    start=False, stop=True)
                k.op("tensor", mmb, reads=["phir", "plor", "c_sel8"], writes=["pp0"])
                for jc in range(4):
                    k.op("vector", lambda v, jc=jc, st=st: v.tensor_scalar(
                        b3_[:, PTA + jc * 2048 + st * 512:PTA + jc * 2048 + (st + 1) * 512],
                        PP[0][:, 0:512], JCC[:, jc:jc + 1], None, ALU.is_equal),
                         reads=["pp0", "jcc"], writes=["pta"])
            for dc in range(DC):
                for st in range(4):
                    ppi = 1 + (dc * 4 + st) % 2
                    def mmsc(t, dc=dc, st=st, ppi=ppi):
                        for jc in range(4):
                            last = t.matmul(PP[ppi][:],
                                            b3_[:, YSA + jc * 1024 + dc * 128:
                                                YSA + jc * 1024 + (dc + 1) * 128],
                                            b3_[:, PTA + jc * 2048 + st * 512:
                                                PTA + jc * 2048 + (st + 1) * 512],
                                            start=(jc == 0), stop=(jc == 3))
                        return last
                    k.op("tensor", mmsc, reads=["ysa", "pta"], writes=[f"pp{ppi}"])
                    stg = b3f[:, 10240 + (st % 2) * 512:10240 + (st % 2 + 1) * 512]
                    k.op("vector", lambda v, stg=stg, ppi=ppi: v.tensor_copy(
                        stg, PP[ppi][:]), reads=[f"pp{ppi}"], writes=[f"st3{st % 2}"])
                    jj = b * 4 + st
                    k.dma("qg", dst_dram[dc * 128:(dc + 1) * 128,
                                         jj * 512:(jj + 1) * 512],
                          stg, reads=[f"st3{st % 2}", dst_key(dc, jj)],
                          writes=[dst_key(dc, jj)], accum_op=ALU.add,
                          stream=f"acc{st % 2}")


def emit_l2_attn(k: Kfw, C: Consts, bf: Bufs, ins, dst_dram, dst_key, tag="a2"):
    aux, scr = bf.aux, bf.scr
    scrf = scr[:, :].bitcast(F32)
    b1_, b2_, b3_ = bf.big[0], bf.big[1], bf.big[2]
    b1f = b1_[:, :].bitcast(F32)
    PP = bf.pp
    DI = 2048
    NCC = 16
    zd = k.dram(f"{tag}_zd", [DI, NT], BF16)

    def xc_chunk(cc):
        return (bf.bigv(1, cc) if cc < 8 else bf.bigv(2, cc - 8))

    def xck(cc):
        return bkey(1, cc) if cc < 8 else bkey(2, cc - 8)

    DLT = b1f[0:1, 0:4096]
    BR = b1f[32:48, 0:4096]
    CR = b1f[64:80, 0:4096]
    BBF = b1_[32:48, 8192:12288]
    CBF = b1_[64:80, 8192:12288]
    DA_T = b1_[:, 12288:14336]
    DBU = b1_[:, 14336:16384]
    HT = b1_[:, 16384:18432]
    HC = b1_[:, 18432:20480]
    DBT = b1_[:, 20480:22528]
    CLT = b1_[:, 22528:24576]
    DBR = b1_[:, 24576:26624]
    ZS = b1_[:, 26624:27648]
    ZEROS = b1_[:, 27648:29696]
    REPC = b1_[:, 29696:31744]
    REPB = b1_[0:80, 32448:32576]
    DCOL = b1f[:, 15872:15888]
    CW = b1f[:, 15888:15952]
    CB = b1f[:, 15952:15968]
    ACOL = b1f[:, 15968 + 0:15968 + 256]
    B1K2 = ["dlt", "brow", "crow", "bbf", "cbf", "dat", "dbu", "ht", "hc",
            "dbt", "clt", "dbr", "zs", "zeros", "repc", "dcol", "cw", "cb",
            "acol", "summ", "repb", "tmpr", "xslab", "cstrip"]

    k.fence("sync", writes=[bkey(1, c) for c in range(8)] +
            [bkey(2, c) for c in range(8)] + [A0, A1, A2, A3])
    for oc in range(32):
        sl = oc % 2
        slab = scr[:, sl * 1024:(sl + 1) * 1024]
        slks = [f"SCR:{2 * sl}", f"SCR:{2 * sl + 1}"]
        k.dma("qg", slab.rearrange("p (kc m) -> p kc m", kc=8),
              ins["l2a_inprojT"][:, oc * 128:(oc + 1) * 128]
              .rearrange("(kc p) m -> p kc m", p=128),
              writes=slks, stream=f"w{sl}")
        for j in range(TJ):
            ppi = j % 2
            def mm(t, slab=slab, j=j, ppi=ppi):
                for kc in range(DC):
                    last = t.matmul(PP[ppi][:], slab[:, kc * 128:(kc + 1) * 128],
                                    bf.bigv(0, kc)[:, j * 512:(j + 1) * 512],
                                    start=(kc == 0), stop=(kc == DC - 1))
                return last
            k.op("tensor", mm, reads=slks + BKALL(0), writes=[f"pp{ppi}"])
            if oc < 16:
                k.op("vector", lambda v, oc=oc, j=j, ppi=ppi: v.tensor_copy(
                    xc_chunk(oc)[:, j * 512:(j + 1) * 512], PP[ppi][:]),
                     reads=[f"pp{ppi}"], writes=[xck(oc)])
            else:
                st_bf = aux[:, (j % 2) * 512:(j % 2) * 512 + 512]
                stk = ["AUX:0"] if j % 2 == 0 else ["AUX:0b"]
                k.op("vector", lambda v, st_bf=st_bf, ppi=ppi: v.tensor_copy(
                    st_bf, PP[ppi][:]), reads=[f"pp{ppi}"], writes=stk)
                k.dma("qs", zd[(oc - 16) * 128:(oc - 15) * 128,
                               j * 512:(j + 1) * 512],
                      st_bf, reads=stk, writes=["zd"], stream=f"st{j % 2}")

    k.fence("sync", writes=BKALL(0) + B1K2 + ["SCR:0", "SCR:1", "SCR:2", "SCR:3",
                                              "AUX:0", "AUX:0b", "AUX:1", "AUX:2",
                                              "AUX:3"])
    k.dma("qg", REPC.rearrange("p (m f) -> p m f", m=16),
          ins["cst_repc"].rearrange("(m p) f -> p m f", p=128),
          writes=["repc"], stream="cst")
    k.dma("qg", REPB[0:16, :], ins["cst_repb"], writes=["repb"], stream="cst")
    k.dma("qg", REPB[32:48, :], ins["cst_repb"], writes=["repb"], stream="cst")
    k.dma("qg", REPB[64:80, :], ins["cst_repb"], writes=["repb"], stream="cst")
    k.dma("qg", scr[:, :].rearrange("p (m f) -> p m f", m=16),
          ins["cst_summ"].rearrange("(m p) f -> p m f", p=128),
          writes=["summ", "SCR:0", "SCR:1", "SCR:2", "SCR:3"], stream="cst")
    k.dma("qs", CW, ins["l2a_convw"], writes=["cw"], stream="cst2")
    k.dma("qs", CB, ins["l2a_convb"], writes=["cb"], stream="cst2")
    k.dma("qs", DCOL, ins["l2a_dcol"], writes=["dcol"], stream="cst2")
    k.dma("qs", ACOL, ins["l2a_alog"], writes=["acol"], stream="cst2")
    k.op("scalar", lambda s: s.activation(ACOL, ACOL, AF.Exp),
         reads=["acol"], writes=["acol"])
    k.op("vector", lambda v: v.tensor_scalar_mul(ACOL, ACOL, -1.0),
         reads=["acol"], writes=["acol"])
    k.op("vector", lambda v: v.memset(ZEROS[:, :], 0.0), writes=["zeros"])
    for cc in range(NCC):
        for b in range(BPC):
            strip = xc_chunk(cc)[:, b * T:(b + 1) * T]
            astrip = aux[:, 0:T]
            k.op("vector", lambda v, strip=strip, astrip=astrip, cc=cc: v.tensor_scalar(
                astrip, strip, CW[:, cc * 4 + 3:cc * 4 + 4], None, ALU.mult),
                 reads=[xck(cc), "cw"], writes=["cstrip"])
            for kk in range(3):
                sh = 3 - kk
                k.op("vector", lambda v, strip=strip, astrip=astrip, cc=cc, kk=kk,
                     sh=sh: v.scalar_tensor_tensor(
                         astrip[:, sh:T], strip[:, 0:T - sh],
                         CW[:, cc * 4 + kk:cc * 4 + kk + 1],
                         astrip[:, sh:T], ALU.mult, ALU.add),
                     reads=[xck(cc), "cw", "cstrip"], writes=["cstrip"])
            k.op("scalar", lambda s, strip=strip, astrip=astrip, cc=cc: s.activation(
                strip, astrip, AF.Silu, bias=CB[:, cc:cc + 1]),
                 reads=["cstrip", "cb"], writes=[xck(cc)])

    for j in range(TJ):
        sl = j % 2
        slab = scr[:, :]
        xslab = aux[:, 0:2048]
        if j == 0:
            k.dma("qg", aux[:, 0:528].rearrange("p (kc m) -> p kc m", kc=16),
                  ins["l2a_xprojT"].rearrange("(kc p) m -> p kc m", p=128),
                  writes=["xslab", "cstrip"], stream="w0")
        def mmx(t, j=j):
            for kc in range(NCC):
                t.matmul(PP[4][0:1, :], xslab[:, kc * 33 + 0:kc * 33 + 1],
                         xc_chunk(kc)[:, j * 512:(j + 1) * 512],
                         start=(kc == 0), stop=(kc == NCC - 1))
            for kc in range(NCC):
                t.matmul(PP[4][32:48, :], xslab[:, kc * 33 + 1:kc * 33 + 17],
                         xc_chunk(kc)[:, j * 512:(j + 1) * 512],
                         start=(kc == 0), stop=(kc == NCC - 1),
                         tile_position=(0, 32))
            for kc in range(NCC):
                last = t.matmul(PP[4][64:80, :], xslab[:, kc * 33 + 17:kc * 33 + 33],
                                xc_chunk(kc)[:, j * 512:(j + 1) * 512],
                                start=(kc == 0), stop=(kc == NCC - 1),
                                tile_position=(0, 64))
            return last
        k.op("tensor", mmx, reads=["xslab"] + [xck(c) for c in range(16)],
             writes=["pp4"])
        k.op("vector", lambda v, j=j: v.tensor_copy(DLT[:, j * 512:(j + 1) * 512],
                                                    PP[4][0:1, :]),
             reads=["pp4"], writes=["dlt"])
        k.op("vector", lambda v, j=j: v.tensor_copy(BR[:, j * 512:(j + 1) * 512],
                                                    PP[4][32:48, :]),
             reads=["pp4"], writes=["brow"])
        k.op("vector", lambda v, j=j: v.tensor_copy(CR[:, j * 512:(j + 1) * 512],
                                                    PP[4][64:80, :]),
             reads=["pp4"], writes=["crow"])
    tmpr = b1f[0:1, 4096:8192]
    k.op("scalar", lambda s: s.activation(tmpr, DLT, AF.Abs),
         reads=["dlt"], writes=["tmpr"])
    k.op("scalar", lambda s: s.activation(tmpr, tmpr, AF.Exp, scale=-1.0),
         reads=["tmpr"], writes=["tmpr"])
    k.op("vector", lambda v: v.tensor_scalar_add(tmpr, tmpr, 1.0),
         reads=["tmpr"], writes=["tmpr"])
    k.op("scalar", lambda s: s.activation(tmpr, tmpr, AF.Ln),
         reads=["tmpr"], writes=["tmpr"])
    k.op("vector", lambda v: v.tensor_scalar_max(DLT, DLT, 0.0),
         reads=["dlt"], writes=["dlt"])
    k.op("vector", lambda v: v.tensor_add(DLT, DLT, tmpr),
         reads=["dlt", "tmpr"], writes=["dlt"])
    k.op("vector", lambda v: v.tensor_copy(BBF[:, :], BR),
         reads=["brow"], writes=["bbf"])
    k.op("vector", lambda v: v.tensor_copy(CBF[:, :], CR),
         reads=["crow"], writes=["cbf"])

    for b in range(BPC):
        bs = b * T
        for s4 in range(4):
            ppi = 4 + s4 % 2
            k.op("tensor", lambda t, bs=bs, s4=s4, ppi=ppi: t.matmul(
                PP[ppi][:], C.ones_row[:, :],
                DLT[:, bs + s4 * 512:bs + (s4 + 1) * 512], start=True, stop=True),
                 reads=["dlt", "c_onesr"], writes=[f"pp{ppi}"])
            k.op("vector", lambda v, s4=s4, ppi=ppi: v.tensor_copy(
                DBR[:, s4 * 512:(s4 + 1) * 512], PP[ppi][:]),
                 reads=[f"pp{ppi}"], writes=["dbr"])
        for s4 in range(4):
            k.op("tensor", lambda t, bs=bs, s4=s4: t.matmul(
                PP[4][:], REPB[32:48, :],
                BBF[:, bs + s4 * 512:bs + (s4 + 1) * 512], start=True, stop=True,
                tile_position=(32, 0)),
                 reads=["bbf", "repb"], writes=["pp4"])
            k.op("vector", lambda v, s4=s4: v.tensor_tensor(
                DBT[:, s4 * 512:(s4 + 1) * 512], DBR[:, s4 * 512:(s4 + 1) * 512],
                PP[4][:], ALU.mult),
                 reads=["pp4", "dbr"], writes=["dbt"])
            k.op("tensor", lambda t, bs=bs, s4=s4: t.matmul(
                PP[5][:], REPB[64:80, :],
                CBF[:, bs + s4 * 512:bs + (s4 + 1) * 512], start=True, stop=True,
                tile_position=(64, 0)),
                 reads=["cbf", "repb"], writes=["pp5"])
            k.op("vector", lambda v, s4=s4: v.tensor_copy(
                CLT[:, s4 * 512:(s4 + 1) * 512], PP[5][:]),
                 reads=["pp5"], writes=["clt"])
        for cc in range(NCC):
            for jj in range(16):
                lt = cc * 16 + jj
                acol = ACOL[:, lt:lt + 1]
                for s4 in range(4):
                    ppi = 4 + s4 % 2
                    k.op("tensor", lambda t, cc=cc, jj=jj, bs=bs, s4=s4, ppi=ppi:
                         t.matmul(PP[ppi][:], REPC[:, jj * 128:(jj + 1) * 128],
                                  xc_chunk(cc)[:, bs + s4 * 512:bs + (s4 + 1) * 512],
                                  start=True, stop=True),
                         reads=[xck(cc), "repc"], writes=[f"pp{ppi}"])
                    k.op("vector", lambda v, s4=s4, ppi=ppi: v.tensor_tensor(
                        DBU[:, s4 * 512:(s4 + 1) * 512],
                        DBT[:, s4 * 512:(s4 + 1) * 512], PP[ppi][:], ALU.mult),
                         reads=[f"pp{ppi}", "dbt"], writes=["dbu"])
                k.op("scalar", lambda s, acol=acol: s.activation(
                    DA_T[:, :], DBR[:, :], AF.Exp, scale=acol),
                     reads=["dbr", "acol"], writes=["dat"])
                k.op("vector", lambda v: v.tensor_tensor_scan(
                    HT[:, :], DA_T[:, :], DBU[:, :], 0.0, ALU.mult, ALU.add),
                     reads=["dat", "dbu"], writes=["ht"])
                k.op("gpsimd", lambda g: g.tensor_tensor(HC[:, :], HT[:, :],
                                                         CLT[:, :], ALU.mult),
                     reads=["ht", "clt"], writes=["hc"])
                def mmy(t, jj=jj):
                    for s4 in range(4):
                        last = t.matmul(PP[s4][:], scr[:, jj * 128:(jj + 1) * 128],
                                        HC[:, s4 * 512:(s4 + 1) * 512],
                                        start=(jj == 0), stop=(jj == 15))
                    return last
                k.op("tensor", mmy, reads=["hc", "summ"],
                     writes=["pp0", "pp1", "pp2", "pp3"])
            for s4 in range(4):
                zs = ZS[:, (s4 % 2) * 512:(s4 % 2) * 512 + 512]
                zsk = f"zs{s4 % 2}"
                k.dma("qg", zs, zd[cc * 128:(cc + 1) * 128,
                                   bs + s4 * 512:bs + (s4 + 1) * 512],
                      reads=["zd"], writes=[zsk], stream=f"zl{s4 % 2}")
                k.op("scalar", lambda s, zs=zs: s.activation(zs, zs, AF.Silu),
                     reads=[zsk], writes=[zsk])
                k.op("vector", lambda v, cc=cc, bs=bs, s4=s4: v.scalar_tensor_tensor(
                    HC[:, s4 * 512:(s4 + 1) * 512],
                    xc_chunk(cc)[:, bs + s4 * 512:bs + (s4 + 1) * 512],
                    DCOL[:, cc:cc + 1], PP[s4][:], ALU.mult, ALU.add),
                     reads=[xck(cc), "dcol", f"pp{s4}"], writes=["hc"])
                k.op("vector", lambda v, cc=cc, bs=bs, s4=s4, zs=zs: v.tensor_tensor(
                    xc_chunk(cc)[:, bs + s4 * 512:bs + (s4 + 1) * 512],
                    HC[:, s4 * 512:(s4 + 1) * 512], zs, ALU.mult),
                     reads=["hc", zsk], writes=[xck(cc)])

    for oc in range(DC):
        sl = oc % 2
        slab = aux[:, sl * 2048:(sl + 1) * 2048]
        slk = ["AUX:0", "AUX:2"][sl]
        k.dma("qg", slab.rearrange("p (kc m) -> p kc m", kc=16),
              ins["l2a_outprojT"][:, oc * 128:(oc + 1) * 128]
              .rearrange("(kc p) m -> p kc m", p=128),
              writes=[slk, {0: "AUX:1", 1: "AUX:3"}[sl], "cstrip", "xslab"],
              stream=f"w{sl}")
        for j in range(TJ):
            ppi = 6 + j % 2
            def mm(t, slab=slab, j=j, ppi=ppi):
                for kc in range(NCC):
                    last = t.matmul(PP[ppi][:], slab[:, kc * 128:(kc + 1) * 128],
                                    xc_chunk(kc)[:, j * 512:(j + 1) * 512],
                                    start=(kc == 0), stop=(kc == NCC - 1))
                return last
            k.op("tensor", mm, reads=[slk] + [xck(c) for c in range(16)],
                 writes=[f"pp{ppi}"])
            stage = scrf[:, (j % 2) * 512:(j % 2) * 512 + 512]
            stks = [f"SCR:{2 * (j % 2)}", f"SCR:{2 * (j % 2) + 1}"]
            k.op("vector", lambda v, stage=stage, ppi=ppi: v.tensor_copy(
                stage, PP[ppi][:]), reads=[f"pp{ppi}"], writes=stks)
            k.dma("qg", dst_dram[oc * 128:(oc + 1) * 128, j * 512:(j + 1) * 512],
                  stage, reads=stks + [dst_key(oc, j)], writes=[dst_key(oc, j)],
                  accum_op=ALU.add, stream=f"acc{j % 2}")


def mk_scan_consts():
    p = np.arange(128)
    repc = np.zeros((16, 128, 128), np.float32)
    summ = np.zeros((16, 128, 128), np.float32)
    for m in range(16):
        for pp_ in range(128):
            r = m * 8 + pp_ // 16
            repc[m, r, pp_] = 1.0
            summ[m, pp_, r] = 1.0
    repb = (p[None, :] % 16 == np.arange(16)[:, None]).astype(np.float32)
    return {"cst_repc": repc.reshape(2048, 128), "cst_summ": summ.reshape(2048, 128),
            "cst_repb": np.ascontiguousarray(repb)}


def emit_l2_ffn(k: Kfw, C: Consts, bf: Bufs, ins, dst_dram, dst_key, n_cores,
                tag="f2", sim_only=False, mm_out=None, mm_in=None):
    aux, scr = bf.aux, bf.scr
    auxf = aux[:, :].bitcast(F32)
    scrf = scr[:, :].bitcast(F32)
    b1_, b2_, b3_ = bf.big[0], bf.big[1], bf.big[2]
    PP = bf.pp
    H, DH = 16, 64
    vtd = k.dram(f"{tag}_vtd", [NT, D], BF16)
    ccin = k.dram(f"{tag}_ccin", [1, 2], F32)
    ccout = k.dram(f"{tag}_ccout", [1, 2], F32)

    SQT = aux[:, 0:1024]
    SIMR = auxf[:, 512:1536]
    POSR = aux[0:1, 3072:7168 - 3072 + 3072] if False else aux[0:1, 1024 + 2048:1024 + 2048 + 0]
    POSR = aux[0:1, 0:4096]
    B1K5 = ["vp5", "vn5", "mpc", "mnc", "posr", "mpr", "mnr", "simr", "sq5",
            "simacc", "n2r", "qgn", "ho", "thr", "pt5a", "pt5b", "rbp", "rbn",
            "xpn", "xnn", "vnh15", "vtile"]

    def vp_ap(h, kt):
        return b1_[:, h * 1040 + kt * 65:h * 1040 + (kt + 1) * 65]

    def vn_ap(h, kt):
        if h < 15:
            return b1_[:, 16640 + h * 1040 + kt * 65:16640 + h * 1040 + (kt + 1) * 65]
        return aux[:, kt * 65:(kt + 1) * 65]

    k.fence("sync", writes=[bkey(1, c) for c in range(8)] +
            [bkey(2, c) for c in range(8)] + ["SCR:0", "SCR:1", "SCR:2", "SCR:3"])
    for dst_i, base, scl in (((1, 0, 0.125),) if sim_only else
                             ((1, 0, 0.125), (2, 1024, 1.0))):
        for oc in range(DC):
            sl = oc % 2
            slab = scr[:, sl * 1024:(sl + 1) * 1024]
            slks = [f"SCR:{2 * sl}", f"SCR:{2 * sl + 1}"]
            k.dma("qg", slab.rearrange("p (kc m) -> p kc m", kc=8),
                  ins["l2f_wqkv"][:, base + oc * 128:base + (oc + 1) * 128]
                  .rearrange("(kc p) m -> p kc m", p=128),
                  writes=slks, stream=f"w{sl}")
            for j in range(TJ):
                ppi = j % 2
                def mm(t, slab=slab, j=j, ppi=ppi):
                    for kc in range(DC):
                        last = t.matmul(PP[ppi][:], slab[:, kc * 128:(kc + 1) * 128],
                                        bf.bigv(0, kc)[:, j * 512:(j + 1) * 512],
                                        start=(kc == 0), stop=(kc == DC - 1))
                    return last
                k.op("tensor", mm, reads=slks + BKALL(0), writes=[f"pp{ppi}"])
                k.op("scalar", lambda s, dst_i=dst_i, oc=oc, j=j, ppi=ppi, scl=scl:
                     s.activation(bf.bigv(dst_i, oc)[:, j * 512:(j + 1) * 512],
                                  PP[ppi][:], AF.Identity, scale=scl),
                     reads=[f"pp{ppi}", bkey(dst_i, oc)], writes=[bkey(dst_i, oc)])
    for nb in ([] if sim_only else range(4)):
        slab = scr[:, 0:2048]
        slks = ["SCR:0", "SCR:1", "SCR:2", "SCR:3"]
        k.dma("qg", slab.rearrange("p (kc m) -> p kc m", kc=8),
              ins["l2f_wqkv"][:, 2048 + nb * 256:2048 + (nb + 1) * 256]
              .rearrange("(kc p) m -> p kc m", p=128),
              writes=slks, stream="w0")
        for tc in range(NT // 128):
            ppi = 4 + tc % 2
            def mm(t, tc=tc, slab=slab, ppi=ppi):
                for kc in range(DC):
                    last = t.matmul(PP[ppi][:, 0:256],
                                    bf.bigv(0, kc)[:, tc * 128:(tc + 1) * 128],
                                    slab[:, kc * 256:(kc + 1) * 256],
                                    start=(kc == 0), stop=(kc == DC - 1))
                return last
            k.op("tensor", mm, reads=slks + BKALL(0), writes=[f"pp{ppi}"])
            stb = aux[:, 1024 + (tc % 2) * 256:1024 + (tc % 2) * 256 + 256]
            stk = f"vst{tc % 2}"
            k.op("vector", lambda v, stb=stb, ppi=ppi: v.tensor_copy(
                stb, PP[ppi][:, 0:256]), reads=[f"pp{ppi}"], writes=[stk])
            k.dma("qs", vtd[tc * 128:(tc + 1) * 128, nb * 256:(nb + 1) * 256],
                  stb, reads=[stk], writes=[f"{tag}vd"], stream=f"st{tc % 2}")

    k.fence("sync", writes=BKALL(0) + B1K5 + [A0, A1, A2, A3])
    b1f = b1_[:, :].bitcast(F32)
    HO = b1_[:, 0:2]
    QGC = b1f[:, 4:12]
    N2E = b1f[0:2, 16:24]
    QGN = b1_[:, 64:80]
    HI2 = b1_[0:2, 96:224]
    SACC = b1f[0:2, 128:640]
    RSQ = b1f[0:2, 640:1152]
    MMX = b1f[0:1, 1152:1160]
    SIMR = b1f[0:1, 2048:6144]
    SEXC = b1f[0:1, 6144:10240]

    PROWS = b1_[0:1, 20480:24576]
    MPRR = b1_[0:1, 24576:28672]
    MNRR = b1_[0:1, 28672:32768]
    SQB = aux[:, 0:512]
    posd = k.dram(f"{tag}_posd", [1, NT], BF16)

    k.op("vector", lambda v: v.tensor_scalar(HO[:, 0:1], C.chan[:], 64.0, None,
                                             ALU.is_lt), reads=["c_chan"], writes=["ho"])
    k.op("vector", lambda v: v.tensor_scalar(HO[:, 1:2], C.chan[:], 64.0, None,
                                             ALU.is_ge), reads=["c_chan"], writes=["ho"])
    pp6b = PP[6][:, :].bitcast(BF16)
    k.op("tensor", lambda t: t.transpose(pp6b[0:2, 0:128], HO[:, :], C.ident[:, :]),
         reads=["ho", "c_ident"], writes=["pp6"])
    k.op("vector", lambda v: v.tensor_copy(HI2[:, :], pp6b[0:2, 0:128]),
         reads=["pp6"], writes=["ho"])
    N2B = b1_[0:2, 2336:2344]
    k.dma("qs", QGC, ins["l2f_qg"], writes=["qgn"], stream="bias")
    k.op("scalar", lambda s: s.activation(SQB[:, 0:8], QGC, AF.Square),
         reads=["qgn"], writes=["sq5"])
    k.op("tensor", lambda t: t.matmul(PP[6][0:2, 0:8], HO[:, :], SQB[:, 0:8],
                                      start=True, stop=True),
         reads=["ho", "sq5"], writes=["pp6"])
    k.op("vector", lambda v: v.tensor_copy(N2E, PP[6][0:2, 0:8]),
         reads=["pp6"], writes=["n2r"])
    k.op("scalar", lambda s: s.activation(N2E, N2E, AF.Ln), reads=["n2r"], writes=["n2r"])
    k.op("scalar", lambda s: s.activation(N2B, N2E, AF.Exp, scale=-0.5),
         reads=["n2r"], writes=["n2r"])
    k.op("tensor", lambda t: t.matmul(PP[6][:, 0:8], HI2[:, :], N2B,
                                      start=True, stop=True),
         reads=["ho", "n2r"], writes=["pp6"])
    k.op("vector", lambda v: v.tensor_tensor(QGN[:, 0:8], QGC, PP[6][:, 0:8],
                                             ALU.mult),
         reads=["qgn", "pp6"], writes=["qgn"])
    for jt in range(TJ):
        k.op("vector", lambda v: v.memset(SACC, 0.0), writes=["simacc"])
        for hc in range(DC):
            k.op("scalar", lambda s, hc=hc, jt=jt: s.activation(
                SQB, bf.bigv(1, hc)[:, jt * 512:(jt + 1) * 512], AF.Square),
                 reads=[bkey(1, hc)], writes=["sq5"])
            k.op("tensor", lambda t: t.matmul(PP[6][0:2, :], HO[:, :], SQB,
                                              start=True, stop=True),
                 reads=["sq5", "ho"], writes=["pp6"])
            k.op("tensor", lambda t, hc=hc, jt=jt: t.matmul(
                PP[7][0:2, :], QGN[:, 2 * hc:2 * hc + 2],
                bf.bigv(1, hc)[:, jt * 512:(jt + 1) * 512], start=True, stop=True),
                 reads=["qgn", bkey(1, hc)], writes=["pp7"])
            k.op("vector", lambda v: v.tensor_copy(RSQ, PP[6][0:2, :]),
                 reads=["pp6"], writes=["n2r"])
            k.op("scalar", lambda s: s.activation(RSQ, RSQ, AF.Ln),
                 reads=["n2r"], writes=["n2r"])
            k.op("scalar", lambda s: s.activation(RSQ, RSQ, AF.Exp, scale=-0.5),
                 reads=["n2r"], writes=["n2r"])
            k.op("vector", lambda v: v.tensor_tensor(RSQ, RSQ, PP[7][0:2, :],
                                                     ALU.mult),
                 reads=["n2r", "pp7"], writes=["n2r"])
            k.op("vector", lambda v: v.tensor_add(SACC, SACC, RSQ),
                 reads=["n2r", "simacc"], writes=["simacc"])
        k.op("tensor", lambda t: t.matmul(PP[6][0:1, :], C.ones_col[0:2, :], SACC,
                                          start=True, stop=True),
             reads=["simacc", "c_onesc"], writes=["pp6"])
        k.op("vector", lambda v, jt=jt: v.tensor_scalar(
            SIMR[:, jt * 512:(jt + 1) * 512], PP[6][0:1, :], 1.0 / 16.0, None,
            ALU.mult),
             reads=["pp6"], writes=["simr"])
    for val, col in ((-3.0e38, 0), (3.0e38, 1)):
        k.op("vector", lambda v: v.tensor_copy(SEXC, SIMR),
             reads=["simr"], writes=["sexc"])
        for b in range(BPC):
            k.op("vector", lambda v, b=b, val=val: v.memset(
                SEXC[:, b * T:b * T + 5], val), reads=["sexc"], writes=["sexc"])
        if col == 0:
            k.op("vector", lambda v: v.tensor_reduce(MMX[:, 0:1], SEXC, AX.X,
                                                     ALU.max),
                 reads=["sexc"], writes=["thr"])
        else:
            k.op("vector", lambda v: v.tensor_reduce(MMX[:, 1:2], SEXC, AX.X,
                                                     ALU.max, negate=True),
                 reads=["sexc"], writes=["thr"])
    k.op("vector", lambda v: v.tensor_scalar_mul(SEXC, SEXC, -1.0),
         reads=["sexc"], writes=["sexc"])
    k.op("vector", lambda v: v.tensor_reduce(MMX[:, 1:2], SEXC, AX.X, ALU.max),
         reads=["sexc"], writes=["thr"])
    if sim_only:
        k.dma("qs", mm_out, MMX[:, 0:2], reads=["thr"], writes=["mmout"],
              stream="cc")
        return
    if mm_in is not None:
        MMI = b1f[0:1, 1176:1192]
        k.dma("qs", MMI, mm_in, writes=["ccst", "thr"], stream="cc")
        k.op("vector", lambda v: v.tensor_reduce(MMX[:, 0:1], MMI[:, 0:8],
                                                 AX.X, ALU.max),
             reads=["ccst"], writes=["thr"])
        k.op("vector", lambda v: v.tensor_reduce(MMX[:, 1:2], MMI[:, 8:16],
                                                 AX.X, ALU.max),
             reads=["ccst"], writes=["thr"])
    elif n_cores > 1:
        stb = aux[:, 512:516].bitcast(F32)[0:1, 0:2] if False else None
        k.op("vector", lambda v: v.tensor_copy(auxf[0:1, 256:258], MMX[:, 0:2]),
             reads=["thr"], writes=["ccst"])
        k.dma("qs", ccin, auxf[0:1, 256:258], reads=["ccst"], writes=["ccin"],
              stream="cc")
        def ccop(g):
            return g.collective_compute(
                "AllReduce", ALU.max,
                [[cid for cid in range(n_cores)]],
                [ccin.opt()], [ccout.opt()])
        k.qop("gpsimd", ccop, reads=["ccin"], writes=["ccout"])
        k.dma("qs", auxf[0:1, 256:258], ccout, reads=["ccout"], writes=["ccst"],
              stream="cc")
        k.op("vector", lambda v: v.tensor_copy(MMX[:, 0:2], auxf[0:1, 256:258]),
             reads=["ccst"], writes=["thr"])
    k.op("vector", lambda v: v.tensor_scalar_mul(MMX[:, 2:3], MMX[:, 0:1], 0.9),
         reads=["thr"], writes=["thr"])
    k.op("vector", lambda v: v.scalar_tensor_tensor(MMX[:, 2:3], MMX[:, 1:2],
                                                    -0.1, MMX[:, 2:3],
                                                    ALU.mult, ALU.add),
         reads=["thr"], writes=["thr"])
    k.op("vector", lambda v: v.tensor_scalar(PROWS, SIMR, MMX[0:1, 2:3], None,
                                             ALU.is_gt),
         reads=["simr", "thr"], writes=["posr"])
    k.op("vector", lambda v: v.tensor_scalar(MNRR, PROWS, -1.0, 1.0, ALU.mult,
                                             ALU.add),
         reads=["posr"], writes=["mnr"])
    k.op("vector", lambda v: v.tensor_copy(MPRR, PROWS), reads=["posr"],
         writes=["mpr"])
    for b in range(BPC):
        k.op("vector", lambda v, b=b: v.memset(MPRR[:, b * T:b * T + 5], 1.0),
             reads=["mpr"], writes=["mpr"])
        k.op("vector", lambda v, b=b: v.memset(MNRR[:, b * T:b * T + 5], 1.0),
             reads=["mnr"], writes=["mnr"])
        k.op("vector", lambda v, b=b: v.memset(PROWS[:, b * T:b * T + 5], 0.5),
             reads=["posr"], writes=["posr"])
    k.dma("qs", posd, PROWS, reads=["posr"], writes=["posd"], stream="cc")
    MPC = auxf[:, 520:552]
    MNC = auxf[:, 552:584]
    pp7bT = PP[7]
    pp7b = PP[7][:, :].bitcast(BF16)
    for g in range(8):
        def trm(t, g=g):
            for q in range(4):
                tc = g * 4 + q
                t.transpose(pp7b[:, 2 * q:2 * q + 1],
                            MPRR[:, tc * 128:(tc + 1) * 128], C.ident[0:1, 0:1])
                last = t.transpose(pp7b[:, 8 + 2 * q:9 + 2 * q],
                                   MNRR[:, tc * 128:(tc + 1) * 128],
                                   C.ident[0:1, 0:1])
            return last
        k.op("tensor", trm, reads=["mpr", "mnr", "c_ident"], writes=["pp7"])
        k.op("vector", lambda v, g=g: v.tensor_copy(
            MPC[:, g * 4:(g + 1) * 4],
            pp7b[:, 0:8].rearrange("p (a b) -> p a b", b=2)[:, :, 0:1]),
             reads=["pp7"], writes=["mpc"])
        k.op("vector", lambda v, g=g: v.tensor_copy(
            MNC[:, g * 4:(g + 1) * 4],
            pp7b[:, 8:16].rearrange("p (a b) -> p a b", b=2)[:, :, 0:1]),
             reads=["pp7"], writes=["mnc"])

    k.fence("sync", writes=["vp5", "vn5", "vnh15", "vtile", "prow", "ones64",
                            "neg10", "xpn", "xnn", "rbp", "rbn", "simr", "sexc",
                            "posr", "mpr", "mnr", "thr", "simacc", "n2r", "qgn",
                            "ho", "sq5", "ccst", "SCR:0", "SCR:1", "SCR:2",
                            "SCR:3", "vst0", "vst1"])
    ONES64 = aux[:, 3744:3808]
    NEG10 = auxf[:, 584:585]
    XPN = aux[:, 1184:1696]
    XNN = aux[:, 1696:2208]
    k.op("vector", lambda v: v.memset(ONES64[:, :], 1.0), writes=["ones64"])
    k.op("vector", lambda v: v.memset(NEG10, -10.0), writes=["neg10"])
    prow = scr[0:1, 1024:1536]
    for b in range(BPC):
        for kt in range(16):
            vt = aux[:, 2208:3232]
            k.dma("qg", vt, vtd[(b * 16 + kt) * 128:(b * 16 + kt + 1) * 128, :],
                  reads=[f"{tag}vd"], writes=["vtile"], stream="vt")
            for which, msk, mkey in (("p", MPC, "mpc"), ("n", MNC, "mnc")):
                nh = 16 if which == "p" else 15
                boff = (0 if which == "p" else 16640) + kt * 65
                def build(v, kt=kt, msk=msk, which=which, nh=nh, boff=boff, b=b):
                    col = msk[:, b * 16 + kt:b * 16 + kt + 1]
                    oap = bass.AP(b1_[:, 0:1].tensor, boff,
                                  [[32768, 128], [1040, nh], [1, 64]])
                    iap = bass.AP(vt.tensor, 2208, [[4096, 128], [64, nh], [1, 64]])
                    v.tensor_scalar(oap, iap, col, None, ALU.mult)
                    dap = bass.AP(b1_[:, 0:1].tensor, boff + 64,
                                  [[32768, 128], [1040, nh], [1, 1]])
                    return v.tensor_copy(dap, col.broadcast_to((128, nh, 1)))
                k.op("vector", build, reads=["vtile", mkey],
                     writes=["vp5" if which == "p" else "vn5"])
            def buildn15(v, b=b, kt=kt):
                col = MNC[:, b * 16 + kt:b * 16 + kt + 1]
                v.tensor_scalar(aux[:, kt * 65:kt * 65 + 64], vt[:, 15 * 64:16 * 64],
                                col, None, ALU.mult)
                return v.tensor_copy(aux[:, kt * 65 + 64:kt * 65 + 65], col)
            k.op("vector", buildn15, reads=["vtile", "mnc"], writes=["vnh15"])
        for qg in range(4):
            q0 = b * T + qg * 512
            k.dma("qg", prow, posd[0:1, q0:q0 + 512], reads=["posd"],
                  writes=["prow"], stream="pr")
            k.op("tensor", lambda t: t.matmul(PP[6][0:64, :], ONES64[0:1, :],
                                              prow, start=True, stop=True),
                 reads=["prow", "ones64"], writes=["pp6"])
            for h in range(16):
                hc, rb = h // 2, (h % 2) * 64
                for kt in range(16):
                    k0 = b * T + kt * 128
                    ps, psk = PP[kt % 2], f"pp{kt % 2}"
                    k.op("tensor", lambda t, ps=ps, hc=hc, rb=rb, k0=k0, q0=q0:
                         t.matmul(ps[:], bf.bigv(2, hc)[rb:rb + 64, k0:k0 + 128],
                                  bf.bigv(1, hc)[rb:rb + 64, q0:q0 + 512],
                                  start=True, stop=True),
                         reads=[bkey(2, hc), bkey(1, hc)], writes=[psk])
                    pt = scr[:, (kt % 2) * 512:(kt % 2) * 512 + 512]
                    ptk = f"SCR:{kt % 2}"
                    k.op("scalar", lambda s, pt=pt, ps=ps: s.activation(
                        pt, ps[:], AF.Exp, bias=NEG10),
                         reads=[psk, "neg10"], writes=[ptk])
                    k.op("tensor", lambda t, pt=pt, h=h, kt=kt: t.matmul(
                        PP[2][0:65, :], vp_ap(h, kt), pt,
                        start=(kt == 0), stop=(kt == 15)),
                         reads=[ptk, "vp5"], writes=["pp2"])
                    k.op("tensor", lambda t, pt=pt, h=h, kt=kt: t.matmul(
                        PP[3][0:65, :], vn_ap(h, kt), pt,
                        start=(kt == 0), stop=(kt == 15)),
                         reads=[ptk, "vn5", "vnh15"], writes=["pp3"])
                rbp = scr[64:65, 1536:2048]
                rbn = aux[64:65, 3232:3744]
                def recp(v, rbp=rbp):
                    with k.nc.allow_low_precision(
                            reason="bf16 recip of softmax den; P already bf16"):
                        return v.reciprocal(rbp, PP[2][64:65, :])
                def recn(v, rbn=rbn):
                    with k.nc.allow_low_precision(
                            reason="bf16 recip of softmax den; P already bf16"):
                        return v.reciprocal(rbn, PP[3][64:65, :])
                k.op("vector", recp, reads=["pp2"], writes=["rbp", "SCR:3"])
                k.op("vector", recn, reads=["pp3"], writes=["rbn"])
                k.op("tensor", lambda t, rbp=rbp: t.matmul(
                    PP[4][0:64, :], ONES64[64:65, :], rbp, start=True, stop=True,
                    tile_position=(64, 0)),
                     reads=["rbp", "SCR:3", "ones64"], writes=["pp4"])
                k.op("tensor", lambda t, rbn=rbn: t.matmul(
                    PP[5][0:64, :], ONES64[64:65, :], rbn, start=True, stop=True,
                    tile_position=(64, 0)),
                     reads=["rbn", "ones64"], writes=["pp5"])
                k.op("vector", lambda v: v.tensor_copy(scr[0:64, 0:512],
                                                       PP[4][0:64, :]),
                     reads=["pp4"], writes=["SCR:0"])
                k.op("vector", lambda v: v.tensor_copy(scr[0:64, 512:1024],
                                                       PP[5][0:64, :]),
                     reads=["pp5"], writes=["SCR:1"])
                k.op("vector", lambda v: v.tensor_tensor(
                    XPN[0:64, :], PP[2][0:64, :], scr[0:64, 0:512], ALU.mult),
                     reads=["pp2", "SCR:0"], writes=["xpn"])
                k.op("vector", lambda v: v.tensor_tensor(
                    XNN[0:64, :], PP[3][0:64, :], scr[0:64, 512:1024], ALU.mult),
                     reads=["pp3", "SCR:1"], writes=["xnn"])
                k.op("vector", lambda v: v.tensor_sub(XPN[0:64, :], XPN[0:64, :],
                                                      XNN[0:64, :]),
                     reads=["xpn", "xnn"], writes=["xpn"])
                k.op("vector", lambda v: v.tensor_tensor(
                    XPN[0:64, :], XPN[0:64, :], PP[6][0:64, :], ALU.mult),
                     reads=["xpn", "pp6"], writes=["xpn"])
                k.op("vector", lambda v: v.tensor_add(XPN[0:64, :], XPN[0:64, :],
                                                      XNN[0:64, :]),
                     reads=["xpn", "xnn"], writes=["xpn"])
                okey = f"ao5/{hc}/{h % 2}/{b}/{qg}"
                if rb == 0:
                    k.op("vector", lambda v, hc=hc, q0=q0: v.tensor_copy(
                        bf.bigv(1, hc)[0:64, q0:q0 + 512], XPN[0:64, :]),
                         reads=["xpn", bkey(1, hc)], writes=[okey])
                else:
                    k.op("tensor", lambda t: t.matmul(
                        PP[7][64:128, :], C.ident[0:64, 0:64], XPN[0:64, :],
                        start=True, stop=True, tile_position=(0, 64)),
                         reads=["xpn", "c_ident"], writes=["pp7"])
                    k.op("vector", lambda v, hc=hc, q0=q0: v.tensor_copy(
                        bf.bigv(1, hc)[64:128, q0:q0 + 512], PP[7][64:128, :]),
                         reads=["pp7", bkey(1, hc)], writes=[okey])
    k.fence("sync", reads=[f"ao5/{hc}/{hf}/{b}/{qg}" for hc in range(8)
                           for hf in range(2) for b in range(2) for qg in range(4)],
            writes=[bkey(1, c) for c in range(8)])

    BPJ = auxf[:, 326:334]
    k.dma("qs", BPJ, ins["l2f_bproj"], writes=["bpj", "vnh15"], stream="bias")
    for oc in range(DC):
        sl = oc % 2
        slab = scr[:, sl * 1024:(sl + 1) * 1024]
        slks = [f"SCR:{2 * sl}", f"SCR:{2 * sl + 1}"]
        k.dma("qg", slab.rearrange("p (kc m) -> p kc m", kc=8),
              ins["l2f_wproj"][:, oc * 128:(oc + 1) * 128]
              .rearrange("(kc p) m -> p kc m", p=128),
              writes=slks, stream=f"w{sl}")
        for j in range(TJ):
            ppi = j % 2
            def mm(t, slab=slab, j=j, ppi=ppi):
                for kc in range(DC):
                    last = t.matmul(PP[ppi][:], slab[:, kc * 128:(kc + 1) * 128],
                                    bf.bigv(1, kc)[:, j * 512:(j + 1) * 512],
                                    start=(kc == 0), stop=(kc == DC - 1))
                return last
            k.op("tensor", mm, reads=slks + [bkey(1, c) for c in range(8)],
                 writes=[f"pp{ppi}"])
            stage = aux[:, 1184 + (j % 2) * 1024:1184 + (j % 2) * 1024 + 1024].bitcast(F32)
            stk = f"stg5{j % 2}"
            k.op("scalar", lambda s, stage=stage, oc=oc, ppi=ppi: s.activation(
                stage, PP[ppi][:], AF.Identity, bias=BPJ[:, oc:oc + 1]),
                 reads=[f"pp{ppi}", "bpj"], writes=[stk, "xpn", "xnn", "vtile"])
            k.dma("qg", dst_dram[oc * 128:(oc + 1) * 128, j * 512:(j + 1) * 512],
                  stage, reads=[stk, dst_key(oc, j)], writes=[dst_key(oc, j)],
                  accum_op=ALU.add, stream=f"acc{j % 2}")


def emit_l3_attn(k: Kfw, C: Consts, bf: Bufs, ins, dst_dram, dst_key, tag="a3"):
    aux, scr = bf.aux, bf.scr
    scrf = scr[:, :].bitcast(F32)
    b1_, b2_, b3_ = bf.big[0], bf.big[1], bf.big[2]
    b1f = b1_[:, :].bitcast(F32)
    PP = bf.pp
    DI, NCC = 2048, 16
    zd = k.dram(f"{tag}_zd", [DI, NT], BF16)
    deltad = k.dram(f"{tag}_dd", [DI, NT], BF16)

    def xc_chunk(cc):
        return (bf.bigv(1, cc) if cc < 8 else bf.bigv(2, cc - 8))

    def xck(cc):
        return bkey(1, cc) if cc < 8 else bkey(2, cc - 8)

    DTR = b1_[0:64, 0:4096]
    DTW = b1_[0:64, 4096:6144]
    BBF = b1_[64:80, 0:4096]
    CBF = b1_[64:80, 4096:8192]
    DA_T = b1_[:, 8192:10240]
    DBU = b1_[:, 10240:12288]
    HT = b1_[:, 12288:14336]
    HC3 = b1_[:, 14336:16384]
    DLS = b1_[:, 16384:18432]
    CLT = b1_[:, 18432:20480]
    BLT = b1_[:, 20480:22528]
    REPC3 = b1_[:, 22528:24576]
    UD = b1_[:, 24576:26624]
    DCS = b1_[:, 26624:28672]
    ZS3 = b1_[:, 28672:29696]
    REPB3 = b1_[0:80, 29696:29824]
    CW3 = b1f[:, 14912:14976]
    CB3 = b1f[:, 14976:14992]
    DTB = b1f[:, 14992:15008]
    DCOL3 = b1f[:, 15008:15024]
    ACOL3 = b1f[:, 15024:15280]
    B1K3 = ["dtr", "bbf3", "cbf3", "dat3", "dbu3", "ht3",
            "hc3", "dls", "clt3", "blt3", "repc3", "dtw", "ud", "dcs", "zs3",
            "repb3", "cw3", "cb3", "dtb", "dcol3", "acol3",
            "xslab3", "dstg0", "dstg1", "ab0", "ab1"]

    k.fence("sync", writes=[bkey(1, c) for c in range(8)] +
            [bkey(2, c) for c in range(8)] + [A0, A1, A2, A3])
    for oc in range(32):
        sl = oc % 2
        slab = scr[:, sl * 1024:(sl + 1) * 1024]
        slks = [f"SCR:{2 * sl}", f"SCR:{2 * sl + 1}"]
        k.dma("qg", slab.rearrange("p (kc m) -> p kc m", kc=8),
              ins["l3a_inprojT"][:, oc * 128:(oc + 1) * 128]
              .rearrange("(kc p) m -> p kc m", p=128),
              writes=slks, stream=f"w{sl}")
        for j in range(TJ):
            ppi = j % 2
            def mm(t, slab=slab, j=j, ppi=ppi):
                for kc in range(DC):
                    last = t.matmul(PP[ppi][:], slab[:, kc * 128:(kc + 1) * 128],
                                    bf.bigv(0, kc)[:, j * 512:(j + 1) * 512],
                                    start=(kc == 0), stop=(kc == DC - 1))
                return last
            k.op("tensor", mm, reads=slks + BKALL(0), writes=[f"pp{ppi}"])
            if oc < 16:
                k.op("vector", lambda v, oc=oc, j=j, ppi=ppi: v.tensor_copy(
                    xc_chunk(oc)[:, j * 512:(j + 1) * 512], PP[ppi][:]),
                     reads=[f"pp{ppi}"], writes=[xck(oc)])
            else:
                st_bf = aux[:, (j % 2) * 512:(j % 2) * 512 + 512]
                stk = ["zst0", "zst1"][j % 2]
                k.op("vector", lambda v, st_bf=st_bf, ppi=ppi: v.tensor_copy(
                    st_bf, PP[ppi][:]), reads=[f"pp{ppi}"], writes=[stk, "AUX:0"])
                k.dma("qs", zd[(oc - 16) * 128:(oc - 15) * 128,
                               j * 512:(j + 1) * 512],
                      st_bf, reads=[stk], writes=["zd3"], stream=f"st{j % 2}")

    k.fence("sync", writes=BKALL(0) + B1K3 + ["zst0", "zst1", "SCR:0", "SCR:1",
                                              "SCR:2", "SCR:3"])
    k.dma("qg", REPC3.rearrange("p (m f) -> p m f", m=16),
          ins["cst_repc"].rearrange("(m p) f -> p m f", p=128),
          writes=["repc3"], stream="cst")
    k.dma("qg", REPB3[64:80, :], ins["cst_repb"], writes=["repb3"], stream="cst")
    k.dma("qs", CW3, ins["l3a_convw"], writes=["cw3"], stream="cst2")
    k.dma("qs", CB3, ins["l3a_convb"], writes=["cb3"], stream="cst2")
    k.dma("qs", DTB, ins["l3a_dtb"], writes=["dtb"], stream="cst2")
    k.dma("qs", DCOL3, ins["l3a_dcol"], writes=["dcol3"], stream="cst2")
    k.dma("qs", ACOL3, ins["l3a_alog"], writes=["acol3"], stream="cst2")
    k.dma("qg", DTW, ins["l3a_dtprojT"], writes=["dtw"], stream="cst")
    k.dma("qg", scr[:, :].rearrange("p (m f) -> p m f", m=16),
          ins["cst_summ"].rearrange("(m p) f -> p m f", p=128),
          writes=["summ3", "SCR:0", "SCR:1", "SCR:2", "SCR:3"], stream="cst")
    k.op("scalar", lambda s: s.activation(ACOL3, ACOL3, AF.Exp),
         reads=["acol3"], writes=["acol3"])
    k.op("vector", lambda v: v.tensor_scalar_mul(ACOL3, ACOL3, -1.0),
         reads=["acol3"], writes=["acol3"])
    for cc in range(NCC):
        for b in range(BPC):
            strip = xc_chunk(cc)[:, b * T:(b + 1) * T]
            astrip = aux[:, 0:T]
            k.op("vector", lambda v, strip=strip, astrip=astrip, cc=cc: v.tensor_scalar(
                astrip, strip, CW3[:, cc * 4 + 3:cc * 4 + 4], None, ALU.mult),
                 reads=[xck(cc), "cw3"], writes=["cstrip3", "AUX:0", "AUX:1"])
            for kk in range(3):
                sh = 3 - kk
                k.op("vector", lambda v, strip=strip, astrip=astrip, cc=cc, kk=kk,
                     sh=sh: v.scalar_tensor_tensor(
                         astrip[:, sh:T], strip[:, 0:T - sh],
                         CW3[:, cc * 4 + kk:cc * 4 + kk + 1],
                         astrip[:, sh:T], ALU.mult, ALU.add),
                     reads=[xck(cc), "cw3", "cstrip3"], writes=["cstrip3"])
            k.op("scalar", lambda s, strip=strip, astrip=astrip, cc=cc: s.activation(
                strip, astrip, AF.Silu, bias=CB3[:, cc:cc + 1]),
                 reads=["cstrip3", "cb3"], writes=[xck(cc)])

    for j in range(TJ):
        xslab = aux[:, 0:1536]
        if j == 0:
            k.dma("qg", xslab.rearrange("p (kc m) -> p kc m", kc=16),
                  ins["l3a_xprojT"].rearrange("(kc p) m -> p kc m", p=128),
                  writes=["xslab3", "cstrip3", "AUX:0", "AUX:1"], stream="w0")
        def mmx(t, j=j, xslab=xslab):
            for kc in range(NCC):
                t.matmul(PP[4][0:64, :], xslab[:, kc * 96 + 0:kc * 96 + 64],
                         xc_chunk(kc)[:, j * 512:(j + 1) * 512],
                         start=(kc == 0), stop=(kc == NCC - 1))
            for kc in range(NCC):
                t.matmul(PP[4][64:80, :], xslab[:, kc * 96 + 64:kc * 96 + 80],
                         xc_chunk(kc)[:, j * 512:(j + 1) * 512],
                         start=(kc == 0), stop=(kc == NCC - 1),
                         tile_position=(0, 64))
            for kc in range(NCC):
                last = t.matmul(PP[5][64:80, :], xslab[:, kc * 96 + 80:kc * 96 + 96],
                                xc_chunk(kc)[:, j * 512:(j + 1) * 512],
                                start=(kc == 0), stop=(kc == NCC - 1),
                                tile_position=(0, 64))
            return last
        k.op("tensor", mmx, reads=["xslab3"] + [xck(c) for c in range(16)],
             writes=["pp4", "pp5"])
        k.op("vector", lambda v, j=j: v.tensor_copy(DTR[:, j * 512:(j + 1) * 512],
                                                    PP[4][0:64, :]),
             reads=["pp4"], writes=["dtr"])
        k.op("vector", lambda v, j=j: v.tensor_copy(BBF[:, j * 512:(j + 1) * 512],
                                                    PP[4][64:80, :]),
             reads=["pp4"], writes=["bbf3"])
        k.op("vector", lambda v, j=j: v.tensor_copy(CBF[:, j * 512:(j + 1) * 512],
                                                    PP[5][64:80, :]),
             reads=["pp5"], writes=["cbf3"])

    for cc in range(NCC):
        for j in range(TJ):
            ppi = j % 2
            k.op("tensor", lambda t, cc=cc, j=j, ppi=ppi: t.matmul(
                PP[ppi][:], DTW[:, cc * 128:(cc + 1) * 128],
                DTR[:, j * 512:(j + 1) * 512], start=True, stop=True),
                 reads=["dtw", "dtr"], writes=[f"pp{ppi}"])
            t_re = scrf[:, (j % 2) * 256:(j % 2) * 256 + 256] if False else None
            t_ab = aux[:, 1536 + (j % 2) * 512:1536 + (j % 2) * 512 + 512]
            stg = aux[:, 2560 + (j % 2) * 512:2560 + (j % 2) * 512 + 512]
            abk = ["ab0", "ab1"][j % 2]
            stk = ["dstg0", "dstg1"][j % 2]
            k.op("scalar", lambda s, t_ab=t_ab, cc=cc, ppi=ppi: s.activation(
                t_ab, PP[ppi][:], AF.Abs, bias=DTB[:, cc:cc + 1]),
                 reads=[f"pp{ppi}", "dtb"], writes=[abk, "AUX:1", "AUX:2"])
            k.op("scalar", lambda s, t_ab=t_ab: s.activation(t_ab, t_ab, AF.Exp,
                                                             scale=-1.0),
                 reads=[abk], writes=[abk])
            k.op("vector", lambda v, t_ab=t_ab: v.tensor_scalar_add(t_ab, t_ab, 1.0),
                 reads=[abk], writes=[abk])
            k.op("scalar", lambda s, t_ab=t_ab: s.activation(t_ab, t_ab, AF.Ln),
                 reads=[abk], writes=[abk])
            k.op("vector", lambda v, stg=stg, cc=cc, ppi=ppi: v.tensor_scalar(
                stg, PP[ppi][:], DTB[:, cc:cc + 1], 0.0, ALU.add, ALU.max),
                 reads=[f"pp{ppi}", "dtb"], writes=[stk, "AUX:2", "AUX:3"])
            k.op("vector", lambda v, stg=stg, t_ab=t_ab: v.tensor_add(stg, stg, t_ab),
                 reads=[stk, abk], writes=[stk])
            k.dma("qs", deltad[cc * 128:(cc + 1) * 128, j * 512:(j + 1) * 512],
                  stg, reads=[stk], writes=["dd3"], stream=f"st{j % 2}")

    for b in range(BPC):
        bs = b * T
        for s4 in range(4):
            k.op("tensor", lambda t, bs=bs, s4=s4: t.matmul(
                PP[4][:], REPB3[64:80, :],
                BBF[:, bs + s4 * 512:bs + (s4 + 1) * 512], start=True, stop=True,
                tile_position=(64, 0)),
                 reads=["bbf3", "repb3"], writes=["pp4"])
            k.op("vector", lambda v, s4=s4: v.tensor_copy(
                BLT[:, s4 * 512:(s4 + 1) * 512], PP[4][:]),
                 reads=["pp4"], writes=["blt3"])
            k.op("tensor", lambda t, bs=bs, s4=s4: t.matmul(
                PP[5][:], REPB3[64:80, :],
                CBF[:, bs + s4 * 512:bs + (s4 + 1) * 512], start=True, stop=True,
                tile_position=(64, 0)),
                 reads=["cbf3", "repb3"], writes=["pp5"])
            k.op("vector", lambda v, s4=s4: v.tensor_copy(
                CLT[:, s4 * 512:(s4 + 1) * 512], PP[5][:]),
                 reads=["pp5"], writes=["clt3"])
        for cc in range(NCC):
            k.dma("qg", DCS, deltad[cc * 128:(cc + 1) * 128, bs:bs + T],
                  reads=["dd3"], writes=["dcs"], stream="dl")
            k.op("vector", lambda v, cc=cc, bs=bs: v.tensor_tensor(
                UD[:, :], DCS[:, :], xc_chunk(cc)[:, bs:bs + T], ALU.mult),
                 reads=["dcs", xck(cc)], writes=["ud"])
            for jj in range(16):
                lt = cc * 16 + jj
                acol = ACOL3[:, lt:lt + 1]
                for s4 in range(4):
                    pu, puk = PP[4 + s4 % 2], f"pp{4 + s4 % 2}"
                    pd, pdk = PP[6 + s4 % 2], f"pp{6 + s4 % 2}"
                    k.op("tensor", lambda t, jj=jj, s4=s4, pu=pu: t.matmul(
                        pu[:], REPC3[:, jj * 128:(jj + 1) * 128],
                        UD[:, s4 * 512:(s4 + 1) * 512], start=True, stop=True),
                         reads=["ud", "repc3"], writes=[puk])
                    k.op("tensor", lambda t, jj=jj, s4=s4, pd=pd: t.matmul(
                        pd[:], REPC3[:, jj * 128:(jj + 1) * 128],
                        DCS[:, s4 * 512:(s4 + 1) * 512], start=True, stop=True),
                         reads=["dcs", "repc3"], writes=[pdk])
                    k.op("vector", lambda v, s4=s4, pu=pu: v.tensor_tensor(
                        DBU[:, s4 * 512:(s4 + 1) * 512],
                        BLT[:, s4 * 512:(s4 + 1) * 512], pu[:], ALU.mult),
                         reads=[puk, "blt3"], writes=["dbu3"])
                    k.op("vector", lambda v, s4=s4, pd=pd: v.tensor_copy(
                        DLS[:, s4 * 512:(s4 + 1) * 512], pd[:]),
                         reads=[pdk], writes=["dls"])
                k.op("scalar", lambda s, acol=acol: s.activation(
                    DA_T[:, :], DLS[:, :], AF.Exp, scale=acol),
                     reads=["dls", "acol3"], writes=["dat3"])
                k.op("vector", lambda v: v.tensor_tensor_scan(
                    HT[:, :], DA_T[:, :], DBU[:, :], 0.0, ALU.mult, ALU.add),
                     reads=["dat3", "dbu3"], writes=["ht3"])
                k.op("gpsimd", lambda g: g.tensor_tensor(HC3[:, :], HT[:, :],
                                                         CLT[:, :], ALU.mult),
                     reads=["ht3", "clt3"], writes=["hc3"])
                def mmy(t, jj=jj):
                    for s4 in range(4):
                        last = t.matmul(PP[s4][:], scr[:, jj * 128:(jj + 1) * 128],
                                        HC3[:, s4 * 512:(s4 + 1) * 512],
                                        start=(jj == 0), stop=(jj == 15))
                    return last
                k.op("tensor", mmy, reads=["hc3", "summ3"],
                     writes=["pp0", "pp1", "pp2", "pp3"])
            for s4 in range(4):
                zs = ZS3[:, (s4 % 2) * 512:(s4 % 2) * 512 + 512]
                zsk = f"zs3{s4 % 2}"
                k.dma("qg", zs, zd[cc * 128:(cc + 1) * 128,
                                   bs + s4 * 512:bs + (s4 + 1) * 512],
                      reads=["zd3"], writes=[zsk, "zs3"], stream=f"zl{s4 % 2}")
                k.op("scalar", lambda s, zs=zs: s.activation(zs, zs, AF.Silu),
                     reads=[zsk], writes=[zsk])
                k.op("vector", lambda v, cc=cc, bs=bs, s4=s4: v.scalar_tensor_tensor(
                    HC3[:, s4 * 512:(s4 + 1) * 512],
                    xc_chunk(cc)[:, bs + s4 * 512:bs + (s4 + 1) * 512],
                    DCOL3[:, cc:cc + 1], PP[s4][:], ALU.mult, ALU.add),
                     reads=[xck(cc), "dcol3", f"pp{s4}"], writes=["hc3"])
                k.op("vector", lambda v, cc=cc, bs=bs, s4=s4, zs=zs: v.tensor_tensor(
                    xc_chunk(cc)[:, bs + s4 * 512:bs + (s4 + 1) * 512],
                    HC3[:, s4 * 512:(s4 + 1) * 512], zs, ALU.mult),
                     reads=["hc3", zsk], writes=[xck(cc)])

    for oc in range(DC):
        sl = oc % 2
        slab = aux[:, sl * 2048:(sl + 1) * 2048]
        k.dma("qg", slab.rearrange("p (kc m) -> p kc m", kc=16),
              ins["l3a_outprojT"][:, oc * 128:(oc + 1) * 128]
              .rearrange("(kc p) m -> p kc m", p=128),
              writes=[["AUX:0", "AUX:1"], ["AUX:2", "AUX:3"]][sl] +
                     ["cstrip3", "xslab3", "ab0", "ab1", "dstg0", "dstg1"],
              stream=f"w{sl}")
        for j in range(TJ):
            ppi = 6 + j % 2
            def mm(t, slab=slab, j=j, ppi=ppi):
                for kc in range(NCC):
                    last = t.matmul(PP[ppi][:], slab[:, kc * 128:(kc + 1) * 128],
                                    xc_chunk(kc)[:, j * 512:(j + 1) * 512],
                                    start=(kc == 0), stop=(kc == NCC - 1))
                return last
            k.op("tensor", mm, reads=[["AUX:0", "AUX:1"], ["AUX:2", "AUX:3"]][sl] +
                 [xck(c) for c in range(16)], writes=[f"pp{ppi}"])
            stage = scrf[:, (j % 2) * 512:(j % 2) * 512 + 512]
            stks = [f"SCR:{2 * (j % 2)}", f"SCR:{2 * (j % 2) + 1}", "summ3"]
            k.op("vector", lambda v, stage=stage, ppi=ppi: v.tensor_copy(
                stage, PP[ppi][:]), reads=[f"pp{ppi}"], writes=stks)
            k.dma("qg", dst_dram[oc * 128:(oc + 1) * 128, j * 512:(j + 1) * 512],
                  stage, reads=stks + [dst_key(oc, j)], writes=[dst_key(oc, j)],
                  accum_op=ALU.add, stream=f"acc{j % 2}")


def emit_l3_ffn(k: Kfw, C: Consts, bf: Bufs, ins, dst_dram, dst_key, tag="f3"):
    aux, scr = bf.aux, bf.scr
    b1_, b2_, b3_ = bf.big[0], bf.big[1], bf.big[2]
    b1f = b1_[:, :].bitcast(F32)
    PP = bf.pp
    B1K7 = ["sr", "wsr", "bis3", "wrc", "b1c7", "b2c7", "h7",
            "u7", "s17", "stg7", "xstg0", "xstg1"]
    SR = b3_[:, :].bitcast(F32)[0:1, 8192:12288]
    WSR = b3_[:, :].bitcast(F32)[0:1, 12288:16384]
    BIS3 = aux[:, 3168:3200].bitcast(F32)[0:1, :]
    LO3, HI3, MID3 = BIS3[:, 0:1], BIS3[:, 1:2], BIS3[:, 2:3]
    CNT3, SEL3, NS3, T13 = (BIS3[:, 3:4], BIS3[:, 4:5], BIS3[:, 5:6],
                            BIS3[:, 6:7])
    WRC = aux[:, 3200:3216].bitcast(F32)
    B1C = aux[:, 3072:3136].bitcast(F32)
    B2C = aux[:, 3136:3152].bitcast(F32)
    WRCB = aux[:, 3152:3160]
    MSCR = aux[:, 3232:4096].bitcast(F32)[0:1, 0:432]

    k.fence("sync", writes=[bkey(1, c) for c in range(8)] +
            [bkey(2, c) for c in range(8)] + [A0, A1, A2, A3] + B1K7 +
            ["SCR:0", "SCR:1", "SCR:2", "SCR:3"])
    k.dma("qs", WRC, ins["l3f_wrc"], writes=["wrc"], stream="bias")
    k.dma("qs", B1C, ins["l3f_b1c"], writes=["b1c7"], stream="bias")
    k.dma("qs", B2C, ins["l3f_b2c"], writes=["b2c7"], stream="bias")
    k.op("vector", lambda v: v.tensor_copy(WRCB, WRC), reads=["wrc", "b1c7",
                                                              "b2c7"],
         writes=["wrc"])
    for j in range(TJ):
        def mms(t, j=j):
            for dc in range(DC):
                last = t.matmul(PP[6][0:1, :], WRCB[:, dc:dc + 1],
                                bf.bigv(0, dc)[:, j * 512:(j + 1) * 512],
                                start=(dc == 0), stop=(dc == DC - 1))
            return last
        k.op("tensor", mms, reads=["wrc"] + BKALL(0), writes=["pp6"])
        k.op("vector", lambda v, j=j: v.tensor_copy(SR[:, j * 512:(j + 1) * 512],
                                                    PP[6][0:1, :]),
             reads=["pp6"], writes=["sr"])
    for b in range(BPC):
        srb = SR[:, b * T:(b + 1) * T]
        k.op("vector", lambda v, srb=srb: v.tensor_reduce(LO3, srb, AX.X, ALU.min),
             reads=["sr"], writes=["bis3"])
        k.op("vector", lambda v, srb=srb: v.tensor_reduce(HI3, srb, AX.X, ALU.max),
             reads=["sr"], writes=["bis3"])
        for it in range(45):
            k.op("vector", lambda v: v.tensor_add(MID3, LO3, HI3),
                 reads=["bis3"], writes=["bis3"])
            k.op("vector", lambda v: v.tensor_scalar_mul(MID3, MID3, 0.5),
                 reads=["bis3"], writes=["bis3"])
            k.op("vector", lambda v, srb=srb: v.tensor_scalar(
                WSR[:, 0:2048], srb, MID3, None, ALU.is_gt, op1=ALU.add,
                accum_out=CNT3),
                 reads=["sr", "bis3"], writes=["wsr", "bis3"])
            k.op("vector", lambda v: v.tensor_scalar(SEL3, CNT3, 1024.0, None,
                                                     ALU.is_ge),
                 reads=["bis3"], writes=["bis3"])
            k.op("vector", lambda v: v.tensor_sub(T13, MID3, LO3),
                 reads=["bis3"], writes=["bis3"])
            k.op("vector", lambda v: v.tensor_mul(T13, T13, SEL3),
                 reads=["bis3"], writes=["bis3"])
            k.op("vector", lambda v: v.tensor_add(LO3, LO3, T13),
                 reads=["bis3"], writes=["bis3"])
            k.op("vector", lambda v: v.tensor_scalar(NS3, SEL3, -1.0, 1.0,
                                                     ALU.mult, ALU.add),
                 reads=["bis3"], writes=["bis3"])
            k.op("vector", lambda v: v.tensor_sub(T13, MID3, HI3),
                 reads=["bis3"], writes=["bis3"])
            k.op("vector", lambda v: v.tensor_mul(T13, T13, NS3),
                 reads=["bis3"], writes=["bis3"])
            k.op("vector", lambda v: v.tensor_add(HI3, HI3, T13),
                 reads=["bis3"], writes=["bis3"])
        k.op("vector", lambda v, b=b: v.tensor_copy(
            BIS3[:, 8 + b:9 + b], LO3), reads=["bis3"], writes=["bis3"])
    k.op("scalar", lambda s: s.activation(WSR, SR, AF.Sigmoid),
         reads=["sr"], writes=["wsr"])
    for b in range(BPC):
        k.op("vector", lambda v, b=b: v.tensor_scalar(
            SR[:, b * T:(b + 1) * T], SR[:, b * T:(b + 1) * T],
            BIS3[0:1, 8 + b:9 + b], None, ALU.is_gt),
             reads=["sr", "bis3"], writes=["sr"])
    k.op("vector", lambda v: v.tensor_tensor(WSR, WSR, SR, ALU.mult),
         reads=["wsr", "sr"], writes=["wsr"])
    for c in range(DC):
        for j in range(TJ):
            stg = aux[:, (j % 2) * 1024:(j % 2) * 1024 + 1024].bitcast(F32)
            stk = [f"xstg{j % 2}", f"AUX:{j % 2}"]
            k.op("vector", lambda v, c=c, j=j, stg=stg: v.tensor_copy(
                stg, bf.bigv(0, c)[:, j * 512:(j + 1) * 512]),
                 reads=[bkey(0, c)], writes=stk)
            k.dma("qg", dst_dram[c * 128:(c + 1) * 128, j * 512:(j + 1) * 512],
                  stg, reads=stk + [dst_key(c, j)], writes=[dst_key(c, j)],
                  accum_op=ALU.add, stream=f"acc{j % 2}")
    k.dma("qg", b2_[:, :].rearrange("p (dc i) -> p dc i", dc=8),
          ins["l3f_w1"].rearrange("(dc p) i -> p dc i", p=128),
          writes=[bkey(1, c) for c in range(8)], stream="w1")
    k.fence("sync", writes=["h7", "xstg0", "xstg1", "u7", "s17", "stg7",
                             "AUX:0", "AUX:1", "AUX:2", "AUX:3"])
    GC1, GC2 = 0.7978845608028654, 0.044715
    U7 = aux[:, 0:1024].bitcast(F32)
    S17 = aux[:, 1024:2048].bitcast(F32)
    for tb in range(TJ):
        for ic in range(32):
            ph, phk = PP[4 + ic % 2], f"pp{4 + ic % 2}"
            def mmh(t, ic=ic, tb=tb, ph=ph):
                for dc in range(DC):
                    last = t.matmul(ph[:], b2_[:, dc * 4096 + ic * 128:
                                               dc * 4096 + (ic + 1) * 128],
                                    bf.bigv(0, dc)[:, tb * 512:(tb + 1) * 512],
                                    start=(dc == 0), stop=(dc == DC - 1))
                return last
            k.op("tensor", mmh, reads=[bkey(1, c) for c in range(8)] + BKALL(0),
                 writes=[phk])
            k.op("scalar", lambda s, ic=ic, ph=ph: s.activation(
                U7, ph[:], AF.Identity, bias=B1C[:, ic:ic + 1]),
                 reads=[phk, "b1c7"], writes=["u7", "AUX:0", "AUX:1"])
            k.op("scalar", lambda s: s.activation(S17, U7, AF.Square),
                 reads=["u7"], writes=["s17", "AUX:2", "AUX:3"])
            k.op("vector", lambda v: v.tensor_mul(S17, S17, U7),
                 reads=["s17", "u7"], writes=["s17"])
            k.op("vector", lambda v: v.scalar_tensor_tensor(
                S17, S17, GC2, U7, ALU.mult, ALU.add),
                 reads=["s17", "u7"], writes=["s17"])
            k.op("scalar", lambda s: s.activation(S17, S17, AF.Tanh, scale=GC1),
                 reads=["s17"], writes=["s17"])
            k.op("vector", lambda v: v.tensor_scalar(S17, S17, 0.5, 0.5,
                                                     ALU.mult, ALU.add),
                 reads=["s17"], writes=["s17"])
            k.op("vector", lambda v, ic=ic: v.tensor_tensor(
                b3_[:, ic * 512:(ic + 1) * 512], S17, U7, ALU.mult),
                 reads=["s17", "u7"], writes=["h7"])
        k.op("tensor", lambda t, tb=tb: t.matmul(
            PP[6][:], C.ones_row[:, :],
            WSR[:, tb * 512:(tb + 1) * 512], start=True, stop=True),
             reads=["wsr", "c_onesr"], writes=["pp6"])
        for dc in range(DC):
            ppi = dc % 2
            for ih in range(2):
                slab = scr[:, :]
                slks = ["SCR:0", "SCR:1", "SCR:2", "SCR:3"]
                k.dma("qg", slab.rearrange("p (a m) -> p a m", a=16),
                      ins["l3f_w2"][ih * 2048:(ih + 1) * 2048,
                                    dc * 128:(dc + 1) * 128]
                      .rearrange("(a p) m -> p a m", p=128),
                      writes=slks, stream="w0")
                def mmy(t, dc=dc, ih=ih, ppi=ppi):
                    for a in range(16):
                        ic = ih * 16 + a
                        last = t.matmul(PP[ppi][:], scr[:, a * 128:(a + 1) * 128],
                                        b3_[:, ic * 512:(ic + 1) * 512],
                                        start=(ic == 0), stop=(ic == 31))
                    return last
                k.op("tensor", mmy, reads=slks + ["h7"], writes=[f"pp{ppi}"])
            stg = aux[:, 2048:3072].bitcast(F32)
            stk = "stg7"
            k.op("scalar", lambda s, stg=stg, dc=dc, ppi=ppi: s.activation(
                stg, PP[ppi][:], AF.Identity, bias=B2C[:, dc:dc + 1]),
                 reads=[f"pp{ppi}", "b2c7"], writes=[stk])
            k.op("vector", lambda v, stg=stg: v.tensor_tensor(stg, stg, PP[6][:],
                                                              ALU.mult),
                 reads=[stk, "pp6"], writes=[stk])
            k.dma("qg", dst_dram[dc * 128:(dc + 1) * 128, tb * 512:(tb + 1) * 512],
                  stg, reads=[stk, dst_key(dc, tb)], writes=[dst_key(dc, tb)],
                  accum_op=ALU.add, stream="acc0")


_WA = {
    "l0a_wqT": [1024, 1024], "l0a_wkT": [1024, 1024], "l0a_wvT": [1024, 1024],
    "l0a_woT": [1024, 1024], "l0a_bo": [128, 8],
    "l0f_wgredT": [1024, 16], "l0f_wg": [8, 16], "l0f_w1": [8, 1024, 4096],
    "l0f_b1": [8, 4096], "l0f_w2": [8, 4096, 1024], "l0f_b2": [8, 1024],
    "l1a_wq": [1024, 1024], "l1a_wk": [1024, 1024], "l1a_wv": [1024, 1024],
    "l1a_bq": [128, 8], "l1a_bk": [128, 8], "l1a_bv": [1, 1024],
    "l1f_gwT": [1024, 8], "l1f_wgT": [8, 1024, 4096],
    "l1f_wuT": [8, 1024, 4096], "l1f_wdT": [8, 4096, 1024],
    "l2a_inprojT": [1024, 4096], "l2a_convw": [128, 64], "l2a_convb": [128, 16],
    "l2a_xprojT": [2048, 33], "l2a_alog": [128, 256], "l2a_dcol": [128, 16],
    "l2a_outprojT": [2048, 1024],
    "l2f_wqkv": [1024, 3072], "l2f_qg": [128, 8],
    "cst_repc": [2048, 128], "cst_summ": [2048, 128], "cst_repb": [16, 128],
}
_WB = {
    "l2f_wqkv": [1024, 3072], "l2f_qg": [128, 8], "l2f_wproj": [1024, 1024],
    "l2f_bproj": [128, 8],
    "l3a_inprojT": [1024, 4096], "l3a_convw": [128, 64], "l3a_convb": [128, 16],
    "l3a_xprojT": [2048, 96], "l3a_dtprojT": [64, 2048], "l3a_dtb": [128, 16],
    "l3a_alog": [128, 256], "l3a_dcol": [128, 16], "l3a_outprojT": [2048, 1024],
    "l3f_wrc": [128, 8], "l3f_b1c": [128, 32], "l3f_b2c": [128, 8],
    "l3f_w1": [1024, 4096], "l3f_w2": [4096, 1024],
    "cst_repc": [2048, 128], "cst_summ": [2048, 128], "cst_repb": [16, 128],
}


def _init_res(k, out, xin, dst_key):
    for c in range(DC):
        for j in range(TJ):
            k.dma("qs", out[c * 128:(c + 1) * 128, j * 512:(j + 1) * 512],
                  xin[c * 128:(c + 1) * 128, j * 512:(j + 1) * 512],
                  writes=[dst_key(c, j)], stream="init")


def _build_a():
    nc = bass.Bass("TRN2", target_bir_lowering=False)
    xin = nc.dram_tensor("xin", [1024, 4096], F32, kind="ExternalInput").ap()
    ins = {nm: nc.dram_tensor(nm, shp, F32, kind="ExternalInput").ap()
           for nm, shp in _WA.items()}
    out = nc.dram_tensor("out", [1024, 4096], F32, kind="ExternalOutput").ap()
    mm = nc.dram_tensor("mm", [1, 2], F32, kind="ExternalOutput").ap()
    k = Kfw(nc)
    bf = Bufs(k)
    C = Consts(k)
    dst_key = lambda c, j: f"xres/{c}/{j}"
    _init_res(k, out, xin, dst_key)
    xnd = k.dram("xnd", [1024, 4096], BF16)
    emit_ln(k, C, bf, out, dst_key, dst_i=0, tag="n0", hold_i=1)
    emit_l0_attn(k, C, bf, ins, out, dst_key, tag="a0")
    emit_ln(k, C, bf, out, dst_key, dst_i=0, tag="n1", hold_i=1, dram_out=xnd)
    emit_l0_ffn(k, C, bf, ins, xnd, out, dst_key, tag="f0")
    emit_ln(k, C, bf, out, dst_key, dst_i=0, tag="n2", hold_i=1)
    emit_l1_attn(k, C, bf, ins, None, out, dst_key, tag="a1")
    emit_ln(k, C, bf, out, dst_key, dst_i=0, tag="n3", hold_i=1)
    emit_l1_ffn(k, C, bf, ins, out, dst_key, tag="f1")
    emit_ln(k, C, bf, out, dst_key, dst_i=0, tag="n4", hold_i=1)
    emit_l2_attn(k, C, bf, ins, out, dst_key, tag="a2")
    emit_ln(k, C, bf, out, dst_key, dst_i=0, tag="n5", hold_i=1)
    emit_l2_ffn(k, C, bf, ins, None, dst_key, 1, tag="f2", sim_only=True,
                mm_out=mm)
    k.build()
    return nc


def _build_b():
    nc = bass.Bass("TRN2", target_bir_lowering=False)
    xin = nc.dram_tensor("xin", [1024, 4096], F32, kind="ExternalInput").ap()
    mmall = nc.dram_tensor("mmall", [1, 16], F32, kind="ExternalInput").ap()
    ins = {nm: nc.dram_tensor(nm, shp, F32, kind="ExternalInput").ap()
           for nm, shp in _WB.items()}
    out = nc.dram_tensor("out", [1024, 4096], F32, kind="ExternalOutput").ap()
    k = Kfw(nc)
    bf = Bufs(k)
    C = Consts(k)
    dst_key = lambda c, j: f"xres/{c}/{j}"
    _init_res(k, out, xin, dst_key)
    emit_ln(k, C, bf, out, dst_key, dst_i=0, tag="n5", hold_i=1)
    emit_l2_ffn(k, C, bf, ins, out, dst_key, 1, tag="f2", mm_in=mmall)
    emit_ln(k, C, bf, out, dst_key, dst_i=0, tag="n6", hold_i=1)
    emit_l3_attn(k, C, bf, ins, out, dst_key, tag="a3")
    emit_ln(k, C, bf, out, dst_key, dst_i=0, tag="n7", hold_i=1)
    emit_l3_ffn(k, C, bf, ins, out, dst_key, tag="f3")
    k.build()
    return nc


def _cw_stage(conv_w):
    cw = np.asarray(conv_w, np.float32)[:, 0, :]
    return np.ascontiguousarray(
        cw.reshape(16, 128, 4).transpose(1, 0, 2).reshape(128, 64))


def _alog_stage(a_log):
    return np.ascontiguousarray(
        np.asarray(a_log, np.float32).reshape(-1).reshape(256, 128).T)


def _stage_weights(I):
    cst = mk_scan_consts()
    g_info = np.asarray(I["L2_ffn_g_info"], np.float32)
    wA = {
        "l0a_wqT": _t(I["L0_attn_Wq"]), "l0a_wkT": _t(I["L0_attn_Wk"]),
        "l0a_wvT": _t(I["L0_attn_Wv"]), "l0a_woT": _t(I["L0_attn_Wo"]),
        "l0a_bo": _col128(I["L0_attn_bo"]),
        "l0f_wgredT": _t(I["L0_ffn_wg_red"]), "l0f_wg": _c(I["L0_ffn_wg"]),
        "l0f_w1": _c(I["L0_ffn_w1"]), "l0f_b1": _c(I["L0_ffn_b1"]),
        "l0f_w2": _c(I["L0_ffn_w2"]), "l0f_b2": _c(I["L0_ffn_b2"]),
        "l1a_wq": _c(I["L1_attn_Wq"]), "l1a_wk": _c(I["L1_attn_Wk"]),
        "l1a_wv": _c(I["L1_attn_Wv"]), "l1a_bq": _col128(I["L1_attn_bq"]),
        "l1a_bk": _col128(I["L1_attn_bk"]),
        "l1a_bv": np.ascontiguousarray(
            np.asarray(I["L1_attn_bv"], np.float32)[None, :]),
        "l1f_gwT": _t(I["L1_ffn_gate_w"]),
        "l1f_wgT": np.ascontiguousarray(
            np.asarray(I["L1_ffn_w_gate"], np.float32).transpose(0, 2, 1)),
        "l1f_wuT": np.ascontiguousarray(
            np.asarray(I["L1_ffn_w_up"], np.float32).transpose(0, 2, 1)),
        "l1f_wdT": np.ascontiguousarray(
            np.asarray(I["L1_ffn_w_down"], np.float32).transpose(0, 2, 1)),
        "l2a_inprojT": _t(I["L2_attn_in_proj_w"]),
        "l2a_convw": _cw_stage(I["L2_attn_conv_w"]),
        "l2a_convb": _col128(I["L2_attn_conv_b"]),
        "l2a_xprojT": _t(I["L2_attn_x_proj_w"]),
        "l2a_alog": _alog_stage(I["L2_attn_A_log"]),
        "l2a_dcol": _col128(I["L2_attn_D_param"]),
        "l2a_outprojT": _t(I["L2_attn_out_proj_w"]),
        "l2f_wqkv": _c(I["L2_ffn_W_qkv"]), "l2f_qg": _col128(g_info[0, 0]),
    }
    wA.update(cst)
    wB = {
        "l2f_wqkv": wA["l2f_wqkv"], "l2f_qg": wA["l2f_qg"],
        "l2f_wproj": _c(I["L2_ffn_W_proj"]),
        "l2f_bproj": _col128(I["L2_ffn_b_proj"]),
        "l3a_inprojT": _t(I["L3_attn_in_proj_w"]),
        "l3a_convw": _cw_stage(I["L3_attn_conv_w"]),
        "l3a_convb": _col128(I["L3_attn_conv_b"]),
        "l3a_xprojT": _t(I["L3_attn_x_proj_w"]),
        "l3a_dtprojT": _t(I["L3_attn_dt_proj_w"]),
        "l3a_dtb": _col128(I["L3_attn_dt_proj_b"]),
        "l3a_alog": _alog_stage(I["L3_attn_A_log"]),
        "l3a_dcol": _col128(I["L3_attn_D"]),
        "l3a_outprojT": _t(I["L3_attn_out_proj_w"]),
        "l3f_wrc": _col128(I["L3_ffn_w_router"]),
        "l3f_b1c": _col128(I["L3_ffn_b1"]),
        "l3f_b2c": _col128(I["L3_ffn_b2"]),
        "l3f_w1": _c(I["L3_ffn_W1"]), "l3f_w2": _c(I["L3_ffn_W2"]),
    }
    wB.update(cst)
    return wA, wB


def kernel(**inputs):
    x = np.asarray(inputs["L0_attn_x"], np.float32)
    wA, wB = _stage_weights(inputs)
    xfm = [np.ascontiguousarray(
        x[2 * c:2 * c + 2].transpose(2, 0, 1).reshape(1024, 4096))
        for c in range(NCORES)]
    nc_a = _build_a()
    res_a = run_bass_kernel_spmd(
        nc_a, [dict(xin=xfm[c], **wA) for c in range(NCORES)],
        core_ids=list(range(NCORES))).results
    mmall = np.zeros((1, 16), np.float32)
    for c in range(NCORES):
        mm = np.asarray(res_a[c]["mm"], np.float32)
        mmall[0, c] = mm[0, 0]
        mmall[0, 8 + c] = mm[0, 1]
    nc_b = _build_b()
    res_b = run_bass_kernel_spmd(
        nc_b, [dict(xin=np.asarray(res_a[c]["out"], np.float32), mmall=mmall,
                    **wB) for c in range(NCORES)],
        core_ids=list(range(NCORES))).results
    out = np.empty((16, 2048, 1024), np.float32)
    for c in range(NCORES):
        out[2 * c:2 * c + 2] = np.asarray(res_b[c]["out"], np.float32) \
            .reshape(1024, 2, 2048).transpose(1, 2, 0)
    return out


# version 04 (final)
# speedup vs baseline: 1.8575x; 1.2770x over previous
import contextlib
from collections import defaultdict

import numpy as np
import concourse.bass as bass
import concourse.mybir as mybir
from concourse.bass_utils import run_bass_kernel_spmd

F32 = mybir.dt.float32
BF16 = mybir.dt.bfloat16
I16 = mybir.dt.int16
AF = mybir.ActivationFunctionType
ALU = mybir.AluOpType
AX = mybir.AxisListType

NCORES = 8
BPC = 2
T = 2048
NT = BPC * T
D = 1024
DC = D // 128
TJ = NT // 512

COMPUTE_ENGINES = ("tensor", "vector", "scalar", "gpsimd", "sync")


class Kfw:
    def __init__(self, nc):
        self.nc = nc
        self.es = contextlib.ExitStack()
        self.sems = {}
        for e in COMPUTE_ENGINES:
            self.sems[e] = self.es.enter_context(nc.semaphore(f"S_{e}"))
        self.dma_queues = {"qs": "sync", "qa": "scalar", "qg": "gpsimd"}
        self.cnt = defaultdict(int)
        self.waited = defaultdict(dict)
        self.regions = {}
        self.cmds = defaultdict(list)
        self.stream_consumers = defaultdict(list)

    def sb(self, name, shape, dtype):
        return self.es.enter_context(self.nc.sbuf_tensor(name, list(shape), dtype))

    def psum(self, name, shape, dtype=F32):
        return self.es.enter_context(self.nc.psum_tensor(name, list(shape), dtype))

    def dram(self, name, shape, dtype, kind="Internal"):
        return self.nc.dram_tensor(name, list(shape), dtype, kind=kind).ap()

    def _reg(self, key):
        r = self.regions.get(key)
        if r is None:
            r = {"w": {}, "r": {}}
            self.regions[key] = r
        return r

    def _dma_sem(self, queue, stream):
        name = f"{queue}:{stream}"
        if name not in self.sems:
            self.sems[name] = self.es.enter_context(
                self.nc.semaphore(f"S_{queue}_{stream}"))
        return name

    def _plan_waits(self, reads, writes):
        need = {}
        for key in reads:
            for s, v in self._reg(key)["w"].items():
                need[s] = max(need.get(s, 0), v)
        for key in writes:
            r = self._reg(key)
            for s, v in r["w"].items():
                need[s] = max(need.get(s, 0), v)
            for s, v in r["r"].items():
                need[s] = max(need.get(s, 0), v)
        for s in list(need):
            if ":" in s:
                need[s] = self.cnt[s]
        return sorted(need.items())

    def _mark(self, sem_name, val, reads, writes):
        for key in reads:
            r = self._reg(key)
            r["r"][sem_name] = max(r["r"].get(sem_name, 0), val)
        for key in writes:
            r = self._reg(key)
            r["w"] = {sem_name: val}
            r["r"] = {}

    def _note_stream_waits(self, need, my_sem, my_val):
        for s, _v in need:
            if ":" in s:
                self.stream_consumers[s].append((my_sem, my_val))

    def _forced_boundary(self, sname, need):
        pend = self.stream_consumers.pop(sname, None)
        if not pend:
            return []
        out = {}
        for cs, cv in pend:
            if any(s == cs and v >= cv for s, v in need):
                continue
            out[cs] = max(out.get(cs, 0), cv)
        return sorted(out.items())

    def _emit_wait(self, eng, engine_obj, sem_name, val):
        if val <= 0:
            return
        if self.waited[eng].get(sem_name, 0) >= val:
            return
        engine_obj.wait_ge(self.sems[sem_name], val)
        self.waited[eng][sem_name] = val

    def op(self, eng, fn, reads=(), writes=()):
        need = self._plan_waits(reads, writes)
        self.cnt[eng] += 1
        val = self.cnt[eng]
        self._note_stream_waits(need, eng, val)
        self._mark(eng, val, reads, writes)

        def closure(engine_obj, need=need, fn=fn, sem=self.sems[eng], eng=eng):
            for s_name, v in need:
                self._emit_wait(eng, engine_obj, s_name, v)
            last = fn(engine_obj)
            assert last is not None, f"op on {eng} returned None"
            last.then_inc(sem, 1)

        self.cmds[eng].append(closure)
        return val

    def dma(self, queue, out, in_, reads=(), writes=(), stream="d", **kw):
        eng = self.dma_queues[queue]
        sname = self._dma_sem(queue, stream)
        need = self._plan_waits(reads, writes)
        need = self._forced_boundary(sname, need) + need
        self.cnt[sname] += 16
        val = self.cnt[sname]
        self._note_stream_waits(need, sname, val)
        self._mark(sname, val, reads, writes)

        def closure(engine_obj, need=need, out=out, in_=in_, sem=self.sems[sname],
                    kw=kw, eng=eng):
            for s_name, v in need:
                self._emit_wait(eng, engine_obj, s_name, v)
            engine_obj.dma_start(out=out, in_=in_, **kw).then_inc(sem, 16)

        self.cmds[eng].append(closure)
        return val

    def qop(self, eng, fn, reads=(), writes=(), queue="qcc", stream="cc"):
        sname = self._dma_sem(queue, stream)
        need = self._plan_waits(reads, writes)
        need = self._forced_boundary(sname, need) + need
        self.cnt[sname] += 16
        val = self.cnt[sname]
        self._note_stream_waits(need, sname, val)
        self._mark(sname, val, reads, writes)

        def closure(engine_obj, need=need, fn=fn, sem=self.sems[sname], eng=eng):
            for s_name, v in need:
                self._emit_wait(eng, engine_obj, s_name, v)
            fn(engine_obj).then_inc(sem, 16)

        self.cmds[eng].append(closure)
        return val

    def fence(self, eng, reads=(), writes=()):
        return self.op(eng, lambda e: e.nop(), reads=reads, writes=writes)

    def wait_all(self, eng):
        snap = [(s, self.cnt[s]) for s in list(self.cnt) if self.cnt[s] > 0]

        def closure(engine_obj, snap=snap, eng=eng):
            for s_name, v in snap:
                self._emit_wait(eng, engine_obj, s_name, v)

        self.cmds[eng].append(closure)

    def build(self):
        self.wait_all("sync")
        with self.nc.Block() as blk:
            for eng in COMPUTE_ENGINES:
                cmds = self.cmds[eng]
                if not cmds:
                    continue

                def runner(engine_obj, cmds=cmds):
                    for f in cmds:
                        f(engine_obj)

                getattr(blk, eng)(runner)
        self.es.close()


class Bufs:
    def __init__(self, k: Kfw):
        self.big = [k.sb(f"BIG{i}", [128, 32768], BF16) for i in (1, 2, 3)]
        self.aux = k.sb("AUX", [128, 4096], BF16)
        self.misc = self.aux[:, 1024:4096].bitcast(F32)
        self.scr = k.sb("SCR", [128, 2048], BF16)
        self.pp = [k.psum(f"PP{i}", [128, 512], F32) for i in range(8)]

    def bigv(self, i, c):
        return self.big[i][:, c * 4096:(c + 1) * 4096]

    def bigf32(self, i, c):
        return self.big[i][:, :].bitcast(F32)[:, c * 2048:(c + 1) * 2048]


def bkey(i, c):
    return f"B{i}:{c}"


def emit_tri_le(k, C, dst_ap, wkeys):
    k.op("vector", lambda v: v.tensor_scalar(dst_ap, C.iota16[:, 0:128],
                                             C.chan[:], None, ALU.is_ge),
         reads=["c_iota", "c_chan"], writes=list(wkeys))


BKALL = lambda i: [bkey(i, c) for c in range(8)]


class Consts:
    def __init__(self, k: Kfw):
        self.ones_col = k.sb("c_ones_col", [128, 1], F32)
        self.ones_row = k.sb("c_ones_row", [1, 128], F32)
        self.iota16 = k.sb("c_iota16", [128, 512], I16)
        self.chan = k.sb("c_chan", [128, 1], F32)
        self.ident = k.sb("c_ident", [128, 128], BF16)
        self.sel8 = k.sb("c_sel8", [40, 8 * 128], BF16)
        self.ones_bf = k.sb("c_ones_bf", [128, 1], BF16)
        k.op("gpsimd", lambda g: g.iota(self.iota16[:], [[1, 512]], channel_multiplier=0,
                                        allow_small_or_imprecise_dtypes=True),
             writes=["c_iota"])
        k.op("gpsimd", lambda g: g.iota(self.chan[:], [[0, 1]], channel_multiplier=1,
                                        allow_small_or_imprecise_dtypes=True),
             writes=["c_chan"])
        k.op("vector", lambda v: v.memset(self.ones_col[:], 1.0), writes=["c_onesc"])
        k.op("vector", lambda v: v.memset(self.ones_bf[:], 1.0), writes=["c_onesb"])
        k.op("vector", lambda v: v.memset(self.ones_row[:], 1.0), writes=["c_onesr"])
        k.op("vector", lambda v: v.tensor_scalar(self.ident[:], self.iota16[:, 0:128],
                                                 self.chan[:], None, ALU.is_equal),
             reads=["c_iota", "c_chan"], writes=["c_ident"])

        def mk_sel8(v):
            for e in range(8):
                v.tensor_scalar(self.sel8[0:8, e * 128:(e + 1) * 128],
                                self.chan[0:8, 0:1].broadcast_to((8, 128)),
                                float(e), None, ALU.is_equal)
                last = v.tensor_scalar(self.sel8[32:40, e * 128:(e + 1) * 128],
                                       self.chan[32:40, 0:1].broadcast_to((8, 128)),
                                       float(32 + e), None, ALU.is_equal)
            return last

        k.op("vector", mk_sel8, reads=["c_chan"], writes=["c_sel8"])


def emit_ln(k: Kfw, C: Consts, bf: Bufs, src_dram, src_keys, dst_i, tag,
            hold_i=1, dram_out=None):
    misc = bf.misc
    m_row = misc[0:1, 0:512]
    v_row = misc[0:1, 512:1024]
    b_row = misc[0:1, 1024:1536]
    PSTAT, PSTATQ, PB0, PB1 = bf.pp[5], bf.pp[4], bf.pp[6], bf.pp[7]
    holdf32 = bf.big[hold_i][:, :].bitcast(F32)
    scrf = bf.scr[:, :].bitcast(F32)

    hk = [f"lnh_{c}" for c in range(DC)]
    k.fence("sync", reads=[], writes=[bkey(hold_i, cb) for cb in range(8)] + hk +
            ["pp4", "pp5", "pp6", "pp7", "pp5s", "pp4q", "lnsq", "lnsq2",
             "ln_m", "ln_v", "ln_b"])
    for j in range(TJ):
        for c in range(DC):
            tile = holdf32[:, c * 512:(c + 1) * 512]
            k.dma("qs", tile, src_dram[c * 128:(c + 1) * 128, j * 512:(j + 1) * 512],
                  reads=[src_keys(c, j)], writes=[hk[c]], stream=f"ln{c}")
            k.op("scalar", lambda s, tile=tile: s.activation(scrf[:, 0:512], tile, AF.Square),
                 reads=[hk[c]], writes=["lnsq", "SCR:0", "SCR:1"])
            k.op("tensor", lambda t, tile=tile, c=c: t.matmul(
                PSTAT[0:1, :], C.ones_col[:], tile, start=(c == 0), stop=(c == DC - 1)),
                 reads=[hk[c], "c_onesc"], writes=["pp5s"])
            k.op("tensor", lambda t, c=c: t.matmul(
                PSTATQ[0:1, :], C.ones_col[:], scrf[:, 0:512], start=(c == 0),
                stop=(c == DC - 1)),
                 reads=["lnsq", "SCR:0", "SCR:1", "c_onesc"], writes=["pp4q"])
        k.op("vector", lambda v: v.tensor_scalar(m_row, PSTAT[0:1, :], 1.0 / D, None,
                                                 ALU.mult),
             reads=["pp5s"], writes=["ln_m", "AUX:1"])
        k.op("vector", lambda v: v.tensor_scalar(v_row, PSTATQ[0:1, :], 1.0 / D, None,
                                                 ALU.mult),
             reads=["pp4q"], writes=["ln_v", "AUX:2"])
        k.op("vector", lambda v: v.scalar_tensor_tensor(b_row, m_row, -1.0, m_row,
                                                        ALU.mult, ALU.mult),
             reads=["ln_m", "AUX:1"], writes=["ln_b", "AUX:3"])
        k.op("vector", lambda v: v.tensor_add(v_row, v_row, b_row),
             reads=["ln_v", "ln_b", "AUX:3"], writes=["ln_v", "AUX:2"])
        k.op("vector", lambda v: v.tensor_scalar(v_row, v_row, 0.0, 1e-5,
                                                 ALU.max, ALU.add),
             reads=["ln_v"], writes=["ln_v", "AUX:2"])
        k.op("scalar", lambda s: s.activation(v_row, v_row, AF.Ln),
             reads=["ln_v"], writes=["ln_v", "AUX:2"])
        k.op("scalar", lambda s: s.activation(v_row, v_row, AF.Exp, scale=-0.5),
             reads=["ln_v"], writes=["ln_v", "AUX:2"])
        k.op("vector", lambda v: v.scalar_tensor_tensor(b_row, m_row, -1.0, v_row,
                                                        ALU.mult, ALU.mult),
             reads=["ln_m", "ln_v", "AUX:1", "AUX:2"], writes=["ln_b", "AUX:3"])
        k.op("tensor", lambda t: t.matmul(PB0[:], C.ones_row[:], v_row,
                                          start=True, stop=True),
             reads=["ln_v", "AUX:2", "c_onesr"], writes=["pp6"])
        k.op("tensor", lambda t: t.matmul(PB1[:], C.ones_row[:], b_row,
                                          start=True, stop=True),
             reads=["ln_b", "AUX:3", "c_onesr"], writes=["pp7"])
        for c in range(DC):
            tile = holdf32[:, c * 512:(c + 1) * 512]
            k.op("vector", lambda v, tile=tile: v.tensor_tensor(
                scrf[:, 512:1024], tile, PB0[:], ALU.mult),
                 reads=[hk[c], "pp6"], writes=["lnsq2", "SCR:2", "SCR:3"])
            k.op("vector", lambda v, c=c, j=j: v.tensor_tensor(
                bf.bigv(dst_i, c)[:, j * 512:(j + 1) * 512], scrf[:, 512:1024],
                PB1[:], ALU.add),
                 reads=["lnsq2", "SCR:2", "SCR:3", "pp7"], writes=[bkey(dst_i, c)])
            if dram_out is not None:
                k.dma("qs", dram_out[c * 128:(c + 1) * 128, j * 512:(j + 1) * 512],
                      bf.bigv(dst_i, c)[:, j * 512:(j + 1) * 512],
                      reads=[bkey(dst_i, c)], writes=["xnd"], stream=f"xno{c % 2}")
    k.fence("sync", reads=hk, writes=[bkey(hold_i, cb) for cb in range(8)])


def _t(a):
    return np.ascontiguousarray(np.asarray(a, dtype=np.float32).T)


def _c(a):
    return np.ascontiguousarray(np.asarray(a, dtype=np.float32))


def _col128(v):
    v = np.asarray(v, dtype=np.float32)
    return np.ascontiguousarray(v.reshape(-1, 128).T)


A0, A1, A2, A3 = "AUX:0", "AUX:1", "AUX:2", "AUX:3"
S0, S1, S2, S3 = "SCR:0", "SCR:1", "SCR:2", "SCR:3"


def emit_l0_attn(k: Kfw, C: Consts, bf: Bufs, ins, dst_dram, dst_key, tag="a0"):
    aux = bf.aux
    scr = bf.scr
    scrf = scr[:, :].bitcast(F32)
    vtd = k.dram(f"{tag}_vtd", [NT, D], BF16)
    ktd = k.dram(f"{tag}_ktd", [NT, D], BF16)
    PP = bf.pp
    H, DH, NCH = 16, 64, T // 128

    for which, wd, outd, do_phi in (("v", ins["l0a_wvT"], vtd, False),
                                    ("kt", ins["l0a_wkT"], ktd, True)):
        for nb in range(2):
            k.dma("qg", aux[:, :].rearrange("p (kc m) -> p kc m", kc=8),
                  wd[:, nb * 512:(nb + 1) * 512].rearrange("(kc p) m -> p kc m", p=128),
                  writes=[A0, A1, A2, A3], stream="wtok")
            for tc in range(NT // 128):
                ppi = tc % 2
                def mm(t, tc=tc, ppi=ppi):
                    for kc in range(DC):
                        last = t.matmul(PP[ppi][:], bf.bigv(0, kc)[:, tc * 128:(tc + 1) * 128],
                                        aux[:, kc * 512:(kc + 1) * 512],
                                        start=(kc == 0), stop=(kc == DC - 1))
                    return last
                k.op("tensor", mm, reads=[A0, A1, A2, A3] + BKALL(0), writes=[f"pp{ppi}"])
                st_bf = scr[:, ppi * 512:ppi * 512 + 512]
                stk = f"SCR:{ppi}"
                if do_phi:
                    est = scr[:, 1024 + ppi * 512:1024 + (ppi + 1) * 512]
                    estk = f"SCR:{2 + ppi}"
                    k.op("vector", lambda v, ppi=ppi: v.tensor_scalar_min(
                        PP[4][:], PP[ppi][:], 0.0), reads=[f"pp{ppi}"], writes=["pp4"])
                    k.op("scalar", lambda s, est=est: s.activation(est, PP[4][:], AF.Exp),
                         reads=["pp4"], writes=[estk])
                    k.op("vector", lambda v, st_bf=st_bf, ppi=ppi, est=est:
                         v.scalar_tensor_tensor(
                         st_bf, PP[ppi][:], 0.0, est, ALU.max, ALU.add),
                         reads=[f"pp{ppi}", estk], writes=[stk])
                else:
                    k.op("vector", lambda v, st_bf=st_bf, ppi=ppi: v.tensor_copy(
                        st_bf, PP[ppi][:]), reads=[f"pp{ppi}"], writes=[stk])
                k.dma("qs", outd[tc * 128:(tc + 1) * 128, nb * 512:(nb + 1) * 512],
                      st_bf, reads=[stk], writes=[f"{tag}{which}d"],
                      stream=f"st{ppi}")

    for dst_i, wd in ((1, ins["l0a_wqT"]), (2, ins["l0a_wkT"])):
        for oc in range(DC):
            sl = oc % 2
            slab = scr[:, sl * 1024:(sl + 1) * 1024]
            slks = [f"SCR:{2 * sl}", f"SCR:{2 * sl + 1}"]
            k.dma("qg", slab.rearrange("p (kc m) -> p kc m", kc=8),
                  wd[:, oc * 128:(oc + 1) * 128].rearrange("(kc p) m -> p kc m", p=128),
                  writes=slks, stream=f"w{sl}")
            for j in range(TJ):
                ppi = j % 2
                def mm(t, slab=slab, j=j, ppi=ppi):
                    for kc in range(DC):
                        last = t.matmul(PP[ppi][:], slab[:, kc * 128:(kc + 1) * 128],
                                        bf.bigv(0, kc)[:, j * 512:(j + 1) * 512],
                                        start=(kc == 0), stop=(kc == DC - 1))
                    return last
                k.op("tensor", mm, reads=slks + BKALL(0), writes=[f"pp{ppi}"])
                est = aux[:, ppi * 1024:ppi * 1024 + 512]
                estk = ["AUX:0", "AUX:1"][ppi]
                k.op("vector", lambda v, ppi=ppi: v.tensor_scalar_min(
                    PP[4][:], PP[ppi][:], 0.0), reads=[f"pp{ppi}"], writes=["pp4"])
                k.op("scalar", lambda s, est=est: s.activation(est, PP[4][:], AF.Exp),
                     reads=["pp4"], writes=[estk])
                wkeys = [f"pq{dst_i}/{oc}/{hf}/{j}" for hf in range(2)]
                k.op("vector", lambda v, dst_i=dst_i, oc=oc, j=j, ppi=ppi, est=est:
                     v.scalar_tensor_tensor(
                         bf.bigv(dst_i, oc)[:, j * 512:(j + 1) * 512],
                         PP[ppi][:], 0.0, est, ALU.max, ALU.add),
                     reads=[f"pp{ppi}", estk, bkey(dst_i, oc)], writes=wkeys)

    emit_tri_le(k, C, scr[:, 1280:1408], [S2])
    misc2 = aux[:, 2048:4096].bitcast(F32)
    den_row = misc2[0:1, 320:448]
    recip_row = misc2[0:1, 192:320]
    bo_sb = misc2[:, 448:456]
    at_sb = scr[:, 512:640]

    for b in range(BPC):
        for h in range(H):
            hp, half = h // 2, h % 2
            rb = half * 64
            pqv = bf.bigv(1, hp)
            pkv = bf.bigv(2, hp)
            S_f = misc2[rb:rb + 64, 0:65]
            S_bf = aux[rb:rb + 64, 2304:2369]
            rb_sb = scr[rb:rb + 64, 768:896]
            k.dma("qs", aux[:, 0:1024].rearrange("p (i f) -> p i f", i=16),
                  vtd[b * T:(b + 1) * T, h * DH:(h + 1) * DH]
                  .rearrange("(i p) f -> p i f", p=128),
                  reads=[f"{tag}vd"], writes=[A0], stream="vt")
            k.dma("qs", aux[:, 1024:2048].rearrange("p (i f) -> p i f", i=16),
                  ktd[b * T:(b + 1) * T, h * DH:(h + 1) * DH]
                  .rearrange("(i p) f -> p i f", p=128),
                  reads=[f"{tag}ktd"], writes=[A1], stream="kt")
            k.op("vector", lambda v, S_f=S_f: v.memset(S_f, 0.0), writes=[A2])
            k.op("vector", lambda v, S_bf=S_bf: v.memset(S_bf, 0.0), writes=[A2])
            for i in range(NCH):
                s0 = b * T + i * 128
                pa, pak = PP[i % 2], f"pp{i % 2}"
                pn, pnk = PP[2 + i % 2], f"pp{2 + i % 2}"
                qk = f"pq1/{hp}/{half}/{b * 4 + i // 4}"
                kk = f"pq2/{hp}/{half}/{b * 4 + i // 4}"
                k.op("tensor", lambda t, pa=pa, s0=s0, pqv=pqv, pkv=pkv, rb=rb:
                     t.matmul(pa[:, 0:128], pkv[rb:rb + 64, s0:s0 + 128],
                              pqv[rb:rb + 64, s0:s0 + 128], start=True, stop=True),
                     reads=[qk, kk], writes=[pak])
                k.op("vector", lambda v, pa=pa: v.tensor_tensor(
                    at_sb, pa[:, 0:128], scr[:, 1280:1408], ALU.mult),
                     reads=[pak, S2], writes=[S1])
                def num_mm(t, pn=pn, i=i, s0=s0, pqv=pqv, rb=rb, S_bf=S_bf):
                    t.matmul(pn[rb:rb + 64, 0:128], aux[:, i * 64:(i + 1) * 64], at_sb,
                             start=True, stop=False, tile_position=(0, rb))
                    t.matmul(pn[rb:rb + 64, 0:128], S_bf[:, 0:64],
                             pqv[rb:rb + 64, s0:s0 + 128], start=False, stop=True,
                             tile_position=(rb, rb))
                    t.matmul(PP[6][0:1, 0:128], C.ones_bf[:, 0:1], at_sb,
                             start=True, stop=False)
                    return t.matmul(PP[6][0:1, 0:128], S_bf[:, 64:65],
                                    pqv[rb:rb + 64, s0:s0 + 128], start=False, stop=True)
                k.op("tensor", num_mm, reads=[S1, A0, A2, qk, "c_onesb"],
                     writes=[pnk, "pp6"])
                def s_mm(t, i=i, rb=rb):
                    t.matmul(PP[5][rb:rb + 64, 0:64],
                             aux[:, 1024 + i * 64:1024 + (i + 1) * 64],
                             aux[:, i * 64:(i + 1) * 64], start=True, stop=True,
                             tile_position=(0, rb))
                    return t.matmul(PP[5][rb:rb + 64, 64:65],
                                    aux[:, 1024 + i * 64:1024 + (i + 1) * 64],
                                    C.ones_bf[:, 0:1], start=True, stop=True,
                                    tile_position=(0, rb))
                k.op("tensor", s_mm, reads=[A0, A1, "c_onesb"], writes=["pp5"])
                k.op("vector", lambda v: v.tensor_scalar_add(
                    den_row, PP[6][0:1, 0:128], 1e-5), reads=["pp6"], writes=[A2])
                k.op("vector", lambda v: v.reciprocal(recip_row, den_row),
                     reads=[A2], writes=[A2])
                k.op("tensor", lambda t, rb=rb: t.matmul(
                    PP[4][rb:rb + 64, 0:128], C.ones_row[:, 0:64], recip_row,
                    start=True, stop=True, tile_position=(0, rb)),
                     reads=[A2, "c_onesr"], writes=["pp4"])
                k.op("vector", lambda v, rb_sb=rb_sb, rb=rb: v.tensor_copy(
                    rb_sb, PP[4][rb:rb + 64, 0:128]), reads=["pp4"], writes=[S1])
                k.op("vector", lambda v, pn=pn, s0=s0, rb=rb, pqv=pqv, rb_sb=rb_sb:
                     v.tensor_tensor(pqv[rb:rb + 64, s0:s0 + 128],
                                     pn[rb:rb + 64, 0:128], rb_sb, ALU.mult),
                     reads=[pnk, S1, qk], writes=[qk])
                k.op("vector", lambda v, S_f=S_f, rb=rb: v.tensor_tensor(
                    S_f, S_f, PP[5][rb:rb + 64, 0:65], ALU.add),
                     reads=["pp5", A2], writes=[A2])
                k.op("vector", lambda v, S_f=S_f, S_bf=S_bf: v.tensor_copy(S_bf, S_f),
                     reads=[A2], writes=[A2])
    k.fence("sync", reads=[f"pq1/{hp}/{hf}/{x}" for hp in range(8) for hf in range(2)
                           for x in range(8)],
            writes=[bkey(1, c) for c in range(8)])

    k.dma("qs", bo_sb, ins["l0a_bo"], writes=["bias0", A2], stream="bias")
    for oc in range(DC):
        sl = oc % 2
        slab = aux[:, sl * 1024:(sl + 1) * 1024]
        slk = [A0, A1][sl]
        k.dma("qg", slab.rearrange("p (kc m) -> p kc m", kc=8),
              ins["l0a_woT"][:, oc * 128:(oc + 1) * 128]
              .rearrange("(kc p) m -> p kc m", p=128),
              writes=[slk], stream=f"w{sl}")
        for j in range(TJ):
            ppi = j % 2
            def mm(t, slab=slab, j=j, ppi=ppi):
                for kc in range(DC):
                    last = t.matmul(PP[ppi][:], slab[:, kc * 128:(kc + 1) * 128],
                                    bf.bigv(1, kc)[:, j * 512:(j + 1) * 512],
                                    start=(kc == 0), stop=(kc == DC - 1))
                return last
            k.op("tensor", mm, reads=[slk] + [bkey(1, c) for c in range(8)],
                 writes=[f"pp{ppi}"])
            stage = scrf[:, ppi * 512:(ppi + 1) * 512]
            stks = [f"SCR:{2 * ppi}", f"SCR:{2 * ppi + 1}"]
            k.op("scalar", lambda s, stage=stage, oc=oc, ppi=ppi: s.activation(
                stage, PP[ppi][:], AF.Identity, bias=bo_sb[:, oc:oc + 1]),
                 reads=[f"pp{ppi}", "bias0"], writes=stks)
            k.dma("qg", dst_dram[oc * 128:(oc + 1) * 128, j * 512:(j + 1) * 512],
                  stage, reads=stks + [dst_key(oc, j)], writes=[dst_key(oc, j)],
                  accum_op=ALU.add, stream=f"acc{ppi}")


def emit_l0_ffn(k: Kfw, C: Consts, bf: Bufs, ins, xnd, dst_dram, dst_key, tag="f0"):
    b1_ = bf.big[0]
    b1f = b1_[:, :].bitcast(F32)
    aux, scr = bf.aux, bf.scr
    PP = bf.pp
    STG = [b1f[:, 8192 + s * 512:8192 + (s + 1) * 512] for s in range(2)]
    WM = b1_[0:8, 18432:22528]
    WB = [b1_[:, 22528 + s * 512:22528 + (s + 1) * 512] for s in range(2)]
    WGU = b1_[0:8, 23552:23568]
    WGUT = b1_[0:16, 23568:23576]
    WRT = b1_[:, 23680:23808]
    WMT = b1_[:, 23808:23816]
    NM = b1f[0:128, 12160:12161]
    ET = b1f[:, 12164:12172]
    RS = b1f[:, 12172:12173]
    SC = b1f[:, 12173:12174]
    WG = b1f[0:8, 12176:12192]
    WGS = b1f[0:8, 12192:12208]
    SS = b1f[0:8, 12208:12209]
    B1ROW = b1_[0:1, 24448:28544]
    B2ROW = b1_[0:1, 28544:29568]
    ones512 = b1_[0:1, 29568:30080]
    MKEYS = (["xr", "wmrows", "wmt", "wgu", "wgut", "wrt", "wgf", "wgs", "ss",
              "nm", "et", "rs", "sc", "stg0", "stg1", "wb0", "wb1", "b1row",
              "b2row", "ones512"] + [f"h/{i}" for i in range(32)])
    k.fence("sync", writes=[bkey(0, c) for c in range(8)] + MKEYS)

    k.dma("qs", WG, ins["l0f_wg"], writes=["wgf"], stream="bias")
    k.dma("qg", WRT.rearrange("p (dc f) -> p dc f", dc=8),
          ins["l0f_wgredT"].rearrange("(dc p) f -> p dc f", p=128),
          writes=["wrt"], stream="bias2")
    k.op("scalar", lambda s: s.activation(WGS, WG, AF.Square),
         reads=["wgf"], writes=["wgs"])
    k.op("vector", lambda v: v.tensor_reduce(SS, WGS, AX.X, ALU.add),
         reads=["wgs"], writes=["ss"])
    k.op("vector", lambda v: v.tensor_scalar_max(SS, SS, 1e-8),
         reads=["ss"], writes=["ss"])
    k.op("scalar", lambda s: s.activation(SS, SS, AF.Ln), reads=["ss"], writes=["ss"])
    k.op("scalar", lambda s: s.activation(SS, SS, AF.Exp, scale=-0.5),
         reads=["ss"], writes=["ss"])
    k.op("vector", lambda v: v.tensor_scalar(WGU, WG, SS, None, ALU.mult),
         reads=["wgf", "ss"], writes=["wgu"])
    ppbf = PP[4][:, :].bitcast(BF16)
    k.op("tensor", lambda t: t.transpose(ppbf[0:16, 0:8], WGU, C.ident[0:8, 0:8]),
         reads=["wgu", "c_ident"], writes=["pp4"])
    k.op("vector", lambda v: v.tensor_copy(WGUT, ppbf[0:16, 0:8]),
         reads=["pp4"], writes=["wgut"])
    for tb in range(TJ):
        k.dma("qg", aux[:, :].rearrange("p (dc f) -> p dc f", dc=8),
              xnd[:, tb * 512:(tb + 1) * 512].rearrange("(dc p) f -> p dc f", p=128),
              reads=["xnd"], writes=[A0, A1, A2, A3], stream="xnl")
        def mmxr(t, tb=tb):
            for dc in range(DC):
                last = t.matmul(PP[6][0:16, :], WRT[:, dc * 16:(dc + 1) * 16],
                                aux[:, dc * 512:(dc + 1) * 512],
                                start=(dc == 0), stop=(dc == DC - 1))
            return last
        k.op("tensor", mmxr, reads=["wrt", A0, A1, A2, A3], writes=["pp6"])
        k.op("vector", lambda v, tb=tb: v.tensor_copy(
            b1_[0:16, tb * 512:(tb + 1) * 512], PP[6][0:16, :]),
             reads=["pp6"], writes=["xr"])
    for tc in range(NT // 128):
        k.op("tensor", lambda t, tc=tc: t.matmul(
            PP[7][:, 0:8], b1_[0:16, tc * 128:(tc + 1) * 128], WGUT,
            start=True, stop=True), reads=["xr", "wgut"], writes=["pp7"])
        k.op("vector", lambda v: v.tensor_reduce(NM, PP[7][:, 0:8], AX.X, ALU.max,
                                                 negate=True),
             reads=["pp7"], writes=["nm"])
        k.op("scalar", lambda s: s.activation(ET, PP[7][:, 0:8], AF.Exp, bias=NM,
                                              accum_out=RS),
             reads=["pp7", "nm"], writes=["et", "rs"])
        k.op("vector", lambda v: v.reciprocal(SC, RS), reads=["rs"], writes=["sc"])
        k.op("vector", lambda v: v.tensor_scalar(ET, PP[7][:, 0:8], NM, 0.0,
                                                 ALU.add, ALU.is_equal),
             reads=["pp7", "nm"], writes=["et"])
        k.op("vector", lambda v: v.tensor_scalar(WMT, ET, SC, None, ALU.mult),
             reads=["et", "sc"], writes=["wmt"])
        k.op("tensor", lambda t: t.transpose(ppbf[0:8, 0:128], WMT,
                                             C.ident[:, :]),
             reads=["wmt", "c_ident"], writes=["pp4"])
        k.op("vector", lambda v, tc=tc: v.tensor_copy(
            WM[:, tc * 128:(tc + 1) * 128], ppbf[0:8, 0:128]),
             reads=["pp4"], writes=["wmrows"])

    k.op("vector", lambda v: v.memset(ones512, 1.0), writes=["ones512"])

    for e in range(8):
        k.dma("qg", bf.big[1][:, :].rearrange("p (dc i) -> p dc i", dc=8),
              ins["l0f_w1"][e].rearrange("(dc p) i -> p dc i", p=128),
              writes=BKALL(1), stream="w1")
        k.dma("qg", bf.big[2][:, :].rearrange("p (ic d) -> p ic d", ic=32),
              ins["l0f_w2"][e].rearrange("(ic p) d -> p ic d", p=128),
              writes=BKALL(2), stream="w2")
        k.dma("qg", B1ROW, ins["l0f_b1"][e:e + 1, :],
              writes=["b1row"], stream="b1a")
        k.dma("qg", B2ROW, ins["l0f_b2"][e:e + 1, :],
              writes=["b2row"], stream="b2")
        for tb in range(TJ):
            k.dma("qg", aux[:, :].rearrange("p (dc f) -> p dc f", dc=8),
                  xnd[:, tb * 512:(tb + 1) * 512].rearrange("(dc p) f -> p dc f", p=128),
                  reads=["xnd"], writes=[A0, A1, A2, A3], stream="xnl")
            k.op("tensor", lambda t, e=e, tb=tb: t.matmul(
                PP[5][:], C.sel8[0:8, e * 128:(e + 1) * 128],
                WM[:, tb * 512:(tb + 1) * 512], start=True, stop=True),
                 reads=["wmrows", "c_sel8"], writes=["pp5"])
            wbs = WB[tb % 2]
            k.op("vector", lambda v, wbs=wbs: v.tensor_copy(wbs, PP[5][:]),
                 reads=["pp5"], writes=[f"wb{tb % 2}"])
            for ic in range(32):
                ph, phk = PP[ic % 2], f"pp{ic % 2}"
                def mmh(t, ic=ic, ph=ph):
                    for dc in range(DC):
                        t.matmul(ph[:], bf.big[1][:, dc * 4096 + ic * 128:
                                                  dc * 4096 + (ic + 1) * 128],
                                 aux[:, dc * 512:(dc + 1) * 512],
                                 start=(dc == 0), stop=False)
                    return t.matmul(ph[:], B1ROW[:, ic * 128:(ic + 1) * 128],
                                    ones512, start=False, stop=True)
                k.op("tensor", mmh,
                     reads=BKALL(1) + [A0, A1, A2, A3, "b1row", "ones512"],
                     writes=[phk])
                k.op("scalar", lambda s, ic=ic, ph=ph: s.activation(
                    b1_[:, ic * 512:(ic + 1) * 512], ph[:], AF.Gelu),
                     reads=[phk], writes=[f"h/{ic}"])
            for dc in range(DC):
                py, pyk = PP[2 + dc % 2], f"pp{2 + dc % 2}"
                def mmy(t, dc=dc, py=py):
                    for ic in range(32):
                        t.matmul(py[:], bf.big[2][:, ic * 1024 + dc * 128:
                                                  ic * 1024 + (dc + 1) * 128],
                                 b1_[:, ic * 512:(ic + 1) * 512],
                                 start=(ic == 0), stop=False)
                    return t.matmul(py[:], B2ROW[:, dc * 128:(dc + 1) * 128],
                                    ones512, start=False, stop=True)
                k.op("tensor", mmy,
                     reads=BKALL(2) + [f"h/{ic}" for ic in range(32)] +
                           ["b2row", "ones512"],
                     writes=[pyk])
                stg = STG[dc % 2]
                k.op("vector", lambda v, stg=stg, py=py, tb=tb: v.tensor_tensor(
                    stg, py[:], WB[tb % 2], ALU.mult),
                     reads=[pyk, f"wb{tb % 2}"], writes=[f"stg{dc % 2}"])
                k.dma("qg", dst_dram[dc * 128:(dc + 1) * 128,
                                     tb * 512:(tb + 1) * 512],
                      stg, reads=[f"stg{dc % 2}", dst_key(dc, tb)],
                      writes=[dst_key(dc, tb)], accum_op=ALU.add,
                      stream=f"acc{dc % 2}")


def emit_l1_attn(k: Kfw, C: Consts, bf: Bufs, ins, xnd_unused, dst_dram, dst_key,
                 tag="a1"):
    aux, scr = bf.aux, bf.scr
    auxf = aux[:, :].bitcast(F32)
    scrf = scr[:, :].bitcast(F32)
    PP = bf.pp
    vtd = k.dram(f"{tag}_vtd", [NT, D], BF16)
    b1_ = bf.big[0]

    acc = [auxf[:, s * 512:(s + 1) * 512] for s in range(2)]
    otok = aux[:, 2048:3072]
    nmc = auxf[:, 1536:1537]
    nm2c = auxf[:, 1537:1538]
    corrc = auxf[:, 1538:1539]
    lc = auxf[:, 1539:1540]
    rsc = auxf[:, 1540:1541]
    mxc = auxf[:, 1541:1542]
    rlc = auxf[:, 1542:1543]
    bqs = auxf[:, 1544:1552]
    bks = auxf[:, 1552:1560]
    bvrow = scr[0:1, 1024:2048]
    onesr = aux[0:1, 3120:3248]

    L1KEYS = ["l1stats", "l1bias", "l1ones", "l1acc0", "l1acc1", "l1ot",
              "l1l", "l1rs"]
    k.fence("sync", writes=[A0, A1, A2, A3] + L1KEYS)
    k.dma("qs", bqs, ins["l1a_bq"], writes=["l1bias"], stream="bias")
    k.dma("qs", bks, ins["l1a_bk"], writes=["l1bias"], stream="bias")
    k.dma("qg", bvrow, ins["l1a_bv"], writes=["l1bias", "SCR:2", "SCR:3"],
          stream="bias2")
    k.op("vector", lambda v: v.tensor_scalar_mul(bqs, bqs, 1.0 / 32.0),
         reads=["l1bias"], writes=["l1bias"])
    k.op("vector", lambda v: v.memset(onesr, 1.0), writes=["l1ones"])

    for nb in range(2):
        k.dma("qg", bf.big[1][:, 0:4096].rearrange("p (kc m) -> p kc m", kc=8),
              ins["l1a_wv"][:, nb * 512:(nb + 1) * 512]
              .rearrange("(kc p) m -> p kc m", p=128),
              writes=[bkey(1, 0)], stream="wtok")
        for tc in range(NT // 128):
            ppi = tc % 2
            def mm(t, tc=tc, ppi=ppi, nb=nb):
                for kc in range(DC):
                    t.matmul(PP[ppi][:], bf.bigv(0, kc)[:, tc * 128:(tc + 1) * 128],
                             bf.big[1][:, kc * 512:(kc + 1) * 512],
                             start=(kc == 0), stop=False)
                return t.matmul(PP[ppi][:], onesr,
                                bvrow[:, nb * 512:(nb + 1) * 512],
                                start=False, stop=True)
            k.op("tensor", mm, reads=[bkey(1, 0), "l1ones", "l1bias", "SCR:2",
                                      "SCR:3"] + BKALL(0),
                 writes=[f"pp{ppi}"])
            st_bf = scr[:, ppi * 512:ppi * 512 + 512]
            stk = f"SCR:{ppi}"
            k.op("vector", lambda v, st_bf=st_bf, ppi=ppi: v.tensor_copy(
                st_bf, PP[ppi][:]), reads=[f"pp{ppi}"], writes=[stk])
            k.dma("qs", vtd[tc * 128:(tc + 1) * 128, nb * 512:(nb + 1) * 512],
                  st_bf, reads=[stk], writes=[f"{tag}vd"], stream=f"st{ppi}")

    for dst_i, wd, bias_ap, bkf, scl in ((1, ins["l1a_wq"], bqs, "l1bias", 1 / 32.0),
                                         (2, ins["l1a_wk"], bks, "l1bias", 1.0)):
        for oc in range(DC):
            sl = oc % 2
            slab = scr[:, sl * 1024:(sl + 1) * 1024]
            slks = [f"SCR:{2 * sl}", f"SCR:{2 * sl + 1}"]
            k.dma("qg", slab.rearrange("p (kc m) -> p kc m", kc=8),
                  wd[:, oc * 128:(oc + 1) * 128].rearrange("(kc p) m -> p kc m", p=128),
                  writes=slks, stream=f"w{sl}")
            for j in range(TJ):
                ppi = j % 2
                def mm(t, slab=slab, j=j, ppi=ppi):
                    for kc in range(DC):
                        last = t.matmul(PP[ppi][:], slab[:, kc * 128:(kc + 1) * 128],
                                        bf.bigv(0, kc)[:, j * 512:(j + 1) * 512],
                                        start=(kc == 0), stop=(kc == DC - 1))
                    return last
                k.op("tensor", mm, reads=slks + BKALL(0), writes=[f"pp{ppi}"])
                k.op("scalar", lambda s, dst_i=dst_i, oc=oc, j=j, ppi=ppi,
                     bias_ap=bias_ap, scl=scl: s.activation(
                         bf.bigv(dst_i, oc)[:, j * 512:(j + 1) * 512], PP[ppi][:],
                         AF.Identity, bias=bias_ap[:, oc:oc + 1], scale=scl),
                     reads=[f"pp{ppi}", bkf, bkey(dst_i, oc)], writes=[bkey(dst_i, oc)])

    k.fence("sync", writes=BKALL(0))
    k.dma("qs", b1_[:, :].rearrange("p (ktc dv) -> p ktc dv", ktc=32),
          vtd[:, :].rearrange("(ktc p) dv -> p ktc dv", p=128),
          reads=[f"{tag}vd"], writes=BKALL(0), stream="vrl")

    for b in range(BPC):
        for qt in range(T // 128):
            q0 = b * T + qt * 128
            k.op("vector", lambda v: v.memset(nmc, 3e38), writes=["l1stats"])
            k.op("vector", lambda v: v.memset(lc, 0.0), writes=["l1stats"])
            k.op("vector", lambda v: v.memset(acc[0], 0.0), writes=["l1acc0"])
            k.op("vector", lambda v: v.memset(acc[1], 0.0), writes=["l1acc1"])
            for kt in range(4):
                k0 = b * T + kt * 512
                ps, psk = PP[kt % 2], f"pp{kt % 2}"
                def mms(t, q0=q0, k0=k0, ps=ps):
                    for dc in range(DC):
                        last = t.matmul(ps[:], bf.bigv(1, dc)[:, q0:q0 + 128],
                                        bf.bigv(2, dc)[:, k0:k0 + 512],
                                        start=(dc == 0), stop=(dc == DC - 1))
                    return last
                k.op("tensor", mms, reads=[bkey(1, c) for c in range(8)] +
                     [bkey(2, c) for c in range(8)], writes=[psk])
                k.op("vector", lambda v, ps=ps: v.tensor_reduce(
                    mxc, ps[:], AX.X, ALU.max, negate=True),
                     reads=[psk], writes=["l1stats"])
                k.op("vector", lambda v: v.tensor_tensor(nm2c, nmc, mxc, ALU.min),
                     reads=["l1stats"], writes=["l1stats"])
                k.op("vector", lambda v: v.tensor_sub(corrc, nm2c, nmc),
                     reads=["l1stats"], writes=["l1stats"])
                k.op("scalar", lambda s: s.activation(corrc, corrc, AF.Exp),
                     reads=["l1stats"], writes=["l1stats"])
                k.op("vector", lambda v: v.tensor_copy(nmc, nm2c),
                     reads=["l1stats"], writes=["l1stats"])
                pt_sb = scr[:, 0:512]
                k.op("scalar", lambda s, ps=ps: s.activation(
                    pt_sb, ps[:], AF.Exp, bias=nm2c, accum_out=rsc),
                     reads=[psk, "l1stats"], writes=["SCR:0", "l1rs"])
                k.op("vector", lambda v: v.scalar_tensor_tensor(
                    lc, lc, corrc, rsc, ALU.mult, ALU.add),
                     reads=["l1stats", "l1rs", "l1l"], writes=["l1l"])
                pp6b = PP[6][:, :].bitcast(BF16)
                def trs(t):
                    for ks in range(4):
                        last = t.transpose(pp6b[:, ks * 128:(ks + 1) * 128],
                                           pt_sb[:, ks * 128:(ks + 1) * 128],
                                           C.ident[:, :])
                    return last
                k.op("tensor", trs, reads=["SCR:0", "c_ident"], writes=["pp6"])
                k.op("vector", lambda v: v.tensor_copy(scr[:, 1024:1536],
                                                       pp6b[:, 0:512]),
                     reads=["pp6"], writes=["SCR:2"])
                pva, pvb = PP[2 + 2 * (kt % 2)], PP[3 + 2 * (kt % 2)]
                pvak, pvbk = f"pp{2 + 2 * (kt % 2)}", f"pp{3 + 2 * (kt % 2)}"
                def mmpv(t, kt=kt, b=b, pva=pva, pvb=pvb):
                    for ks in range(4):
                        ktc = b * 16 + kt * 4 + ks
                        t.matmul(pva[:], scr[:, 1024 + ks * 128:1024 + (ks + 1) * 128],
                                 b1_[:, ktc * 1024:ktc * 1024 + 512],
                                 start=(ks == 0), stop=(ks == 3))
                    for ks in range(4):
                        ktc = b * 16 + kt * 4 + ks
                        last = t.matmul(pvb[:],
                                        scr[:, 1024 + ks * 128:1024 + (ks + 1) * 128],
                                        b1_[:, ktc * 1024 + 512:(ktc + 1) * 1024],
                                        start=(ks == 0), stop=(ks == 3))
                    return last
                k.op("tensor", mmpv, reads=["SCR:2"] + BKALL(0),
                     writes=[pvak, pvbk])
                k.op("vector", lambda v, pva=pva: v.scalar_tensor_tensor(
                    acc[0], acc[0], corrc, pva[:], ALU.mult, ALU.add),
                     reads=[pvak, "l1stats", "l1acc0"], writes=["l1acc0"])
                k.op("vector", lambda v, pvb=pvb: v.scalar_tensor_tensor(
                    acc[1], acc[1], corrc, pvb[:], ALU.mult, ALU.add),
                     reads=[pvbk, "l1stats", "l1acc1"], writes=["l1acc1"])
            k.op("vector", lambda v: v.reciprocal(rlc, lc),
                 reads=["l1l"], writes=["l1stats"])
            k.op("vector", lambda v: v.tensor_scalar(otok[:, 0:512], acc[0], rlc,
                                                     None, ALU.mult),
                 reads=["l1acc0", "l1stats"], writes=["l1ot"])
            k.op("vector", lambda v: v.tensor_scalar(otok[:, 512:1024], acc[1], rlc,
                                                     None, ALU.mult),
                 reads=["l1acc1", "l1stats"], writes=["l1ot"])
            pp7b = PP[7][:, :].bitcast(BF16)
            def tro(t):
                for dc in range(DC):
                    last = t.transpose(pp7b[:, dc * 128:(dc + 1) * 128],
                                       otok[:, dc * 128:(dc + 1) * 128],
                                       C.ident[:, :])
                return last
            k.op("tensor", tro, reads=["l1ot", "c_ident"], writes=["pp7"])
            k.op("vector", lambda v: v.tensor_copy(scrf[:, 0:1024], pp7b[:, 0:1024]),
                 reads=["pp7"], writes=["SCR:0", "SCR:1", "SCR:2", "SCR:3"])
            jj = q0 // 512
            k.dma("qg", dst_dram[:, q0:q0 + 128].rearrange("(dc p) t -> p dc t", p=128),
                  scrf[:, 0:1024].rearrange("p (dc t) -> p dc t", dc=8),
                  reads=["SCR:0", "SCR:1", "SCR:2", "SCR:3"] +
                        [dst_key(c, jj) for c in range(8)],
                  writes=[dst_key(c, jj) for c in range(8)],
                  accum_op=ALU.add, stream=f"acc{qt % 2}")
    k.fence("sync", reads=L1KEYS, writes=[A0, A1, A2, A3])


def emit_l1_ffn(k: Kfw, C: Consts, bf: Bufs, ins, dst_dram, dst_key, tag="f1"):
    aux, scr = bf.aux, bf.scr
    auxf = aux[:, :].bitcast(F32)
    b1_, b2_, b3_ = bf.big[0], bf.big[1], bf.big[2]
    b1f = b1_[:, :].bitcast(F32)
    b3f = b3_[:, :].bitcast(F32)
    PP = bf.pp
    KJ = 512

    AFFR = auxf[:, 0:2048]
    AFFT = b3_[:, 18432:18688]
    GWT = b3_[:, 18688:18752]
    SILU = b3_[:, 18752:19776]
    NM3 = b3f[:, 9888:9889]
    RS3 = b3f[:, 9889:9890]
    SC3 = b3f[:, 9890:9891]
    AFF32 = b3f[:, 9892:9900]
    MASKA = b1_[0:64, 16384:18432]
    ZER = b1_[0:64, 18432:20480]
    POSA = b1f[0:64, 10240:12288]
    POST = b1f[:, 12288:13312]
    PHI_R = b1_[0:64, 26624:28672]
    PLO_R = b1_[0:64, 28672:30720]
    BIS = b1f[:, 15360:15392]
    LO, HI, MID = BIS[:, 0:1], BIS[:, 1:2], BIS[:, 2:3]
    CNT, SEL, NSEL, TT1 = BIS[:, 3:4], BIS[:, 4:5], BIS[:, 5:6], BIS[:, 6:7]
    JCC = b1f[:, 15392:15396]
    PTA = 0
    XSA = 8192
    YSA = 12288
    SLA = 16384
    GSEL = b3f[:, 9900:9904]

    B1K = ["maska", "zer", "posa", "post", "bis", "jcc", "phir", "plor", "h3"]
    B3K = ["pta", "xsa", "ysa", "sla0", "sla1", "slb0", "slb1", "afft", "gwt",
           "silu", "st3"]

    k.fence("sync", writes=[A0, A1, A2, A3, "affr"])
    k.fence("sync", writes=BKALL(2) + B3K + ["nm3", "rs3", "sc3", "aff32", "gsel", "rden"])
    k.op("vector", lambda v: v.memset(AFFR[0:64, :], 0.0), writes=["affr"])
    k.dma("qg", GWT.rearrange("p (dc f) -> p dc f", dc=8),
          ins["l1f_gwT"].rearrange("(dc p) f -> p dc f", p=128),
          writes=["gwt"], stream="bias")
    for tc in range(NT // 128):
        def mml(t, tc=tc):
            for dc in range(DC):
                last = t.matmul(PP[0][:, 0:8], bf.bigv(0, dc)[:, tc * 128:(tc + 1) * 128],
                                GWT[:, dc * 8:(dc + 1) * 8],
                                start=(dc == 0), stop=(dc == DC - 1))
            return last
        k.op("tensor", mml, reads=["gwt"] + BKALL(0), writes=["pp0"])
        k.op("vector", lambda v: v.tensor_reduce(NM3, PP[0][:, 0:8], AX.X, ALU.max,
                                                 negate=True),
             reads=["pp0"], writes=["nm3"])
        k.op("scalar", lambda s: s.activation(AFF32, PP[0][:, 0:8], AF.Exp, bias=NM3,
                                              accum_out=RS3),
             reads=["pp0", "nm3"], writes=["aff32", "rs3"])
        k.op("vector", lambda v: v.reciprocal(SC3, RS3), reads=["rs3"], writes=["sc3"])
        k.op("vector", lambda v, tc=tc: v.tensor_scalar(AFFT[:, tc * 8:(tc + 1) * 8],
                                                        AFF32, SC3, None, ALU.mult),
             reads=["aff32", "sc3"], writes=["afft"])
    for b in range(BPC):
        rb = 32 * b
        tp_m = (0, rb)
        for sl in range(4):
            s0 = b * T + sl * 512
            c0 = b * 32 * 64 if False else (sl * 512)
            def mmr(t, s0=s0, rb=rb):
                for dc in range(DC):
                    last = t.matmul(PP[0][rb:rb + 8, 0:512],
                                    GWT[:, dc * 8:(dc + 1) * 8],
                                    bf.bigv(0, dc)[:, s0:s0 + 512],
                                    start=(dc == 0), stop=(dc == DC - 1),
                                    tile_position=(0, rb))
                return last
            k.op("tensor", mmr, reads=["gwt"] + BKALL(0), writes=["pp0"])
            dst = AFFR[rb:rb + 8, sl * 512:(sl + 1) * 512]
            k.op("scalar", lambda s, dst=dst, rb=rb: s.activation(
                dst, PP[0][rb:rb + 8, 0:512], AF.Exp),
                 reads=["pp0"], writes=["affr"])
            k.op("tensor", lambda t, dst=dst, rb=rb: t.matmul(
                PP[1][0:1, 0:512], C.ones_col[rb:rb + 8, :], dst,
                start=True, stop=True, tile_position=(rb, 0)),
                 reads=["affr", "c_onesc"], writes=["pp1"])
            k.op("vector", lambda v: v.reciprocal(b3f[0:1, 9904:10416],
                                                  PP[1][0:1, 0:512]),
                 reads=["pp1"], writes=["rden"])
            k.op("tensor", lambda t, rb=rb: t.matmul(
                PP[2][rb:rb + 8, 0:512], C.ones_row[:, 0:8],
                b3f[0:1, 9904:10416], start=True, stop=True,
                tile_position=(0, rb)),
                 reads=["rden", "c_onesr"], writes=["pp2"])
            k.op("vector", lambda v, dst=dst, rb=rb: v.tensor_tensor(
                dst, dst, PP[2][rb:rb + 8, 0:512], ALU.mult),
                 reads=["affr", "pp2"], writes=["affr"])

    k.fence("sync", writes=[bkey(1, c) for c in range(8)])
    pp2b = PP[2][:, :].bitcast(BF16)
    for tc in range(NT // 128):
        def trx(t, tc=tc):
            for dc in range(DC):
                last = t.transpose(pp2b[:, dc * 128:(dc + 1) * 128],
                                   bf.bigv(0, dc)[:, tc * 128:(tc + 1) * 128],
                                   C.ident[:, :])
            return last
        k.op("tensor", trx, reads=BKALL(0) + ["c_ident"], writes=["pp2"])
        k.op("vector", lambda v, tc=tc: v.tensor_copy(
            b2_[:, tc * 1024:(tc + 1) * 1024], pp2b[:, 0:1024]),
             reads=["pp2"], writes=[bkey(1, tc // 4)])

    k.fence("sync", writes=BKALL(0) + B1K)
    k.op("vector", lambda v: v.memset(LO, 0.0), writes=["bis"])
    k.op("vector", lambda v: v.memset(HI, 1.0), writes=["bis"])
    k.op("vector", lambda v: v.memset(ZER[:, :], 0.0), writes=["zer"])
    for it in range(40):
        k.op("vector", lambda v: v.tensor_add(MID, LO, HI), reads=["bis"], writes=["bis"])
        k.op("vector", lambda v: v.tensor_scalar_mul(MID, MID, 0.5),
             reads=["bis"], writes=["bis"])
        k.op("vector", lambda v: v.tensor_scalar(MASKA[:, :], AFFR[0:64, :], MID[0:64],
                                                 None, ALU.is_gt,
                                                 op1=ALU.add, accum_out=CNT[0:64]),
             reads=["affr", "bis"], writes=["maska", "bis"])
        k.op("vector", lambda v: v.tensor_scalar(SEL, CNT, float(KJ), None, ALU.is_ge),
             reads=["bis"], writes=["bis"])
        k.op("vector", lambda v: v.tensor_sub(TT1, MID, LO), reads=["bis"], writes=["bis"])
        k.op("vector", lambda v: v.tensor_mul(TT1, TT1, SEL), reads=["bis"], writes=["bis"])
        k.op("vector", lambda v: v.tensor_add(LO, LO, TT1), reads=["bis"], writes=["bis"])
        k.op("vector", lambda v: v.tensor_scalar(NSEL, SEL, -1.0, 1.0, ALU.mult,
                                                 ALU.add), reads=["bis"], writes=["bis"])
        k.op("vector", lambda v: v.tensor_sub(TT1, MID, HI), reads=["bis"], writes=["bis"])
        k.op("vector", lambda v: v.tensor_mul(TT1, TT1, NSEL), reads=["bis"], writes=["bis"])
        k.op("vector", lambda v: v.tensor_add(HI, HI, TT1), reads=["bis"], writes=["bis"])
    k.op("vector", lambda v: v.tensor_scalar(MASKA[:, :], AFFR[0:64, :], LO[0:64],
                                             None, ALU.is_gt),
         reads=["affr", "bis"], writes=["maska"])
    k.op("vector", lambda v: v.tensor_tensor_scan(POSA, MASKA[:, :], ZER[:, :], 0.0,
                                                  ALU.add, ALU.add),
         reads=["maska", "zer"], writes=["posa"])
    k.op("vector", lambda v: v.tensor_tensor(POSA, POSA, MASKA[:, :], ALU.mult),
         reads=["posa", "maska"], writes=["posa"])
    k.op("vector", lambda v: v.tensor_scalar_add(POSA, POSA, -1.0),
         reads=["posa"], writes=["posa"])
    k.op("vector", lambda v: v.tensor_scalar(PHI_R[:, :], POSA, 256.0, 256.0,
                                             ALU.is_ge, ALU.mult),
         reads=["posa"], writes=["phir"])
    k.op("vector", lambda v: v.tensor_tensor(PLO_R[:, :], POSA, PHI_R[:, :],
                                             ALU.subtract),
         reads=["posa", "phir"], writes=["plor"])
    pp3b = PP[3][:, :].bitcast(BF16)
    for sc in range(16):
        def trp(t, sc=sc):
            t.transpose(pp3b[:, 0:64], PHI_R[:, sc * 128:(sc + 1) * 128],
                        C.ident[0:64, 0:64])
            return t.transpose(pp3b[:, 64:128], PLO_R[:, sc * 128:(sc + 1) * 128],
                               C.ident[0:64, 0:64])
        k.op("tensor", trp, reads=["phir", "plor", "c_ident"], writes=["pp3"])
        k.op("vector", lambda v: v.tensor_copy(SILU[:, 0:64], pp3b[:, 0:64]),
             reads=["pp3"], writes=["silu"])
        k.op("vector", lambda v, sc=sc: v.tensor_tensor(
            POST[:, sc * 64:(sc + 1) * 64], SILU[:, 0:64], pp3b[:, 64:128], ALU.add),
             reads=["silu", "pp3"], writes=["post"])
    for jc in range(4):
        k.op("vector", lambda v, jc=jc: v.tensor_scalar_add(JCC[:, jc:jc + 1],
                                                            C.chan[:], float(jc * 128)),
             reads=["c_chan"], writes=["jcc"])

    for e in range(8):
        for b in range(BPC):
            lane = 32 * b + e
            for sc in range(16):
                k.op("vector", lambda v, sc=sc, lane=lane: v.tensor_scalar(
                    b3_[:, PTA + sc * 512:PTA + (sc + 1) * 512], C.iota16[:],
                    POST[:, sc * 64 + lane:sc * 64 + lane + 1], None, ALU.is_equal),
                     reads=["post", "c_iota"], writes=["pta"])
            for dc in range(DC):
                ppi = dc % 2
                def mmx(t, dc=dc, b=b, ppi=ppi):
                    for sc in range(16):
                        tc = b * 16 + sc
                        last = t.matmul(PP[ppi][:],
                                        b2_[:, tc * 1024 + dc * 128:tc * 1024 + (dc + 1) * 128],
                                        b3_[:, PTA + sc * 512:PTA + (sc + 1) * 512],
                                        start=(sc == 0), stop=(sc == 15))
                    return last
                k.op("tensor", mmx, reads=["pta"] + [bkey(1, c) for c in range(8)],
                     writes=[f"pp{ppi}"])
                k.op("vector", lambda v, dc=dc, ppi=ppi: v.tensor_copy(
                    b3_[:, XSA + dc * 512:XSA + (dc + 1) * 512], PP[ppi][:]),
                     reads=[f"pp{ppi}"], writes=["xsa"])
            def mmg(t, b=b, e=e):
                for jc in range(4):
                    for sc in range(16):
                        tc = b * 16 + sc
                        last = t.matmul(PP[2][:, jc:jc + 1],
                                        b3_[:, PTA + sc * 512 + jc * 128:
                                            PTA + sc * 512 + (jc + 1) * 128],
                                        AFFT[:, tc * 8 + e:tc * 8 + e + 1],
                                        start=(sc == 0), stop=(sc == 15))
                return last
            k.op("tensor", mmg, reads=["pta", "afft"], writes=["pp2"])
            k.op("vector", lambda v: v.tensor_copy(GSEL, PP[2][:, 0:4]),
                 reads=["pp2"], writes=["gsel"])
            SLB = 22528
            for ic in range(32):
                sl = ic % 2
                slab = b3_[:, SLA + sl * 1024:SLA + (sl + 1) * 1024]
                slk = f"sla{sl}"
                slabu = b3_[:, SLB + sl * 1024:SLB + (sl + 1) * 1024]
                slku = f"slb{sl}"
                k.dma("qg", slab.rearrange("p (dc m) -> p dc m", dc=8),
                      ins["l1f_wgT"][e][:, ic * 128:(ic + 1) * 128]
                      .rearrange("(dc p) m -> p dc m", p=128),
                      writes=[slk], stream=f"w{sl}")
                k.dma("qg", slabu.rearrange("p (dc m) -> p dc m", dc=8),
                      ins["l1f_wuT"][e][:, ic * 128:(ic + 1) * 128]
                      .rearrange("(dc p) m -> p dc m", p=128),
                      writes=[slku], stream=f"u{sl}")
                def mmhg(t, slab=slab):
                    for dc in range(DC):
                        last = t.matmul(PP[4][:], slab[:, dc * 128:(dc + 1) * 128],
                                        b3_[:, XSA + dc * 512:XSA + (dc + 1) * 512],
                                        start=(dc == 0), stop=(dc == DC - 1))
                    return last
                k.op("tensor", mmhg, reads=[slk, "xsa"], writes=["pp4"])
                def mmhu(t, slabu=slabu):
                    for dc in range(DC):
                        last = t.matmul(PP[5][:], slabu[:, dc * 128:(dc + 1) * 128],
                                        b3_[:, XSA + dc * 512:XSA + (dc + 1) * 512],
                                        start=(dc == 0), stop=(dc == DC - 1))
                    return last
                k.op("tensor", mmhu, reads=[slku, "xsa"], writes=["pp5"])
                k.op("scalar", lambda s: s.activation(SILU[:, 0:512], PP[4][:], AF.Silu),
                     reads=["pp4"], writes=["silu"])
                k.op("vector", lambda v, ic=ic: v.tensor_tensor(
                    b1_[:, ic * 512:(ic + 1) * 512], SILU[:, 0:512], PP[5][:], ALU.mult),
                     reads=["silu", "pp5"], writes=["h3"])
            for ic in range(32):
                sl = ic % 2
                slab = b3_[:, SLA + sl * 1024:SLA + (sl + 1) * 1024]
                slk = f"sla{sl}"
                k.dma("qg", slab.rearrange("p (h m) -> p h m", h=2),
                      ins["l1f_wdT"][e][ic * 128:(ic + 1) * 128, :]
                      .rearrange("p (h m) -> p h m", h=2),
                      writes=[slk], stream=f"w{sl}")
                def mmy(t, ic=ic, slab=slab):
                    for jc in range(4):
                        for hh in range(2):
                            last = t.matmul(PP[jc * 2 + hh][:],
                                            b1_[:, ic * 512 + jc * 128:
                                                ic * 512 + (jc + 1) * 128],
                                            slab[:, hh * 512:(hh + 1) * 512],
                                            start=(ic == 0), stop=(ic == 31))
                    return last
                k.op("tensor", mmy, reads=[slk, "h3"],
                     writes=[f"pp{x}" for x in range(8)])
            for jc in range(4):
                for hh in range(2):
                    k.op("vector", lambda v, jc=jc, hh=hh: v.tensor_scalar(
                        b3_[:, YSA + jc * 1024 + hh * 512:YSA + jc * 1024 + (hh + 1) * 512],
                        PP[jc * 2 + hh][:], GSEL[:, jc:jc + 1], None, ALU.mult),
                         reads=[f"pp{jc * 2 + hh}", "gsel"], writes=["ysa"])
            for st in range(4):
                def mmb(t, b=b, e=e, st=st):
                    t.matmul(PP[0][:, 0:512],
                             C.sel8[32 * b:32 * b + 8, e * 128:(e + 1) * 128],
                             PHI_R[32 * b:32 * b + 8, st * 512:(st + 1) * 512],
                             start=True, stop=False)
                    return t.matmul(PP[0][:, 0:512],
                                    C.sel8[32 * b:32 * b + 8, e * 128:(e + 1) * 128],
                                    PLO_R[32 * b:32 * b + 8, st * 512:(st + 1) * 512],
                                    start=False, stop=True)
                k.op("tensor", mmb, reads=["phir", "plor", "c_sel8"], writes=["pp0"])
                for jc in range(4):
                    k.op("vector", lambda v, jc=jc, st=st: v.tensor_scalar(
                        b3_[:, PTA + jc * 2048 + st * 512:PTA + jc * 2048 + (st + 1) * 512],
                        PP[0][:, 0:512], JCC[:, jc:jc + 1], None, ALU.is_equal),
                         reads=["pp0", "jcc"], writes=["pta"])
            for dc in range(DC):
                for st in range(4):
                    ppi = 1 + (dc * 4 + st) % 2
                    def mmsc(t, dc=dc, st=st, ppi=ppi):
                        for jc in range(4):
                            last = t.matmul(PP[ppi][:],
                                            b3_[:, YSA + jc * 1024 + dc * 128:
                                                YSA + jc * 1024 + (dc + 1) * 128],
                                            b3_[:, PTA + jc * 2048 + st * 512:
                                                PTA + jc * 2048 + (st + 1) * 512],
                                            start=(jc == 0), stop=(jc == 3))
                        return last
                    k.op("tensor", mmsc, reads=["ysa", "pta"], writes=[f"pp{ppi}"])
                    stg = b3f[:, 10240 + (st % 2) * 512:10240 + (st % 2 + 1) * 512]
                    k.op("vector", lambda v, stg=stg, ppi=ppi: v.tensor_copy(
                        stg, PP[ppi][:]), reads=[f"pp{ppi}"], writes=[f"st3{st % 2}"])
                    jj = b * 4 + st
                    k.dma("qg", dst_dram[dc * 128:(dc + 1) * 128,
                                         jj * 512:(jj + 1) * 512],
                          stg, reads=[f"st3{st % 2}", dst_key(dc, jj)],
                          writes=[dst_key(dc, jj)], accum_op=ALU.add,
                          stream=f"acc{st % 2}")


def emit_l2_attn(k: Kfw, C: Consts, bf: Bufs, ins, dst_dram, dst_key, tag="a2"):
    aux, scr = bf.aux, bf.scr
    scrf = scr[:, :].bitcast(F32)
    b1_, b2_, b3_ = bf.big[0], bf.big[1], bf.big[2]
    b1f = b1_[:, :].bitcast(F32)
    PP = bf.pp
    DI = 2048
    NCC = 16
    zd = k.dram(f"{tag}_zd", [DI, NT], BF16)

    def xc_chunk(cc):
        return (bf.bigv(1, cc) if cc < 8 else bf.bigv(2, cc - 8))

    def xck(cc):
        return bkey(1, cc) if cc < 8 else bkey(2, cc - 8)

    DLT = b1f[0:1, 0:4096]
    BR = b1f[32:48, 0:4096]
    CR = b1f[64:80, 0:4096]
    BBF = b1_[32:48, 8192:12288]
    CBF = b1_[64:80, 8192:12288]
    DA_T = b1_[:, 12288:14336]
    DBU = b1_[:, 14336:16384]
    HT = b1_[:, 16384:18432]
    HC = b1_[:, 18432:20480]
    DBT = b1_[:, 20480:22528]
    CLT = b1_[:, 22528:24576]
    DBR = b1_[:, 24576:26624]
    ZS = b1_[:, 26624:27648]
    HT2 = b1_[:, 27648:29696]
    DA2B = bf.aux[:, 0:2048]
    DBU2B = bf.aux[:, 2048:4096]
    REPC = b1_[:, 29696:31744]
    REPB = b1_[0:80, 32448:32576]
    DCOL = b1f[:, 15872:15888]
    CW = b1f[:, 15888:15952]
    CB = b1f[:, 15952:15968]
    ACOL = b1f[:, 15968 + 0:15968 + 256]
    B1K2 = ["dlt", "brow", "crow", "bbf", "cbf", "dat", "dbu", "ht", "hc",
            "dbt", "clt", "dbr", "zs", "ht2", "repc", "dcol", "cw", "cb",
            "acol", "summ", "repb", "tmpr", "xslab", "cstrip"]

    k.fence("sync", writes=[bkey(1, c) for c in range(8)] +
            [bkey(2, c) for c in range(8)] + [A0, A1, A2, A3])
    for oc in range(32):
        sl = oc % 2
        slab = scr[:, sl * 1024:(sl + 1) * 1024]
        slks = [f"SCR:{2 * sl}", f"SCR:{2 * sl + 1}"]
        k.dma("qg", slab.rearrange("p (kc m) -> p kc m", kc=8),
              ins["l2a_inprojT"][:, oc * 128:(oc + 1) * 128]
              .rearrange("(kc p) m -> p kc m", p=128),
              writes=slks, stream=f"w{sl}")
        for j in range(TJ):
            ppi = j % 2
            def mm(t, slab=slab, j=j, ppi=ppi):
                for kc in range(DC):
                    last = t.matmul(PP[ppi][:], slab[:, kc * 128:(kc + 1) * 128],
                                    bf.bigv(0, kc)[:, j * 512:(j + 1) * 512],
                                    start=(kc == 0), stop=(kc == DC - 1))
                return last
            k.op("tensor", mm, reads=slks + BKALL(0), writes=[f"pp{ppi}"])
            if oc < 16:
                k.op("vector", lambda v, oc=oc, j=j, ppi=ppi: v.tensor_copy(
                    xc_chunk(oc)[:, j * 512:(j + 1) * 512], PP[ppi][:]),
                     reads=[f"pp{ppi}"], writes=[xck(oc)])
            else:
                st_bf = aux[:, (j % 2) * 512:(j % 2) * 512 + 512]
                stk = ["AUX:0"] if j % 2 == 0 else ["AUX:0b"]
                k.op("vector", lambda v, st_bf=st_bf, ppi=ppi: v.tensor_copy(
                    st_bf, PP[ppi][:]), reads=[f"pp{ppi}"], writes=stk)
                k.dma("qs", zd[(oc - 16) * 128:(oc - 15) * 128,
                               j * 512:(j + 1) * 512],
                      st_bf, reads=stk, writes=["zd"], stream=f"st{j % 2}")

    k.fence("sync", writes=BKALL(0) + B1K2 + ["SCR:0", "SCR:1", "SCR:2", "SCR:3",
                                              "AUX:0", "AUX:0b", "AUX:1", "AUX:2",
                                              "AUX:3"])
    k.dma("qg", REPC.rearrange("p (m f) -> p m f", m=16),
          ins["cst_repc"].rearrange("(m p) f -> p m f", p=128),
          writes=["repc"], stream="cst")
    k.dma("qg", REPB[0:16, :], ins["cst_repb"], writes=["repb"], stream="cst")
    k.dma("qg", REPB[32:48, :], ins["cst_repb"], writes=["repb"], stream="cst")
    k.dma("qg", REPB[64:80, :], ins["cst_repb"], writes=["repb"], stream="cst")
    k.dma("qg", scr[:, :].rearrange("p (m f) -> p m f", m=16),
          ins["cst_summ"].rearrange("(m p) f -> p m f", p=128),
          writes=["summ", "SCR:0", "SCR:1", "SCR:2", "SCR:3"], stream="cst")
    k.dma("qs", CW, ins["l2a_convw"], writes=["cw"], stream="cst2")
    k.dma("qs", CB, ins["l2a_convb"], writes=["cb"], stream="cst2")
    k.dma("qs", DCOL, ins["l2a_dcol"], writes=["dcol"], stream="cst2")
    k.dma("qs", ACOL, ins["l2a_alog"], writes=["acol"], stream="cst2")
    k.op("scalar", lambda s: s.activation(ACOL, ACOL, AF.Exp),
         reads=["acol"], writes=["acol"])
    k.op("vector", lambda v: v.tensor_scalar_mul(ACOL, ACOL, -1.0),
         reads=["acol"], writes=["acol"])
    for cc in range(NCC):
        for b in range(BPC):
            strip = xc_chunk(cc)[:, b * T:(b + 1) * T]
            astrip = aux[:, 0:T]
            k.op("vector", lambda v, strip=strip, astrip=astrip, cc=cc: v.tensor_scalar(
                astrip, strip, CW[:, cc * 4 + 3:cc * 4 + 4], None, ALU.mult),
                 reads=[xck(cc), "cw"], writes=["cstrip"])
            for kk in range(3):
                sh = 3 - kk
                k.op("vector", lambda v, strip=strip, astrip=astrip, cc=cc, kk=kk,
                     sh=sh: v.scalar_tensor_tensor(
                         astrip[:, sh:T], strip[:, 0:T - sh],
                         CW[:, cc * 4 + kk:cc * 4 + kk + 1],
                         astrip[:, sh:T], ALU.mult, ALU.add),
                     reads=[xck(cc), "cw", "cstrip"], writes=["cstrip"])
            k.op("scalar", lambda s, strip=strip, astrip=astrip, cc=cc: s.activation(
                strip, astrip, AF.Silu, bias=CB[:, cc:cc + 1]),
                 reads=["cstrip", "cb"], writes=[xck(cc)])

    for j in range(TJ):
        sl = j % 2
        slab = scr[:, :]
        xslab = aux[:, 0:2048]
        if j == 0:
            k.dma("qg", aux[:, 0:528].rearrange("p (kc m) -> p kc m", kc=16),
                  ins["l2a_xprojT"].rearrange("(kc p) m -> p kc m", p=128),
                  writes=["xslab", "cstrip"], stream="w0")
        def mmx(t, j=j):
            for kc in range(NCC):
                t.matmul(PP[4][0:1, :], xslab[:, kc * 33 + 0:kc * 33 + 1],
                         xc_chunk(kc)[:, j * 512:(j + 1) * 512],
                         start=(kc == 0), stop=(kc == NCC - 1))
            for kc in range(NCC):
                t.matmul(PP[4][32:48, :], xslab[:, kc * 33 + 1:kc * 33 + 17],
                         xc_chunk(kc)[:, j * 512:(j + 1) * 512],
                         start=(kc == 0), stop=(kc == NCC - 1),
                         tile_position=(0, 32))
            for kc in range(NCC):
                last = t.matmul(PP[4][64:80, :], xslab[:, kc * 33 + 17:kc * 33 + 33],
                                xc_chunk(kc)[:, j * 512:(j + 1) * 512],
                                start=(kc == 0), stop=(kc == NCC - 1),
                                tile_position=(0, 64))
            return last
        k.op("tensor", mmx, reads=["xslab"] + [xck(c) for c in range(16)],
             writes=["pp4"])
        k.op("vector", lambda v, j=j: v.tensor_copy(DLT[:, j * 512:(j + 1) * 512],
                                                    PP[4][0:1, :]),
             reads=["pp4"], writes=["dlt"])
        k.op("vector", lambda v, j=j: v.tensor_copy(BR[:, j * 512:(j + 1) * 512],
                                                    PP[4][32:48, :]),
             reads=["pp4"], writes=["brow"])
        k.op("vector", lambda v, j=j: v.tensor_copy(CR[:, j * 512:(j + 1) * 512],
                                                    PP[4][64:80, :]),
             reads=["pp4"], writes=["crow"])
    tmpr = b1f[0:1, 4096:8192]
    k.op("scalar", lambda s: s.activation(tmpr, DLT, AF.Abs),
         reads=["dlt"], writes=["tmpr"])
    k.op("scalar", lambda s: s.activation(tmpr, tmpr, AF.Exp, scale=-1.0),
         reads=["tmpr"], writes=["tmpr"])
    k.op("vector", lambda v: v.tensor_scalar_add(tmpr, tmpr, 1.0),
         reads=["tmpr"], writes=["tmpr"])
    k.op("scalar", lambda s: s.activation(tmpr, tmpr, AF.Ln),
         reads=["tmpr"], writes=["tmpr"])
    k.op("vector", lambda v: v.tensor_scalar_max(DLT, DLT, 0.0),
         reads=["dlt"], writes=["dlt"])
    k.op("vector", lambda v: v.tensor_add(DLT, DLT, tmpr),
         reads=["dlt", "tmpr"], writes=["dlt"])
    k.op("vector", lambda v: v.tensor_copy(BBF[:, :], BR),
         reads=["brow"], writes=["bbf"])
    k.op("vector", lambda v: v.tensor_copy(CBF[:, :], CR),
         reads=["crow"], writes=["cbf"])

    for b in range(BPC):
        bs = b * T
        for s4 in range(4):
            ppi = 4 + s4 % 2
            k.op("tensor", lambda t, bs=bs, s4=s4, ppi=ppi: t.matmul(
                PP[ppi][:], C.ones_row[:, :],
                DLT[:, bs + s4 * 512:bs + (s4 + 1) * 512], start=True, stop=True),
                 reads=["dlt", "c_onesr"], writes=[f"pp{ppi}"])
            k.op("vector", lambda v, s4=s4, ppi=ppi: v.tensor_copy(
                DBR[:, s4 * 512:(s4 + 1) * 512], PP[ppi][:]),
                 reads=[f"pp{ppi}"], writes=["dbr"])
        for s4 in range(4):
            k.op("tensor", lambda t, bs=bs, s4=s4: t.matmul(
                PP[4][:], REPB[32:48, :],
                BBF[:, bs + s4 * 512:bs + (s4 + 1) * 512], start=True, stop=True,
                tile_position=(32, 0)),
                 reads=["bbf", "repb"], writes=["pp4"])
            k.op("vector", lambda v, s4=s4: v.tensor_tensor(
                DBT[:, s4 * 512:(s4 + 1) * 512], DBR[:, s4 * 512:(s4 + 1) * 512],
                PP[4][:], ALU.mult),
                 reads=["pp4", "dbr"], writes=["dbt"])
            k.op("tensor", lambda t, bs=bs, s4=s4: t.matmul(
                PP[5][:], REPB[64:80, :],
                CBF[:, bs + s4 * 512:bs + (s4 + 1) * 512], start=True, stop=True,
                tile_position=(64, 0)),
                 reads=["cbf", "repb"], writes=["pp5"])
            k.op("vector", lambda v, s4=s4: v.tensor_copy(
                CLT[:, s4 * 512:(s4 + 1) * 512], PP[5][:]),
                 reads=["pp5"], writes=["clt"])
        for cc in range(NCC):
            pend_mmy = None
            for jj in range(16):
                lt = cc * 16 + jj
                acol = ACOL[:, lt:lt + 1]
                dat, datk = (DA_T, "dat") if jj % 2 == 0 else (DA2B, "dat2b")
                dbu, dbuk = (DBU, "dbu") if jj % 2 == 0 else (DBU2B, "dbu2b")
                datck = [datk] + (["AUX:0", "AUX:1"] if jj % 2 else [])
                dbuck = [dbuk] + (["AUX:2", "AUX:3"] if jj % 2 else [])
                def mmu(t, cc=cc, jj=jj, bs=bs):
                    for s4 in range(4):
                        last = t.matmul(PP[4 + s4][:],
                                        REPC[:, jj * 128:(jj + 1) * 128],
                                        xc_chunk(cc)[:, bs + s4 * 512:
                                                      bs + (s4 + 1) * 512],
                                        start=True, stop=True)
                    return last
                k.op("tensor", mmu, reads=[xck(cc), "repc"],
                     writes=["pp4", "pp5", "pp6", "pp7"])
                def dbuop(v, dbu=dbu):
                    for s4 in range(4):
                        last = v.tensor_tensor(
                            dbu[:, s4 * 512:(s4 + 1) * 512],
                            DBT[:, s4 * 512:(s4 + 1) * 512],
                            PP[4 + s4][:], ALU.mult)
                    return last
                k.op("vector", dbuop,
                     reads=["pp4", "pp5", "pp6", "pp7", "dbt"], writes=dbuck)
                if pend_mmy is not None:
                    k.op("tensor", pend_mmy[0], reads=["hc", "summ"],
                         writes=["pp0", "pp1", "pp2", "pp3"])
                    pend_mmy = None
                k.op("scalar", lambda s, acol=acol, dat=dat: s.activation(
                    dat[:, :], DBR[:, :], AF.Exp, scale=acol),
                     reads=["dbr", "acol"], writes=datck)
                ht, htk = (HT, "ht") if jj % 2 == 0 else (HT2, "ht2")
                k.op("vector", lambda v, ht=ht, dat=dat, dbu=dbu:
                     v.tensor_tensor_scan(
                         ht[:, :], dat[:, :], dbu[:, :], 0.0, ALU.mult, ALU.add),
                     reads=datck + dbuck, writes=[htk])
                k.op("gpsimd", lambda g, ht=ht: g.tensor_tensor(HC[:, :], ht[:, :],
                                                                CLT[:, :], ALU.mult),
                     reads=[htk, "clt"], writes=["hc"])
                def mmy(t, jj=jj):
                    for s4 in range(4):
                        last = t.matmul(PP[s4][:], scr[:, jj * 128:(jj + 1) * 128],
                                        HC[:, s4 * 512:(s4 + 1) * 512],
                                        start=(jj == 0), stop=(jj == 15))
                    return last
                pend_mmy = (mmy, jj)
            k.op("tensor", pend_mmy[0], reads=["hc", "summ"],
                 writes=["pp0", "pp1", "pp2", "pp3"])
            for s4 in range(4):
                zs = ZS[:, (s4 % 2) * 512:(s4 % 2) * 512 + 512]
                zsk = f"zs{s4 % 2}"
                k.dma("qg", zs, zd[cc * 128:(cc + 1) * 128,
                                   bs + s4 * 512:bs + (s4 + 1) * 512],
                      reads=["zd"], writes=[zsk], stream=f"zl{s4 % 2}")
                k.op("scalar", lambda s, zs=zs: s.activation(zs, zs, AF.Silu),
                     reads=[zsk], writes=[zsk])
                k.op("vector", lambda v, cc=cc, bs=bs, s4=s4: v.scalar_tensor_tensor(
                    HC[:, s4 * 512:(s4 + 1) * 512],
                    xc_chunk(cc)[:, bs + s4 * 512:bs + (s4 + 1) * 512],
                    DCOL[:, cc:cc + 1], PP[s4][:], ALU.mult, ALU.add),
                     reads=[xck(cc), "dcol", f"pp{s4}"], writes=["hc"])
                k.op("vector", lambda v, cc=cc, bs=bs, s4=s4, zs=zs: v.tensor_tensor(
                    xc_chunk(cc)[:, bs + s4 * 512:bs + (s4 + 1) * 512],
                    HC[:, s4 * 512:(s4 + 1) * 512], zs, ALU.mult),
                     reads=["hc", zsk], writes=[xck(cc)])

    for oc in range(DC):
        sl = oc % 2
        slab = aux[:, sl * 2048:(sl + 1) * 2048]
        slk = ["AUX:0", "AUX:2"][sl]
        k.dma("qg", slab.rearrange("p (kc m) -> p kc m", kc=16),
              ins["l2a_outprojT"][:, oc * 128:(oc + 1) * 128]
              .rearrange("(kc p) m -> p kc m", p=128),
              writes=[slk, {0: "AUX:1", 1: "AUX:3"}[sl], "cstrip", "xslab",
                      "dat2b", "dbu2b"],
              stream=f"w{sl}")
        for j in range(TJ):
            ppi = 6 + j % 2
            def mm(t, slab=slab, j=j, ppi=ppi):
                for kc in range(NCC):
                    last = t.matmul(PP[ppi][:], slab[:, kc * 128:(kc + 1) * 128],
                                    xc_chunk(kc)[:, j * 512:(j + 1) * 512],
                                    start=(kc == 0), stop=(kc == NCC - 1))
                return last
            k.op("tensor", mm, reads=[slk] + [xck(c) for c in range(16)],
                 writes=[f"pp{ppi}"])
            stage = scrf[:, (j % 2) * 512:(j % 2) * 512 + 512]
            stks = [f"SCR:{2 * (j % 2)}", f"SCR:{2 * (j % 2) + 1}"]
            k.op("vector", lambda v, stage=stage, ppi=ppi: v.tensor_copy(
                stage, PP[ppi][:]), reads=[f"pp{ppi}"], writes=stks)
            k.dma("qg", dst_dram[oc * 128:(oc + 1) * 128, j * 512:(j + 1) * 512],
                  stage, reads=stks + [dst_key(oc, j)], writes=[dst_key(oc, j)],
                  accum_op=ALU.add, stream=f"acc{j % 2}")


def mk_scan_consts():
    p = np.arange(128)
    repc = np.zeros((16, 128, 128), np.float32)
    summ = np.zeros((16, 128, 128), np.float32)
    for m in range(16):
        for pp_ in range(128):
            r = m * 8 + pp_ // 16
            repc[m, r, pp_] = 1.0
            summ[m, pp_, r] = 1.0
    repb = (p[None, :] % 16 == np.arange(16)[:, None]).astype(np.float32)
    return {"cst_repc": repc.reshape(2048, 128), "cst_summ": summ.reshape(2048, 128),
            "cst_repb": np.ascontiguousarray(repb)}


def emit_l2_ffn(k: Kfw, C: Consts, bf: Bufs, ins, dst_dram, dst_key, n_cores,
                tag="f2", sim_only=False, mm_out=None, mm_in=None):
    aux, scr = bf.aux, bf.scr
    auxf = aux[:, :].bitcast(F32)
    scrf = scr[:, :].bitcast(F32)
    b1_, b2_, b3_ = bf.big[0], bf.big[1], bf.big[2]
    PP = bf.pp
    H, DH = 16, 64
    vtd = k.dram(f"{tag}_vtd", [NT, D], BF16)
    ccin = k.dram(f"{tag}_ccin", [1, 2], F32)
    ccout = k.dram(f"{tag}_ccout", [1, 2], F32)

    SQT = aux[:, 0:1024]
    SIMR = auxf[:, 512:1536]
    POSR = aux[0:1, 3072:7168 - 3072 + 3072] if False else aux[0:1, 1024 + 2048:1024 + 2048 + 0]
    POSR = aux[0:1, 0:4096]
    B1K5 = ["vp5", "vn5", "mpc", "mnc", "posr", "mpr", "mnr", "simr", "sq5",
            "simacc", "n2r", "qgn", "ho", "thr", "pt5a", "pt5b", "rbp", "rbn",
            "xpn", "xnn", "vnh15", "vtile"]

    def vp_ap(h, kt):
        return b1_[:, h * 1040 + kt * 65:h * 1040 + (kt + 1) * 65]

    def vn_ap(h, kt):
        if h < 15:
            return b1_[:, 16640 + h * 1040 + kt * 65:16640 + h * 1040 + (kt + 1) * 65]
        return aux[:, kt * 65:(kt + 1) * 65]

    k.fence("sync", writes=[bkey(1, c) for c in range(8)] +
            [bkey(2, c) for c in range(8)] + ["SCR:0", "SCR:1", "SCR:2", "SCR:3"])
    for dst_i, base, scl in (((1, 0, 0.125),) if sim_only else
                             ((1, 0, 0.125), (2, 1024, 1.0))):
        for oc in range(DC):
            sl = oc % 2
            slab = scr[:, sl * 1024:(sl + 1) * 1024]
            slks = [f"SCR:{2 * sl}", f"SCR:{2 * sl + 1}"]
            k.dma("qg", slab.rearrange("p (kc m) -> p kc m", kc=8),
                  ins["l2f_wqkv"][:, base + oc * 128:base + (oc + 1) * 128]
                  .rearrange("(kc p) m -> p kc m", p=128),
                  writes=slks, stream=f"w{sl}")
            for j in range(TJ):
                ppi = j % 2
                def mm(t, slab=slab, j=j, ppi=ppi):
                    for kc in range(DC):
                        last = t.matmul(PP[ppi][:], slab[:, kc * 128:(kc + 1) * 128],
                                        bf.bigv(0, kc)[:, j * 512:(j + 1) * 512],
                                        start=(kc == 0), stop=(kc == DC - 1))
                    return last
                k.op("tensor", mm, reads=slks + BKALL(0), writes=[f"pp{ppi}"])
                k.op("scalar", lambda s, dst_i=dst_i, oc=oc, j=j, ppi=ppi, scl=scl:
                     s.activation(bf.bigv(dst_i, oc)[:, j * 512:(j + 1) * 512],
                                  PP[ppi][:], AF.Identity, scale=scl),
                     reads=[f"pp{ppi}", bkey(dst_i, oc)], writes=[bkey(dst_i, oc)])
    for nb in ([] if sim_only else range(4)):
        slab = scr[:, 0:2048]
        slks = ["SCR:0", "SCR:1", "SCR:2", "SCR:3"]
        k.dma("qg", slab.rearrange("p (kc m) -> p kc m", kc=8),
              ins["l2f_wqkv"][:, 2048 + nb * 256:2048 + (nb + 1) * 256]
              .rearrange("(kc p) m -> p kc m", p=128),
              writes=slks, stream="w0")
        for tc in range(NT // 128):
            ppi = 4 + tc % 2
            def mm(t, tc=tc, slab=slab, ppi=ppi):
                for kc in range(DC):
                    last = t.matmul(PP[ppi][:, 0:256],
                                    bf.bigv(0, kc)[:, tc * 128:(tc + 1) * 128],
                                    slab[:, kc * 256:(kc + 1) * 256],
                                    start=(kc == 0), stop=(kc == DC - 1))
                return last
            k.op("tensor", mm, reads=slks + BKALL(0), writes=[f"pp{ppi}"])
            stb = aux[:, 1024 + (tc % 2) * 256:1024 + (tc % 2) * 256 + 256]
            stk = f"vst{tc % 2}"
            k.op("vector", lambda v, stb=stb, ppi=ppi: v.tensor_copy(
                stb, PP[ppi][:, 0:256]), reads=[f"pp{ppi}"], writes=[stk])
            k.dma("qs", vtd[tc * 128:(tc + 1) * 128, nb * 256:(nb + 1) * 256],
                  stb, reads=[stk], writes=[f"{tag}vd"], stream=f"st{tc % 2}")

    k.fence("sync", writes=BKALL(0) + B1K5 + [A0, A1, A2, A3])
    b1f = b1_[:, :].bitcast(F32)
    HO = b1_[:, 0:2]
    QGC = b1f[:, 4:12]
    N2E = b1f[0:2, 16:24]
    QGN = b1_[:, 64:80]
    HI2 = b1_[0:2, 96:224]
    SACC = b1f[0:2, 128:640]
    RSQ = b1f[0:2, 640:1152]
    MMX = b1f[0:1, 1152:1160]
    SIMR = b1f[0:1, 2048:6144]
    SEXC = b1f[0:1, 6144:10240]

    PROWS = b1_[0:1, 20480:24576]
    MPRR = b1_[0:1, 24576:28672]
    MNRR = b1_[0:1, 28672:32768]
    SQB = aux[:, 0:512]
    posd = k.dram(f"{tag}_posd", [1, NT], BF16)

    k.op("vector", lambda v: v.tensor_scalar(HO[:, 0:1], C.chan[:], 64.0, None,
                                             ALU.is_lt), reads=["c_chan"], writes=["ho"])
    k.op("vector", lambda v: v.tensor_scalar(HO[:, 1:2], C.chan[:], 64.0, None,
                                             ALU.is_ge), reads=["c_chan"], writes=["ho"])
    pp6b = PP[6][:, :].bitcast(BF16)
    k.op("tensor", lambda t: t.transpose(pp6b[0:2, 0:128], HO[:, :], C.ident[:, :]),
         reads=["ho", "c_ident"], writes=["pp6"])
    k.op("vector", lambda v: v.tensor_copy(HI2[:, :], pp6b[0:2, 0:128]),
         reads=["pp6"], writes=["ho"])
    N2B = b1_[0:2, 2336:2344]
    k.dma("qs", QGC, ins["l2f_qg"], writes=["qgn"], stream="bias")
    k.op("scalar", lambda s: s.activation(SQB[:, 0:8], QGC, AF.Square),
         reads=["qgn"], writes=["sq5"])
    k.op("tensor", lambda t: t.matmul(PP[6][0:2, 0:8], HO[:, :], SQB[:, 0:8],
                                      start=True, stop=True),
         reads=["ho", "sq5"], writes=["pp6"])
    k.op("vector", lambda v: v.tensor_copy(N2E, PP[6][0:2, 0:8]),
         reads=["pp6"], writes=["n2r"])
    k.op("scalar", lambda s: s.activation(N2E, N2E, AF.Ln), reads=["n2r"], writes=["n2r"])
    k.op("scalar", lambda s: s.activation(N2B, N2E, AF.Exp, scale=-0.5),
         reads=["n2r"], writes=["n2r"])
    k.op("tensor", lambda t: t.matmul(PP[6][:, 0:8], HI2[:, :], N2B,
                                      start=True, stop=True),
         reads=["ho", "n2r"], writes=["pp6"])
    k.op("vector", lambda v: v.tensor_tensor(QGN[:, 0:8], QGC, PP[6][:, 0:8],
                                             ALU.mult),
         reads=["qgn", "pp6"], writes=["qgn"])
    for jt in range(TJ):
        k.op("vector", lambda v: v.memset(SACC, 0.0), writes=["simacc"])
        for hc in range(DC):
            k.op("scalar", lambda s, hc=hc, jt=jt: s.activation(
                SQB, bf.bigv(1, hc)[:, jt * 512:(jt + 1) * 512], AF.Square),
                 reads=[bkey(1, hc)], writes=["sq5"])
            k.op("tensor", lambda t: t.matmul(PP[6][0:2, :], HO[:, :], SQB,
                                              start=True, stop=True),
                 reads=["sq5", "ho"], writes=["pp6"])
            k.op("tensor", lambda t, hc=hc, jt=jt: t.matmul(
                PP[7][0:2, :], QGN[:, 2 * hc:2 * hc + 2],
                bf.bigv(1, hc)[:, jt * 512:(jt + 1) * 512], start=True, stop=True),
                 reads=["qgn", bkey(1, hc)], writes=["pp7"])
            k.op("vector", lambda v: v.tensor_copy(RSQ, PP[6][0:2, :]),
                 reads=["pp6"], writes=["n2r"])
            k.op("scalar", lambda s: s.activation(RSQ, RSQ, AF.Ln),
                 reads=["n2r"], writes=["n2r"])
            k.op("scalar", lambda s: s.activation(RSQ, RSQ, AF.Exp, scale=-0.5),
                 reads=["n2r"], writes=["n2r"])
            k.op("vector", lambda v: v.tensor_tensor(RSQ, RSQ, PP[7][0:2, :],
                                                     ALU.mult),
                 reads=["n2r", "pp7"], writes=["n2r"])
            k.op("vector", lambda v: v.tensor_add(SACC, SACC, RSQ),
                 reads=["n2r", "simacc"], writes=["simacc"])
        k.op("tensor", lambda t: t.matmul(PP[6][0:1, :], C.ones_col[0:2, :], SACC,
                                          start=True, stop=True),
             reads=["simacc", "c_onesc"], writes=["pp6"])
        k.op("vector", lambda v, jt=jt: v.tensor_scalar(
            SIMR[:, jt * 512:(jt + 1) * 512], PP[6][0:1, :], 1.0 / 16.0, None,
            ALU.mult),
             reads=["pp6"], writes=["simr"])
    for val, col in ((-3.0e38, 0), (3.0e38, 1)):
        k.op("vector", lambda v: v.tensor_copy(SEXC, SIMR),
             reads=["simr"], writes=["sexc"])
        for b in range(BPC):
            k.op("vector", lambda v, b=b, val=val: v.memset(
                SEXC[:, b * T:b * T + 5], val), reads=["sexc"], writes=["sexc"])
        if col == 0:
            k.op("vector", lambda v: v.tensor_reduce(MMX[:, 0:1], SEXC, AX.X,
                                                     ALU.max),
                 reads=["sexc"], writes=["thr"])
        else:
            k.op("vector", lambda v: v.tensor_reduce(MMX[:, 1:2], SEXC, AX.X,
                                                     ALU.max, negate=True),
                 reads=["sexc"], writes=["thr"])
    k.op("vector", lambda v: v.tensor_scalar_mul(SEXC, SEXC, -1.0),
         reads=["sexc"], writes=["sexc"])
    k.op("vector", lambda v: v.tensor_reduce(MMX[:, 1:2], SEXC, AX.X, ALU.max),
         reads=["sexc"], writes=["thr"])
    if sim_only:
        k.dma("qs", mm_out, MMX[:, 0:2], reads=["thr"], writes=["mmout"],
              stream="cc")
        return
    if mm_in is not None:
        MMI = b1f[0:1, 1176:1192]
        k.dma("qs", MMI, mm_in, writes=["ccst", "thr"], stream="cc")
        k.op("vector", lambda v: v.tensor_reduce(MMX[:, 0:1], MMI[:, 0:8],
                                                 AX.X, ALU.max),
             reads=["ccst"], writes=["thr"])
        k.op("vector", lambda v: v.tensor_reduce(MMX[:, 1:2], MMI[:, 8:16],
                                                 AX.X, ALU.max),
             reads=["ccst"], writes=["thr"])
    elif n_cores > 1:
        stb = aux[:, 512:516].bitcast(F32)[0:1, 0:2] if False else None
        k.op("vector", lambda v: v.tensor_copy(auxf[0:1, 256:258], MMX[:, 0:2]),
             reads=["thr"], writes=["ccst"])
        k.dma("qs", ccin, auxf[0:1, 256:258], reads=["ccst"], writes=["ccin"],
              stream="cc")
        def ccop(g):
            return g.collective_compute(
                "AllReduce", ALU.max,
                [[cid for cid in range(n_cores)]],
                [ccin.opt()], [ccout.opt()])
        k.qop("gpsimd", ccop, reads=["ccin"], writes=["ccout"])
        k.dma("qs", auxf[0:1, 256:258], ccout, reads=["ccout"], writes=["ccst"],
              stream="cc")
        k.op("vector", lambda v: v.tensor_copy(MMX[:, 0:2], auxf[0:1, 256:258]),
             reads=["ccst"], writes=["thr"])
    k.op("vector", lambda v: v.tensor_scalar_mul(MMX[:, 2:3], MMX[:, 0:1], 0.9),
         reads=["thr"], writes=["thr"])
    k.op("vector", lambda v: v.scalar_tensor_tensor(MMX[:, 2:3], MMX[:, 1:2],
                                                    -0.1, MMX[:, 2:3],
                                                    ALU.mult, ALU.add),
         reads=["thr"], writes=["thr"])
    k.op("vector", lambda v: v.tensor_scalar(PROWS, SIMR, MMX[0:1, 2:3], None,
                                             ALU.is_gt),
         reads=["simr", "thr"], writes=["posr"])
    k.op("vector", lambda v: v.tensor_scalar(MNRR, PROWS, -1.0, 1.0, ALU.mult,
                                             ALU.add),
         reads=["posr"], writes=["mnr"])
    k.op("vector", lambda v: v.tensor_copy(MPRR, PROWS), reads=["posr"],
         writes=["mpr"])
    for b in range(BPC):
        k.op("vector", lambda v, b=b: v.memset(MPRR[:, b * T:b * T + 5], 1.0),
             reads=["mpr"], writes=["mpr"])
        k.op("vector", lambda v, b=b: v.memset(MNRR[:, b * T:b * T + 5], 1.0),
             reads=["mnr"], writes=["mnr"])
        k.op("vector", lambda v, b=b: v.memset(PROWS[:, b * T:b * T + 5], 0.5),
             reads=["posr"], writes=["posr"])
    k.dma("qs", posd, PROWS, reads=["posr"], writes=["posd"], stream="cc")
    MPC = auxf[:, 520:552]
    MNC = auxf[:, 552:584]
    pp7bT = PP[7]
    pp7b = PP[7][:, :].bitcast(BF16)
    for g in range(8):
        def trm(t, g=g):
            for q in range(4):
                tc = g * 4 + q
                t.transpose(pp7b[:, 2 * q:2 * q + 1],
                            MPRR[:, tc * 128:(tc + 1) * 128], C.ident[0:1, 0:1])
                last = t.transpose(pp7b[:, 8 + 2 * q:9 + 2 * q],
                                   MNRR[:, tc * 128:(tc + 1) * 128],
                                   C.ident[0:1, 0:1])
            return last
        k.op("tensor", trm, reads=["mpr", "mnr", "c_ident"], writes=["pp7"])
        k.op("vector", lambda v, g=g: v.tensor_copy(
            MPC[:, g * 4:(g + 1) * 4],
            pp7b[:, 0:8].rearrange("p (a b) -> p a b", b=2)[:, :, 0:1]),
             reads=["pp7"], writes=["mpc"])
        k.op("vector", lambda v, g=g: v.tensor_copy(
            MNC[:, g * 4:(g + 1) * 4],
            pp7b[:, 8:16].rearrange("p (a b) -> p a b", b=2)[:, :, 0:1]),
             reads=["pp7"], writes=["mnc"])

    k.fence("sync", writes=["vp5", "vn5", "vnh15", "vtile", "prow", "ones64",
                            "neg10", "xpn", "xnn", "rbp", "rbn", "simr", "sexc",
                            "posr", "mpr", "mnr", "thr", "simacc", "n2r", "qgn",
                            "ho", "sq5", "ccst", "SCR:0", "SCR:1", "SCR:2",
                            "SCR:3", "vst0", "vst1"])
    ONES64 = aux[:, 3744:3808]
    NEG10 = auxf[:, 584:585]
    XPN = aux[:, 1184:1696]
    XNN = aux[:, 1696:2208]
    k.op("vector", lambda v: v.memset(ONES64[:, :], 1.0), writes=["ones64"])
    k.op("vector", lambda v: v.memset(NEG10, -10.0), writes=["neg10"])
    prow = scr[0:1, 1024:1536]
    for b in range(BPC):
        for kt in range(16):
            vt = aux[:, 2208:3232]
            k.dma("qg", vt, vtd[(b * 16 + kt) * 128:(b * 16 + kt + 1) * 128, :],
                  reads=[f"{tag}vd"], writes=["vtile"], stream="vt")
            for which, msk, mkey in (("p", MPC, "mpc"), ("n", MNC, "mnc")):
                nh = 16 if which == "p" else 15
                boff = (0 if which == "p" else 16640) + kt * 65
                def build(v, kt=kt, msk=msk, which=which, nh=nh, boff=boff, b=b):
                    col = msk[:, b * 16 + kt:b * 16 + kt + 1]
                    oap = bass.AP(b1_[:, 0:1].tensor, boff,
                                  [[32768, 128], [1040, nh], [1, 64]])
                    iap = bass.AP(vt.tensor, 2208, [[4096, 128], [64, nh], [1, 64]])
                    v.tensor_scalar(oap, iap, col, None, ALU.mult)
                    dap = bass.AP(b1_[:, 0:1].tensor, boff + 64,
                                  [[32768, 128], [1040, nh], [1, 1]])
                    return v.tensor_copy(dap, col.broadcast_to((128, nh, 1)))
                k.op("vector", build, reads=["vtile", mkey],
                     writes=["vp5" if which == "p" else "vn5"])
            def buildn15(v, b=b, kt=kt):
                col = MNC[:, b * 16 + kt:b * 16 + kt + 1]
                v.tensor_scalar(aux[:, kt * 65:kt * 65 + 64], vt[:, 15 * 64:16 * 64],
                                col, None, ALU.mult)
                return v.tensor_copy(aux[:, kt * 65 + 64:kt * 65 + 65], col)
            k.op("vector", buildn15, reads=["vtile", "mnc"], writes=["vnh15"])
        for qg in range(4):
            q0 = b * T + qg * 512
            k.dma("qg", prow, posd[0:1, q0:q0 + 512], reads=["posd"],
                  writes=["prow"], stream="pr")
            k.op("tensor", lambda t: t.matmul(PP[6][0:64, :], ONES64[0:1, :],
                                              prow, start=True, stop=True),
                 reads=["prow", "ones64"], writes=["pp6"])
            for h in range(16):
                hc, rb = h // 2, (h % 2) * 64
                pend_pv = None
                for kt in range(16):
                    k0 = b * T + kt * 128
                    ps, psk = PP[kt % 2], f"pp{kt % 2}"
                    k.op("tensor", lambda t, ps=ps, hc=hc, rb=rb, k0=k0, q0=q0:
                         t.matmul(ps[:], bf.bigv(2, hc)[rb:rb + 64, k0:k0 + 128],
                                  bf.bigv(1, hc)[rb:rb + 64, q0:q0 + 512],
                                  start=True, stop=True),
                         reads=[bkey(2, hc), bkey(1, hc)], writes=[psk])
                    if pend_pv is not None:
                        pvp, pvn, pptk = pend_pv
                        k.op("tensor", pvp, reads=[pptk, "vp5"], writes=["pp2"])
                        k.op("tensor", pvn, reads=[pptk, "vn5", "vnh15"],
                             writes=["pp3"])
                    pt = scr[:, (kt % 2) * 512:(kt % 2) * 512 + 512]
                    ptk = f"SCR:{kt % 2}"
                    k.op("scalar", lambda s, pt=pt, ps=ps: s.activation(
                        pt, ps[:], AF.Exp, bias=NEG10),
                         reads=[psk, "neg10"], writes=[ptk])
                    def pvp(t, pt=pt, h=h, kt=kt):
                        return t.matmul(PP[2][0:65, :], vp_ap(h, kt), pt,
                                        start=(kt == 0), stop=(kt == 15))
                    def pvn(t, pt=pt, h=h, kt=kt):
                        return t.matmul(PP[3][0:65, :], vn_ap(h, kt), pt,
                                        start=(kt == 0), stop=(kt == 15))
                    pend_pv = (pvp, pvn, ptk)
                pvp, pvn, pptk = pend_pv
                k.op("tensor", pvp, reads=[pptk, "vp5"], writes=["pp2"])
                k.op("tensor", pvn, reads=[pptk, "vn5", "vnh15"], writes=["pp3"])
                rbp = scr[64:65, 1536:2048]
                rbn = aux[64:65, 3232:3744]
                def recp(v, rbp=rbp):
                    with k.nc.allow_low_precision(
                            reason="bf16 recip of softmax den; P already bf16"):
                        return v.reciprocal(rbp, PP[2][64:65, :])
                def recn(v, rbn=rbn):
                    with k.nc.allow_low_precision(
                            reason="bf16 recip of softmax den; P already bf16"):
                        return v.reciprocal(rbn, PP[3][64:65, :])
                k.op("vector", recp, reads=["pp2"], writes=["rbp", "SCR:3"])
                k.op("vector", recn, reads=["pp3"], writes=["rbn"])
                k.op("tensor", lambda t, rbp=rbp: t.matmul(
                    PP[4][0:64, :], ONES64[64:65, :], rbp, start=True, stop=True,
                    tile_position=(64, 0)),
                     reads=["rbp", "SCR:3", "ones64"], writes=["pp4"])
                k.op("tensor", lambda t, rbn=rbn: t.matmul(
                    PP[5][0:64, :], ONES64[64:65, :], rbn, start=True, stop=True,
                    tile_position=(64, 0)),
                     reads=["rbn", "ones64"], writes=["pp5"])
                k.op("vector", lambda v: v.tensor_copy(scr[0:64, 0:512],
                                                       PP[4][0:64, :]),
                     reads=["pp4"], writes=["SCR:0"])
                k.op("vector", lambda v: v.tensor_copy(scr[0:64, 512:1024],
                                                       PP[5][0:64, :]),
                     reads=["pp5"], writes=["SCR:1"])
                k.op("vector", lambda v: v.tensor_tensor(
                    XPN[0:64, :], PP[2][0:64, :], scr[0:64, 0:512], ALU.mult),
                     reads=["pp2", "SCR:0"], writes=["xpn"])
                k.op("vector", lambda v: v.tensor_tensor(
                    XNN[0:64, :], PP[3][0:64, :], scr[0:64, 512:1024], ALU.mult),
                     reads=["pp3", "SCR:1"], writes=["xnn"])
                k.op("vector", lambda v: v.tensor_sub(XPN[0:64, :], XPN[0:64, :],
                                                      XNN[0:64, :]),
                     reads=["xpn", "xnn"], writes=["xpn"])
                k.op("vector", lambda v: v.tensor_tensor(
                    XPN[0:64, :], XPN[0:64, :], PP[6][0:64, :], ALU.mult),
                     reads=["xpn", "pp6"], writes=["xpn"])
                k.op("vector", lambda v: v.tensor_add(XPN[0:64, :], XPN[0:64, :],
                                                      XNN[0:64, :]),
                     reads=["xpn", "xnn"], writes=["xpn"])
                okey = f"ao5/{hc}/{h % 2}/{b}/{qg}"
                if rb == 0:
                    k.op("vector", lambda v, hc=hc, q0=q0: v.tensor_copy(
                        bf.bigv(1, hc)[0:64, q0:q0 + 512], XPN[0:64, :]),
                         reads=["xpn", bkey(1, hc)], writes=[okey])
                else:
                    k.op("tensor", lambda t: t.matmul(
                        PP[7][64:128, :], C.ident[0:64, 0:64], XPN[0:64, :],
                        start=True, stop=True, tile_position=(0, 64)),
                         reads=["xpn", "c_ident"], writes=["pp7"])
                    k.op("vector", lambda v, hc=hc, q0=q0: v.tensor_copy(
                        bf.bigv(1, hc)[64:128, q0:q0 + 512], PP[7][64:128, :]),
                         reads=["pp7", bkey(1, hc)], writes=[okey])
    k.fence("sync", reads=[f"ao5/{hc}/{hf}/{b}/{qg}" for hc in range(8)
                           for hf in range(2) for b in range(2) for qg in range(4)],
            writes=[bkey(1, c) for c in range(8)])

    BPJ = auxf[:, 326:334]
    k.dma("qs", BPJ, ins["l2f_bproj"], writes=["bpj", "vnh15"], stream="bias")
    for oc in range(DC):
        sl = oc % 2
        slab = scr[:, sl * 1024:(sl + 1) * 1024]
        slks = [f"SCR:{2 * sl}", f"SCR:{2 * sl + 1}"]
        k.dma("qg", slab.rearrange("p (kc m) -> p kc m", kc=8),
              ins["l2f_wproj"][:, oc * 128:(oc + 1) * 128]
              .rearrange("(kc p) m -> p kc m", p=128),
              writes=slks, stream=f"w{sl}")
        for j in range(TJ):
            ppi = j % 2
            def mm(t, slab=slab, j=j, ppi=ppi):
                for kc in range(DC):
                    last = t.matmul(PP[ppi][:], slab[:, kc * 128:(kc + 1) * 128],
                                    bf.bigv(1, kc)[:, j * 512:(j + 1) * 512],
                                    start=(kc == 0), stop=(kc == DC - 1))
                return last
            k.op("tensor", mm, reads=slks + [bkey(1, c) for c in range(8)],
                 writes=[f"pp{ppi}"])
            stage = aux[:, 1184 + (j % 2) * 1024:1184 + (j % 2) * 1024 + 1024].bitcast(F32)
            stk = f"stg5{j % 2}"
            k.op("scalar", lambda s, stage=stage, oc=oc, ppi=ppi: s.activation(
                stage, PP[ppi][:], AF.Identity, bias=BPJ[:, oc:oc + 1]),
                 reads=[f"pp{ppi}", "bpj"], writes=[stk, "xpn", "xnn", "vtile"])
            k.dma("qg", dst_dram[oc * 128:(oc + 1) * 128, j * 512:(j + 1) * 512],
                  stage, reads=[stk, dst_key(oc, j)], writes=[dst_key(oc, j)],
                  accum_op=ALU.add, stream=f"acc{j % 2}")


def emit_l3_attn(k: Kfw, C: Consts, bf: Bufs, ins, dst_dram, dst_key, tag="a3"):
    aux, scr = bf.aux, bf.scr
    scrf = scr[:, :].bitcast(F32)
    b1_, b2_, b3_ = bf.big[0], bf.big[1], bf.big[2]
    b1f = b1_[:, :].bitcast(F32)
    PP = bf.pp
    DI, NCC = 2048, 16
    zd = k.dram(f"{tag}_zd", [DI, NT], BF16)
    deltad = k.dram(f"{tag}_dd", [DI, NT], BF16)

    def xc_chunk(cc):
        return (bf.bigv(1, cc) if cc < 8 else bf.bigv(2, cc - 8))

    def xck(cc):
        return bkey(1, cc) if cc < 8 else bkey(2, cc - 8)

    DTR = b1_[0:64, 0:4096]
    DTW = b1_[0:64, 4096:6144]
    BBF = b1_[64:80, 0:4096]
    CBF = b1_[64:80, 4096:8192]
    DA_T = b1_[:, 8192:10240]
    DBU = b1_[:, 10240:12288]
    DBU3B = bf.aux[:, 0:2048]
    HT = b1_[:, 12288:14336]
    HC3 = b1_[:, 14336:16384]
    DA3B = b1_[:, 16384:18432]
    CLT = b1_[:, 18432:20480]
    BLT = b1_[:, 20480:22528]
    REPC3 = b1_[:, 22528:24576]
    UD = b1_[:, 24576:26624]
    DCS = b1_[:, 26624:28672]
    ZS3 = b1_[:, 28672:29696]
    HT3B = b1_[:, 30560:32608]
    REPB3 = b1_[0:80, 29696:29824]
    CW3 = b1f[:, 14912:14976]
    CB3 = b1f[:, 14976:14992]
    DTB = b1f[:, 14992:15008]
    DCOL3 = b1f[:, 15008:15024]
    ACOL3 = b1f[:, 15024:15280]
    B1K3 = ["dtr", "bbf3", "cbf3", "dat3", "dat3b", "dbu3", "ht3", "ht3b",
            "hc3", "clt3", "blt3", "repc3", "dtw", "ud", "dcs", "zs3",
            "repb3", "cw3", "cb3", "dtb", "dcol3", "acol3",
            "xslab3", "dstg0", "dstg1", "ab0", "ab1"]

    k.fence("sync", writes=[bkey(1, c) for c in range(8)] +
            [bkey(2, c) for c in range(8)] + [A0, A1, A2, A3])
    for oc in range(32):
        sl = oc % 2
        slab = scr[:, sl * 1024:(sl + 1) * 1024]
        slks = [f"SCR:{2 * sl}", f"SCR:{2 * sl + 1}"]
        k.dma("qg", slab.rearrange("p (kc m) -> p kc m", kc=8),
              ins["l3a_inprojT"][:, oc * 128:(oc + 1) * 128]
              .rearrange("(kc p) m -> p kc m", p=128),
              writes=slks, stream=f"w{sl}")
        for j in range(TJ):
            ppi = j % 2
            def mm(t, slab=slab, j=j, ppi=ppi):
                for kc in range(DC):
                    last = t.matmul(PP[ppi][:], slab[:, kc * 128:(kc + 1) * 128],
                                    bf.bigv(0, kc)[:, j * 512:(j + 1) * 512],
                                    start=(kc == 0), stop=(kc == DC - 1))
                return last
            k.op("tensor", mm, reads=slks + BKALL(0), writes=[f"pp{ppi}"])
            if oc < 16:
                k.op("vector", lambda v, oc=oc, j=j, ppi=ppi: v.tensor_copy(
                    xc_chunk(oc)[:, j * 512:(j + 1) * 512], PP[ppi][:]),
                     reads=[f"pp{ppi}"], writes=[xck(oc)])
            else:
                st_bf = aux[:, (j % 2) * 512:(j % 2) * 512 + 512]
                stk = ["zst0", "zst1"][j % 2]
                k.op("vector", lambda v, st_bf=st_bf, ppi=ppi: v.tensor_copy(
                    st_bf, PP[ppi][:]), reads=[f"pp{ppi}"], writes=[stk, "AUX:0"])
                k.dma("qs", zd[(oc - 16) * 128:(oc - 15) * 128,
                               j * 512:(j + 1) * 512],
                      st_bf, reads=[stk], writes=["zd3"], stream=f"st{j % 2}")

    k.fence("sync", writes=BKALL(0) + B1K3 + ["zst0", "zst1", "SCR:0", "SCR:1",
                                              "SCR:2", "SCR:3"])
    k.dma("qg", REPC3.rearrange("p (m f) -> p m f", m=16),
          ins["cst_repc"].rearrange("(m p) f -> p m f", p=128),
          writes=["repc3"], stream="cst")
    k.dma("qg", REPB3[64:80, :], ins["cst_repb"], writes=["repb3"], stream="cst")
    k.dma("qs", CW3, ins["l3a_convw"], writes=["cw3"], stream="cst2")
    k.dma("qs", CB3, ins["l3a_convb"], writes=["cb3"], stream="cst2")
    k.dma("qs", DTB, ins["l3a_dtb"], writes=["dtb"], stream="cst2")
    k.dma("qs", DCOL3, ins["l3a_dcol"], writes=["dcol3"], stream="cst2")
    k.dma("qs", ACOL3, ins["l3a_alog"], writes=["acol3"], stream="cst2")
    k.dma("qg", DTW, ins["l3a_dtprojT"], writes=["dtw"], stream="cst")
    k.dma("qg", scr[:, :].rearrange("p (m f) -> p m f", m=16),
          ins["cst_summ"].rearrange("(m p) f -> p m f", p=128),
          writes=["summ3", "SCR:0", "SCR:1", "SCR:2", "SCR:3"], stream="cst")
    k.op("scalar", lambda s: s.activation(ACOL3, ACOL3, AF.Exp),
         reads=["acol3"], writes=["acol3"])
    k.op("vector", lambda v: v.tensor_scalar_mul(ACOL3, ACOL3, -1.0),
         reads=["acol3"], writes=["acol3"])
    for cc in range(NCC):
        for b in range(BPC):
            strip = xc_chunk(cc)[:, b * T:(b + 1) * T]
            astrip = aux[:, 0:T]
            k.op("vector", lambda v, strip=strip, astrip=astrip, cc=cc: v.tensor_scalar(
                astrip, strip, CW3[:, cc * 4 + 3:cc * 4 + 4], None, ALU.mult),
                 reads=[xck(cc), "cw3"], writes=["cstrip3", "AUX:0", "AUX:1"])
            for kk in range(3):
                sh = 3 - kk
                k.op("vector", lambda v, strip=strip, astrip=astrip, cc=cc, kk=kk,
                     sh=sh: v.scalar_tensor_tensor(
                         astrip[:, sh:T], strip[:, 0:T - sh],
                         CW3[:, cc * 4 + kk:cc * 4 + kk + 1],
                         astrip[:, sh:T], ALU.mult, ALU.add),
                     reads=[xck(cc), "cw3", "cstrip3"], writes=["cstrip3"])
            k.op("scalar", lambda s, strip=strip, astrip=astrip, cc=cc: s.activation(
                strip, astrip, AF.Silu, bias=CB3[:, cc:cc + 1]),
                 reads=["cstrip3", "cb3"], writes=[xck(cc)])

    for j in range(TJ):
        xslab = aux[:, 0:1536]
        if j == 0:
            k.dma("qg", xslab.rearrange("p (kc m) -> p kc m", kc=16),
                  ins["l3a_xprojT"].rearrange("(kc p) m -> p kc m", p=128),
                  writes=["xslab3", "cstrip3", "AUX:0", "AUX:1"], stream="w0")
        def mmx(t, j=j, xslab=xslab):
            for kc in range(NCC):
                t.matmul(PP[4][0:64, :], xslab[:, kc * 96 + 0:kc * 96 + 64],
                         xc_chunk(kc)[:, j * 512:(j + 1) * 512],
                         start=(kc == 0), stop=(kc == NCC - 1))
            for kc in range(NCC):
                t.matmul(PP[4][64:80, :], xslab[:, kc * 96 + 64:kc * 96 + 80],
                         xc_chunk(kc)[:, j * 512:(j + 1) * 512],
                         start=(kc == 0), stop=(kc == NCC - 1),
                         tile_position=(0, 64))
            for kc in range(NCC):
                last = t.matmul(PP[5][64:80, :], xslab[:, kc * 96 + 80:kc * 96 + 96],
                                xc_chunk(kc)[:, j * 512:(j + 1) * 512],
                                start=(kc == 0), stop=(kc == NCC - 1),
                                tile_position=(0, 64))
            return last
        k.op("tensor", mmx, reads=["xslab3"] + [xck(c) for c in range(16)],
             writes=["pp4", "pp5"])
        k.op("vector", lambda v, j=j: v.tensor_copy(DTR[:, j * 512:(j + 1) * 512],
                                                    PP[4][0:64, :]),
             reads=["pp4"], writes=["dtr"])
        k.op("vector", lambda v, j=j: v.tensor_copy(BBF[:, j * 512:(j + 1) * 512],
                                                    PP[4][64:80, :]),
             reads=["pp4"], writes=["bbf3"])
        k.op("vector", lambda v, j=j: v.tensor_copy(CBF[:, j * 512:(j + 1) * 512],
                                                    PP[5][64:80, :]),
             reads=["pp5"], writes=["cbf3"])

    for cc in range(NCC):
        for j in range(TJ):
            ppi = j % 2
            k.op("tensor", lambda t, cc=cc, j=j, ppi=ppi: t.matmul(
                PP[ppi][:], DTW[:, cc * 128:(cc + 1) * 128],
                DTR[:, j * 512:(j + 1) * 512], start=True, stop=True),
                 reads=["dtw", "dtr"], writes=[f"pp{ppi}"])
            t_re = scrf[:, (j % 2) * 256:(j % 2) * 256 + 256] if False else None
            t_ab = aux[:, 1536 + (j % 2) * 512:1536 + (j % 2) * 512 + 512]
            stg = aux[:, 2560 + (j % 2) * 512:2560 + (j % 2) * 512 + 512]
            abk = ["ab0", "ab1"][j % 2]
            stk = ["dstg0", "dstg1"][j % 2]
            k.op("scalar", lambda s, t_ab=t_ab, cc=cc, ppi=ppi: s.activation(
                t_ab, PP[ppi][:], AF.Abs, bias=DTB[:, cc:cc + 1]),
                 reads=[f"pp{ppi}", "dtb"], writes=[abk, "AUX:1", "AUX:2"])
            k.op("scalar", lambda s, t_ab=t_ab: s.activation(t_ab, t_ab, AF.Exp,
                                                             scale=-1.0),
                 reads=[abk], writes=[abk])
            k.op("vector", lambda v, t_ab=t_ab: v.tensor_scalar_add(t_ab, t_ab, 1.0),
                 reads=[abk], writes=[abk])
            k.op("scalar", lambda s, t_ab=t_ab: s.activation(t_ab, t_ab, AF.Ln),
                 reads=[abk], writes=[abk])
            k.op("vector", lambda v, stg=stg, cc=cc, ppi=ppi: v.tensor_scalar(
                stg, PP[ppi][:], DTB[:, cc:cc + 1], 0.0, ALU.add, ALU.max),
                 reads=[f"pp{ppi}", "dtb"], writes=[stk, "AUX:2", "AUX:3"])
            k.op("vector", lambda v, stg=stg, t_ab=t_ab: v.tensor_add(stg, stg, t_ab),
                 reads=[stk, abk], writes=[stk])
            k.dma("qs", deltad[cc * 128:(cc + 1) * 128, j * 512:(j + 1) * 512],
                  stg, reads=[stk], writes=["dd3"], stream=f"st{j % 2}")

    for b in range(BPC):
        bs = b * T
        for s4 in range(4):
            k.op("tensor", lambda t, bs=bs, s4=s4: t.matmul(
                PP[4][:], REPB3[64:80, :],
                BBF[:, bs + s4 * 512:bs + (s4 + 1) * 512], start=True, stop=True,
                tile_position=(64, 0)),
                 reads=["bbf3", "repb3"], writes=["pp4"])
            k.op("vector", lambda v, s4=s4: v.tensor_copy(
                BLT[:, s4 * 512:(s4 + 1) * 512], PP[4][:]),
                 reads=["pp4"], writes=["blt3"])
            k.op("tensor", lambda t, bs=bs, s4=s4: t.matmul(
                PP[5][:], REPB3[64:80, :],
                CBF[:, bs + s4 * 512:bs + (s4 + 1) * 512], start=True, stop=True,
                tile_position=(64, 0)),
                 reads=["cbf3", "repb3"], writes=["pp5"])
            k.op("vector", lambda v, s4=s4: v.tensor_copy(
                CLT[:, s4 * 512:(s4 + 1) * 512], PP[5][:]),
                 reads=["pp5"], writes=["clt3"])
        for cc in range(NCC):
            k.dma("qg", DCS, deltad[cc * 128:(cc + 1) * 128, bs:bs + T],
                  reads=["dd3"], writes=["dcs"], stream="dl")
            k.op("vector", lambda v, cc=cc, bs=bs: v.tensor_tensor(
                UD[:, :], DCS[:, :], xc_chunk(cc)[:, bs:bs + T], ALU.mult),
                 reads=["dcs", xck(cc)], writes=["ud"])
            pend_mmy = None
            for jj in range(16):
                lt = cc * 16 + jj
                acol = ACOL3[:, lt:lt + 1]
                pb = 4 if jj % 2 == 0 else 6
                db = 6 if jj % 2 == 0 else 4
                dat, datk = (DA_T, "dat3") if jj % 2 == 0 else (DA3B, "dat3b")
                dbu, dbuk = (DBU, "dbu3") if jj % 2 == 0 else (DBU3B, "dbu3b")
                dbuck = [dbuk] + (["AUX:0", "AUX:1"] if jj % 2 else [])
                for half in range(2):
                    s4s = (0, 1) if half == 0 else (2, 3)
                    def mmd(t, jj=jj, db=db, s4s=s4s):
                        for s4 in s4s:
                            last = t.matmul(PP[db + s4 % 2][:],
                                            REPC3[:, jj * 128:(jj + 1) * 128],
                                            DCS[:, s4 * 512:(s4 + 1) * 512],
                                            start=True, stop=True)
                        return last
                    k.op("tensor", mmd, reads=["dcs", "repc3"],
                         writes=[f"pp{db}", f"pp{db + 1}"])
                    def mmu(t, jj=jj, pb=pb, s4s=s4s):
                        for s4 in s4s:
                            last = t.matmul(PP[pb + s4 % 2][:],
                                            REPC3[:, jj * 128:(jj + 1) * 128],
                                            UD[:, s4 * 512:(s4 + 1) * 512],
                                            start=True, stop=True)
                        return last
                    k.op("tensor", mmu, reads=["ud", "repc3"],
                         writes=[f"pp{pb}", f"pp{pb + 1}"])
                    def ada(s, db=db, acol=acol, dat=dat, s4s=s4s):
                        for s4 in s4s:
                            last = s.activation(dat[:, s4 * 512:(s4 + 1) * 512],
                                                PP[db + s4 % 2][:], AF.Exp,
                                                scale=acol)
                        return last
                    k.op("scalar", ada, reads=[f"pp{db}", f"pp{db + 1}", "acol3"],
                         writes=[datk])
                    def dbuop(v, pb=pb, dbu=dbu, s4s=s4s):
                        for s4 in s4s:
                            last = v.tensor_tensor(
                                dbu[:, s4 * 512:(s4 + 1) * 512],
                                BLT[:, s4 * 512:(s4 + 1) * 512],
                                PP[pb + s4 % 2][:], ALU.mult)
                        return last
                    k.op("vector", dbuop, reads=[f"pp{pb}", f"pp{pb + 1}", "blt3"],
                         writes=dbuck)
                if pend_mmy is not None:
                    k.op("tensor", pend_mmy[0], reads=["hc3", "summ3"],
                         writes=["pp0", "pp1", "pp2", "pp3"])
                    pend_mmy = None
                ht, htk = (HT, "ht3") if jj % 2 == 0 else (HT3B, "ht3b")
                k.op("vector", lambda v, ht=ht, dat=dat, dbu=dbu:
                     v.tensor_tensor_scan(
                         ht[:, :], dat[:, :], dbu[:, :], 0.0, ALU.mult, ALU.add),
                     reads=[datk] + dbuck, writes=[htk])
                k.op("gpsimd", lambda g, ht=ht: g.tensor_tensor(HC3[:, :], ht[:, :],
                                                                CLT[:, :], ALU.mult),
                     reads=[htk, "clt3"], writes=["hc3"])
                def mmy(t, jj=jj):
                    for s4 in range(4):
                        last = t.matmul(PP[s4][:], scr[:, jj * 128:(jj + 1) * 128],
                                        HC3[:, s4 * 512:(s4 + 1) * 512],
                                        start=(jj == 0), stop=(jj == 15))
                    return last
                pend_mmy = (mmy, jj)
            k.op("tensor", pend_mmy[0], reads=["hc3", "summ3"],
                 writes=["pp0", "pp1", "pp2", "pp3"])
            for s4 in range(4):
                zs = ZS3[:, (s4 % 2) * 512:(s4 % 2) * 512 + 512]
                zsk = f"zs3{s4 % 2}"
                k.dma("qg", zs, zd[cc * 128:(cc + 1) * 128,
                                   bs + s4 * 512:bs + (s4 + 1) * 512],
                      reads=["zd3"], writes=[zsk, "zs3"], stream=f"zl{s4 % 2}")
                k.op("scalar", lambda s, zs=zs: s.activation(zs, zs, AF.Silu),
                     reads=[zsk], writes=[zsk])
                k.op("vector", lambda v, cc=cc, bs=bs, s4=s4: v.scalar_tensor_tensor(
                    HC3[:, s4 * 512:(s4 + 1) * 512],
                    xc_chunk(cc)[:, bs + s4 * 512:bs + (s4 + 1) * 512],
                    DCOL3[:, cc:cc + 1], PP[s4][:], ALU.mult, ALU.add),
                     reads=[xck(cc), "dcol3", f"pp{s4}"], writes=["hc3"])
                k.op("vector", lambda v, cc=cc, bs=bs, s4=s4, zs=zs: v.tensor_tensor(
                    xc_chunk(cc)[:, bs + s4 * 512:bs + (s4 + 1) * 512],
                    HC3[:, s4 * 512:(s4 + 1) * 512], zs, ALU.mult),
                     reads=["hc3", zsk], writes=[xck(cc)])

    for oc in range(DC):
        sl = oc % 2
        slab = aux[:, sl * 2048:(sl + 1) * 2048]
        k.dma("qg", slab.rearrange("p (kc m) -> p kc m", kc=16),
              ins["l3a_outprojT"][:, oc * 128:(oc + 1) * 128]
              .rearrange("(kc p) m -> p kc m", p=128),
              writes=[["AUX:0", "AUX:1"], ["AUX:2", "AUX:3"]][sl] +
                     ["cstrip3", "xslab3", "ab0", "ab1", "dstg0", "dstg1",
                      "dbu3b"],
              stream=f"w{sl}")
        for j in range(TJ):
            ppi = 6 + j % 2
            def mm(t, slab=slab, j=j, ppi=ppi):
                for kc in range(NCC):
                    last = t.matmul(PP[ppi][:], slab[:, kc * 128:(kc + 1) * 128],
                                    xc_chunk(kc)[:, j * 512:(j + 1) * 512],
                                    start=(kc == 0), stop=(kc == NCC - 1))
                return last
            k.op("tensor", mm, reads=[["AUX:0", "AUX:1"], ["AUX:2", "AUX:3"]][sl] +
                 [xck(c) for c in range(16)], writes=[f"pp{ppi}"])
            stage = scrf[:, (j % 2) * 512:(j % 2) * 512 + 512]
            stks = [f"SCR:{2 * (j % 2)}", f"SCR:{2 * (j % 2) + 1}", "summ3"]
            k.op("vector", lambda v, stage=stage, ppi=ppi: v.tensor_copy(
                stage, PP[ppi][:]), reads=[f"pp{ppi}"], writes=stks)
            k.dma("qg", dst_dram[oc * 128:(oc + 1) * 128, j * 512:(j + 1) * 512],
                  stage, reads=stks + [dst_key(oc, j)], writes=[dst_key(oc, j)],
                  accum_op=ALU.add, stream=f"acc{j % 2}")


def emit_l3_ffn(k: Kfw, C: Consts, bf: Bufs, ins, dst_dram, dst_key, tag="f3"):
    aux, scr = bf.aux, bf.scr
    b1_, b2_, b3_ = bf.big[0], bf.big[1], bf.big[2]
    b1f = b1_[:, :].bitcast(F32)
    PP = bf.pp
    B1K7 = ["sr", "wsr", "bis3", "wrc", "b1c7", "b2c7", "h7",
            "u7", "s17", "stg7", "xstg0", "xstg1"]
    SR = b3_[:, :].bitcast(F32)[0:1, 8192:12288]
    WSR = b3_[:, :].bitcast(F32)[0:1, 12288:16384]
    BIS3 = aux[:, 3168:3200].bitcast(F32)[0:1, :]
    LO3, HI3, MID3 = BIS3[:, 0:1], BIS3[:, 1:2], BIS3[:, 2:3]
    CNT3, SEL3, NS3, T13 = (BIS3[:, 3:4], BIS3[:, 4:5], BIS3[:, 5:6],
                            BIS3[:, 6:7])
    WRC = aux[:, 3200:3216].bitcast(F32)
    B1C = aux[:, 3072:3136].bitcast(F32)
    B2C = aux[:, 3136:3152].bitcast(F32)
    WRCB = aux[:, 3152:3160]
    MSCR = aux[:, 3232:4096].bitcast(F32)[0:1, 0:432]

    k.fence("sync", writes=[bkey(1, c) for c in range(8)] +
            [bkey(2, c) for c in range(8)] + [A0, A1, A2, A3] + B1K7 +
            ["SCR:0", "SCR:1", "SCR:2", "SCR:3"])
    k.dma("qs", WRC, ins["l3f_wrc"], writes=["wrc"], stream="bias")
    k.dma("qs", B1C, ins["l3f_b1c"], writes=["b1c7"], stream="bias")
    k.dma("qs", B2C, ins["l3f_b2c"], writes=["b2c7"], stream="bias")
    k.op("vector", lambda v: v.tensor_copy(WRCB, WRC), reads=["wrc", "b1c7",
                                                              "b2c7"],
         writes=["wrc"])
    for j in range(TJ):
        def mms(t, j=j):
            for dc in range(DC):
                last = t.matmul(PP[6][0:1, :], WRCB[:, dc:dc + 1],
                                bf.bigv(0, dc)[:, j * 512:(j + 1) * 512],
                                start=(dc == 0), stop=(dc == DC - 1))
            return last
        k.op("tensor", mms, reads=["wrc"] + BKALL(0), writes=["pp6"])
        k.op("vector", lambda v, j=j: v.tensor_copy(SR[:, j * 512:(j + 1) * 512],
                                                    PP[6][0:1, :]),
             reads=["pp6"], writes=["sr"])
    for b in range(BPC):
        srb = SR[:, b * T:(b + 1) * T]
        k.op("vector", lambda v, srb=srb: v.tensor_reduce(LO3, srb, AX.X, ALU.min),
             reads=["sr"], writes=["bis3"])
        k.op("vector", lambda v, srb=srb: v.tensor_reduce(HI3, srb, AX.X, ALU.max),
             reads=["sr"], writes=["bis3"])
        for it in range(45):
            k.op("vector", lambda v: v.tensor_add(MID3, LO3, HI3),
                 reads=["bis3"], writes=["bis3"])
            k.op("vector", lambda v: v.tensor_scalar_mul(MID3, MID3, 0.5),
                 reads=["bis3"], writes=["bis3"])
            k.op("vector", lambda v, srb=srb: v.tensor_scalar(
                WSR[:, 0:2048], srb, MID3, None, ALU.is_gt, op1=ALU.add,
                accum_out=CNT3),
                 reads=["sr", "bis3"], writes=["wsr", "bis3"])
            k.op("vector", lambda v: v.tensor_scalar(SEL3, CNT3, 1024.0, None,
                                                     ALU.is_ge),
                 reads=["bis3"], writes=["bis3"])
            k.op("vector", lambda v: v.tensor_sub(T13, MID3, LO3),
                 reads=["bis3"], writes=["bis3"])
            k.op("vector", lambda v: v.tensor_mul(T13, T13, SEL3),
                 reads=["bis3"], writes=["bis3"])
            k.op("vector", lambda v: v.tensor_add(LO3, LO3, T13),
                 reads=["bis3"], writes=["bis3"])
            k.op("vector", lambda v: v.tensor_scalar(NS3, SEL3, -1.0, 1.0,
                                                     ALU.mult, ALU.add),
                 reads=["bis3"], writes=["bis3"])
            k.op("vector", lambda v: v.tensor_sub(T13, MID3, HI3),
                 reads=["bis3"], writes=["bis3"])
            k.op("vector", lambda v: v.tensor_mul(T13, T13, NS3),
                 reads=["bis3"], writes=["bis3"])
            k.op("vector", lambda v: v.tensor_add(HI3, HI3, T13),
                 reads=["bis3"], writes=["bis3"])
        k.op("vector", lambda v, b=b: v.tensor_copy(
            BIS3[:, 8 + b:9 + b], LO3), reads=["bis3"], writes=["bis3"])
    k.op("scalar", lambda s: s.activation(WSR, SR, AF.Sigmoid),
         reads=["sr"], writes=["wsr"])
    for b in range(BPC):
        k.op("vector", lambda v, b=b: v.tensor_scalar(
            SR[:, b * T:(b + 1) * T], SR[:, b * T:(b + 1) * T],
            BIS3[0:1, 8 + b:9 + b], None, ALU.is_gt),
             reads=["sr", "bis3"], writes=["sr"])
    k.op("vector", lambda v: v.tensor_tensor(WSR, WSR, SR, ALU.mult),
         reads=["wsr", "sr"], writes=["wsr"])
    for c in range(DC):
        for j in range(TJ):
            stg = aux[:, (j % 2) * 1024:(j % 2) * 1024 + 1024].bitcast(F32)
            stk = [f"xstg{j % 2}", f"AUX:{j % 2}"]
            k.op("vector", lambda v, c=c, j=j, stg=stg: v.tensor_copy(
                stg, bf.bigv(0, c)[:, j * 512:(j + 1) * 512]),
                 reads=[bkey(0, c)], writes=stk)
            k.dma("qg", dst_dram[c * 128:(c + 1) * 128, j * 512:(j + 1) * 512],
                  stg, reads=stk + [dst_key(c, j)], writes=[dst_key(c, j)],
                  accum_op=ALU.add, stream=f"acc{j % 2}")
    k.dma("qg", b2_[:, :].rearrange("p (dc i) -> p dc i", dc=8),
          ins["l3f_w1"].rearrange("(dc p) i -> p dc i", p=128),
          writes=[bkey(1, c) for c in range(8)], stream="w1")
    k.fence("sync", writes=["h7", "xstg0", "xstg1", "u7", "s17", "stg7",
                             "AUX:0", "AUX:1", "AUX:2", "AUX:3"])
    GC1, GC2 = 0.7978845608028654, 0.044715
    U7 = aux[:, 0:1024].bitcast(F32)
    S17 = aux[:, 1024:2048].bitcast(F32)
    for tb in range(TJ):
        for ic in range(32):
            ph, phk = PP[4 + ic % 2], f"pp{4 + ic % 2}"
            def mmh(t, ic=ic, tb=tb, ph=ph):
                for dc in range(DC):
                    last = t.matmul(ph[:], b2_[:, dc * 4096 + ic * 128:
                                               dc * 4096 + (ic + 1) * 128],
                                    bf.bigv(0, dc)[:, tb * 512:(tb + 1) * 512],
                                    start=(dc == 0), stop=(dc == DC - 1))
                return last
            k.op("tensor", mmh, reads=[bkey(1, c) for c in range(8)] + BKALL(0),
                 writes=[phk])
            k.op("scalar", lambda s, ic=ic, ph=ph: s.activation(
                U7, ph[:], AF.Identity, bias=B1C[:, ic:ic + 1]),
                 reads=[phk, "b1c7"], writes=["u7", "AUX:0", "AUX:1"])
            k.op("scalar", lambda s: s.activation(S17, U7, AF.Square),
                 reads=["u7"], writes=["s17", "AUX:2", "AUX:3"])
            k.op("vector", lambda v: v.tensor_mul(S17, S17, U7),
                 reads=["s17", "u7"], writes=["s17"])
            k.op("vector", lambda v: v.scalar_tensor_tensor(
                S17, S17, GC2, U7, ALU.mult, ALU.add),
                 reads=["s17", "u7"], writes=["s17"])
            k.op("scalar", lambda s: s.activation(S17, S17, AF.Tanh, scale=GC1),
                 reads=["s17"], writes=["s17"])
            k.op("vector", lambda v: v.tensor_scalar(S17, S17, 0.5, 0.5,
                                                     ALU.mult, ALU.add),
                 reads=["s17"], writes=["s17"])
            k.op("vector", lambda v, ic=ic: v.tensor_tensor(
                b3_[:, ic * 512:(ic + 1) * 512], S17, U7, ALU.mult),
                 reads=["s17", "u7"], writes=["h7"])
        k.op("tensor", lambda t, tb=tb: t.matmul(
            PP[6][:], C.ones_row[:, :],
            WSR[:, tb * 512:(tb + 1) * 512], start=True, stop=True),
             reads=["wsr", "c_onesr"], writes=["pp6"])
        for dc in range(DC):
            ppi = dc % 2
            for ih in range(2):
                slab = scr[:, :]
                slks = ["SCR:0", "SCR:1", "SCR:2", "SCR:3"]
                k.dma("qg", slab.rearrange("p (a m) -> p a m", a=16),
                      ins["l3f_w2"][ih * 2048:(ih + 1) * 2048,
                                    dc * 128:(dc + 1) * 128]
                      .rearrange("(a p) m -> p a m", p=128),
                      writes=slks, stream="w0")
                def mmy(t, dc=dc, ih=ih, ppi=ppi):
                    for a in range(16):
                        ic = ih * 16 + a
                        last = t.matmul(PP[ppi][:], scr[:, a * 128:(a + 1) * 128],
                                        b3_[:, ic * 512:(ic + 1) * 512],
                                        start=(ic == 0), stop=(ic == 31))
                    return last
                k.op("tensor", mmy, reads=slks + ["h7"], writes=[f"pp{ppi}"])
            stg = aux[:, 2048:3072].bitcast(F32)
            stk = "stg7"
            k.op("scalar", lambda s, stg=stg, dc=dc, ppi=ppi: s.activation(
                stg, PP[ppi][:], AF.Identity, bias=B2C[:, dc:dc + 1]),
                 reads=[f"pp{ppi}", "b2c7"], writes=[stk])
            k.op("vector", lambda v, stg=stg: v.tensor_tensor(stg, stg, PP[6][:],
                                                              ALU.mult),
                 reads=[stk, "pp6"], writes=[stk])
            k.dma("qg", dst_dram[dc * 128:(dc + 1) * 128, tb * 512:(tb + 1) * 512],
                  stg, reads=[stk, dst_key(dc, tb)], writes=[dst_key(dc, tb)],
                  accum_op=ALU.add, stream="acc0")


_WA = {
    "l0a_wqT": [1024, 1024], "l0a_wkT": [1024, 1024], "l0a_wvT": [1024, 1024],
    "l0a_woT": [1024, 1024], "l0a_bo": [128, 8],
    "l0f_wgredT": [1024, 16], "l0f_wg": [8, 16], "l0f_w1": [8, 1024, 4096],
    "l0f_b1": [8, 4096], "l0f_w2": [8, 4096, 1024], "l0f_b2": [8, 1024],
    "l1a_wq": [1024, 1024], "l1a_wk": [1024, 1024], "l1a_wv": [1024, 1024],
    "l1a_bq": [128, 8], "l1a_bk": [128, 8], "l1a_bv": [1, 1024],
    "l1f_gwT": [1024, 8], "l1f_wgT": [8, 1024, 4096],
    "l1f_wuT": [8, 1024, 4096], "l1f_wdT": [8, 4096, 1024],
    "l2a_inprojT": [1024, 4096], "l2a_convw": [128, 64], "l2a_convb": [128, 16],
    "l2a_xprojT": [2048, 33], "l2a_alog": [128, 256], "l2a_dcol": [128, 16],
    "l2a_outprojT": [2048, 1024],
    "l2f_wqkv": [1024, 3072], "l2f_qg": [128, 8],
    "cst_repc": [2048, 128], "cst_summ": [2048, 128], "cst_repb": [16, 128],
}
_WB = {
    "l2f_wqkv": [1024, 3072], "l2f_qg": [128, 8], "l2f_wproj": [1024, 1024],
    "l2f_bproj": [128, 8],
    "l3a_inprojT": [1024, 4096], "l3a_convw": [128, 64], "l3a_convb": [128, 16],
    "l3a_xprojT": [2048, 96], "l3a_dtprojT": [64, 2048], "l3a_dtb": [128, 16],
    "l3a_alog": [128, 256], "l3a_dcol": [128, 16], "l3a_outprojT": [2048, 1024],
    "l3f_wrc": [128, 8], "l3f_b1c": [128, 32], "l3f_b2c": [128, 8],
    "l3f_w1": [1024, 4096], "l3f_w2": [4096, 1024],
    "cst_repc": [2048, 128], "cst_summ": [2048, 128], "cst_repb": [16, 128],
}


def _init_res(k, out, xin, dst_key):
    for c in range(DC):
        for j in range(TJ):
            k.dma("qs", out[c * 128:(c + 1) * 128, j * 512:(j + 1) * 512],
                  xin[c * 128:(c + 1) * 128, j * 512:(j + 1) * 512],
                  writes=[dst_key(c, j)], stream="init")


def _build_a():
    nc = bass.Bass("TRN2", target_bir_lowering=False)
    xin = nc.dram_tensor("xin", [1024, 4096], F32, kind="ExternalInput").ap()
    ins = {nm: nc.dram_tensor(nm, shp, F32, kind="ExternalInput").ap()
           for nm, shp in _WA.items()}
    out = nc.dram_tensor("out", [1024, 4096], F32, kind="ExternalOutput").ap()
    mm = nc.dram_tensor("mm", [1, 2], F32, kind="ExternalOutput").ap()
    k = Kfw(nc)
    bf = Bufs(k)
    C = Consts(k)
    dst_key = lambda c, j: f"xres/{c}/{j}"
    _init_res(k, out, xin, dst_key)
    xnd = k.dram("xnd", [1024, 4096], BF16)
    emit_ln(k, C, bf, out, dst_key, dst_i=0, tag="n0", hold_i=1)
    emit_l0_attn(k, C, bf, ins, out, dst_key, tag="a0")
    emit_ln(k, C, bf, out, dst_key, dst_i=0, tag="n1", hold_i=1, dram_out=xnd)
    emit_l0_ffn(k, C, bf, ins, xnd, out, dst_key, tag="f0")
    emit_ln(k, C, bf, out, dst_key, dst_i=0, tag="n2", hold_i=1)
    emit_l1_attn(k, C, bf, ins, None, out, dst_key, tag="a1")
    emit_ln(k, C, bf, out, dst_key, dst_i=0, tag="n3", hold_i=1)
    emit_l1_ffn(k, C, bf, ins, out, dst_key, tag="f1")
    emit_ln(k, C, bf, out, dst_key, dst_i=0, tag="n4", hold_i=1)
    emit_l2_attn(k, C, bf, ins, out, dst_key, tag="a2")
    emit_ln(k, C, bf, out, dst_key, dst_i=0, tag="n5", hold_i=1)
    emit_l2_ffn(k, C, bf, ins, None, dst_key, 1, tag="f2", sim_only=True,
                mm_out=mm)
    k.build()
    return nc


def _build_b():
    nc = bass.Bass("TRN2", target_bir_lowering=False)
    xin = nc.dram_tensor("xin", [1024, 4096], F32, kind="ExternalInput").ap()
    mmall = nc.dram_tensor("mmall", [1, 16], F32, kind="ExternalInput").ap()
    ins = {nm: nc.dram_tensor(nm, shp, F32, kind="ExternalInput").ap()
           for nm, shp in _WB.items()}
    out = nc.dram_tensor("out", [1024, 4096], F32, kind="ExternalOutput").ap()
    k = Kfw(nc)
    bf = Bufs(k)
    C = Consts(k)
    dst_key = lambda c, j: f"xres/{c}/{j}"
    _init_res(k, out, xin, dst_key)
    emit_ln(k, C, bf, out, dst_key, dst_i=0, tag="n5", hold_i=1)
    emit_l2_ffn(k, C, bf, ins, out, dst_key, 1, tag="f2", mm_in=mmall)
    emit_ln(k, C, bf, out, dst_key, dst_i=0, tag="n6", hold_i=1)
    emit_l3_attn(k, C, bf, ins, out, dst_key, tag="a3")
    emit_ln(k, C, bf, out, dst_key, dst_i=0, tag="n7", hold_i=1)
    emit_l3_ffn(k, C, bf, ins, out, dst_key, tag="f3")
    k.build()
    return nc


def _cw_stage(conv_w):
    cw = np.asarray(conv_w, np.float32)[:, 0, :]
    return np.ascontiguousarray(
        cw.reshape(16, 128, 4).transpose(1, 0, 2).reshape(128, 64))


def _alog_stage(a_log):
    return np.ascontiguousarray(
        np.asarray(a_log, np.float32).reshape(-1).reshape(256, 128).T)


def _stage_weights(I):
    cst = mk_scan_consts()
    g_info = np.asarray(I["L2_ffn_g_info"], np.float32)
    wA = {
        "l0a_wqT": _t(I["L0_attn_Wq"]), "l0a_wkT": _t(I["L0_attn_Wk"]),
        "l0a_wvT": _t(I["L0_attn_Wv"]), "l0a_woT": _t(I["L0_attn_Wo"]),
        "l0a_bo": _col128(I["L0_attn_bo"]),
        "l0f_wgredT": _t(I["L0_ffn_wg_red"]), "l0f_wg": _c(I["L0_ffn_wg"]),
        "l0f_w1": _c(I["L0_ffn_w1"]), "l0f_b1": _c(I["L0_ffn_b1"]),
        "l0f_w2": _c(I["L0_ffn_w2"]), "l0f_b2": _c(I["L0_ffn_b2"]),
        "l1a_wq": _c(I["L1_attn_Wq"]), "l1a_wk": _c(I["L1_attn_Wk"]),
        "l1a_wv": _c(I["L1_attn_Wv"]), "l1a_bq": _col128(I["L1_attn_bq"]),
        "l1a_bk": _col128(I["L1_attn_bk"]),
        "l1a_bv": np.ascontiguousarray(
            np.asarray(I["L1_attn_bv"], np.float32)[None, :]),
        "l1f_gwT": _t(I["L1_ffn_gate_w"]),
        "l1f_wgT": np.ascontiguousarray(
            np.asarray(I["L1_ffn_w_gate"], np.float32).transpose(0, 2, 1)),
        "l1f_wuT": np.ascontiguousarray(
            np.asarray(I["L1_ffn_w_up"], np.float32).transpose(0, 2, 1)),
        "l1f_wdT": np.ascontiguousarray(
            np.asarray(I["L1_ffn_w_down"], np.float32).transpose(0, 2, 1)),
        "l2a_inprojT": _t(I["L2_attn_in_proj_w"]),
        "l2a_convw": _cw_stage(I["L2_attn_conv_w"]),
        "l2a_convb": _col128(I["L2_attn_conv_b"]),
        "l2a_xprojT": _t(I["L2_attn_x_proj_w"]),
        "l2a_alog": _alog_stage(I["L2_attn_A_log"]),
        "l2a_dcol": _col128(I["L2_attn_D_param"]),
        "l2a_outprojT": _t(I["L2_attn_out_proj_w"]),
        "l2f_wqkv": _c(I["L2_ffn_W_qkv"]), "l2f_qg": _col128(g_info[0, 0]),
    }
    wA.update(cst)
    wB = {
        "l2f_wqkv": wA["l2f_wqkv"], "l2f_qg": wA["l2f_qg"],
        "l2f_wproj": _c(I["L2_ffn_W_proj"]),
        "l2f_bproj": _col128(I["L2_ffn_b_proj"]),
        "l3a_inprojT": _t(I["L3_attn_in_proj_w"]),
        "l3a_convw": _cw_stage(I["L3_attn_conv_w"]),
        "l3a_convb": _col128(I["L3_attn_conv_b"]),
        "l3a_xprojT": _t(I["L3_attn_x_proj_w"]),
        "l3a_dtprojT": _t(I["L3_attn_dt_proj_w"]),
        "l3a_dtb": _col128(I["L3_attn_dt_proj_b"]),
        "l3a_alog": _alog_stage(I["L3_attn_A_log"]),
        "l3a_dcol": _col128(I["L3_attn_D"]),
        "l3a_outprojT": _t(I["L3_attn_out_proj_w"]),
        "l3f_wrc": _col128(I["L3_ffn_w_router"]),
        "l3f_b1c": _col128(I["L3_ffn_b1"]),
        "l3f_b2c": _col128(I["L3_ffn_b2"]),
        "l3f_w1": _c(I["L3_ffn_W1"]), "l3f_w2": _c(I["L3_ffn_W2"]),
    }
    wB.update(cst)
    return wA, wB


def kernel(**inputs):
    x = np.asarray(inputs["L0_attn_x"], np.float32)
    wA, wB = _stage_weights(inputs)
    xfm = [np.ascontiguousarray(
        x[2 * c:2 * c + 2].transpose(2, 0, 1).reshape(1024, 4096))
        for c in range(NCORES)]
    nc_a = _build_a()
    res_a = run_bass_kernel_spmd(
        nc_a, [dict(xin=xfm[c], **wA) for c in range(NCORES)],
        core_ids=list(range(NCORES))).results
    mmall = np.zeros((1, 16), np.float32)
    for c in range(NCORES):
        mm = np.asarray(res_a[c]["mm"], np.float32)
        mmall[0, c] = mm[0, 0]
        mmall[0, 8 + c] = mm[0, 1]
    nc_b = _build_b()
    res_b = run_bass_kernel_spmd(
        nc_b, [dict(xin=np.asarray(res_a[c]["out"], np.float32), mmall=mmall,
                    **wB) for c in range(NCORES)],
        core_ids=list(range(NCORES))).results
    out = np.empty((16, 2048, 1024), np.float32)
    for c in range(NCORES):
        out[2 * c:2 * c + 2] = np.asarray(res_b[c]["out"], np.float32) \
            .reshape(1024, 2, 2048).transpose(1, 2, 0)
    return out
